# version 07
# baseline (speedup 1.0000x reference)
import os
os.environ.setdefault("TRN_ALLOW_ICI_ALL_TO_ALL", "1")

import numpy as np
import ml_dtypes
import concourse.bass as bass
import concourse.mybir as mybir
from concourse.bass_utils import run_bass_kernel_spmd
from contextlib import ExitStack

F32 = mybir.dt.float32
BF16 = mybir.dt.bfloat16
AF = mybir.ActivationFunctionType
ALU = mybir.AluOpType
AX = mybir.AxisListType.X

MAGIC = float(np.float32(1.5 * 2 ** 23))
NCORES = 8
NT = 2048
D = 1024
FF = 4096
NTILE = NT // 128
NCH = D // 128
NFT = FF // 128


def _bf(x):
    return np.ascontiguousarray(np.asarray(x, np.float32)).astype(ml_dtypes.bfloat16)


def host_constants():
    k = np.arange(128)
    W1c = np.exp(-2j * np.pi * np.outer(k, k) / 128.0)
    F1r, F1i = W1c.real.astype(np.float32), W1c.imag.astype(np.float32)
    W2c = np.exp(+2j * np.pi * np.outer(k, k) / 128.0)
    F2r, F2i = W2c.real.astype(np.float32), W2c.imag.astype(np.float32)
    cR1h = np.concatenate([F1r[0:64, 0:65], F1i[0:64, 0:65]], axis=1)
    cG1 = F1r[0:64, :]
    cG2 = F1i[0:64, :]
    cG1d = np.concatenate([cG1, cG1], axis=0)
    cG2d = np.concatenate([cG2, cG2], axis=0)
    cR3 = np.concatenate([F2r, F2i], axis=1)
    cR4h = np.concatenate([-F2i, F2r], axis=1)
    a = np.arange(64)
    k65 = np.arange(65)
    alpha = np.where((k65 == 0) | (k65 == 64), 1.0, 2.0)[:, None].astype(np.float32)
    CO2 = (alpha * np.cos(2 * np.pi * np.outer(k65, a) / 128.0) / 16384.0).astype(np.float32)
    SI2n = (-alpha * np.sin(2 * np.pi * np.outer(k65, a) / 128.0) / 16384.0).astype(np.float32)
    CO2s = np.zeros((65, 64), np.float32); CO2s[:, 1:] = CO2[:, :-1]
    SI2ns = np.zeros((65, 64), np.float32); SI2ns[:, 1:] = SI2n[:, :-1]
    pos = np.arange(4096, dtype=np.float32)[:, None]
    div = np.exp(np.arange(0, 64, 2, dtype=np.float32) * -(np.log(10000.0) / 64.0))
    pe = np.zeros((4096, 64), np.float32)
    pe[:, 0::2] = np.sin(pos * div)
    pe[:, 1::2] = np.cos(pos * div)
    dec = np.exp(-0.01 * np.arange(4096, dtype=np.float32)).reshape(1, 4096)
    return dict(
        cR1h=_bf(cR1h), cG1d=_bf(cG1d), cG2d=_bf(cG2d), cR3=_bf(cR3), cR4h=_bf(cR4h),
        cCO2=_bf(CO2), cSI2n=_bf(SI2n), cCO2s=_bf(CO2s), cSI2ns=_bf(SI2ns),
        cG1f=np.ascontiguousarray(cG1.astype(np.float32)),
        cG2f=np.ascontiguousarray(cG2.astype(np.float32)),
        cDec=np.ascontiguousarray(dec.astype(np.float32)),
        cI128=np.eye(128, dtype=np.float32),
        cOnes=np.ones((1, 128), np.float32),
    )


CONSTS = host_constants()


def build_nc():
    nc = bass.Bass("TRN2", target_bir_lowering=False, debug=False, detect_race_conditions=False)
    DI = lambda nm, s, dt=F32: nc.dram_tensor(nm, s, dt, kind="ExternalInput").ap()
    x_in = DI("x_in", [NT, D])
    Wu_d = DI("Wu", [D, D]); Wv_d = DI("Wv", [D, D]); Wo_d = DI("Wo", [D, D])
    bu_d = DI("bu", [D]); bv_d = DI("bv", [D]); bo_d = DI("bo", [D])
    W1_d = DI("W1", [64, 64]); W2_d = DI("W2", [64, 64])
    b1f_d = DI("b1f", [64]); b2f_d = DI("b2f", [64])
    W3s_d = DI("W3s", [64, 128]); b3s_d = DI("b3s", [128])
    peT_d = DI("peT", [64, 4096])
    w1T_d = DI("w1T", [D, FF]); w1n_d = DI("w1n", [FF, D]); b1v_d = DI("b1v", [FF])
    w2T_d = DI("w2T", [FF, D]); w2n_d = DI("w2n", [D, FF]); b2v_d = DI("b2v", [D])
    cns = {k: DI("d" + k, list(v.shape), BF16 if v.dtype == ml_dtypes.bfloat16 else F32)
           for k, v in CONSTS.items()}
    y_out = nc.dram_tensor("y_out", [NT, D], F32, kind="ExternalOutput").ap()
    DBG = os.environ.get("ATH_DEBUG", "") == "1"
    if DBG:
        dbg_oA1 = nc.dram_tensor("dbg_oA1", [NCORES, 128, NT], BF16, kind="ExternalOutput").ap()
        dbg_iA2 = nc.dram_tensor("dbg_iA2", [NCORES, 128, NT], BF16, kind="ExternalOutput").ap()
        dbg_B8 = nc.dram_tensor("dbg_B8", [128, NTILE * D], BF16, kind="ExternalOutput").ap()
        dbg_qx = nc.dram_tensor("dbg_qx", [128, NCH * NT], BF16, kind="ExternalOutput").ap()
        dbg_gsc = nc.dram_tensor("dbg_gsc", [1, 8], F32, kind="ExternalOutput").ap()
        dbg_hdr = nc.dram_tensor("dbg_hdr", [2, 128, NT], BF16, kind="ExternalOutput").ap()
    iA1 = nc.dram_tensor("iA1", [NCORES, 128, NT], BF16).ap()
    oA1 = nc.dram_tensor("oA1", [NCORES, 128, NT], BF16).ap()
    iA2 = nc.dram_tensor("iA2", [NCORES, 128, NT], BF16).ap()
    oA2 = nc.dram_tensor("oA2", [NCORES, 128, NT], BF16).ap()
    mxi1 = nc.dram_tensor("mxi1", [1, 1], F32).ap()
    mxo1 = nc.dram_tensor("mxo1", [1, 1], F32).ap()
    mxi2 = nc.dram_tensor("mxi2", [1, 1], F32).ap()
    mxo2 = nc.dram_tensor("mxo2", [1, 1], F32).ap()
    hdr = nc.dram_tensor("hdr", [NFT, 128, NT], BF16).ap()
    qw2d = nc.dram_tensor("qw2d", [NCH, 128, NFT * 128], BF16).ap()
    hscr = nc.dram_tensor("hscr", [128, 4096], BF16).ap()
    x2lo = nc.dram_tensor("x2lo", [NT, D], BF16).ap()

    with ExitStack() as ctx:
        def sbuf(nm, shape, dt=BF16, scope=None):
            return (scope or ctx).enter_context(nc.sbuf_tensor(nm, shape, dt))

        def psum(nm):
            return ctx.enter_context(nc.psum_tensor(nm, [128, 512], F32))

        def sem(nm):
            return ctx.enter_context(nc.semaphore(nm))

        B8 = sbuf("B8", [128, NTILE * D])
        BA = sbuf("BA", [128, NCH * NT])
        BB = sbuf("BB", [128, NCH * NT])
        BC = sbuf("BC", [128, NCH * NT])
        hstg = sbuf("hstg", [128, NT], F32)
        stage = sbuf("stage", [128, 1024], F32)
        lnst = sbuf("lnst", [128, 1024])
        sb_c = {k: sbuf("s" + k, list(v.shape),
                        BF16 if v.dtype == ml_dtypes.bfloat16 else F32)
                for k, v in CONSTS.items() if k != "cDec"}
        bias_u = sbuf("bias_u", [128, NCH], F32)
        bias_v = sbuf("bias_v", [128, NCH], F32)
        bias_o = sbuf("bias_o", [128, NCH], F32)
        b3sb = sbuf("b3sb", [128, 1], F32)
        b1fb = sbuf("b1fb", [64, 1], F32)
        b2fb = sbuf("b2fb", [64, 1], F32)
        b1t = sbuf("b1t", [128, NFT], F32)
        b2t = sbuf("b2t", [128, NCH], F32)
        rw1 = sbuf("rw1", [128, NFT], F32)
        rw2 = sbuf("rw2", [128, NCH], F32)
        sw1 = sbuf("sw1", [128, NFT], F32)
        sw2 = sbuf("sw2", [128, NCH], F32)
        sxw1 = sbuf("sxw1", [128, NFT], F32)
        bq1 = sbuf("bq1", [128, NFT], F32)
        exv1 = sbuf("exv1", [128, NFT], F32)
        shw2 = sbuf("shw2", [128, NCH], F32)
        bq2 = sbuf("bq2", [128, NCH], F32)
        exv2 = sbuf("exv2", [128, NCH], F32)
        mstats = sbuf("mstats", [128, 3 * NTILE], F32)
        bnb = sbuf("bnb", [128, 24], F32)
        agg = sbuf("agg", [128, 16], F32)
        lmax = sbuf("lmax", [128, 2], F32)
        gsc = sbuf("gsc", [1, 8], F32)
        rxb = sbuf("rxb", [128, 4], F32)
        lob = sbuf("lob", [128, 1024], BF16)
        W1sb = sbuf("W1sb", [64, 64], F32)
        W2sb = sbuf("W2sb", [64, 64], F32)
        W3sb = sbuf("W3sb", [64, 128], F32)
        psA = psum("psA"); psB = psum("psB"); psC = psum("psC"); psD = psum("psD")
        psE = psum("psE"); psF = psum("psF"); psG = psum("psG"); psH = psum("psH")
        PS4 = [psA, psB, psC, psD]
        SEMNAMES = ["CC", "MT", "MV", "MS", "MG",
                    "Gx", "Gw", "Gv0", "Gv1", "Ga", "Gm", "Gh", "Gq",
                    "Lc", "Lp", "Ls", "Lq2", "Lw0", "Lw1", "Lh", "Lq",
                    "Tln", "Tzh", "Tz0", "Tz1", "Ty", "Trow", "Tat", "Tqx", "Tq", "Tout",
                    "O0", "O1"]
        SEMS = {nm: sem(nm) for nm in SEMNAMES}
        attn_sc = ExitStack()
        pe_sc = ExitStack()
        conv_sc = ExitStack()
        ffnA_sc = ExitStack()
        ffnB_sc = ExitStack()
        block = ctx.enter_context(nc.Block())

        n = {nm: 0 for nm in SEMNAMES}
        streams = {e: [] for e in ("tensor", "vector", "scalar", "sync", "gpsimd")}

        class _SkipRest(Exception):
            pass

        _STOP = os.environ.get("ATH_STOP_PHASE", "")

        def ckpt(name):
            if _STOP == name:
                raise _SkipRest()

        def dma(eng, sm, out, in_, slow=False):
            n[sm] += 16
            hs = SEMS[sm]
            streams[eng].append(lambda e, out=out, in_=in_, hs=hs, slow=slow:
                                e.dma_start(out=out, in_=in_,
                                            allow_slow_non_contiguous=slow).then_inc(hs, 16))
            return n[sm]

        def dmat(eng, sm, out, in_):
            n[sm] += 16
            hs = SEMS[sm]
            streams[eng].append(lambda e, out=out, in_=in_, hs=hs:
                                e.dma_start_transpose(out, in_).then_inc(hs, 16))
            return n[sm]

        def wait(eng, sm, v):
            if v <= 0:
                return
            hs = SEMS[sm]
            streams[eng].append(lambda e, hs=hs, v=v: e.wait_ge(hs, v))

        def instr(eng, f, inc=None):
            if inc is None:
                streams[eng].append(f)
                return None
            n[inc] += 1
            hs = SEMS[inc]
            streams[eng].append(lambda e, f=f, hs=hs: f(e).then_inc(hs, 1))
            return n[inc]

        BAv = BA[:, :].rearrange("p (c q) -> p c q", c=NCH)

        try:
            x_tiled = x_in.rearrange("(i p) d -> i p d", p=128)
            xld = [dma("gpsimd", "Gx", B8[:, i * D:(i + 1) * D], x_tiled[i]) for i in range(NTILE)]
            wbuf = sbuf("wbuf", [128, NCH * D], scope=attn_sc)
            wsc = sbuf("wsc", [128, 1024], F32, scope=attn_sc)
            wu_ld = dma("gpsimd", "Gw", wbuf[:, :].rearrange("p (c d) -> p c d", c=NCH),
                        Wu_d.rearrange("(c p) d -> p c d", p=128))
            cld = {k: dma("scalar", "Lc", sb_c[k][:, :], cns[k][:, :]) for k in CONSTS
                   if k != "cDec"}
            dma("scalar", "Lc", bias_u[:, :], bu_d.rearrange("(c p) -> p c", p=128), slow=True)
            dma("scalar", "Lc", bias_v[:, :], bv_d.rearrange("(c p) -> p c", p=128), slow=True)
            v_bo = dma("scalar", "Lc", bias_o[:, :], bo_d.rearrange("(c p) -> p c", p=128), slow=True)
            dma("scalar", "Lc", b3sb[:, :], b3s_d.rearrange("(p o) -> p o", o=1))
            dma("scalar", "Lc", b1fb[:, :], b1f_d.rearrange("(p o) -> p o", o=1))
            dma("scalar", "Lc", b2fb[:, :], b2f_d.rearrange("(p o) -> p o", o=1))
            dma("scalar", "Lc", b1t[:, :], b1v_d.rearrange("(t p) -> p t", p=128), slow=True)
            v_b2t = dma("scalar", "Lc", b2t[:, :], b2v_d.rearrange("(t p) -> p t", p=128), slow=True)
            dma("scalar", "Lc", W1sb[:, :], W1_d[:, :])
            dma("scalar", "Lc", W2sb[:, :], W2_d[:, :])
            lw3 = dma("scalar", "Lc", W3sb[:, :], W3s_d[:, :])

            def emit_ln_stats_pair(xtA, xtB):
                chains = []
                for (xt, c, b) in ((xtA, 0, 0), (xtB, 8, 12)):
                    chains.append([
                        lambda v, xt=xt, b=b: v.bn_stats(bnb[:, b:b + 6], xt[:, 0:512]),
                        lambda v, xt=xt, b=b: v.bn_stats(bnb[:, b + 6:b + 12], xt[:, 512:1024]),
                        lambda v, c=c, b=b: v.reduce_sum(agg[:, c:c + 1], bnb[:, b + 1:b + 12:3], axis=AX),
                        lambda v, c=c, b=b: v.tensor_tensor(agg[:, c + 2:c + 6], bnb[:, b + 1:b + 12:3], bnb[:, b + 1:b + 12:3], ALU.mult),
                        lambda v, c=c: v.reduce_sum(agg[:, c + 1:c + 2], agg[:, c + 2:c + 6], axis=AX),
                        lambda v, c=c, b=b: v.reduce_sum(agg[:, c + 6:c + 7], bnb[:, b + 2:b + 12:3], axis=AX),
                        lambda v, c=c: v.tensor_scalar(agg[:, c:c + 1], agg[:, c:c + 1], 0.25, None, ALU.mult),
                        lambda v, c=c: v.tensor_tensor(agg[:, c + 7:c + 8], agg[:, c:c + 1], agg[:, c:c + 1], ALU.mult),
                        lambda v, c=c: v.tensor_scalar(agg[:, c + 6:c + 7], agg[:, c + 6:c + 7], 1.0 / 1024.0, None, ALU.mult),
                        lambda v, c=c: v.tensor_scalar(agg[:, c + 1:c + 2], agg[:, c + 1:c + 2], 0.25, None, ALU.mult),
                        lambda v, c=c: v.tensor_tensor(agg[:, c + 6:c + 7], agg[:, c + 6:c + 7], agg[:, c + 1:c + 2], ALU.add),
                        lambda v, c=c: v.tensor_tensor(agg[:, c + 6:c + 7], agg[:, c + 6:c + 7], agg[:, c + 7:c + 8], ALU.subtract),
                        lambda v, c=c: v.tensor_scalar(agg[:, c + 6:c + 7], agg[:, c + 6:c + 7], 1e-5, None, ALU.add),
                        lambda v, c=c: v.reciprocal(agg[:, c + 1:c + 2], agg[:, c + 6:c + 7]),
                    ])
                last = None
                for k in range(len(chains[0])):
                    for t in range(2):
                        last = instr("vector", chains[t][k],
                                     inc=("MV" if k == len(chains[0]) - 1 else None))
                return last

            lnst3 = lnst[:, :].rearrange("p (c q) -> p c q", q=128)
            stg3 = lob[:, :].rearrange("p (c q) -> p c q", q=128)
            BAt4 = BA[:, :].rearrange("p (c i q) -> p c i q", c=NCH, i=NTILE)
            prev_tt = 0
            for pi in range(NTILE // 2):
                iA, iB = 2 * pi, 2 * pi + 1
                xtA = B8[:, iA * D:(iA + 1) * D]
                xtB = B8[:, iB * D:(iB + 1) * D]
                wait("vector", "Gx", xld[-1])
                if pi > 0:
                    wait("vector", "Tln", prev_tt)
                vrec = emit_ln_stats_pair(xtA, xtB)
                wait("scalar", "MV", vrec)
                instr("scalar", lambda s: s.activation(agg[:, 2:3], agg[:, 1:2], AF.Sqrt))
                vsq = instr("scalar", lambda s: s.activation(agg[:, 10:11], agg[:, 9:10], AF.Sqrt), inc="MS")
                wait("vector", "MS", vsq)
                vnA = instr("vector", lambda v, xtA=xtA: v.tensor_scalar(
                    lnst[:, :], xtA, agg[:, 0:1], agg[:, 2:3], ALU.subtract, ALU.mult), inc="MV")
                vnB = instr("vector", lambda v, xtB=xtB: v.tensor_scalar(
                    lob[:, :], xtB, agg[:, 8:9], agg[:, 10:11], ALU.subtract, ALU.mult), inc="MV")
                wait("sync", "MV", vnA)
                dmat("sync", "Tln", BAt4[:, :, iA, :], lnst3)
                wait("sync", "MV", vnB)
                prev_tt = dmat("sync", "Tln", BAt4[:, :, iB, :], stg3)
            lnT_all = prev_tt
            ckpt("p1")

            def emit_proj(evac):
                prev_ev = 0
                for m in range(NCH):
                    if prev_ev:
                        wait("tensor", evac_sem, prev_ev)
                    for c in range(NCH):
                        for s in range(4):
                            instr("tensor", lambda t, m=m, c=c, s=s: t.matmul(
                                PS4[s][:, 0:512],
                                wbuf[:, c * D + m * 128: c * D + (m + 1) * 128],
                                BAv[:, c, s * 512:(s + 1) * 512],
                                start=(c == 0), stop=(c == NCH - 1)),
                                inc=("MT" if c == NCH - 1 else None))
                    prev_ev = evac(m, n["MT"])
                return prev_ev

            wait("tensor", "Tln", lnT_all)
            wait("tensor", "Gw", wu_ld)
            wait("vector", "Lc", n["Lc"])
            evac_sem = "MV"

            def evac_u(m, mtv):
                last = None
                for s in range(4):
                    wait("vector", "MT", mtv)
                    last = instr("vector", lambda v, m=m, s=s: v.tensor_scalar(
                        BB[:, m * NT + s * 512: m * NT + (s + 1) * 512],
                        PS4[s][:, 0:512], bias_u[:, m:m + 1], None, ALU.add), inc="MV")
                return last

            emit_proj(evac_u)
            u_mm_done = n["MT"]
            u_done = n["MV"]
            ckpt("p2u")

            wait("gpsimd", "MT", u_mm_done)
            wv_ld = dma("gpsimd", "Gw", wbuf[:, :].rearrange("p (c d) -> p c d", c=NCH),
                        Wv_d.rearrange("(c p) d -> p c d", p=128))
            wait("tensor", "Gw", wv_ld)

            vst = [0, 0]

            def evac_v(m, mtv):
                for s in range(4):
                    half = s % 2
                    wait("scalar", "MT", mtv)
                    wait("scalar", "Gv0" if half == 0 else "Gv1", vst[half])
                    vv = instr("scalar", lambda sc, m=m, s=s, half=half: sc.activation(
                        stage[:, half * 512:(half + 1) * 512], PS4[s][:, 0:512], AF.Identity,
                        bias=bias_v[:, m:m + 1], scale=1.0), inc="MS")
                    wait("gpsimd", "MS", vv)
                    vst[half] = dma("gpsimd", "Gv0" if half == 0 else "Gv1",
                                    iA1[m, :, s * 512:(s + 1) * 512],
                                    stage[:, half * 512:(half + 1) * 512])
                return None

            evac_sem = "MS"

            def evac_v_wrap(m, mtv):
                evac_v(m, mtv)
                return n["MS"]

            wait("scalar", "Lc", n["Lc"])
            emit_proj(evac_v_wrap)
            v_mm_done = n["MT"]
            v_store_done = None
            ckpt("p2")

            wait("gpsimd", "Gv0", vst[0])
            wait("gpsimd", "Gv1", vst[1])
            a2a1 = instr("gpsimd", lambda g: g.collective_compute(
                "AllToAll", ALU.bypass, replica_groups=[list(range(NCORES))],
                ins=[iA1.tensor.ap().opt()], outs=[oA1.tensor.ap().opt()]), inc="CC")
            wait("gpsimd", "CC", a2a1)

            peb = sbuf("peb", [64, 2 * 512], F32, scope=pe_sc)
            decb = sbuf("decb", [1, 4096], F32, scope=pe_sc)
            dma("scalar", "Lc", decb[0:1, :], cns["cDec"][0:1, :])
            g1s = BA[:, 0:8192].bitcast(F32)
            g2s = BA[:, 8192:16384].bitcast(F32)

            wait("tensor", "Lc", n["Lc"])
            prev_ms = 0
            pe_mt = {}
            for s8 in range(8):
                if s8 >= 1:
                    wait("scalar", "MT", pe_mt[s8 - 1])
                lv = dma("scalar", "Lp", peb[0:64, (s8 % 2) * 512:(s8 % 2) * 512 + 512],
                         peT_d[:, s8 * 512:(s8 + 1) * 512])
                wait("tensor", "Lp", lv)
                if prev_ms:
                    wait("tensor", "MS", prev_ms)
                vm = instr("tensor", lambda t, s8=s8: t.matmul(
                    psE[0:64, 0:512], W1sb[0:64, :],
                    peb[0:64, (s8 % 2) * 512:(s8 % 2) * 512 + 512],
                    start=True, stop=True), inc="MT")
                pe_mt[s8] = vm
                wait("scalar", "MT", vm)
                prev_ms = instr("scalar", lambda sc, s8=s8: sc.activation(
                    g1s[0:64, s8 * 512:(s8 + 1) * 512], psE[0:64, 0:512], AF.Identity,
                    bias=b1fb[0:64, 0:1], scale=1.0), inc="MS")

            def emit_gelu(buf):
                last = None
                for hh in range(4):
                    src = buf[0:64, hh * 1024:(hh + 1) * 1024]
                    t1 = hstg[0:64, 0:1024]
                    t2 = hstg[0:64, 1024:2048]
                    vq = instr("scalar", lambda s, src=src, t1=t1: s.activation(t1, src, AF.Square), inc="MS")
                    wait("vector", "MS", vq)
                    for a, b in ((0, 512), (512, 1024)):
                        instr("vector", lambda v, t1=t1, a=a, b=b: v.tensor_scalar(
                            t1[:, a:b], t1[:, a:b], 0.044715, 1.0, ALU.mult, ALU.add))
                    vw = None
                    for a, b in ((0, 512), (512, 1024)):
                        vw = instr("vector", lambda v, t1=t1, src=src, a=a, b=b: v.tensor_tensor(
                            t1[:, a:b], t1[:, a:b], src[:, a:b], ALU.mult), inc="MV")
                    wait("scalar", "MV", vw)
                    vt = instr("scalar", lambda s, t1=t1, t2=t2: s.activation(
                        t2, t1, AF.Tanh, scale=0.7978845608028654), inc="MS")
                    wait("vector", "MS", vt)
                    for a, b in ((0, 512), (512, 1024)):
                        instr("vector", lambda v, t2=t2, a=a, b=b: v.tensor_scalar(
                            t2[:, a:b], t2[:, a:b], 1.0, 0.5, ALU.add, ALU.mult))
                    for a, b in ((0, 512), (512, 1024)):
                        last = instr("vector", lambda v, src=src, t2=t2, a=a, b=b: v.tensor_tensor(
                            src[:, a:b], t2[:, a:b], src[:, a:b], ALU.mult), inc="MV")
                return last

            wait("scalar", "MS", n["MS"])
            vg1 = emit_gelu(g1s)
            wait("tensor", "MV", vg1)
            prev_ms = 0
            for s8 in range(8):
                if prev_ms:
                    wait("tensor", "MS", prev_ms)
                vm = instr("tensor", lambda t, s8=s8: t.matmul(
                    psE[0:64, 0:512], W2sb[0:64, :], g1s[0:64, s8 * 512:(s8 + 1) * 512],
                    start=True, stop=True), inc="MT")
                wait("scalar", "MT", vm)
                prev_ms = instr("scalar", lambda sc, s8=s8: sc.activation(
                    g2s[0:64, s8 * 512:(s8 + 1) * 512], psE[0:64, 0:512], AF.Identity,
                    bias=b2fb[0:64, 0:1], scale=1.0), inc="MS")
            wait("scalar", "MS", n["MS"])
            vg2 = emit_gelu(g2s)
            wait("tensor", "MV", vg2)
            wait("vector", "Gv0", vst[0])
            wait("vector", "Gv1", vst[1])
            prev_hv = 0
            hst_prev = {}
            for s8 in range(8):
                par8 = s8 % 2
                if prev_hv:
                    wait("tensor", "MV", prev_hv)
                instr("tensor", lambda t, s8=s8: t.matmul(
                    psE[:, 0:512], W3sb[0:64, :], g2s[0:64, s8 * 512:(s8 + 1) * 512],
                    start=True, stop=True))
                vdm = instr("tensor", lambda t, s8=s8: t.matmul(
                    psF[:, 0:512], sb_c["cOnes"][0:1, :], decb[0:1, s8 * 512:(s8 + 1) * 512],
                    start=True, stop=True), inc="MT")
                wait("vector", "MT", vdm)
                if s8 >= 2:
                    wait("vector", "Gh", hst_prev[s8 - 2])
                for a, b in ((0, 256), (256, 512)):
                    instr("vector", lambda v, a=a, b=b: v.tensor_scalar(
                        stage[:, a:b], psE[:, a:b], b3sb[:, 0:1], None, ALU.add))
                for a, b in ((0, 256), (256, 512)):
                    prev_hv = instr("vector", lambda v, par8=par8, a=a, b=b: v.tensor_tensor(
                        hstg[:, par8 * 512 + a:par8 * 512 + b], stage[:, a:b], psF[:, a:b], ALU.mult), inc="MV")
                wait("gpsimd", "MV", prev_hv)
                hst_prev[s8] = dma("gpsimd", "Gh", hscr[:, s8 * 512:(s8 + 1) * 512],
                                   hstg[:, par8 * 512:(par8 + 1) * 512])
            h_done = hst_prev[7]
            ckpt("p3h")
            pe_sc.close()

            zh32 = sbuf("zh32", [64, 128 * 64], BF16, scope=conv_sc)
            zv = sbuf("zv", [64, 2 * 256], BF16, scope=conv_sc)
            at_b = sbuf("at_b", [128, 2 * 195], BF16, scope=conv_sc)
            ath32 = sbuf("ath32", [64, 195], F32, scope=conv_sc)
            hb32 = sbuf("hb32", [128, 2 * 260], F32, scope=conv_sc)
            vh32 = sbuf("vh32", [128, 2 * 260], F32, scope=conv_sc)
            tmpb = sbuf("tmpb", [128, 2 * 520], F32, scope=conv_sc)
            ypb = sbuf("ypb", [128, 2 * 260], BF16, scope=conv_sc)
            esb = sbuf("esb", [65, 2 * 1024], BF16, scope=conv_sc)
            ysb = sbuf("ysb", [64, 2 * 256], BF16, scope=conv_sc)

            wait("sync", "Gh", h_done)
            wait("sync", "MT", pe_mt[7])
            zh_all = dma("sync", "Tzh", zh32[:, :].rearrange("a (c b) -> a c b", b=64),
                         hscr.rearrange("c (a b) -> a c b", b=64))

            w1n_t = w1n_d.rearrange("(t p) d -> t p d", p=128)
            wait("scalar", "MT", pe_mt[7])
            for ft in range(NFT):
                lv = dma("scalar", "Ls", wsc[:, :], w1n_t[ft])
                wait("vector", "Ls", lv)
                vr = instr("vector", lambda v, ft=ft: v.reduce_max(
                    sw1[:, ft:ft + 1], wsc[:, :], axis=AX, apply_absolute_value=True), inc="MV")
                wait("scalar", "MV", vr)
            w2n_t = w2n_d.rearrange("(t p) f -> t p f", p=128)
            for ct in range(NCH):
                for pc in range(4):
                    lv = dma("scalar", "Ls", wsc[:, :], w2n_t[ct][:, pc * 1024:(pc + 1) * 1024])
                    wait("vector", "Ls", lv)
                    vr = instr("vector", lambda v, ct=ct, pc=pc: v.reduce_max(
                        mstats[:, 4 * ct + pc:4 * ct + pc + 1], wsc[:, :],
                        axis=AX, apply_absolute_value=True), inc="MV")
                    wait("scalar", "MV", vr)
            for ct in range(NCH):
                instr("vector", lambda v, ct=ct: v.reduce_max(
                    sw2[:, ct:ct + 1], mstats[:, 4 * ct:4 * ct + 4], axis=AX))
            instr("vector", lambda v: v.tensor_scalar(sw1[:, :], sw1[:, :], 1e-8, 1.0 / 127.0, ALU.max, ALU.mult))
            instr("vector", lambda v: v.tensor_scalar(sw2[:, :], sw2[:, :], 1e-8, 1.0 / 127.0, ALU.max, ALU.mult))
            vrw1 = instr("vector", lambda v: v.reciprocal(rw1[:, :], sw1[:, :]), inc="MV")
            vrw2 = instr("vector", lambda v: v.reciprocal(rw2[:, :], sw2[:, :]), inc="MV")

            conv_c_mt = {}
            yst = {}
            unit_cfree = {}
            unit_dfree = {}
            dh_mt = {}
            ad_mt = {}
            pr_mg = {}
            hbfree = {}
            yv_mv = {}
            ap_mt = {}
            e_mv = {}
            b_mt = {}
            y_mv = {}
            uc = 0
            pc_ = 0
            ps_ms0 = n["MS"]
            ps_mv0 = n["MV"]
            CP = [psA, psG]
            DPb = [psB, psH]
            DPa = [psC, psE]
            wait("tensor", "Tzh", zh_all)
            wait("tensor", "MS", ps_ms0)
            wait("tensor", "MV", ps_mv0)
            wait("sync", "CC", a2a1)
            for dl in range(int(os.environ.get("ATH_CONV_N", "128"))):
                par = dl % 2
                slot = dl % 2
                if dl >= 2:
                    wait("sync", "MT", conv_c_mt[dl - 2])
                zvv = zv[:, par * 256:(par + 1) * 256].rearrange("a (b q) -> a b q", b=4)
                for hi in range(2):
                    for b4 in range(4):
                        zvld = dma("sync", "Tz0" if par == 0 else "Tz1",
                                   zvv[hi * 32:(hi + 1) * 32, b4],
                                   oA1[2 * b4 + hi, dl].rearrange("(ah qq) -> ah qq", qq=64))
                cb, db = CP[uc % 2], DPb[uc % 2]
                if uc >= 2:
                    wait("tensor", *unit_cfree[uc - 2])
                elif uc == 0:
                    wait("tensor", "MS", ps_ms0)
                hcm = instr("tensor", lambda t, dl=dl, cb=cb: t.matmul(
                    cb[0:64, 0:130], zh32[:, dl * 64:(dl + 1) * 64], sb_c["cR1h"][:, :],
                    start=True, stop=True), inc="MT")
                wait("scalar", "MT", hcm)
                if dl >= 1:
                    wait("scalar", "MT", dh_mt[dl - 1])
                instr("scalar", lambda s, cb=cb: s.activation(
                    ath32[0:64, 65:195], cb[0:64, 0:130], AF.Identity))
                hae = instr("scalar", lambda s, cb=cb: s.activation(
                    ath32[0:64, 0:65], cb[0:64, 65:130], AF.Identity, scale=-1.0), inc="MS")
                unit_cfree[uc] = ("MS", hae)
                if uc >= 2:
                    wait("tensor", *unit_dfree[uc - 2])
                elif uc == 0:
                    wait("tensor", "MS", ps_ms0)
                wait("tensor", "MS", hae)
                instr("tensor", lambda t, db=db: t.matmul(
                    db[:, 0:130], sb_c["cG1f"][:, :], ath32[0:64, 65:195], start=True, stop=False))
                hdm = instr("tensor", lambda t, db=db: t.matmul(
                    db[:, 0:130], sb_c["cG2f"][:, :], ath32[0:64, 0:130], start=False, stop=True), inc="MT")
                dh_mt[dl] = hdm
                wait("scalar", "MT", hdm)
                if dl >= 2:
                    wait("scalar", "MG", hbfree[dl - 2])
                instr("scalar", lambda s, db=db, slot=slot: s.activation(
                    hb32[:, slot * 260:slot * 260 + 130], db[:, 0:130], AF.Identity))
                hEv = instr("scalar", lambda s, db=db, slot=slot: s.activation(
                    hb32[:, slot * 260 + 130:slot * 260 + 260], db[:, 0:130], AF.Identity), inc="MS")
                unit_dfree[uc] = ("MS", hEv)
                uc += 1
                for pp in range(2):
                    ps_ = pc_ % 2
                    cb, db, da = CP[uc % 2], DPb[uc % 2], DPa[uc % 2]
                    if uc >= 2:
                        wait("tensor", *unit_cfree[uc - 2])
                    wait("tensor", "Tz0" if par == 0 else "Tz1", zvld)
                    cm = instr("tensor", lambda t, cb=cb, par=par, pp=pp: t.matmul(
                        cb[0:128, 0:130], zv[:, par * 256 + pp * 128:par * 256 + (pp + 1) * 128],
                        sb_c["cR1h"][:, :], start=True, stop=True), inc="MT")
                    if pp == 1:
                        conv_c_mt[dl] = cm
                    wait("vector", "MT", cm)
                    if pc_ >= 2:
                        wait("vector", "MT", ad_mt[pc_ - 2])
                    instr("vector", lambda v, cb=cb, ps_=ps_: v.tensor_copy(
                        at_b[:, ps_ * 195 + 65:ps_ * 195 + 195], cb[:, 0:130]))
                    ae = instr("vector", lambda v, cb=cb, ps_=ps_: v.tensor_scalar(
                        at_b[:, ps_ * 195:ps_ * 195 + 65], cb[:, 65:130], -1.0, None, ALU.mult), inc="MV")
                    unit_cfree[uc] = ("MV", ae)
                    if uc >= 2:
                        wait("tensor", *unit_dfree[uc - 2])
                    wait("tensor", "MV", ae)
                    dm = None
                    for sg, pb in ((1, db), (0, da)):
                        instr("tensor", lambda t, pb=pb, ps_=ps_, sg=sg: t.matmul(
                            pb[:, 0:130], sb_c["cG1d"][sg * 64:sg * 64 + 64, :],
                            at_b[sg * 64:sg * 64 + 64, ps_ * 195 + 65:ps_ * 195 + 195],
                            start=True, stop=False))
                        dm = instr("tensor", lambda t, pb=pb, ps_=ps_, sg=sg: t.matmul(
                            pb[:, 0:130], sb_c["cG2d"][sg * 64:sg * 64 + 64, :],
                            at_b[sg * 64:sg * 64 + 64, ps_ * 195:ps_ * 195 + 130],
                            start=False, stop=True), inc="MT")
                    ad_mt[pc_] = dm
                    wait("scalar", "MT", dm)
                    if pc_ >= 2:
                        wait("scalar", "MG", pr_mg[pc_ - 2])
                    instr("scalar", lambda s, db=db, ps_=ps_: s.activation(
                        vh32[:, ps_ * 260:ps_ * 260 + 130], db[:, 0:130], AF.Identity))
                    vEv = instr("scalar", lambda s, da=da, ps_=ps_: s.activation(
                        vh32[:, ps_ * 260 + 130:ps_ * 260 + 260], da[:, 0:130], AF.Identity), inc="MS")
                    unit_dfree[uc] = ("MS", vEv)
                    uc += 1
                    wait("gpsimd", "MS", max(vEv, hEv))
                    if pc_ >= 2:
                        wait("gpsimd", "MV", yv_mv[pc_ - 2])
                    vv4 = vh32[:, ps_ * 260:(ps_ + 1) * 260].rearrange("p (s t c) -> p s t c", s=2, t=2)
                    hb4 = hb32[:, slot * 260:(slot + 1) * 260].rearrange("p (s t c) -> p s t c", s=2, t=2)
                    tp4 = tmpb[:, ps_ * 520:(ps_ + 1) * 520].rearrange("p (q s c) -> p q s c", q=4, s=2)
                    instr("gpsimd", lambda g, vv4=vv4, hb4=hb4, tp4=tp4: g.tensor_tensor(
                        tp4[:, 0], vv4[:, :, 0], hb4[:, :, 0], ALU.mult))
                    instr("gpsimd", lambda g, vv4=vv4, hb4=hb4, tp4=tp4: g.tensor_tensor(
                        tp4[:, 1], vv4[:, :, 1], hb4[:, :, 1], ALU.mult))
                    instr("gpsimd", lambda g, vv4=vv4, hb4=hb4, tp4=tp4: g.tensor_tensor(
                        tp4[:, 2], vv4[:, :, 0], hb4[:, :, 1], ALU.mult))
                    pg = instr("gpsimd", lambda g, vv4=vv4, hb4=hb4, tp4=tp4: g.tensor_tensor(
                        tp4[:, 3], vv4[:, :, 1], hb4[:, :, 0], ALU.mult), inc="MG")
                    pr_mg[pc_] = pg
                    if pp == 1:
                        hbfree[dl] = pg
                    wait("vector", "MG", pg)
                    if pc_ >= 2:
                        wait("vector", "MT", ap_mt[pc_ - 2])
                    yp4 = ypb[:, ps_ * 260:(ps_ + 1) * 260].rearrange("p (s t c) -> p s t c", s=2, t=2)
                    instr("vector", lambda v, yp4=yp4, tp4=tp4: v.tensor_tensor(
                        yp4[:, :, 0], tp4[:, 0], tp4[:, 1], ALU.subtract))
                    ym = instr("vector", lambda v, yp4=yp4, tp4=tp4: v.tensor_tensor(
                        yp4[:, :, 1], tp4[:, 2], tp4[:, 3], ALU.add), inc="MV")
                    yv_mv[pc_] = ym
                    ab = psD
                    if pc_ >= 1:
                        wait("tensor", "MV", e_mv[pc_ - 1])
                    else:
                        wait("tensor", "MS", ps_ms0)
                    wait("tensor", "MV", ym)
                    am = None
                    for sg in range(2):
                        base = ps_ * 260 + (1 - sg) * 130
                        instr("tensor", lambda t, ab=ab, sg=sg, base=base: t.matmul(
                            ab[0:65, sg * 256:(sg + 1) * 256], ypb[:, base:base + 65],
                            sb_c["cR3"][:, :], start=True, stop=False))
                        am = instr("tensor", lambda t, ab=ab, sg=sg, base=base: t.matmul(
                            ab[0:65, sg * 256:(sg + 1) * 256], ypb[:, base + 65:base + 130],
                            sb_c["cR4h"][:, :], start=False, stop=True), inc="MT")
                    ap_mt[pc_] = am
                    wait("vector", "MT", am)
                    if dl >= 2:
                        wait("vector", "MT", b_mt[dl - 2])
                    ee = instr("vector", lambda v, ab=ab, slot=slot, pp=pp: v.tensor_copy(
                        esb[:, slot * 1024 + pp * 512:slot * 1024 + (pp + 1) * 512],
                        ab[0:65, 0:512]), inc="MV")
                    e_mv[pc_] = ee
                    pc_ += 1
                bb = psF
                if dl >= 1:
                    wait("tensor", "MV", y_mv[dl - 1])
                else:
                    wait("tensor", "MV", ps_mv0)
                wait("tensor", "MV", e_mv[pc_ - 1])
                es4 = esb[:, slot * 1024:(slot + 1) * 1024].rearrange("p (u j q) -> p u j q", u=4, j=4)
                instr("tensor", lambda t, bb=bb, es4=es4: t.matmul(
                    bb[0:64, 0:256], sb_c["cCO2"][:, :], es4[:, :, 0], start=True, stop=False))
                instr("tensor", lambda t, bb=bb, es4=es4: t.matmul(
                    bb[0:64, 0:256], sb_c["cSI2n"][:, :], es4[:, :, 2], start=False, stop=False))
                instr("tensor", lambda t, bb=bb, es4=es4: t.matmul(
                    bb[0:64, 0:256], sb_c["cCO2s"][:, :], es4[:, :, 1], start=False, stop=False))
                bm = instr("tensor", lambda t, bb=bb, es4=es4: t.matmul(
                    bb[0:64, 0:256], sb_c["cSI2ns"][:, :], es4[:, :, 3], start=False, stop=True), inc="MT")
                b_mt[dl] = bm
                wait("vector", "MT", bm)
                if dl >= 2:
                    wait("vector", "Ty", yst[dl - 2])
                ye = instr("vector", lambda v, bb=bb, slot=slot: v.tensor_copy(
                    ysb[:, slot * 256:(slot + 1) * 256], bb[0:64, 0:256]), inc="MV")
                y_mv[dl] = ye
                wait("sync", "MV", ye)
                ysv = ysb[:, slot * 256:(slot + 1) * 256].rearrange("a (b q) -> a b q", b=4)
                for hi in range(2):
                    for b4 in range(4):
                        yst[dl] = dma("sync", "Ty",
                                      iA2[2 * b4 + hi, dl].rearrange("(ah qq) -> ah qq", qq=64),
                                      ysv[hi * 32:(hi + 1) * 32, b4])
            conv_out_T = yst[max(yst)] if yst else 0
            ckpt("p5")

            wait("gpsimd", "Ty", conv_out_T)
            a2a2 = instr("gpsimd", lambda g: g.collective_compute(
                "AllToAll", ALU.bypass, replica_groups=[list(range(NCORES))],
                ins=[iA2.tensor.ap().opt()], outs=[oA2.tensor.ap().opt()]), inc="CC")
            wait("gpsimd", "CC", a2a2)
            vcl = dma("gpsimd", "Ga", BC[:, :].rearrange("p (j q) -> p j q", j=NCORES),
                      oA2[:, :, :].rearrange("j p q -> p j q"))

            wait("vector", "Ga", vcl)
            wait("vector", "MV", u_done)
            ypr = instr("vector", lambda v: v.tensor_tensor(BA[:, :], BB[:, :], BC[:, :], ALU.mult), inc="MV")
            wait("gpsimd", "MT", v_mm_done)
            wo_ld = dma("gpsimd", "Gw", wbuf[:, :].rearrange("p (c d) -> p c d", c=NCH),
                        Wo_d.rearrange("(c p) d -> p c d", p=128))
            wait("tensor", "Gw", wo_ld)
            wait("tensor", "MV", ypr)
            evac_sem = "MV"

            BBtile = BB[:, :].rearrange("p (i c q) -> p i c q", i=NTILE, c=NCH)

            def evac_o(m, mtv):
                last = None
                for s in range(4):
                    wait("vector", "MT", mtv)
                    last = instr("vector", lambda v, m=m, s=s: v.tensor_scalar(
                        BBtile[:, 4 * s:4 * s + 4, m, :],
                        PS4[s][:, 0:512], bias_o[:, m:m + 1], None, ALU.add), inc="MV")
                return last

            emit_proj(evac_o)

            res_add = []
            x2lo_t = x2lo.rearrange("(i p) d -> i p d", p=128)
            x2st = {}
            for i in range(NTILE):
                wait("sync", "MV", n["MV"])
                src2 = BB[:, i * D:(i + 1) * D]
                dst3 = lnst[:, :].rearrange("t (c p) -> t c p", p=128)
                tt = dmat("sync", "Tat", dst3, src2)
                wait("vector", "Tat", tt)
                if i >= 1:
                    wait("vector", "Ls", x2st[i - 1])
                for a, b in ((0, 512), (512, 1024)):
                    instr("vector", lambda v, i=i, a=a, b=b: v.tensor_tensor(
                        stage[:, a:b], B8[:, i * D + a:i * D + b], lnst[:, a:b], ALU.add))
                for a, b in ((0, 512), (512, 1024)):
                    instr("vector", lambda v, i=i, a=a, b=b: v.tensor_copy(
                        B8[:, i * D + a:i * D + b], stage[:, a:b]))
                ra = None
                for a, b in ((0, 512), (512, 1024)):
                    ra = instr("vector", lambda v, i=i, a=a, b=b: v.tensor_tensor(
                        lob[:, a:b], stage[:, a:b], B8[:, i * D + a:i * D + b], ALU.subtract), inc="MV")
                res_add.append(ra)
                wait("scalar", "MV", res_add[-1])
                x2st[i] = dma("scalar", "Ls", x2lo_t[i], lob[:, :])

            for pi in range(NTILE // 2):
                iA, iB = 2 * pi, 2 * pi + 1
                xtA = B8[:, iA * D:(iA + 1) * D]
                xtB = B8[:, iB * D:(iB + 1) * D]
                wait("vector", "MV", res_add[iB])
                vrec = emit_ln_stats_pair(xtA, xtB)
                instr("vector", lambda v, iA=iA: v.tensor_copy(mstats[:, 2 * iA:2 * iA + 1], agg[:, 0:1]))
                instr("vector", lambda v, iB=iB: v.tensor_copy(mstats[:, 2 * iB:2 * iB + 1], agg[:, 8:9]))
                wait("scalar", "MV", vrec)
                instr("scalar", lambda s, iA=iA: s.activation(
                    mstats[:, 2 * iA + 1:2 * iA + 2], agg[:, 1:2], AF.Sqrt))
                vsq = instr("scalar", lambda s, iB=iB: s.activation(
                    mstats[:, 2 * iB + 1:2 * iB + 2], agg[:, 9:10], AF.Sqrt), inc="MS")
                wait("vector", "MS", vsq)
                instr("vector", lambda v, xtA=xtA, iA=iA: v.tensor_scalar(
                    stage[:, 0:1024], xtA, mstats[:, 2 * iA:2 * iA + 1], mstats[:, 2 * iA + 1:2 * iA + 2],
                    ALU.subtract, ALU.mult))
                instr("vector", lambda v, xtB=xtB, iB=iB: v.tensor_scalar(
                    hstg[:, 0:1024], xtB, mstats[:, 2 * iB:2 * iB + 1], mstats[:, 2 * iB + 1:2 * iB + 2],
                    ALU.subtract, ALU.mult))
                instr("vector", lambda v, iA=iA: v.reduce_max(
                    mstats[:, 32 + iA:33 + iA], stage[:, 0:1024], axis=AX))
                instr("vector", lambda v, iB=iB: v.reduce_max(
                    mstats[:, 32 + iB:33 + iB], hstg[:, 0:1024], axis=AX), inc="MV")
            wait("vector", "MV", n["MV"])
            instr("vector", lambda v: v.reduce_max(lmax[:, 0:1], mstats[:, 32:48], axis=AX), inc="MV")
            wait("tensor", "MV", n["MV"])
            vtm = instr("tensor", lambda t: t.matmul(
                psE[0:1, 0:128], lmax[:, 0:1], sb_c["cI128"][:, :], start=True, stop=True), inc="MT")
            wait("vector", "MT", vtm)
            vgm = instr("vector", lambda v: v.reduce_max(gsc[0:1, 0:1], psE[0:1, 0:128], axis=AX), inc="MV")
            wait("gpsimd", "MV", vgm)
            mx1 = dma("gpsimd", "Gm", mxi1[:, :], gsc[0:1, 0:1])
            wait("gpsimd", "Gm", mx1)
            ar1 = instr("gpsimd", lambda g: g.collective_compute(
                "AllReduce", ALU.max, replica_groups=[list(range(NCORES))],
                ins=[mxi1.tensor.ap().opt()], outs=[mxo1.tensor.ap().opt()]), inc="CC")
            wait("gpsimd", "CC", ar1)
            mx1b = dma("gpsimd", "Gm", gsc[0:1, 1:2], mxo1[:, :])
            wait("vector", "Gm", mx1b)
            vv = instr("vector", lambda v: v.tensor_scalar(gsc[0:1, 1:2], gsc[0:1, 1:2], 1e-8, None, ALU.max), inc="MV")
            wait("vector", "MV", vv)
            vv = instr("vector", lambda v: v.reciprocal(gsc[0:1, 2:3], gsc[0:1, 1:2]), inc="MV")
            wait("vector", "MV", vv)
            vv = instr("vector", lambda v: v.tensor_scalar(gsc[0:1, 2:3], gsc[0:1, 2:3], 255.0, None, ALU.mult), inc="MV")
            wait("vector", "MV", vv)
            vsx = instr("vector", lambda v: v.tensor_scalar(gsc[0:1, 3:4], gsc[0:1, 1:2], 1.0 / 255.0, None, ALU.mult), inc="MV")
            wait("tensor", "MV", vsx)
            wait("tensor", "MS", n["MS"])
            vbb = instr("tensor", lambda t: t.matmul(
                psE[:, 0:2], sb_c["cOnes"][0:1, :], gsc[0:1, 2:4], start=True, stop=True), inc="MT")
            wait("vector", "MT", vbb)
            vrx = instr("vector", lambda v: v.tensor_copy(rxb[:, 0:2], psE[:, 0:2]), inc="MV")
            wait("vector", "MV", max(vrw1, vrx))
            H1 = ((0, 16), (16, 32))
            for a, b in H1:
                instr("vector", lambda v, a=a, b=b: v.tensor_scalar(
                    sxw1[:, a:b], sw1[:, a:b], rxb[:, 1:2], None, ALU.mult))
            for a, b in H1:
                instr("vector", lambda v, a=a, b=b: v.tensor_tensor(
                    bq1[:, a:b], b1t[:, a:b], rw1[:, a:b], ALU.mult))
            for a, b in H1:
                instr("vector", lambda v, a=a, b=b: v.tensor_scalar(
                    bq1[:, a:b], bq1[:, a:b], rxb[:, 0:1], None, ALU.mult))
            for a, b in H1:
                instr("vector", lambda v, a=a, b=b: v.tensor_scalar(
                    bq1[:, a:b], bq1[:, a:b], MAGIC, MAGIC, ALU.add, ALU.subtract))
            vex = None
            for a, b in H1:
                vex = instr("vector", lambda v, a=a, b=b: v.tensor_tensor(
                    exv1[:, a:b], sxw1[:, a:b], bq1[:, a:b], ALU.mult), inc="MV")
            prev_qt = 0
            HS = ((0, 512), (512, 1024))
            for i in range(NTILE):
                xt = B8[:, i * D:(i + 1) * D]
                wait("vector", "MV", vrx)
                if i > 0:
                    wait("vector", "Tqx", prev_qt)
                for a, b in HS:
                    instr("vector", lambda v, xt=xt, i=i, a=a, b=b: v.tensor_scalar(
                        stage[:, a:b], xt[:, a:b], mstats[:, 2 * i:2 * i + 1], mstats[:, 2 * i + 1:2 * i + 2],
                        ALU.subtract, ALU.mult))
                for a, b in HS:
                    instr("vector", lambda v, a=a, b=b: v.tensor_scalar(
                        stage[:, a:b], stage[:, a:b], rxb[:, 0:1], MAGIC, ALU.mult, ALU.add))
                for a, b in HS:
                    instr("vector", lambda v, a=a, b=b: v.tensor_scalar(
                        stage[:, a:b], stage[:, a:b], MAGIC, 0.0, ALU.subtract, ALU.max))
                vq = None
                for a, b in HS:
                    vq = instr("vector", lambda v, a=a, b=b: v.tensor_scalar(
                        lnst[:, a:b], stage[:, a:b], 255.0, None, ALU.min), inc="MV")
                wait("sync", "MV", vq)
                src3 = lnst[:, :].rearrange("p (c q) -> p c q", q=128)
                dst3 = BA[:, :].rearrange("p (c i q) -> p c i q", c=NCH, i=NTILE)[:, :, i, :]
                prev_qt = dmat("sync", "Tqx", dst3, src3)
            qxT_done = prev_qt
            ckpt("p8")

            conv_sc.close()
            attn_sc.close()
            rwb1 = sbuf("rwb1", [128, FF], BF16, scope=ffnA_sc)
            rwb2 = sbuf("rwb2", [128, D], BF16, scope=ffnA_sc)
            wf1 = sbuf("wf1", [128, 2 * 1024], F32, scope=ffnA_sc)
            qw1 = sbuf("qw1", [128, 2 * 1024], BF16, scope=ffnA_sc)
            wp2 = sbuf("wp2", [128, 1024], F32, scope=ffnA_sc)
            qp2 = sbuf("qp2", [128, 1024], BF16, scope=ffnA_sc)

            wait("gpsimd", "MV", vex)
            wait("scalar", "MV", vex)

            def emit_bcast(rwt, nrows, dst, vgate):
                wait("sync", "MT", n["MT"])
                wait("tensor", "MV", vgate)
                vtr = instr("tensor", lambda t, rwt=rwt, nrows=nrows: t.matmul(
                    psE[0:nrows, 0:128], rwt[:, :], sb_c["cI128"][:, :], start=True, stop=True), inc="MT")
                wait("vector", "MT", vtr)
                vcp = instr("vector", lambda v, rwt=rwt, nrows=nrows: v.tensor_copy(
                    stage[0:nrows, 0:128], psE[0:nrows, 0:128]), inc="MV")
                out = None
                prev = 0
                for h0 in range(0, nrows, 16):
                    rr = min(16, nrows - h0)
                    wait("sync", "MV", max(vcp, prev))
                    row = dma("sync", "Trow",
                              wf1[0:1, 0:rr * 128].rearrange("o (a b) -> o a b", b=128),
                              stage[h0:h0 + rr, 0:128])
                    wait("tensor", "Trow", row)
                    for sI in range((rr * 128) // 512):
                        if prev:
                            wait("tensor", "MV", prev)
                        vbm = instr("tensor", lambda t, sI=sI: t.matmul(
                            psE[:, 0:512], sb_c["cOnes"][0:1, :], wf1[0:1, sI * 512:(sI + 1) * 512],
                            start=True, stop=True), inc="MT")
                        wait("vector", "MT", vbm)
                        out = instr("vector", lambda v, sI=sI, dst=dst, h0=h0: v.tensor_copy(
                            dst[:, h0 * 128 + sI * 512: h0 * 128 + (sI + 1) * 512], psE[:, 0:512]), inc="MV")
                        prev = out
                return out

            wait("sync", "Tqx", qxT_done)
            wait("sync", "Tat", n["Tat"])
            wait("sync", "Tln", n["Tln"])
            bc1 = emit_bcast(rw1, NFT, rwb1, max(vrw1, vex))
            ckpt("p8a")
            bc2 = emit_bcast(rw2, NCH, rwb2, max(vrw2, bc1))
            ckpt("p8b")

            w2T_t = w2T_d.rearrange("(f p) (c q) -> f p c q", p=128, q=128)
            pq_vv = 0
            pq_st = 0
            for fc in range(NFT):
                if pq_st:
                    wait("scalar", "Gq", pq_st)
                lv = dma("scalar", "Lq2", wp2[:, :].rearrange("p (c q) -> p c q", q=128), w2T_t[fc])
                wait("gpsimd", "Lq2", lv)
                wait("gpsimd", "MV", max(bc2, pq_vv))
                vgq = instr("gpsimd", lambda g: g.tensor_tensor(wp2[:, :], wp2[:, :], rwb2[:, :], ALU.mult), inc="MG")
                wait("vector", "MG", vgq)
                pq_vv = instr("vector", lambda v: v.tensor_scalar(
                    qp2[:, :], wp2[:, :], MAGIC, MAGIC, ALU.add, ALU.subtract), inc="MV")
                wait("gpsimd", "MV", pq_vv)
                pq_st = dma("gpsimd", "Gq",
                            qw2d.rearrange("m p (f q) -> p m f q", q=128)[:, :, fc],
                            qp2[:, :].rearrange("p (m q) -> p m q", q=128))
            w2q_done = pq_st
            ckpt("p8q")

            w1T_t = w1T_d.rearrange("(c p) (t q) -> t p c q", p=128, q=128)
            hst_store = {}
            ff1_vq = {}
            for ft in range(NFT):
                par = ft % 2
                if ft == 0:
                    wait("scalar", "MV", bc2)
                if ft >= 2:
                    wait("scalar", "MG", ff1_mg[ft - 2])
                lv = dma("scalar", "Lw0" if par == 0 else "Lw1",
                         wf1[:, par * 1024:(par + 1) * 1024].rearrange("p (c q) -> p c q", q=128),
                         w1T_t[ft])
                wait("gpsimd", "Lw0" if par == 0 else "Lw1", lv)
                wait("gpsimd", "MV", bc1)
                if ft >= 2:
                    wait("gpsimd", "MV", ff1_vq[ft - 2])
                for c8 in range(NCH):
                    instr("gpsimd", lambda g, par=par, c8=c8, ft=ft: g.tensor_tensor(
                        wf1[:, par * 1024 + c8 * 128: par * 1024 + (c8 + 1) * 128],
                        wf1[:, par * 1024 + c8 * 128: par * 1024 + (c8 + 1) * 128],
                        rwb1[:, ft * 128:(ft + 1) * 128], ALU.mult),
                        inc=("MG" if c8 == NCH - 1 else None))
                if ft == 0:
                    ff1_mg = {}
                ff1_mg[ft] = n["MG"]
                wait("vector", "MG", n["MG"])
                if ft >= 2:
                    wait("vector", "MT", ff1_mmlast[ft - 2])
                vq1 = instr("vector", lambda v, par=par: v.tensor_scalar(
                    qw1[:, par * 1024:(par + 1) * 1024], wf1[:, par * 1024:(par + 1) * 1024],
                    MAGIC, MAGIC, ALU.add, ALU.subtract), inc="MV")
                ff1_vq[ft] = vq1
                wait("tensor", "MV", vq1)
                if ft == 0:
                    wait("tensor", "Tqx", qxT_done)
                    ff1_mmlast = {}
                if ft >= 1:
                    wait("tensor", "MS", ff1_ms[ft - 1])
                for c8 in range(NCH):
                    for s in range(4):
                        instr("tensor", lambda t, par=par, c8=c8, s=s: t.matmul(
                            PS4[s][:, 0:512], qw1[:, par * 1024 + c8 * 128: par * 1024 + (c8 + 1) * 128],
                            BAv[:, c8, s * 512:(s + 1) * 512],
                            start=(c8 == 0), stop=(c8 == NCH - 1)),
                            inc=("MT" if c8 == NCH - 1 else None))
                ff1_mmlast[ft] = n["MT"]
                mtv = n["MT"]
                if ft >= 1:
                    wait("scalar", "Gh", hst_store[ft - 1])
                for s in range(4):
                    wait("scalar", "MT", mtv)
                    instr("scalar", lambda sc, ft=ft, s=s: sc.activation(
                        hstg[:, s * 512:(s + 1) * 512],
                        PS4[s][:, 0:512], AF.Relu, bias=exv1[:, ft:ft + 1], scale=sxw1[:, ft:ft + 1]),
                        inc=("MS" if s == 3 else None))
                if ft == 0:
                    ff1_ms = {}
                ff1_ms[ft] = n["MS"]
                wait("vector", "MS", n["MS"])
                vmx = instr("vector", lambda v, ft=ft: v.reduce_max(
                    mstats[:, ft:ft + 1], hstg[:, 0:NT], axis=AX), inc="MV")
                wait("gpsimd", "MV", vmx)
                hst_store[ft] = dma("gpsimd", "Gh", hdr[ft], hstg[:, 0:NT])

            wait("vector", "MV", n["MV"])
            instr("vector", lambda v: v.reduce_max(lmax[:, 1:2], mstats[:, 0:32], axis=AX), inc="MV")
            wait("tensor", "MV", n["MV"])
            vtm2 = instr("tensor", lambda t: t.matmul(
                psE[0:1, 0:128], lmax[:, 1:2], sb_c["cI128"][:, :], start=True, stop=True), inc="MT")
            wait("vector", "MT", vtm2)
            vgm2 = instr("vector", lambda v: v.reduce_max(gsc[0:1, 4:5], psE[0:1, 0:128], axis=AX), inc="MV")
            wait("gpsimd", "MV", vgm2)
            mx2 = dma("gpsimd", "Gm", mxi2[:, :], gsc[0:1, 4:5])
            wait("gpsimd", "Gm", mx2)
            ar2 = instr("gpsimd", lambda g: g.collective_compute(
                "AllReduce", ALU.max, replica_groups=[list(range(NCORES))],
                ins=[mxi2.tensor.ap().opt()], outs=[mxo2.tensor.ap().opt()]), inc="CC")
            wait("gpsimd", "CC", ar2)
            mx2b = dma("gpsimd", "Gm", gsc[0:1, 5:6], mxo2[:, :])
            wait("vector", "Gm", mx2b)
            vv = instr("vector", lambda v: v.tensor_scalar(gsc[0:1, 5:6], gsc[0:1, 5:6], 1e-8, None, ALU.max), inc="MV")
            wait("vector", "MV", vv)
            vv = instr("vector", lambda v: v.reciprocal(gsc[0:1, 6:7], gsc[0:1, 5:6]), inc="MV")
            wait("vector", "MV", vv)
            vv = instr("vector", lambda v: v.tensor_scalar(gsc[0:1, 6:7], gsc[0:1, 6:7], 255.0, None, ALU.mult), inc="MV")
            wait("vector", "MV", vv)
            vsh = instr("vector", lambda v: v.tensor_scalar(gsc[0:1, 7:8], gsc[0:1, 5:6], 1.0 / 255.0, None, ALU.mult), inc="MV")
            wait("tensor", "MV", vsh)
            vbb2 = instr("tensor", lambda t: t.matmul(
                psE[:, 0:2], sb_c["cOnes"][0:1, :], gsc[0:1, 6:8], start=True, stop=True), inc="MT")
            wait("vector", "MT", vbb2)
            vrh = instr("vector", lambda v: v.tensor_copy(rxb[:, 2:4], psE[:, 0:2]), inc="MV")
            wait("vector", "Lc", n["Lc"])
            wait("vector", "MV", vrh)
            H2 = ((0, 4), (4, 8))
            for a, b in H2:
                instr("vector", lambda v, a=a, b=b: v.tensor_scalar(
                    shw2[:, a:b], sw2[:, a:b], rxb[:, 3:4], None, ALU.mult))
            for a, b in H2:
                instr("vector", lambda v, a=a, b=b: v.tensor_tensor(
                    bq2[:, a:b], b2t[:, a:b], rw2[:, a:b], ALU.mult))
            for a, b in H2:
                instr("vector", lambda v, a=a, b=b: v.tensor_scalar(
                    bq2[:, a:b], bq2[:, a:b], rxb[:, 2:3], None, ALU.mult))
            for a, b in H2:
                instr("vector", lambda v, a=a, b=b: v.tensor_scalar(
                    bq2[:, a:b], bq2[:, a:b], MAGIC, MAGIC, ALU.add, ALU.subtract))
            vex2 = None
            for a, b in H2:
                vex2 = instr("vector", lambda v, a=a, b=b: v.tensor_tensor(
                    exv2[:, a:b], shw2[:, a:b], bq2[:, a:b], ALU.mult), inc="MV")
            ckpt("p9")

            ffnA_sc.close()
            qh = sbuf("qh", [128, NFT * 256], BF16, scope=ffnB_sc)
            qw2m = sbuf("qw2m", [128, 2 * 4096], BF16, scope=ffnB_sc)
            hbf = sbuf("hbf", [128, 2048], BF16, scope=ffnB_sc)

            wait("sync", "MV", vex2)
            wait("scalar", "MV", vex2)
            wait("gpsimd", "MV", vex2)
            wait("scalar", "Gh", n["Gh"])

            ffn_ev = []
            qhl = 0
            qw_mt = {}
            qwld = {}
            for s2 in range(8):
                for piece in range(4):
                    if qhl:
                        wait("scalar", "MV", qhl)
                    lv = dma("scalar", "Lh", hbf[:, :].rearrange("p (c q) -> p c q", q=256),
                             hdr[piece * 8:(piece + 1) * 8, :, s2 * 256:(s2 + 1) * 256]
                             .rearrange("f p q -> p f q"))
                    wait("vector", "Lh", lv)
                    wait("vector", "MV", vrh)
                    instr("vector", lambda v: v.tensor_scalar(
                        stage[:, 0:1024], hbf[:, 0:1024], rxb[:, 2:3], MAGIC, ALU.mult, ALU.add))
                    instr("vector", lambda v: v.tensor_scalar(
                        hstg[:, 0:1024], hbf[:, 1024:2048], rxb[:, 2:3], MAGIC, ALU.mult, ALU.add))
                    instr("vector", lambda v, piece=piece: v.tensor_scalar(
                        qh[:, piece * 2048: piece * 2048 + 1024], stage[:, 0:1024],
                        MAGIC, None, ALU.subtract))
                    qhl = instr("vector", lambda v, piece=piece: v.tensor_scalar(
                        qh[:, piece * 2048 + 1024: piece * 2048 + 2048], hstg[:, 0:1024],
                        MAGIC, None, ALU.subtract), inc="MV")
                qh_ready = qhl
                for m in range(NCH):
                    par = m % 2
                    qeng = "sync" if par == 0 else "scalar"
                    qsm = "Tq" if par == 0 else "Lq"
                    wait(qeng, "Gq", w2q_done)
                    key = (s2, m)
                    pkey = (s2, m - 2) if m >= 2 else ((s2 - 1, NCH - 2 + m) if s2 > 0 else None)
                    if pkey is not None:
                        wait(qeng, "MT", qw_mt[pkey])
                    qwld[key] = dma(qeng, qsm,
                                    qw2m[:, par * 4096:(par + 1) * 4096], qw2d[m])
                    wait("tensor", qsm, qwld[key])
                    wait("tensor", "MV", qh_ready)
                    pp = psE if par == 0 else psF
                    if len(ffn_ev) >= 2:
                        wait("tensor", "MV", ffn_ev[-2])
                    for fc in range(NFT):
                        instr("tensor", lambda t, par=par, fc=fc, pp=pp: t.matmul(
                            pp[:, 0:256], qw2m[:, par * 4096 + fc * 128: par * 4096 + (fc + 1) * 128],
                            qh[:, fc * 256:(fc + 1) * 256],
                            start=(fc == 0), stop=(fc == NFT - 1)),
                            inc=("MT" if fc == NFT - 1 else None))
                    qw_mt[key] = n["MT"]
                    wait("vector", "MT", n["MT"])
                    wait("vector", "MV", vex2)
                    ffn_ev.append(instr("vector", lambda v, m=m, s2=s2, pp=pp: v.tensor_scalar(
                        BBtile[:, 2 * s2:2 * s2 + 2, m, :], pp[:, 0:256],
                        bq2[:, m:m + 1], shw2[:, m:m + 1], ALU.add, ALU.mult), inc="MV"))

            ckpt("p10")
            wait("sync", "Trow", n["Trow"])
            wait("scalar", "Ls", n["Ls"])
            y_tiled = y_out.rearrange("(i p) d -> i p d", p=128)
            ffn2_all = n["MV"]
            ost = {}
            out_add = 0
            for i in range(NTILE):
                wait("sync", "MV", ffn2_all if i == 0 else out_add)
                src2 = BB[:, i * D:(i + 1) * D]
                dst3 = lnst[:, :].rearrange("t (c p) -> t c p", p=128)
                tt = dmat("sync", "Tout", dst3, src2)
                if i >= 1:
                    wait("scalar", "MV", out_add)
                lol = dma("scalar", "Lh", lob[:, :], x2lo_t[i])
                wait("vector", "Tout", tt)
                wait("vector", "Lh", lol)
                if i >= 2:
                    wait("vector", "O0" if i % 2 == 0 else "O1", ost[i - 2])
                half = i % 2
                obuf = stage if half == 0 else hstg
                for a, b in ((0, 512), (512, 1024)):
                    instr("vector", lambda v, i=i, obuf=obuf, a=a, b=b: v.tensor_tensor(
                        obuf[:, a:b], B8[:, i * D + a:i * D + b], lnst[:, a:b], ALU.add))
                for a, b in ((0, 512), (512, 1024)):
                    out_add = instr("vector", lambda v, obuf=obuf, a=a, b=b: v.tensor_tensor(
                        obuf[:, a:b], obuf[:, a:b], lob[:, a:b], ALU.add), inc="MV")
                wait("scalar", "MV", out_add)
                ost[i] = dma("scalar", "O0" if i % 2 == 0 else "O1", y_tiled[i], obuf[:, 0:1024])
            if DBG:
                dma("scalar", "O0", dbg_oA1[:, :, :], oA1[:, :, :])
                dma("scalar", "O0", dbg_iA2[:, :, :], iA2[:, :, :])
                dma("scalar", "O0", dbg_B8[:, :], B8[:, :])
                dma("scalar", "O0", dbg_qx[:, :], BA[:, :])
                dma("scalar", "O0", dbg_gsc[:, :], gsc[:, :])
                dma("scalar", "O0", dbg_hdr[0], hdr[0])
                dma("scalar", "O0", dbg_hdr[1], hdr[31])
            wait("scalar", "O0", n["O0"])
            wait("scalar", "O1", n["O1"])
            ffnB_sc.close()


        except _SkipRest:
            pass
        for _sc in (ffnB_sc, ffnA_sc, conv_sc, pe_sc, attn_sc):
            try:
                _sc.close()
            except Exception:
                pass

        @block.sync
        def _(e):
            for fn in streams["sync"]:
                fn(e)

        def _audit():
            actual = {nm: 0 for nm in SEMNAMES}
            byid = {SEMS[nm].sem_num if hasattr(SEMS[nm], "sem_num") else None: nm for nm in SEMNAMES}
            names = {}
            for i in nc.all_instructions():
                si = getattr(i, "sync_info", None)
                if si is None:
                    continue
                for u in si.on_update:
                    if u.ant_name in actual:
                        actual[u.ant_name] += u.update_value
            bad = {k: (actual[k], n[k]) for k in SEMNAMES if actual[k] != n[k]}
            assert not bad, f"semaphore accounting mismatch: {bad}"

        @block.scalar
        def _(e):
            for fn in streams["scalar"]:
                fn(e)

        @block.vector
        def _(e):
            for fn in streams["vector"]:
                fn(e)

        @block.tensor
        def _(e):
            for fn in streams["tensor"]:
                fn(e)

        @block.gpsimd
        def _(e):
            for fn in streams["gpsimd"]:
                fn(e)

        _audit()

    return nc


_NC_CACHE = None


def kernel(L0_attn_x, L0_attn_Wu, L0_attn_bu, L0_attn_Wv, L0_attn_bv,
           L0_attn_W1, L0_attn_b1, L0_attn_W2, L0_attn_b2, L0_attn_W3,
           L0_attn_b3, L0_attn_Wo, L0_attn_bo, L0_ffn_w1, L0_ffn_b1,
           L0_ffn_w2, L0_ffn_b2):
    global _NC_CACHE
    x = np.ascontiguousarray(np.asarray(L0_attn_x, np.float32))
    B, Lq, Dm = x.shape
    if _NC_CACHE is None:
        _NC_CACHE = build_nc()
    nc = _NC_CACHE
    f32c = lambda a: np.ascontiguousarray(np.asarray(a, np.float32))
    pos = np.arange(4096, dtype=np.float32)[:, None]
    div = np.exp(np.arange(0, 64, 2, dtype=np.float32) * -(np.log(10000.0) / 64.0))
    pe = np.zeros((4096, 64), np.float32)
    pe[:, 0::2] = np.sin(pos * div)
    pe[:, 1::2] = np.cos(pos * div)
    com = dict(
        Wu=f32c(L0_attn_Wu), Wv=f32c(L0_attn_Wv), Wo=f32c(L0_attn_Wo),
        bu=f32c(L0_attn_bu), bv=f32c(L0_attn_bv), bo=f32c(L0_attn_bo),
        W1=f32c(L0_attn_W1), W2=f32c(L0_attn_W2),
        b1f=f32c(L0_attn_b1), b2f=f32c(L0_attn_b2),
        peT=np.ascontiguousarray(pe.T),
        w1T=f32c(np.asarray(L0_ffn_w1).T), w1n=f32c(L0_ffn_w1), b1v=f32c(L0_ffn_b1),
        w2T=f32c(np.asarray(L0_ffn_w2).T), w2n=f32c(L0_ffn_w2), b2v=f32c(L0_ffn_b2),
    )
    com.update({"d" + k: v for k, v in CONSTS.items()})
    W3 = f32c(L0_attn_W3)
    b3 = f32c(L0_attn_b3)
    in_maps = []
    for c in range(NCORES):
        b, h = c // 2, c % 2
        m = dict(com)
        m["x_in"] = np.ascontiguousarray(x[b, h * NT:(h + 1) * NT, :])
        m["W3s"] = np.ascontiguousarray(W3[:, c * 128:(c + 1) * 128])
        m["b3s"] = np.ascontiguousarray(b3[c * 128:(c + 1) * 128])
        in_maps.append(m)
    res = run_bass_kernel_spmd(nc, in_maps, core_ids=list(range(NCORES)))
    out = np.empty((B, Lq, Dm), np.float32)
    for c in range(NCORES):
        b, h = c // 2, c % 2
        out[b, h * NT:(h + 1) * NT, :] = res.results[c]["y_out"]
    return out


# version 08
# speedup vs baseline: 1.2499x; 1.2499x over previous
import os
os.environ.setdefault("TRN_ALLOW_ICI_ALL_TO_ALL", "1")

import numpy as np
import ml_dtypes
import concourse.bass as bass
import concourse.mybir as mybir
from concourse.bass_utils import run_bass_kernel_spmd
from contextlib import ExitStack

F32 = mybir.dt.float32
BF16 = mybir.dt.bfloat16
AF = mybir.ActivationFunctionType
ALU = mybir.AluOpType
AX = mybir.AxisListType.X

MAGIC = float(np.float32(1.5 * 2 ** 23))
NCORES = 8
NT = 2048
D = 1024
FF = 4096
NTILE = NT // 128
NCH = D // 128
NFT = FF // 128


def _bf(x):
    return np.ascontiguousarray(np.asarray(x, np.float32)).astype(ml_dtypes.bfloat16)


def host_constants():
    k = np.arange(128)
    W1c = np.exp(-2j * np.pi * np.outer(k, k) / 128.0)
    F1r, F1i = W1c.real.astype(np.float32), W1c.imag.astype(np.float32)
    W2c = np.exp(+2j * np.pi * np.outer(k, k) / 128.0)
    F2r, F2i = W2c.real.astype(np.float32), W2c.imag.astype(np.float32)
    cR1h = np.concatenate([F1r[0:64, 0:65], F1i[0:64, 0:65]], axis=1)
    cG1 = F1r[0:64, :]
    cG2 = F1i[0:64, :]
    cG1d = np.concatenate([cG1, cG1], axis=0)
    cG2d = np.concatenate([cG2, cG2], axis=0)
    cR3 = np.concatenate([F2r, F2i], axis=1)
    cR4h = np.concatenate([-F2i, F2r], axis=1)
    a = np.arange(64)
    k65 = np.arange(65)
    alpha = np.where((k65 == 0) | (k65 == 64), 1.0, 2.0)[:, None].astype(np.float32)
    CO2 = (alpha * np.cos(2 * np.pi * np.outer(k65, a) / 128.0) / 16384.0).astype(np.float32)
    SI2n = (-alpha * np.sin(2 * np.pi * np.outer(k65, a) / 128.0) / 16384.0).astype(np.float32)
    CO2s = np.zeros((65, 64), np.float32); CO2s[:, 1:] = CO2[:, :-1]
    SI2ns = np.zeros((65, 64), np.float32); SI2ns[:, 1:] = SI2n[:, :-1]
    pos = np.arange(4096, dtype=np.float32)[:, None]
    div = np.exp(np.arange(0, 64, 2, dtype=np.float32) * -(np.log(10000.0) / 64.0))
    pe = np.zeros((4096, 64), np.float32)
    pe[:, 0::2] = np.sin(pos * div)
    pe[:, 1::2] = np.cos(pos * div)
    dec = np.exp(-0.01 * np.arange(4096, dtype=np.float32)).reshape(1, 4096)
    return dict(
        cR1h=_bf(cR1h), cG1d=_bf(cG1d), cG2d=_bf(cG2d), cR3=_bf(cR3), cR4h=_bf(cR4h),
        cCO2=_bf(CO2), cSI2n=_bf(SI2n), cCO2s=_bf(CO2s), cSI2ns=_bf(SI2ns),
        cG1f=np.ascontiguousarray(cG1.astype(np.float32)),
        cG2f=np.ascontiguousarray(cG2.astype(np.float32)),
        cDec=np.ascontiguousarray(dec.astype(np.float32)),
        cI128=np.eye(128, dtype=np.float32),
        cOnes=np.ones((1, 128), np.float32),
    )


CONSTS = host_constants()


def build_nc():
    nc = bass.Bass("TRN2", target_bir_lowering=False, debug=False, detect_race_conditions=False)
    DI = lambda nm, s, dt=F32: nc.dram_tensor(nm, s, dt, kind="ExternalInput").ap()
    x_in = DI("x_in", [NT, D])
    Wu_d = DI("Wu", [D, D]); Wv_d = DI("Wv", [D, D]); Wo_d = DI("Wo", [D, D])
    bu_d = DI("bu", [D]); bv_d = DI("bv", [D]); bo_d = DI("bo", [D])
    W1_d = DI("W1", [64, 64]); W2_d = DI("W2", [64, 64])
    b1f_d = DI("b1f", [64]); b2f_d = DI("b2f", [64])
    W3s_d = DI("W3s", [64, 128]); b3s_d = DI("b3s", [128])
    peT_d = DI("peT", [64, 4096])
    w1T_d = DI("w1T", [D, FF]); w1n_d = DI("w1n", [FF, D]); b1v_d = DI("b1v", [FF])
    w2T_d = DI("w2T", [FF, D]); w2n_d = DI("w2n", [D, FF]); b2v_d = DI("b2v", [D])
    cns = {k: DI("d" + k, list(v.shape), BF16 if v.dtype == ml_dtypes.bfloat16 else F32)
           for k, v in CONSTS.items()}
    y_out = nc.dram_tensor("y_out", [NT, D], F32, kind="ExternalOutput").ap()
    DBG = os.environ.get("ATH_DEBUG", "") == "1"
    if DBG:
        dbg_oA1 = nc.dram_tensor("dbg_oA1", [NCORES, 128, NT], BF16, kind="ExternalOutput").ap()
        dbg_iA2 = nc.dram_tensor("dbg_iA2", [NCORES, 128, NT], BF16, kind="ExternalOutput").ap()
        dbg_B8 = nc.dram_tensor("dbg_B8", [128, NTILE * D], BF16, kind="ExternalOutput").ap()
        dbg_qx = nc.dram_tensor("dbg_qx", [128, NCH * NT], BF16, kind="ExternalOutput").ap()
        dbg_gsc = nc.dram_tensor("dbg_gsc", [1, 8], F32, kind="ExternalOutput").ap()
        dbg_hdr = nc.dram_tensor("dbg_hdr", [2, 128, NT], BF16, kind="ExternalOutput").ap()
    iA1 = nc.dram_tensor("iA1", [NCORES, 128, NT], BF16).ap()
    oA1 = nc.dram_tensor("oA1", [NCORES, 128, NT], BF16).ap()
    iA2 = nc.dram_tensor("iA2", [NCORES, 128, NT], BF16).ap()
    oA2 = nc.dram_tensor("oA2", [NCORES, 128, NT], BF16).ap()
    mxi1 = nc.dram_tensor("mxi1", [1, 1], F32).ap()
    mxo1 = nc.dram_tensor("mxo1", [1, 1], F32).ap()
    mxi2 = nc.dram_tensor("mxi2", [1, 1], F32).ap()
    mxo2 = nc.dram_tensor("mxo2", [1, 1], F32).ap()
    hdr = nc.dram_tensor("hdr", [NFT, 128, NT], BF16).ap()
    qw2d = nc.dram_tensor("qw2d", [NCH, 128, NFT * 128], BF16).ap()
    hscr = nc.dram_tensor("hscr", [128, 4096], BF16).ap()
    x2lo = nc.dram_tensor("x2lo", [NT, D], BF16).ap()

    with ExitStack() as ctx:
        def sbuf(nm, shape, dt=BF16, scope=None):
            return (scope or ctx).enter_context(nc.sbuf_tensor(nm, shape, dt))

        def psum(nm):
            return ctx.enter_context(nc.psum_tensor(nm, [128, 512], F32))

        def sem(nm):
            return ctx.enter_context(nc.semaphore(nm))

        B8 = sbuf("B8", [128, NTILE * D])
        BA = sbuf("BA", [128, NCH * NT])
        BB = sbuf("BB", [128, NCH * NT])
        BC = sbuf("BC", [128, NCH * NT])
        hstg = sbuf("hstg", [128, NT], F32)
        stage = sbuf("stage", [128, 1024], F32)
        lnst = sbuf("lnst", [128, 1024])
        sb_c = {k: sbuf("s" + k, list(v.shape),
                        BF16 if v.dtype == ml_dtypes.bfloat16 else F32)
                for k, v in CONSTS.items() if k != "cDec"}
        bias_u = sbuf("bias_u", [128, NCH], F32)
        bias_v = sbuf("bias_v", [128, NCH], F32)
        bias_o = sbuf("bias_o", [128, NCH], F32)
        b3sb = sbuf("b3sb", [128, 1], F32)
        b1fb = sbuf("b1fb", [64, 1], F32)
        b2fb = sbuf("b2fb", [64, 1], F32)
        b1t = sbuf("b1t", [128, NFT], F32)
        b2t = sbuf("b2t", [128, NCH], F32)
        rw1 = sbuf("rw1", [128, NFT], F32)
        rw2 = sbuf("rw2", [128, NCH], F32)
        sw1 = sbuf("sw1", [128, NFT], F32)
        sw2 = sbuf("sw2", [128, NCH], F32)
        sxw1 = sbuf("sxw1", [128, NFT], F32)
        bq1 = sbuf("bq1", [128, NFT], F32)
        exv1 = sbuf("exv1", [128, NFT], F32)
        shw2 = sbuf("shw2", [128, NCH], F32)
        bq2 = sbuf("bq2", [128, NCH], F32)
        exv2 = sbuf("exv2", [128, NCH], F32)
        mstats = sbuf("mstats", [128, 3 * NTILE], F32)
        bnb = sbuf("bnb", [128, 24], F32)
        agg = sbuf("agg", [128, 16], F32)
        lmax = sbuf("lmax", [128, 2], F32)
        gsc = sbuf("gsc", [1, 8], F32)
        rxb = sbuf("rxb", [128, 4], F32)
        lob = sbuf("lob", [128, 1024], BF16)
        W1sb = sbuf("W1sb", [64, 64], F32)
        W2sb = sbuf("W2sb", [64, 64], F32)
        W3sb = sbuf("W3sb", [64, 128], F32)
        psA = psum("psA"); psB = psum("psB"); psC = psum("psC"); psD = psum("psD")
        psE = psum("psE"); psF = psum("psF"); psG = psum("psG"); psH = psum("psH")
        PS4 = [psA, psB, psC, psD]
        SEMNAMES = ["CC", "MT", "MV", "MS", "MG",
                    "Gx", "Gw", "Gv0", "Gv1", "Ga", "Gm", "Gh", "Gq",
                    "Lc", "Lp", "Ls", "Lq2", "Lw0", "Lw1", "Lh", "Lq",
                    "Tln", "Tzh", "Tz0", "Tz1", "Ty", "Ty2", "Trow", "Tat", "Tqx", "Tq", "Tout",
                    "O0", "O1"]
        SEMS = {nm: sem(nm) for nm in SEMNAMES}
        attn_sc = ExitStack()
        pe_sc = ExitStack()
        conv_sc = ExitStack()
        ffnA_sc = ExitStack()
        ffnB_sc = ExitStack()
        block = ctx.enter_context(nc.Block())

        n = {nm: 0 for nm in SEMNAMES}
        streams = {e: [] for e in ("tensor", "vector", "scalar", "sync", "gpsimd")}

        class _SkipRest(Exception):
            pass

        _STOP = os.environ.get("ATH_STOP_PHASE", "")

        def ckpt(name):
            if _STOP == name:
                raise _SkipRest()

        def dma(eng, sm, out, in_, slow=False):
            n[sm] += 16
            hs = SEMS[sm]
            streams[eng].append(lambda e, out=out, in_=in_, hs=hs, slow=slow:
                                e.dma_start(out=out, in_=in_,
                                            allow_slow_non_contiguous=slow).then_inc(hs, 16))
            return n[sm]

        def dmat(eng, sm, out, in_):
            n[sm] += 16
            hs = SEMS[sm]
            streams[eng].append(lambda e, out=out, in_=in_, hs=hs:
                                e.dma_start_transpose(out, in_).then_inc(hs, 16))
            return n[sm]

        def wait(eng, sm, v):
            if v <= 0:
                return
            hs = SEMS[sm]
            streams[eng].append(lambda e, hs=hs, v=v: e.wait_ge(hs, v))

        def instr(eng, f, inc=None):
            if inc is None:
                streams[eng].append(f)
                return None
            n[inc] += 1
            hs = SEMS[inc]
            streams[eng].append(lambda e, f=f, hs=hs: f(e).then_inc(hs, 1))
            return n[inc]

        BAv = BA[:, :].rearrange("p (c q) -> p c q", c=NCH)

        try:
            x_tiled = x_in.rearrange("(i p) d -> i p d", p=128)
            xld = [dma("gpsimd", "Gx", B8[:, i * D:(i + 1) * D], x_tiled[i]) for i in range(NTILE)]
            wbuf = sbuf("wbuf", [128, NCH * D], scope=attn_sc)
            wsc = sbuf("wsc", [128, 1024], F32, scope=attn_sc)
            wu_ld = dma("gpsimd", "Gw", wbuf[:, :].rearrange("p (c d) -> p c d", c=NCH),
                        Wu_d.rearrange("(c p) d -> p c d", p=128))
            cld = {k: dma("scalar", "Lc", sb_c[k][:, :], cns[k][:, :]) for k in CONSTS
                   if k != "cDec"}
            dma("scalar", "Lc", bias_u[:, :], bu_d.rearrange("(c p) -> p c", p=128), slow=True)
            dma("scalar", "Lc", bias_v[:, :], bv_d.rearrange("(c p) -> p c", p=128), slow=True)
            v_bo = dma("scalar", "Lc", bias_o[:, :], bo_d.rearrange("(c p) -> p c", p=128), slow=True)
            dma("scalar", "Lc", b3sb[:, :], b3s_d.rearrange("(p o) -> p o", o=1))
            dma("scalar", "Lc", b1fb[:, :], b1f_d.rearrange("(p o) -> p o", o=1))
            dma("scalar", "Lc", b2fb[:, :], b2f_d.rearrange("(p o) -> p o", o=1))
            dma("scalar", "Lc", b1t[:, :], b1v_d.rearrange("(t p) -> p t", p=128), slow=True)
            v_b2t = dma("scalar", "Lc", b2t[:, :], b2v_d.rearrange("(t p) -> p t", p=128), slow=True)
            dma("scalar", "Lc", W1sb[:, :], W1_d[:, :])
            dma("scalar", "Lc", W2sb[:, :], W2_d[:, :])
            lw3 = dma("scalar", "Lc", W3sb[:, :], W3s_d[:, :])

            def emit_ln_stats_pair(xtA, xtB):
                chains = []
                for (xt, c, b) in ((xtA, 0, 0), (xtB, 8, 12)):
                    chains.append([
                        lambda v, xt=xt, b=b: v.bn_stats(bnb[:, b:b + 6], xt[:, 0:512]),
                        lambda v, xt=xt, b=b: v.bn_stats(bnb[:, b + 6:b + 12], xt[:, 512:1024]),
                        lambda v, c=c, b=b: v.reduce_sum(agg[:, c:c + 1], bnb[:, b + 1:b + 12:3], axis=AX),
                        lambda v, c=c, b=b: v.tensor_tensor(agg[:, c + 2:c + 6], bnb[:, b + 1:b + 12:3], bnb[:, b + 1:b + 12:3], ALU.mult),
                        lambda v, c=c: v.reduce_sum(agg[:, c + 1:c + 2], agg[:, c + 2:c + 6], axis=AX),
                        lambda v, c=c, b=b: v.reduce_sum(agg[:, c + 6:c + 7], bnb[:, b + 2:b + 12:3], axis=AX),
                        lambda v, c=c: v.tensor_scalar(agg[:, c:c + 1], agg[:, c:c + 1], 0.25, None, ALU.mult),
                        lambda v, c=c: v.tensor_tensor(agg[:, c + 7:c + 8], agg[:, c:c + 1], agg[:, c:c + 1], ALU.mult),
                        lambda v, c=c: v.tensor_scalar(agg[:, c + 6:c + 7], agg[:, c + 6:c + 7], 1.0 / 1024.0, None, ALU.mult),
                        lambda v, c=c: v.tensor_scalar(agg[:, c + 1:c + 2], agg[:, c + 1:c + 2], 0.25, None, ALU.mult),
                        lambda v, c=c: v.tensor_tensor(agg[:, c + 6:c + 7], agg[:, c + 6:c + 7], agg[:, c + 1:c + 2], ALU.add),
                        lambda v, c=c: v.tensor_tensor(agg[:, c + 6:c + 7], agg[:, c + 6:c + 7], agg[:, c + 7:c + 8], ALU.subtract),
                        lambda v, c=c: v.tensor_scalar(agg[:, c + 6:c + 7], agg[:, c + 6:c + 7], 1e-5, None, ALU.add),
                        lambda v, c=c: v.reciprocal(agg[:, c + 1:c + 2], agg[:, c + 6:c + 7]),
                    ])
                last = None
                for k in range(len(chains[0])):
                    for t in range(2):
                        last = instr("vector", chains[t][k],
                                     inc=("MV" if k == len(chains[0]) - 1 else None))
                return last

            lnst3 = lnst[:, :].rearrange("p (c q) -> p c q", q=128)
            stg3 = lob[:, :].rearrange("p (c q) -> p c q", q=128)
            BAt4 = BA[:, :].rearrange("p (c i q) -> p c i q", c=NCH, i=NTILE)
            prev_tt = 0
            for pi in range(NTILE // 2):
                iA, iB = 2 * pi, 2 * pi + 1
                xtA = B8[:, iA * D:(iA + 1) * D]
                xtB = B8[:, iB * D:(iB + 1) * D]
                wait("vector", "Gx", xld[-1])
                if pi > 0:
                    wait("vector", "Tln", prev_tt)
                vrec = emit_ln_stats_pair(xtA, xtB)
                wait("scalar", "MV", vrec)
                instr("scalar", lambda s: s.activation(agg[:, 2:3], agg[:, 1:2], AF.Sqrt))
                vsq = instr("scalar", lambda s: s.activation(agg[:, 10:11], agg[:, 9:10], AF.Sqrt), inc="MS")
                wait("vector", "MS", vsq)
                vnA = instr("vector", lambda v, xtA=xtA: v.tensor_scalar(
                    lnst[:, :], xtA, agg[:, 0:1], agg[:, 2:3], ALU.subtract, ALU.mult), inc="MV")
                vnB = instr("vector", lambda v, xtB=xtB: v.tensor_scalar(
                    lob[:, :], xtB, agg[:, 8:9], agg[:, 10:11], ALU.subtract, ALU.mult), inc="MV")
                wait("sync", "MV", vnA)
                dmat("sync", "Tln", BAt4[:, :, iA, :], lnst3)
                wait("sync", "MV", vnB)
                prev_tt = dmat("sync", "Tln", BAt4[:, :, iB, :], stg3)
            lnT_all = prev_tt
            ckpt("p1")

            def emit_proj(evac):
                prev_ev = 0
                for m in range(NCH):
                    if prev_ev:
                        wait("tensor", evac_sem, prev_ev)
                    for c in range(NCH):
                        for s in range(4):
                            instr("tensor", lambda t, m=m, c=c, s=s: t.matmul(
                                PS4[s][:, 0:512],
                                wbuf[:, c * D + m * 128: c * D + (m + 1) * 128],
                                BAv[:, c, s * 512:(s + 1) * 512],
                                start=(c == 0), stop=(c == NCH - 1)),
                                inc=("MT" if c == NCH - 1 else None))
                    prev_ev = evac(m, n["MT"])
                return prev_ev

            wait("tensor", "Tln", lnT_all)
            wait("tensor", "Gw", wu_ld)
            wait("vector", "Lc", n["Lc"])
            evac_sem = "MV"

            def evac_u(m, mtv):
                last = None
                for s in range(4):
                    wait("vector", "MT", mtv)
                    last = instr("vector", lambda v, m=m, s=s: v.tensor_scalar(
                        BB[:, m * NT + s * 512: m * NT + (s + 1) * 512],
                        PS4[s][:, 0:512], bias_u[:, m:m + 1], None, ALU.add), inc="MV")
                return last

            emit_proj(evac_u)
            u_mm_done = n["MT"]
            u_done = n["MV"]
            ckpt("p2u")

            wait("gpsimd", "MT", u_mm_done)
            wv_ld = dma("gpsimd", "Gw", wbuf[:, :].rearrange("p (c d) -> p c d", c=NCH),
                        Wv_d.rearrange("(c p) d -> p c d", p=128))
            wait("tensor", "Gw", wv_ld)

            vst = [0, 0]

            def evac_v(m, mtv):
                for s in range(4):
                    half = s % 2
                    wait("scalar", "MT", mtv)
                    wait("scalar", "Gv0" if half == 0 else "Gv1", vst[half])
                    vv = instr("scalar", lambda sc, m=m, s=s, half=half: sc.activation(
                        stage[:, half * 512:(half + 1) * 512], PS4[s][:, 0:512], AF.Identity,
                        bias=bias_v[:, m:m + 1], scale=1.0), inc="MS")
                    wait("gpsimd", "MS", vv)
                    vst[half] = dma("gpsimd", "Gv0" if half == 0 else "Gv1",
                                    iA1[m, :, s * 512:(s + 1) * 512],
                                    stage[:, half * 512:(half + 1) * 512])
                return None

            evac_sem = "MS"

            def evac_v_wrap(m, mtv):
                evac_v(m, mtv)
                return n["MS"]

            wait("scalar", "Lc", n["Lc"])
            emit_proj(evac_v_wrap)
            v_mm_done = n["MT"]
            v_store_done = None
            ckpt("p2")

            wait("gpsimd", "Gv0", vst[0])
            wait("gpsimd", "Gv1", vst[1])
            a2a1 = instr("gpsimd", lambda g: g.collective_compute(
                "AllToAll", ALU.bypass, replica_groups=[list(range(NCORES))],
                ins=[iA1.tensor.ap().opt()], outs=[oA1.tensor.ap().opt()]), inc="CC")
            wait("gpsimd", "CC", a2a1)

            peb = sbuf("peb", [64, 2 * 512], F32, scope=pe_sc)
            decb = sbuf("decb", [1, 4096], F32, scope=pe_sc)
            dma("scalar", "Lc", decb[0:1, :], cns["cDec"][0:1, :])
            g1s = BA[:, 0:8192].bitcast(F32)
            g2s = BA[:, 8192:16384].bitcast(F32)

            wait("tensor", "Lc", n["Lc"])
            prev_ms = 0
            pe_mt = {}
            for s8 in range(8):
                if s8 >= 1:
                    wait("scalar", "MT", pe_mt[s8 - 1])
                lv = dma("scalar", "Lp", peb[0:64, (s8 % 2) * 512:(s8 % 2) * 512 + 512],
                         peT_d[:, s8 * 512:(s8 + 1) * 512])
                wait("tensor", "Lp", lv)
                if prev_ms:
                    wait("tensor", "MS", prev_ms)
                vm = instr("tensor", lambda t, s8=s8: t.matmul(
                    psE[0:64, 0:512], W1sb[0:64, :],
                    peb[0:64, (s8 % 2) * 512:(s8 % 2) * 512 + 512],
                    start=True, stop=True), inc="MT")
                pe_mt[s8] = vm
                wait("scalar", "MT", vm)
                prev_ms = instr("scalar", lambda sc, s8=s8: sc.activation(
                    g1s[0:64, s8 * 512:(s8 + 1) * 512], psE[0:64, 0:512], AF.Identity,
                    bias=b1fb[0:64, 0:1], scale=1.0), inc="MS")

            def emit_gelu(buf):
                last = None
                for hh in range(4):
                    src = buf[0:64, hh * 1024:(hh + 1) * 1024]
                    t1 = hstg[0:64, 0:1024]
                    t2 = hstg[0:64, 1024:2048]
                    vq = instr("scalar", lambda s, src=src, t1=t1: s.activation(t1, src, AF.Square), inc="MS")
                    wait("vector", "MS", vq)
                    for a, b in ((0, 512), (512, 1024)):
                        instr("vector", lambda v, t1=t1, a=a, b=b: v.tensor_scalar(
                            t1[:, a:b], t1[:, a:b], 0.044715, 1.0, ALU.mult, ALU.add))
                    vw = None
                    for a, b in ((0, 512), (512, 1024)):
                        vw = instr("vector", lambda v, t1=t1, src=src, a=a, b=b: v.tensor_tensor(
                            t1[:, a:b], t1[:, a:b], src[:, a:b], ALU.mult), inc="MV")
                    wait("scalar", "MV", vw)
                    vt = instr("scalar", lambda s, t1=t1, t2=t2: s.activation(
                        t2, t1, AF.Tanh, scale=0.7978845608028654), inc="MS")
                    wait("vector", "MS", vt)
                    for a, b in ((0, 512), (512, 1024)):
                        instr("vector", lambda v, t2=t2, a=a, b=b: v.tensor_scalar(
                            t2[:, a:b], t2[:, a:b], 1.0, 0.5, ALU.add, ALU.mult))
                    for a, b in ((0, 512), (512, 1024)):
                        last = instr("vector", lambda v, src=src, t2=t2, a=a, b=b: v.tensor_tensor(
                            src[:, a:b], t2[:, a:b], src[:, a:b], ALU.mult), inc="MV")
                return last

            wait("scalar", "MS", n["MS"])
            vg1 = emit_gelu(g1s)
            wait("tensor", "MV", vg1)
            prev_ms = 0
            for s8 in range(8):
                if prev_ms:
                    wait("tensor", "MS", prev_ms)
                vm = instr("tensor", lambda t, s8=s8: t.matmul(
                    psE[0:64, 0:512], W2sb[0:64, :], g1s[0:64, s8 * 512:(s8 + 1) * 512],
                    start=True, stop=True), inc="MT")
                wait("scalar", "MT", vm)
                prev_ms = instr("scalar", lambda sc, s8=s8: sc.activation(
                    g2s[0:64, s8 * 512:(s8 + 1) * 512], psE[0:64, 0:512], AF.Identity,
                    bias=b2fb[0:64, 0:1], scale=1.0), inc="MS")
            wait("scalar", "MS", n["MS"])
            vg2 = emit_gelu(g2s)
            wait("tensor", "MV", vg2)
            wait("vector", "Gv0", vst[0])
            wait("vector", "Gv1", vst[1])
            prev_hv = 0
            hst_prev = {}
            for s8 in range(8):
                par8 = s8 % 2
                if prev_hv:
                    wait("tensor", "MV", prev_hv)
                instr("tensor", lambda t, s8=s8: t.matmul(
                    psE[:, 0:512], W3sb[0:64, :], g2s[0:64, s8 * 512:(s8 + 1) * 512],
                    start=True, stop=True))
                vdm = instr("tensor", lambda t, s8=s8: t.matmul(
                    psF[:, 0:512], sb_c["cOnes"][0:1, :], decb[0:1, s8 * 512:(s8 + 1) * 512],
                    start=True, stop=True), inc="MT")
                wait("vector", "MT", vdm)
                if s8 >= 2:
                    wait("vector", "Gh", hst_prev[s8 - 2])
                for a, b in ((0, 256), (256, 512)):
                    instr("vector", lambda v, a=a, b=b: v.tensor_scalar(
                        stage[:, a:b], psE[:, a:b], b3sb[:, 0:1], None, ALU.add))
                for a, b in ((0, 256), (256, 512)):
                    prev_hv = instr("vector", lambda v, par8=par8, a=a, b=b: v.tensor_tensor(
                        hstg[:, par8 * 512 + a:par8 * 512 + b], stage[:, a:b], psF[:, a:b], ALU.mult), inc="MV")
                wait("gpsimd", "MV", prev_hv)
                hst_prev[s8] = dma("gpsimd", "Gh", hscr[:, s8 * 512:(s8 + 1) * 512],
                                   hstg[:, par8 * 512:(par8 + 1) * 512])
            h_done = hst_prev[7]
            ckpt("p3h")
            pe_sc.close()

            zh32 = sbuf("zh32", [64, 128 * 64], BF16, scope=conv_sc)
            zv = sbuf("zv", [64, 4 * 256], BF16, scope=conv_sc)
            at_b = sbuf("at_b", [128, 2 * 195], BF16, scope=conv_sc)
            ath32 = sbuf("ath32", [64, 195], F32, scope=conv_sc)
            hb32 = sbuf("hb32", [128, 2 * 260], F32, scope=conv_sc)
            vh32 = sbuf("vh32", [128, 2 * 260], F32, scope=conv_sc)
            tmpb = sbuf("tmpb", [128, 2 * 520], F32, scope=conv_sc)
            ypb = sbuf("ypb", [128, 2 * 260], BF16, scope=conv_sc)
            esb = sbuf("esb", [65, 2 * 1024], BF16, scope=conv_sc)
            ysb = sbuf("ysb", [64, 4 * 256], BF16, scope=conv_sc)

            wait("sync", "Gh", h_done)
            wait("sync", "MT", pe_mt[7])
            zh_all = dma("sync", "Tzh", zh32[:, :].rearrange("a (c b) -> a c b", b=64),
                         hscr.rearrange("c (a b) -> a c b", b=64))

            w1n_t = w1n_d.rearrange("(t p) d -> t p d", p=128)
            wait("scalar", "MT", pe_mt[7])
            for ft in range(NFT):
                lv = dma("scalar", "Ls", wsc[:, :], w1n_t[ft])
                wait("vector", "Ls", lv)
                vr = instr("vector", lambda v, ft=ft: v.reduce_max(
                    sw1[:, ft:ft + 1], wsc[:, :], axis=AX, apply_absolute_value=True), inc="MV")
                wait("scalar", "MV", vr)
            w2n_t = w2n_d.rearrange("(t p) f -> t p f", p=128)
            for ct in range(NCH):
                for pc in range(4):
                    lv = dma("scalar", "Ls", wsc[:, :], w2n_t[ct][:, pc * 1024:(pc + 1) * 1024])
                    wait("vector", "Ls", lv)
                    vr = instr("vector", lambda v, ct=ct, pc=pc: v.reduce_max(
                        mstats[:, 4 * ct + pc:4 * ct + pc + 1], wsc[:, :],
                        axis=AX, apply_absolute_value=True), inc="MV")
                    wait("scalar", "MV", vr)
            for ct in range(NCH):
                instr("vector", lambda v, ct=ct: v.reduce_max(
                    sw2[:, ct:ct + 1], mstats[:, 4 * ct:4 * ct + 4], axis=AX))
            instr("vector", lambda v: v.tensor_scalar(sw1[:, :], sw1[:, :], 1e-8, 1.0 / 127.0, ALU.max, ALU.mult))
            instr("vector", lambda v: v.tensor_scalar(sw2[:, :], sw2[:, :], 1e-8, 1.0 / 127.0, ALU.max, ALU.mult))
            vrw1 = instr("vector", lambda v: v.reciprocal(rw1[:, :], sw1[:, :]), inc="MV")
            vrw2 = instr("vector", lambda v: v.reciprocal(rw2[:, :], sw2[:, :]), inc="MV")

            conv_c_mt = {}
            yst = {}
            yst2 = {}
            unit_cfree = {}
            unit_dfree = {}
            dh_mt = {}
            ad_mt = {}
            pr_mg = {}
            hbfree = {}
            yv_mv = {}
            ap_mt = {}
            e_mv = {}
            b_mt = {}
            y_mv = {}
            uc = 0
            pc_ = 0
            ps_ms0 = n["MS"]
            ps_mv0 = n["MV"]
            CP = [psA, psG]
            DPb = [psB, psH]
            DPa = [psC, psE]
            wait("tensor", "Tzh", zh_all)
            wait("tensor", "MS", ps_ms0)
            wait("tensor", "MV", ps_mv0)
            wait("sync", "CC", a2a1)
            wait("scalar", "CC", a2a1)
            oA1v = oA1.rearrange("(b hi) p (ah qq) -> p hi ah b qq", hi=2, qq=64)
            iA2v = iA2.rearrange("(b hi) p (ah qq) -> p hi ah b qq", hi=2, qq=64)
            NCONV = int(os.environ.get("ATH_CONV_N", "128"))
            zld0 = {}
            zld1 = {}

            def issue_zv_loads(d):
                if d >= NCONV:
                    return
                sl4 = d % 4
                if d >= 4:
                    wait("sync", "MT", conv_c_mt[d - 4])
                    wait("scalar", "MT", conv_c_mt[d - 4])
                zvd = zv[:, sl4 * 256:(sl4 + 1) * 256].rearrange("a (b q) -> a b q", b=4)
                zld0[d] = dma("sync", "Tz0", zvd[0:32], oA1v[d, 0])
                zld1[d] = dma("scalar", "Tz1", zvd[32:64], oA1v[d, 1])

            for d in range(4):
                issue_zv_loads(d)
            for dl in range(NCONV):
                slot = dl % 2
                sl4 = dl % 4
                cb, db = CP[uc % 2], DPb[uc % 2]
                if uc >= 2:
                    wait("tensor", *unit_cfree[uc - 2])
                elif uc == 0:
                    wait("tensor", "MS", ps_ms0)
                hcm = instr("tensor", lambda t, dl=dl, cb=cb: t.matmul(
                    cb[0:64, 0:130], zh32[:, dl * 64:(dl + 1) * 64], sb_c["cR1h"][:, :],
                    start=True, stop=True), inc="MT")
                wait("scalar", "MT", hcm)
                if dl >= 1:
                    wait("scalar", "MT", dh_mt[dl - 1])
                instr("scalar", lambda s, cb=cb: s.activation(
                    ath32[0:64, 65:195], cb[0:64, 0:130], AF.Identity))
                hae = instr("scalar", lambda s, cb=cb: s.activation(
                    ath32[0:64, 0:65], cb[0:64, 65:130], AF.Identity, scale=-1.0), inc="MS")
                unit_cfree[uc] = ("MS", hae)
                if uc >= 2:
                    wait("tensor", *unit_dfree[uc - 2])
                elif uc == 0:
                    wait("tensor", "MS", ps_ms0)
                wait("tensor", "MS", hae)
                instr("tensor", lambda t, db=db: t.matmul(
                    db[:, 0:130], sb_c["cG1f"][:, :], ath32[0:64, 65:195], start=True, stop=False))
                hdm = instr("tensor", lambda t, db=db: t.matmul(
                    db[:, 0:130], sb_c["cG2f"][:, :], ath32[0:64, 0:130], start=False, stop=True), inc="MT")
                dh_mt[dl] = hdm
                wait("scalar", "MT", hdm)
                if dl >= 2:
                    wait("scalar", "MG", hbfree[dl - 2])
                instr("scalar", lambda s, db=db, slot=slot: s.activation(
                    hb32[:, slot * 260:slot * 260 + 130], db[:, 0:130], AF.Identity))
                hEv = instr("scalar", lambda s, db=db, slot=slot: s.activation(
                    hb32[:, slot * 260 + 130:slot * 260 + 260], db[:, 0:130], AF.Identity), inc="MS")
                unit_dfree[uc] = ("MS", hEv)
                uc += 1
                for pp in range(2):
                    ps_ = pc_ % 2
                    cb, db, da = CP[uc % 2], DPb[uc % 2], DPa[uc % 2]
                    if uc >= 2:
                        wait("tensor", *unit_cfree[uc - 2])
                    wait("tensor", "Tz0", zld0[dl])
                    wait("tensor", "Tz1", zld1[dl])
                    cm = instr("tensor", lambda t, cb=cb, sl4=sl4, pp=pp: t.matmul(
                        cb[0:128, 0:130], zv[:, sl4 * 256 + pp * 128:sl4 * 256 + (pp + 1) * 128],
                        sb_c["cR1h"][:, :], start=True, stop=True), inc="MT")
                    if pp == 1:
                        conv_c_mt[dl] = cm
                        issue_zv_loads(dl + 4)
                    wait("vector", "MT", cm)
                    if pc_ >= 2:
                        wait("vector", "MT", ad_mt[pc_ - 2])
                    instr("vector", lambda v, cb=cb, ps_=ps_: v.tensor_copy(
                        at_b[:, ps_ * 195 + 65:ps_ * 195 + 195], cb[:, 0:130]))
                    ae = instr("vector", lambda v, cb=cb, ps_=ps_: v.tensor_scalar(
                        at_b[:, ps_ * 195:ps_ * 195 + 65], cb[:, 65:130], -1.0, None, ALU.mult), inc="MV")
                    unit_cfree[uc] = ("MV", ae)
                    if uc >= 2:
                        wait("tensor", *unit_dfree[uc - 2])
                    wait("tensor", "MV", ae)
                    dm = None
                    for sg, pb in ((1, db), (0, da)):
                        instr("tensor", lambda t, pb=pb, ps_=ps_, sg=sg: t.matmul(
                            pb[:, 0:130], sb_c["cG1d"][sg * 64:sg * 64 + 64, :],
                            at_b[sg * 64:sg * 64 + 64, ps_ * 195 + 65:ps_ * 195 + 195],
                            start=True, stop=False))
                        dm = instr("tensor", lambda t, pb=pb, ps_=ps_, sg=sg: t.matmul(
                            pb[:, 0:130], sb_c["cG2d"][sg * 64:sg * 64 + 64, :],
                            at_b[sg * 64:sg * 64 + 64, ps_ * 195:ps_ * 195 + 130],
                            start=False, stop=True), inc="MT")
                    ad_mt[pc_] = dm
                    wait("scalar", "MT", dm)
                    if pc_ >= 2:
                        wait("scalar", "MG", pr_mg[pc_ - 2])
                    instr("scalar", lambda s, db=db, ps_=ps_: s.activation(
                        vh32[:, ps_ * 260:ps_ * 260 + 130], db[:, 0:130], AF.Identity))
                    vEv = instr("scalar", lambda s, da=da, ps_=ps_: s.activation(
                        vh32[:, ps_ * 260 + 130:ps_ * 260 + 260], da[:, 0:130], AF.Identity), inc="MS")
                    unit_dfree[uc] = ("MS", vEv)
                    uc += 1
                    wait("gpsimd", "MS", max(vEv, hEv))
                    if pc_ >= 2:
                        wait("gpsimd", "MV", yv_mv[pc_ - 2])
                    vv4 = vh32[:, ps_ * 260:(ps_ + 1) * 260].rearrange("p (s t c) -> p s t c", s=2, t=2)
                    hb4 = hb32[:, slot * 260:(slot + 1) * 260].rearrange("p (s t c) -> p s t c", s=2, t=2)
                    tp4 = tmpb[:, ps_ * 520:(ps_ + 1) * 520].rearrange("p (q s c) -> p q s c", q=4, s=2)
                    instr("gpsimd", lambda g, vv4=vv4, hb4=hb4, tp4=tp4: g.tensor_tensor(
                        tp4[:, 0], vv4[:, :, 0], hb4[:, :, 0], ALU.mult))
                    instr("gpsimd", lambda g, vv4=vv4, hb4=hb4, tp4=tp4: g.tensor_tensor(
                        tp4[:, 1], vv4[:, :, 1], hb4[:, :, 1], ALU.mult))
                    instr("gpsimd", lambda g, vv4=vv4, hb4=hb4, tp4=tp4: g.tensor_tensor(
                        tp4[:, 2], vv4[:, :, 0], hb4[:, :, 1], ALU.mult))
                    pg = instr("gpsimd", lambda g, vv4=vv4, hb4=hb4, tp4=tp4: g.tensor_tensor(
                        tp4[:, 3], vv4[:, :, 1], hb4[:, :, 0], ALU.mult), inc="MG")
                    pr_mg[pc_] = pg
                    if pp == 1:
                        hbfree[dl] = pg
                    wait("vector", "MG", pg)
                    if pc_ >= 2:
                        wait("vector", "MT", ap_mt[pc_ - 2])
                    yp4 = ypb[:, ps_ * 260:(ps_ + 1) * 260].rearrange("p (s t c) -> p s t c", s=2, t=2)
                    instr("vector", lambda v, yp4=yp4, tp4=tp4: v.tensor_tensor(
                        yp4[:, :, 0], tp4[:, 0], tp4[:, 1], ALU.subtract))
                    ym = instr("vector", lambda v, yp4=yp4, tp4=tp4: v.tensor_tensor(
                        yp4[:, :, 1], tp4[:, 2], tp4[:, 3], ALU.add), inc="MV")
                    yv_mv[pc_] = ym
                    ab = psD
                    if pc_ >= 1:
                        wait("tensor", "MV", e_mv[pc_ - 1])
                    else:
                        wait("tensor", "MS", ps_ms0)
                    wait("tensor", "MV", ym)
                    am = None
                    for sg in range(2):
                        base = ps_ * 260 + (1 - sg) * 130
                        instr("tensor", lambda t, ab=ab, sg=sg, base=base: t.matmul(
                            ab[0:65, sg * 256:(sg + 1) * 256], ypb[:, base:base + 65],
                            sb_c["cR3"][:, :], start=True, stop=False))
                        am = instr("tensor", lambda t, ab=ab, sg=sg, base=base: t.matmul(
                            ab[0:65, sg * 256:(sg + 1) * 256], ypb[:, base + 65:base + 130],
                            sb_c["cR4h"][:, :], start=False, stop=True), inc="MT")
                    ap_mt[pc_] = am
                    wait("vector", "MT", am)
                    if dl >= 2:
                        wait("vector", "MT", b_mt[dl - 2])
                    ee = instr("vector", lambda v, ab=ab, slot=slot, pp=pp: v.tensor_copy(
                        esb[:, slot * 1024 + pp * 512:slot * 1024 + (pp + 1) * 512],
                        ab[0:65, 0:512]), inc="MV")
                    e_mv[pc_] = ee
                    pc_ += 1
                bb = psF
                if dl >= 1:
                    wait("tensor", "MV", y_mv[dl - 1])
                else:
                    wait("tensor", "MV", ps_mv0)
                wait("tensor", "MV", e_mv[pc_ - 1])
                es4 = esb[:, slot * 1024:(slot + 1) * 1024].rearrange("p (u j q) -> p u j q", u=4, j=4)
                instr("tensor", lambda t, bb=bb, es4=es4: t.matmul(
                    bb[0:64, 0:256], sb_c["cCO2"][:, :], es4[:, :, 0], start=True, stop=False))
                instr("tensor", lambda t, bb=bb, es4=es4: t.matmul(
                    bb[0:64, 0:256], sb_c["cSI2n"][:, :], es4[:, :, 2], start=False, stop=False))
                instr("tensor", lambda t, bb=bb, es4=es4: t.matmul(
                    bb[0:64, 0:256], sb_c["cCO2s"][:, :], es4[:, :, 1], start=False, stop=False))
                bm = instr("tensor", lambda t, bb=bb, es4=es4: t.matmul(
                    bb[0:64, 0:256], sb_c["cSI2ns"][:, :], es4[:, :, 3], start=False, stop=True), inc="MT")
                b_mt[dl] = bm
                wait("vector", "MT", bm)
                if dl >= 4:
                    wait("vector", "Ty", yst[dl - 4])
                    wait("vector", "Ty2", yst2[dl - 4])
                ye = instr("vector", lambda v, bb=bb, sl4=sl4: v.tensor_copy(
                    ysb[:, sl4 * 256:(sl4 + 1) * 256], bb[0:64, 0:256]), inc="MV")
                y_mv[dl] = ye
                wait("sync", "MV", ye)
                wait("gpsimd", "MV", ye)
                ysv = ysb[:, sl4 * 256:(sl4 + 1) * 256].rearrange("a (b q) -> a b q", b=4)
                yst[dl] = dma("sync", "Ty", iA2v[dl, 0], ysv[0:32])
                yst2[dl] = dma("gpsimd", "Ty2", iA2v[dl, 1], ysv[32:64])
            conv_out_T = yst[max(yst)] if yst else 0
            conv_out_T2 = yst2[max(yst2)] if yst2 else 0
            ckpt("p5")

            wait("gpsimd", "Ty", conv_out_T)
            wait("gpsimd", "Ty2", conv_out_T2)
            a2a2 = instr("gpsimd", lambda g: g.collective_compute(
                "AllToAll", ALU.bypass, replica_groups=[list(range(NCORES))],
                ins=[iA2.tensor.ap().opt()], outs=[oA2.tensor.ap().opt()]), inc="CC")
            wait("gpsimd", "CC", a2a2)
            vcl = dma("gpsimd", "Ga", BC[:, :].rearrange("p (j q) -> p j q", j=NCORES),
                      oA2[:, :, :].rearrange("j p q -> p j q"))

            wait("vector", "Ga", vcl)
            wait("vector", "MV", u_done)
            ypr = instr("vector", lambda v: v.tensor_tensor(BA[:, :], BB[:, :], BC[:, :], ALU.mult), inc="MV")
            wait("gpsimd", "MT", v_mm_done)
            wo_ld = dma("gpsimd", "Gw", wbuf[:, :].rearrange("p (c d) -> p c d", c=NCH),
                        Wo_d.rearrange("(c p) d -> p c d", p=128))
            wait("tensor", "Gw", wo_ld)
            wait("tensor", "MV", ypr)
            evac_sem = "MV"

            BBtile = BB[:, :].rearrange("p (i c q) -> p i c q", i=NTILE, c=NCH)

            def evac_o(m, mtv):
                last = None
                for s in range(4):
                    wait("vector", "MT", mtv)
                    last = instr("vector", lambda v, m=m, s=s: v.tensor_scalar(
                        BBtile[:, 4 * s:4 * s + 4, m, :],
                        PS4[s][:, 0:512], bias_o[:, m:m + 1], None, ALU.add), inc="MV")
                return last

            emit_proj(evac_o)

            res_add = []
            x2lo_t = x2lo.rearrange("(i p) d -> i p d", p=128)
            x2st = {}
            for i in range(NTILE):
                wait("sync", "MV", n["MV"])
                src2 = BB[:, i * D:(i + 1) * D]
                dst3 = lnst[:, :].rearrange("t (c p) -> t c p", p=128)
                tt = dmat("sync", "Tat", dst3, src2)
                wait("vector", "Tat", tt)
                if i >= 1:
                    wait("vector", "Ls", x2st[i - 1])
                for a, b in ((0, 512), (512, 1024)):
                    instr("vector", lambda v, i=i, a=a, b=b: v.tensor_tensor(
                        stage[:, a:b], B8[:, i * D + a:i * D + b], lnst[:, a:b], ALU.add))
                for a, b in ((0, 512), (512, 1024)):
                    instr("vector", lambda v, i=i, a=a, b=b: v.tensor_copy(
                        B8[:, i * D + a:i * D + b], stage[:, a:b]))
                ra = None
                for a, b in ((0, 512), (512, 1024)):
                    ra = instr("vector", lambda v, i=i, a=a, b=b: v.tensor_tensor(
                        lob[:, a:b], stage[:, a:b], B8[:, i * D + a:i * D + b], ALU.subtract), inc="MV")
                res_add.append(ra)
                wait("scalar", "MV", res_add[-1])
                x2st[i] = dma("scalar", "Ls", x2lo_t[i], lob[:, :])

            for pi in range(NTILE // 2):
                iA, iB = 2 * pi, 2 * pi + 1
                xtA = B8[:, iA * D:(iA + 1) * D]
                xtB = B8[:, iB * D:(iB + 1) * D]
                wait("vector", "MV", res_add[iB])
                vrec = emit_ln_stats_pair(xtA, xtB)
                instr("vector", lambda v, iA=iA: v.tensor_copy(mstats[:, 2 * iA:2 * iA + 1], agg[:, 0:1]))
                instr("vector", lambda v, iB=iB: v.tensor_copy(mstats[:, 2 * iB:2 * iB + 1], agg[:, 8:9]))
                wait("scalar", "MV", vrec)
                instr("scalar", lambda s, iA=iA: s.activation(
                    mstats[:, 2 * iA + 1:2 * iA + 2], agg[:, 1:2], AF.Sqrt))
                vsq = instr("scalar", lambda s, iB=iB: s.activation(
                    mstats[:, 2 * iB + 1:2 * iB + 2], agg[:, 9:10], AF.Sqrt), inc="MS")
                wait("vector", "MS", vsq)
                instr("vector", lambda v, xtA=xtA, iA=iA: v.tensor_scalar(
                    stage[:, 0:1024], xtA, mstats[:, 2 * iA:2 * iA + 1], mstats[:, 2 * iA + 1:2 * iA + 2],
                    ALU.subtract, ALU.mult))
                instr("vector", lambda v, xtB=xtB, iB=iB: v.tensor_scalar(
                    hstg[:, 0:1024], xtB, mstats[:, 2 * iB:2 * iB + 1], mstats[:, 2 * iB + 1:2 * iB + 2],
                    ALU.subtract, ALU.mult))
                instr("vector", lambda v, iA=iA: v.reduce_max(
                    mstats[:, 32 + iA:33 + iA], stage[:, 0:1024], axis=AX))
                instr("vector", lambda v, iB=iB: v.reduce_max(
                    mstats[:, 32 + iB:33 + iB], hstg[:, 0:1024], axis=AX), inc="MV")
            wait("vector", "MV", n["MV"])
            instr("vector", lambda v: v.reduce_max(lmax[:, 0:1], mstats[:, 32:48], axis=AX), inc="MV")
            wait("tensor", "MV", n["MV"])
            vtm = instr("tensor", lambda t: t.matmul(
                psE[0:1, 0:128], lmax[:, 0:1], sb_c["cI128"][:, :], start=True, stop=True), inc="MT")
            wait("vector", "MT", vtm)
            vgm = instr("vector", lambda v: v.reduce_max(gsc[0:1, 0:1], psE[0:1, 0:128], axis=AX), inc="MV")
            wait("gpsimd", "MV", vgm)
            mx1 = dma("gpsimd", "Gm", mxi1[:, :], gsc[0:1, 0:1])
            wait("gpsimd", "Gm", mx1)
            ar1 = instr("gpsimd", lambda g: g.collective_compute(
                "AllReduce", ALU.max, replica_groups=[list(range(NCORES))],
                ins=[mxi1.tensor.ap().opt()], outs=[mxo1.tensor.ap().opt()]), inc="CC")
            wait("gpsimd", "CC", ar1)
            mx1b = dma("gpsimd", "Gm", gsc[0:1, 1:2], mxo1[:, :])
            wait("vector", "Gm", mx1b)
            vv = instr("vector", lambda v: v.tensor_scalar(gsc[0:1, 1:2], gsc[0:1, 1:2], 1e-8, None, ALU.max), inc="MV")
            wait("vector", "MV", vv)
            vv = instr("vector", lambda v: v.reciprocal(gsc[0:1, 2:3], gsc[0:1, 1:2]), inc="MV")
            wait("vector", "MV", vv)
            vv = instr("vector", lambda v: v.tensor_scalar(gsc[0:1, 2:3], gsc[0:1, 2:3], 255.0, None, ALU.mult), inc="MV")
            wait("vector", "MV", vv)
            vsx = instr("vector", lambda v: v.tensor_scalar(gsc[0:1, 3:4], gsc[0:1, 1:2], 1.0 / 255.0, None, ALU.mult), inc="MV")
            wait("tensor", "MV", vsx)
            wait("tensor", "MS", n["MS"])
            vbb = instr("tensor", lambda t: t.matmul(
                psE[:, 0:2], sb_c["cOnes"][0:1, :], gsc[0:1, 2:4], start=True, stop=True), inc="MT")
            wait("vector", "MT", vbb)
            vrx = instr("vector", lambda v: v.tensor_copy(rxb[:, 0:2], psE[:, 0:2]), inc="MV")
            wait("vector", "MV", max(vrw1, vrx))
            H1 = ((0, 16), (16, 32))
            for a, b in H1:
                instr("vector", lambda v, a=a, b=b: v.tensor_scalar(
                    sxw1[:, a:b], sw1[:, a:b], rxb[:, 1:2], None, ALU.mult))
            for a, b in H1:
                instr("vector", lambda v, a=a, b=b: v.tensor_tensor(
                    bq1[:, a:b], b1t[:, a:b], rw1[:, a:b], ALU.mult))
            for a, b in H1:
                instr("vector", lambda v, a=a, b=b: v.tensor_scalar(
                    bq1[:, a:b], bq1[:, a:b], rxb[:, 0:1], None, ALU.mult))
            for a, b in H1:
                instr("vector", lambda v, a=a, b=b: v.tensor_scalar(
                    bq1[:, a:b], bq1[:, a:b], MAGIC, MAGIC, ALU.add, ALU.subtract))
            vex = None
            for a, b in H1:
                vex = instr("vector", lambda v, a=a, b=b: v.tensor_tensor(
                    exv1[:, a:b], sxw1[:, a:b], bq1[:, a:b], ALU.mult), inc="MV")
            prev_qt = 0
            HS = ((0, 512), (512, 1024))
            for i in range(NTILE):
                xt = B8[:, i * D:(i + 1) * D]
                wait("vector", "MV", vrx)
                if i > 0:
                    wait("vector", "Tqx", prev_qt)
                for a, b in HS:
                    instr("vector", lambda v, xt=xt, i=i, a=a, b=b: v.tensor_scalar(
                        stage[:, a:b], xt[:, a:b], mstats[:, 2 * i:2 * i + 1], mstats[:, 2 * i + 1:2 * i + 2],
                        ALU.subtract, ALU.mult))
                for a, b in HS:
                    instr("vector", lambda v, a=a, b=b: v.tensor_scalar(
                        stage[:, a:b], stage[:, a:b], rxb[:, 0:1], MAGIC, ALU.mult, ALU.add))
                for a, b in HS:
                    instr("vector", lambda v, a=a, b=b: v.tensor_scalar(
                        stage[:, a:b], stage[:, a:b], MAGIC, 0.0, ALU.subtract, ALU.max))
                vq = None
                for a, b in HS:
                    vq = instr("vector", lambda v, a=a, b=b: v.tensor_scalar(
                        lnst[:, a:b], stage[:, a:b], 255.0, None, ALU.min), inc="MV")
                wait("sync", "MV", vq)
                src3 = lnst[:, :].rearrange("p (c q) -> p c q", q=128)
                dst3 = BA[:, :].rearrange("p (c i q) -> p c i q", c=NCH, i=NTILE)[:, :, i, :]
                prev_qt = dmat("sync", "Tqx", dst3, src3)
            qxT_done = prev_qt
            ckpt("p8")

            conv_sc.close()
            attn_sc.close()
            rwb1 = sbuf("rwb1", [128, FF], BF16, scope=ffnA_sc)
            rwb2 = sbuf("rwb2", [128, D], BF16, scope=ffnA_sc)
            wf1 = sbuf("wf1", [128, 2 * 1024], F32, scope=ffnA_sc)
            qw1 = sbuf("qw1", [128, 2 * 1024], BF16, scope=ffnA_sc)
            wp2 = sbuf("wp2", [128, 1024], F32, scope=ffnA_sc)
            qp2 = sbuf("qp2", [128, 1024], BF16, scope=ffnA_sc)

            wait("gpsimd", "MV", vex)
            wait("scalar", "MV", vex)

            def emit_bcast(rwt, nrows, dst, vgate):
                wait("sync", "MT", n["MT"])
                wait("tensor", "MV", vgate)
                vtr = instr("tensor", lambda t, rwt=rwt, nrows=nrows: t.matmul(
                    psE[0:nrows, 0:128], rwt[:, :], sb_c["cI128"][:, :], start=True, stop=True), inc="MT")
                wait("vector", "MT", vtr)
                vcp = instr("vector", lambda v, rwt=rwt, nrows=nrows: v.tensor_copy(
                    stage[0:nrows, 0:128], psE[0:nrows, 0:128]), inc="MV")
                out = None
                prev = 0
                for h0 in range(0, nrows, 16):
                    rr = min(16, nrows - h0)
                    wait("sync", "MV", max(vcp, prev))
                    row = dma("sync", "Trow",
                              wf1[0:1, 0:rr * 128].rearrange("o (a b) -> o a b", b=128),
                              stage[h0:h0 + rr, 0:128])
                    wait("tensor", "Trow", row)
                    for sI in range((rr * 128) // 512):
                        if prev:
                            wait("tensor", "MV", prev)
                        vbm = instr("tensor", lambda t, sI=sI: t.matmul(
                            psE[:, 0:512], sb_c["cOnes"][0:1, :], wf1[0:1, sI * 512:(sI + 1) * 512],
                            start=True, stop=True), inc="MT")
                        wait("vector", "MT", vbm)
                        out = instr("vector", lambda v, sI=sI, dst=dst, h0=h0: v.tensor_copy(
                            dst[:, h0 * 128 + sI * 512: h0 * 128 + (sI + 1) * 512], psE[:, 0:512]), inc="MV")
                        prev = out
                return out

            wait("sync", "Tqx", qxT_done)
            wait("sync", "Tat", n["Tat"])
            wait("sync", "Tln", n["Tln"])
            bc1 = emit_bcast(rw1, NFT, rwb1, max(vrw1, vex))
            ckpt("p8a")
            bc2 = emit_bcast(rw2, NCH, rwb2, max(vrw2, bc1))
            ckpt("p8b")

            w2T_t = w2T_d.rearrange("(f p) (c q) -> f p c q", p=128, q=128)
            pq_vv = 0
            pq_st = 0
            for fc in range(NFT):
                if pq_st:
                    wait("scalar", "Gq", pq_st)
                lv = dma("scalar", "Lq2", wp2[:, :].rearrange("p (c q) -> p c q", q=128), w2T_t[fc])
                wait("gpsimd", "Lq2", lv)
                wait("gpsimd", "MV", max(bc2, pq_vv))
                vgq = instr("gpsimd", lambda g: g.tensor_tensor(wp2[:, :], wp2[:, :], rwb2[:, :], ALU.mult), inc="MG")
                wait("vector", "MG", vgq)
                pq_vv = instr("vector", lambda v: v.tensor_scalar(
                    qp2[:, :], wp2[:, :], MAGIC, MAGIC, ALU.add, ALU.subtract), inc="MV")
                wait("gpsimd", "MV", pq_vv)
                pq_st = dma("gpsimd", "Gq",
                            qw2d.rearrange("m p (f q) -> p m f q", q=128)[:, :, fc],
                            qp2[:, :].rearrange("p (m q) -> p m q", q=128))
            w2q_done = pq_st
            ckpt("p8q")

            w1T_t = w1T_d.rearrange("(c p) (t q) -> t p c q", p=128, q=128)
            hst_store = {}
            ff1_vq = {}
            for ft in range(NFT):
                par = ft % 2
                if ft == 0:
                    wait("scalar", "MV", bc2)
                if ft >= 2:
                    wait("scalar", "MG", ff1_mg[ft - 2])
                lv = dma("scalar", "Lw0" if par == 0 else "Lw1",
                         wf1[:, par * 1024:(par + 1) * 1024].rearrange("p (c q) -> p c q", q=128),
                         w1T_t[ft])
                wait("gpsimd", "Lw0" if par == 0 else "Lw1", lv)
                wait("gpsimd", "MV", bc1)
                if ft >= 2:
                    wait("gpsimd", "MV", ff1_vq[ft - 2])
                for c8 in range(NCH):
                    instr("gpsimd", lambda g, par=par, c8=c8, ft=ft: g.tensor_tensor(
                        wf1[:, par * 1024 + c8 * 128: par * 1024 + (c8 + 1) * 128],
                        wf1[:, par * 1024 + c8 * 128: par * 1024 + (c8 + 1) * 128],
                        rwb1[:, ft * 128:(ft + 1) * 128], ALU.mult),
                        inc=("MG" if c8 == NCH - 1 else None))
                if ft == 0:
                    ff1_mg = {}
                ff1_mg[ft] = n["MG"]
                wait("vector", "MG", n["MG"])
                if ft >= 2:
                    wait("vector", "MT", ff1_mmlast[ft - 2])
                vq1 = instr("vector", lambda v, par=par: v.tensor_scalar(
                    qw1[:, par * 1024:(par + 1) * 1024], wf1[:, par * 1024:(par + 1) * 1024],
                    MAGIC, MAGIC, ALU.add, ALU.subtract), inc="MV")
                ff1_vq[ft] = vq1
                wait("tensor", "MV", vq1)
                if ft == 0:
                    wait("tensor", "Tqx", qxT_done)
                    ff1_mmlast = {}
                if ft >= 1:
                    wait("tensor", "MS", ff1_ms[ft - 1])
                for c8 in range(NCH):
                    for s in range(4):
                        instr("tensor", lambda t, par=par, c8=c8, s=s: t.matmul(
                            PS4[s][:, 0:512], qw1[:, par * 1024 + c8 * 128: par * 1024 + (c8 + 1) * 128],
                            BAv[:, c8, s * 512:(s + 1) * 512],
                            start=(c8 == 0), stop=(c8 == NCH - 1)),
                            inc=("MT" if c8 == NCH - 1 else None))
                ff1_mmlast[ft] = n["MT"]
                mtv = n["MT"]
                if ft >= 1:
                    wait("scalar", "Gh", hst_store[ft - 1])
                for s in range(4):
                    wait("scalar", "MT", mtv)
                    instr("scalar", lambda sc, ft=ft, s=s: sc.activation(
                        hstg[:, s * 512:(s + 1) * 512],
                        PS4[s][:, 0:512], AF.Relu, bias=exv1[:, ft:ft + 1], scale=sxw1[:, ft:ft + 1]),
                        inc=("MS" if s == 3 else None))
                if ft == 0:
                    ff1_ms = {}
                ff1_ms[ft] = n["MS"]
                wait("vector", "MS", n["MS"])
                vmx = instr("vector", lambda v, ft=ft: v.reduce_max(
                    mstats[:, ft:ft + 1], hstg[:, 0:NT], axis=AX), inc="MV")
                wait("gpsimd", "MV", vmx)
                hst_store[ft] = dma("gpsimd", "Gh", hdr[ft], hstg[:, 0:NT])

            wait("vector", "MV", n["MV"])
            instr("vector", lambda v: v.reduce_max(lmax[:, 1:2], mstats[:, 0:32], axis=AX), inc="MV")
            wait("tensor", "MV", n["MV"])
            vtm2 = instr("tensor", lambda t: t.matmul(
                psE[0:1, 0:128], lmax[:, 1:2], sb_c["cI128"][:, :], start=True, stop=True), inc="MT")
            wait("vector", "MT", vtm2)
            vgm2 = instr("vector", lambda v: v.reduce_max(gsc[0:1, 4:5], psE[0:1, 0:128], axis=AX), inc="MV")
            wait("gpsimd", "MV", vgm2)
            mx2 = dma("gpsimd", "Gm", mxi2[:, :], gsc[0:1, 4:5])
            wait("gpsimd", "Gm", mx2)
            ar2 = instr("gpsimd", lambda g: g.collective_compute(
                "AllReduce", ALU.max, replica_groups=[list(range(NCORES))],
                ins=[mxi2.tensor.ap().opt()], outs=[mxo2.tensor.ap().opt()]), inc="CC")
            wait("gpsimd", "CC", ar2)
            mx2b = dma("gpsimd", "Gm", gsc[0:1, 5:6], mxo2[:, :])
            wait("vector", "Gm", mx2b)
            vv = instr("vector", lambda v: v.tensor_scalar(gsc[0:1, 5:6], gsc[0:1, 5:6], 1e-8, None, ALU.max), inc="MV")
            wait("vector", "MV", vv)
            vv = instr("vector", lambda v: v.reciprocal(gsc[0:1, 6:7], gsc[0:1, 5:6]), inc="MV")
            wait("vector", "MV", vv)
            vv = instr("vector", lambda v: v.tensor_scalar(gsc[0:1, 6:7], gsc[0:1, 6:7], 255.0, None, ALU.mult), inc="MV")
            wait("vector", "MV", vv)
            vsh = instr("vector", lambda v: v.tensor_scalar(gsc[0:1, 7:8], gsc[0:1, 5:6], 1.0 / 255.0, None, ALU.mult), inc="MV")
            wait("tensor", "MV", vsh)
            vbb2 = instr("tensor", lambda t: t.matmul(
                psE[:, 0:2], sb_c["cOnes"][0:1, :], gsc[0:1, 6:8], start=True, stop=True), inc="MT")
            wait("vector", "MT", vbb2)
            vrh = instr("vector", lambda v: v.tensor_copy(rxb[:, 2:4], psE[:, 0:2]), inc="MV")
            wait("vector", "Lc", n["Lc"])
            wait("vector", "MV", vrh)
            H2 = ((0, 4), (4, 8))
            for a, b in H2:
                instr("vector", lambda v, a=a, b=b: v.tensor_scalar(
                    shw2[:, a:b], sw2[:, a:b], rxb[:, 3:4], None, ALU.mult))
            for a, b in H2:
                instr("vector", lambda v, a=a, b=b: v.tensor_tensor(
                    bq2[:, a:b], b2t[:, a:b], rw2[:, a:b], ALU.mult))
            for a, b in H2:
                instr("vector", lambda v, a=a, b=b: v.tensor_scalar(
                    bq2[:, a:b], bq2[:, a:b], rxb[:, 2:3], None, ALU.mult))
            for a, b in H2:
                instr("vector", lambda v, a=a, b=b: v.tensor_scalar(
                    bq2[:, a:b], bq2[:, a:b], MAGIC, MAGIC, ALU.add, ALU.subtract))
            vex2 = None
            for a, b in H2:
                vex2 = instr("vector", lambda v, a=a, b=b: v.tensor_tensor(
                    exv2[:, a:b], shw2[:, a:b], bq2[:, a:b], ALU.mult), inc="MV")
            ckpt("p9")

            ffnA_sc.close()
            qh = sbuf("qh", [128, NFT * 256], BF16, scope=ffnB_sc)
            qw2m = sbuf("qw2m", [128, 4 * 4096], BF16, scope=ffnB_sc)
            hbf = sbuf("hbf", [128, 2048], BF16, scope=ffnB_sc)

            wait("sync", "MV", vex2)
            wait("scalar", "MV", vex2)
            wait("gpsimd", "MV", vex2)
            wait("scalar", "Gh", n["Gh"])

            ffn_ev = []
            qhl = 0
            qw_mt = {}
            qwld = {}
            for s2 in range(8):
                for piece in range(4):
                    if qhl:
                        wait("scalar", "MV", qhl)
                    lv = dma("scalar", "Lh", hbf[:, :].rearrange("p (c q) -> p c q", q=256),
                             hdr[piece * 8:(piece + 1) * 8, :, s2 * 256:(s2 + 1) * 256]
                             .rearrange("f p q -> p f q"))
                    wait("vector", "Lh", lv)
                    wait("vector", "MV", vrh)
                    instr("vector", lambda v: v.tensor_scalar(
                        stage[:, 0:1024], hbf[:, 0:1024], rxb[:, 2:3], MAGIC, ALU.mult, ALU.add))
                    instr("vector", lambda v: v.tensor_scalar(
                        hstg[:, 0:1024], hbf[:, 1024:2048], rxb[:, 2:3], MAGIC, ALU.mult, ALU.add))
                    instr("vector", lambda v, piece=piece: v.tensor_scalar(
                        qh[:, piece * 2048: piece * 2048 + 1024], stage[:, 0:1024],
                        MAGIC, None, ALU.subtract))
                    qhl = instr("vector", lambda v, piece=piece: v.tensor_scalar(
                        qh[:, piece * 2048 + 1024: piece * 2048 + 2048], hstg[:, 0:1024],
                        MAGIC, None, ALU.subtract), inc="MV")
                qh_ready = qhl
                for m in range(NCH):
                    ck = s2 * NCH + m
                    sl4 = ck % 4
                    qeng, qsm = (("sync", "Tq"), ("scalar", "Lq"), ("gpsimd", "Gq"))[ck % 3]
                    wait(qeng, "Gq", w2q_done)
                    key = ck
                    if ck >= 4:
                        wait(qeng, "MT", qw_mt[ck - 4])
                    qwld[key] = dma(qeng, qsm,
                                    qw2m[:, sl4 * 4096:(sl4 + 1) * 4096], qw2d[m])
                    wait("tensor", qsm, qwld[key])
                    wait("tensor", "MV", qh_ready)
                    pp = psE if ck % 2 == 0 else psF
                    if len(ffn_ev) >= 2:
                        wait("tensor", "MV", ffn_ev[-2])
                    for fc in range(NFT):
                        instr("tensor", lambda t, sl4=sl4, fc=fc, pp=pp: t.matmul(
                            pp[:, 0:256], qw2m[:, sl4 * 4096 + fc * 128: sl4 * 4096 + (fc + 1) * 128],
                            qh[:, fc * 256:(fc + 1) * 256],
                            start=(fc == 0), stop=(fc == NFT - 1)),
                            inc=("MT" if fc == NFT - 1 else None))
                    qw_mt[key] = n["MT"]
                    wait("vector", "MT", n["MT"])
                    wait("vector", "MV", vex2)
                    ffn_ev.append(instr("vector", lambda v, m=m, s2=s2, pp=pp: v.tensor_scalar(
                        BBtile[:, 2 * s2:2 * s2 + 2, m, :], pp[:, 0:256],
                        bq2[:, m:m + 1], shw2[:, m:m + 1], ALU.add, ALU.mult), inc="MV"))

            ckpt("p10")
            wait("sync", "Trow", n["Trow"])
            wait("scalar", "Ls", n["Ls"])
            y_tiled = y_out.rearrange("(i p) d -> i p d", p=128)
            ffn2_all = n["MV"]
            ost = {}
            out_add = 0
            for i in range(NTILE):
                wait("sync", "MV", ffn2_all if i == 0 else out_add)
                src2 = BB[:, i * D:(i + 1) * D]
                dst3 = lnst[:, :].rearrange("t (c p) -> t c p", p=128)
                tt = dmat("sync", "Tout", dst3, src2)
                if i >= 1:
                    wait("scalar", "MV", out_add)
                lol = dma("scalar", "Lh", lob[:, :], x2lo_t[i])
                wait("vector", "Tout", tt)
                wait("vector", "Lh", lol)
                if i >= 2:
                    wait("vector", "O0" if i % 2 == 0 else "O1", ost[i - 2])
                half = i % 2
                obuf = stage if half == 0 else hstg
                for a, b in ((0, 512), (512, 1024)):
                    instr("vector", lambda v, i=i, obuf=obuf, a=a, b=b: v.tensor_tensor(
                        obuf[:, a:b], B8[:, i * D + a:i * D + b], lnst[:, a:b], ALU.add))
                for a, b in ((0, 512), (512, 1024)):
                    out_add = instr("vector", lambda v, obuf=obuf, a=a, b=b: v.tensor_tensor(
                        obuf[:, a:b], obuf[:, a:b], lob[:, a:b], ALU.add), inc="MV")
                wait("scalar", "MV", out_add)
                ost[i] = dma("scalar", "O0" if i % 2 == 0 else "O1", y_tiled[i], obuf[:, 0:1024])
            if DBG:
                dma("scalar", "O0", dbg_oA1[:, :, :], oA1[:, :, :])
                dma("scalar", "O0", dbg_iA2[:, :, :], iA2[:, :, :])
                dma("scalar", "O0", dbg_B8[:, :], B8[:, :])
                dma("scalar", "O0", dbg_qx[:, :], BA[:, :])
                dma("scalar", "O0", dbg_gsc[:, :], gsc[:, :])
                dma("scalar", "O0", dbg_hdr[0], hdr[0])
                dma("scalar", "O0", dbg_hdr[1], hdr[31])
            wait("scalar", "O0", n["O0"])
            wait("scalar", "O1", n["O1"])
            ffnB_sc.close()


        except _SkipRest:
            pass
        for _sc in (ffnB_sc, ffnA_sc, conv_sc, pe_sc, attn_sc):
            try:
                _sc.close()
            except Exception:
                pass

        @block.sync
        def _(e):
            for fn in streams["sync"]:
                fn(e)

        def _audit():
            actual = {nm: 0 for nm in SEMNAMES}
            byid = {SEMS[nm].sem_num if hasattr(SEMS[nm], "sem_num") else None: nm for nm in SEMNAMES}
            names = {}
            for i in nc.all_instructions():
                si = getattr(i, "sync_info", None)
                if si is None:
                    continue
                for u in si.on_update:
                    if u.ant_name in actual:
                        actual[u.ant_name] += u.update_value
            bad = {k: (actual[k], n[k]) for k in SEMNAMES if actual[k] != n[k]}
            assert not bad, f"semaphore accounting mismatch: {bad}"

        @block.scalar
        def _(e):
            for fn in streams["scalar"]:
                fn(e)

        @block.vector
        def _(e):
            for fn in streams["vector"]:
                fn(e)

        @block.tensor
        def _(e):
            for fn in streams["tensor"]:
                fn(e)

        @block.gpsimd
        def _(e):
            for fn in streams["gpsimd"]:
                fn(e)

        _audit()

    return nc


_NC_CACHE = None


def kernel(L0_attn_x, L0_attn_Wu, L0_attn_bu, L0_attn_Wv, L0_attn_bv,
           L0_attn_W1, L0_attn_b1, L0_attn_W2, L0_attn_b2, L0_attn_W3,
           L0_attn_b3, L0_attn_Wo, L0_attn_bo, L0_ffn_w1, L0_ffn_b1,
           L0_ffn_w2, L0_ffn_b2):
    global _NC_CACHE
    x = np.ascontiguousarray(np.asarray(L0_attn_x, np.float32))
    B, Lq, Dm = x.shape
    if _NC_CACHE is None:
        _NC_CACHE = build_nc()
    nc = _NC_CACHE
    f32c = lambda a: np.ascontiguousarray(np.asarray(a, np.float32))
    pos = np.arange(4096, dtype=np.float32)[:, None]
    div = np.exp(np.arange(0, 64, 2, dtype=np.float32) * -(np.log(10000.0) / 64.0))
    pe = np.zeros((4096, 64), np.float32)
    pe[:, 0::2] = np.sin(pos * div)
    pe[:, 1::2] = np.cos(pos * div)
    com = dict(
        Wu=f32c(L0_attn_Wu), Wv=f32c(L0_attn_Wv), Wo=f32c(L0_attn_Wo),
        bu=f32c(L0_attn_bu), bv=f32c(L0_attn_bv), bo=f32c(L0_attn_bo),
        W1=f32c(L0_attn_W1), W2=f32c(L0_attn_W2),
        b1f=f32c(L0_attn_b1), b2f=f32c(L0_attn_b2),
        peT=np.ascontiguousarray(pe.T),
        w1T=f32c(np.asarray(L0_ffn_w1).T), w1n=f32c(L0_ffn_w1), b1v=f32c(L0_ffn_b1),
        w2T=f32c(np.asarray(L0_ffn_w2).T), w2n=f32c(L0_ffn_w2), b2v=f32c(L0_ffn_b2),
    )
    com.update({"d" + k: v for k, v in CONSTS.items()})
    W3 = f32c(L0_attn_W3)
    b3 = f32c(L0_attn_b3)
    in_maps = []
    for c in range(NCORES):
        b, h = c // 2, c % 2
        m = dict(com)
        m["x_in"] = np.ascontiguousarray(x[b, h * NT:(h + 1) * NT, :])
        m["W3s"] = np.ascontiguousarray(W3[:, c * 128:(c + 1) * 128])
        m["b3s"] = np.ascontiguousarray(b3[c * 128:(c + 1) * 128])
        in_maps.append(m)
    res = run_bass_kernel_spmd(nc, in_maps, core_ids=list(range(NCORES)))
    out = np.empty((B, Lq, Dm), np.float32)
    for c in range(NCORES):
        b, h = c // 2, c % 2
        out[b, h * NT:(h + 1) * NT, :] = res.results[c]["y_out"]
    return out


# version 09
# speedup vs baseline: 1.2720x; 1.0177x over previous
import os
os.environ.setdefault("TRN_ALLOW_ICI_ALL_TO_ALL", "1")

import numpy as np
import ml_dtypes
import concourse.bass as bass
import concourse.mybir as mybir
from concourse.bass_utils import run_bass_kernel_spmd
from contextlib import ExitStack

F32 = mybir.dt.float32
BF16 = mybir.dt.bfloat16
AF = mybir.ActivationFunctionType
ALU = mybir.AluOpType
AX = mybir.AxisListType.X

MAGIC = float(np.float32(1.5 * 2 ** 23))
NCORES = 8
NT = 2048
D = 1024
FF = 4096
NTILE = NT // 128
NCH = D // 128
NFT = FF // 128


def _bf(x):
    return np.ascontiguousarray(np.asarray(x, np.float32)).astype(ml_dtypes.bfloat16)


def host_constants():
    k = np.arange(128)
    W1c = np.exp(-2j * np.pi * np.outer(k, k) / 128.0)
    F1r, F1i = W1c.real.astype(np.float32), W1c.imag.astype(np.float32)
    W2c = np.exp(+2j * np.pi * np.outer(k, k) / 128.0)
    F2r, F2i = W2c.real.astype(np.float32), W2c.imag.astype(np.float32)
    cR1h = np.concatenate([F1r[0:64, 0:65], F1i[0:64, 0:65]], axis=1)
    cG1 = F1r[0:64, :]
    cG2 = F1i[0:64, :]
    cG1d = np.concatenate([cG1, cG1], axis=0)
    cG2d = np.concatenate([cG2, cG2], axis=0)
    cR3 = np.concatenate([F2r, F2i], axis=1)
    cR4h = np.concatenate([-F2i, F2r], axis=1)
    a = np.arange(64)
    k65 = np.arange(65)
    alpha = np.where((k65 == 0) | (k65 == 64), 1.0, 2.0)[:, None].astype(np.float32)
    CO2 = (alpha * np.cos(2 * np.pi * np.outer(k65, a) / 128.0) / 16384.0).astype(np.float32)
    SI2n = (-alpha * np.sin(2 * np.pi * np.outer(k65, a) / 128.0) / 16384.0).astype(np.float32)
    CO2s = np.zeros((65, 64), np.float32); CO2s[:, 1:] = CO2[:, :-1]
    SI2ns = np.zeros((65, 64), np.float32); SI2ns[:, 1:] = SI2n[:, :-1]
    pos = np.arange(4096, dtype=np.float32)[:, None]
    div = np.exp(np.arange(0, 64, 2, dtype=np.float32) * -(np.log(10000.0) / 64.0))
    pe = np.zeros((4096, 64), np.float32)
    pe[:, 0::2] = np.sin(pos * div)
    pe[:, 1::2] = np.cos(pos * div)
    dec = np.exp(-0.01 * np.arange(4096, dtype=np.float32)).reshape(1, 4096)
    return dict(
        cR1h=_bf(cR1h), cG1d=_bf(cG1d), cG2d=_bf(cG2d), cR3=_bf(cR3), cR4h=_bf(cR4h),
        cCO2=_bf(CO2), cSI2n=_bf(SI2n), cCO2s=_bf(CO2s), cSI2ns=_bf(SI2ns),
        cG1f=np.ascontiguousarray(cG1.astype(np.float32)),
        cG2f=np.ascontiguousarray(cG2.astype(np.float32)),
        cDec=np.ascontiguousarray(dec.astype(np.float32)),
        cI128=np.eye(128, dtype=np.float32),
        cOnes=np.ones((1, 128), np.float32),
    )


CONSTS = host_constants()


def build_nc():
    nc = bass.Bass("TRN2", target_bir_lowering=False, debug=False, detect_race_conditions=False)
    DI = lambda nm, s, dt=F32: nc.dram_tensor(nm, s, dt, kind="ExternalInput").ap()
    x_in = DI("x_in", [NT, D])
    Wu_d = DI("Wu", [D, D]); Wv_d = DI("Wv", [D, D]); Wo_d = DI("Wo", [D, D])
    bu_d = DI("bu", [D]); bv_d = DI("bv", [D]); bo_d = DI("bo", [D])
    W1_d = DI("W1", [64, 64]); W2_d = DI("W2", [64, 64])
    b1f_d = DI("b1f", [64]); b2f_d = DI("b2f", [64])
    W3s_d = DI("W3s", [64, 128]); b3s_d = DI("b3s", [128])
    peT_d = DI("peT", [64, 4096])
    w1T_d = DI("w1T", [D, FF]); w1n_d = DI("w1n", [FF, D]); b1v_d = DI("b1v", [FF])
    w2T_d = DI("w2T", [FF, D]); w2n_d = DI("w2n", [D, FF]); b2v_d = DI("b2v", [D])
    cns = {k: DI("d" + k, list(v.shape), BF16 if v.dtype == ml_dtypes.bfloat16 else F32)
           for k, v in CONSTS.items()}
    y_out = nc.dram_tensor("y_out", [NT, D], F32, kind="ExternalOutput").ap()
    DBG = os.environ.get("ATH_DEBUG", "") == "1"
    if DBG:
        dbg_oA1 = nc.dram_tensor("dbg_oA1", [NCORES, 128, NT], BF16, kind="ExternalOutput").ap()
        dbg_iA2 = nc.dram_tensor("dbg_iA2", [NCORES, 128, NT], BF16, kind="ExternalOutput").ap()
        dbg_B8 = nc.dram_tensor("dbg_B8", [128, NTILE * D], BF16, kind="ExternalOutput").ap()
        dbg_qx = nc.dram_tensor("dbg_qx", [128, NCH * NT], BF16, kind="ExternalOutput").ap()
        dbg_gsc = nc.dram_tensor("dbg_gsc", [1, 8], F32, kind="ExternalOutput").ap()
        dbg_hdr = nc.dram_tensor("dbg_hdr", [2, 128, NT], BF16, kind="ExternalOutput").ap()
    iA1 = nc.dram_tensor("iA1", [NCORES, 128, NT], BF16).ap()
    oA1 = nc.dram_tensor("oA1", [NCORES, 128, NT], BF16).ap()
    iA2 = nc.dram_tensor("iA2", [NCORES, 128, NT], BF16).ap()
    oA2 = nc.dram_tensor("oA2", [NCORES, 128, NT], BF16).ap()
    mxi1 = nc.dram_tensor("mxi1", [1, 1], F32).ap()
    mxo1 = nc.dram_tensor("mxo1", [1, 1], F32).ap()
    mxi2 = nc.dram_tensor("mxi2", [1, 1], F32).ap()
    mxo2 = nc.dram_tensor("mxo2", [1, 1], F32).ap()
    hdr = nc.dram_tensor("hdr", [NFT, 128, NT], BF16).ap()
    qw2d = nc.dram_tensor("qw2d", [NCH, 128, NFT * 128], BF16).ap()
    hscr = nc.dram_tensor("hscr", [128, 4096], BF16).ap()
    x2lo = nc.dram_tensor("x2lo", [NT, D], BF16).ap()

    with ExitStack() as ctx:
        def sbuf(nm, shape, dt=BF16, scope=None):
            return (scope or ctx).enter_context(nc.sbuf_tensor(nm, shape, dt))

        def psum(nm):
            return ctx.enter_context(nc.psum_tensor(nm, [128, 512], F32))

        def sem(nm):
            return ctx.enter_context(nc.semaphore(nm))

        B8 = sbuf("B8", [128, NTILE * D])
        BA = sbuf("BA", [128, NCH * NT])
        BB = sbuf("BB", [128, NCH * NT])
        BC = sbuf("BC", [128, NCH * NT])
        hstg = sbuf("hstg", [128, NT], F32)
        stage = sbuf("stage", [128, 1024], F32)
        lnst = sbuf("lnst", [128, 1024])
        sb_c = {k: sbuf("s" + k, list(v.shape),
                        BF16 if v.dtype == ml_dtypes.bfloat16 else F32)
                for k, v in CONSTS.items() if k != "cDec"}
        bias_u = sbuf("bias_u", [128, NCH], F32)
        bias_v = sbuf("bias_v", [128, NCH], F32)
        bias_o = sbuf("bias_o", [128, NCH], F32)
        b3sb = sbuf("b3sb", [128, 1], F32)
        b1fb = sbuf("b1fb", [64, 1], F32)
        b2fb = sbuf("b2fb", [64, 1], F32)
        b1t = sbuf("b1t", [128, NFT], F32)
        b2t = sbuf("b2t", [128, NCH], F32)
        rw1 = sbuf("rw1", [128, NFT], F32)
        rw2 = sbuf("rw2", [128, NCH], F32)
        sw1 = sbuf("sw1", [128, NFT], F32)
        sw2 = sbuf("sw2", [128, NCH], F32)
        sxw1 = sbuf("sxw1", [128, NFT], F32)
        bq1 = sbuf("bq1", [128, NFT], F32)
        exv1 = sbuf("exv1", [128, NFT], F32)
        shw2 = sbuf("shw2", [128, NCH], F32)
        bq2 = sbuf("bq2", [128, NCH], F32)
        exv2 = sbuf("exv2", [128, NCH], F32)
        mstats = sbuf("mstats", [128, 3 * NTILE], F32)
        bnb = sbuf("bnb", [128, 24], F32)
        agg = sbuf("agg", [128, 16], F32)
        lmax = sbuf("lmax", [128, 2], F32)
        gsc = sbuf("gsc", [1, 8], F32)
        rxb = sbuf("rxb", [128, 4], F32)
        lob = sbuf("lob", [128, 1024], BF16)
        W1sb = sbuf("W1sb", [64, 64], F32)
        W2sb = sbuf("W2sb", [64, 64], F32)
        W3sb = sbuf("W3sb", [64, 128], F32)
        psA = psum("psA"); psB = psum("psB"); psC = psum("psC"); psD = psum("psD")
        psE = psum("psE"); psF = psum("psF"); psG = psum("psG"); psH = psum("psH")
        PS4 = [psA, psB, psC, psD]
        SEMNAMES = ["CC", "MT", "MV", "MS", "MG",
                    "Gx", "Gw", "Gv0", "Gv1", "Ga", "Gm", "Gh", "Gq",
                    "Lc", "Lp", "Ls", "Lq2", "Lw0", "Lw1", "Lh", "Lq",
                    "Tln", "Tzh", "Tz0", "Tz1", "Ty", "Ty2", "Trow", "Tat", "Tqx", "Tq", "Tout",
                    "O0", "O1"]
        SEMS = {nm: sem(nm) for nm in SEMNAMES}
        attn_sc = ExitStack()
        pe_sc = ExitStack()
        conv_sc = ExitStack()
        ffnA_sc = ExitStack()
        ffnB_sc = ExitStack()
        block = ctx.enter_context(nc.Block())

        n = {nm: 0 for nm in SEMNAMES}
        streams = {e: [] for e in ("tensor", "vector", "scalar", "sync", "gpsimd")}

        class _SkipRest(Exception):
            pass

        _STOP = os.environ.get("ATH_STOP_PHASE", "")

        def ckpt(name):
            if _STOP == name:
                raise _SkipRest()

        def dma(eng, sm, out, in_, slow=False):
            n[sm] += 16
            hs = SEMS[sm]
            streams[eng].append(lambda e, out=out, in_=in_, hs=hs, slow=slow:
                                e.dma_start(out=out, in_=in_,
                                            allow_slow_non_contiguous=slow).then_inc(hs, 16))
            return n[sm]

        def dmat(eng, sm, out, in_):
            n[sm] += 16
            hs = SEMS[sm]
            streams[eng].append(lambda e, out=out, in_=in_, hs=hs:
                                e.dma_start_transpose(out, in_).then_inc(hs, 16))
            return n[sm]

        def wait(eng, sm, v):
            if v <= 0:
                return
            hs = SEMS[sm]
            streams[eng].append(lambda e, hs=hs, v=v: e.wait_ge(hs, v))

        def instr(eng, f, inc=None):
            if inc is None:
                streams[eng].append(f)
                return None
            n[inc] += 1
            hs = SEMS[inc]
            streams[eng].append(lambda e, f=f, hs=hs: f(e).then_inc(hs, 1))
            return n[inc]

        BAv = BA[:, :].rearrange("p (c q) -> p c q", c=NCH)

        try:
            x_tiled = x_in.rearrange("(i p) d -> i p d", p=128)
            xld = [dma("gpsimd", "Gx", B8[:, i * D:(i + 1) * D], x_tiled[i]) for i in range(NTILE)]
            wbuf = sbuf("wbuf", [128, NCH * D], scope=attn_sc)
            wsc = sbuf("wsc", [128, 1024], F32, scope=attn_sc)
            wu_ld = dma("gpsimd", "Gw", wbuf[:, :].rearrange("p (c d) -> p c d", c=NCH),
                        Wu_d.rearrange("(c p) d -> p c d", p=128))
            cld = {k: dma("scalar", "Lc", sb_c[k][:, :], cns[k][:, :]) for k in CONSTS
                   if k != "cDec"}
            dma("scalar", "Lc", bias_u[:, :], bu_d.rearrange("(c p) -> p c", p=128), slow=True)
            dma("scalar", "Lc", bias_v[:, :], bv_d.rearrange("(c p) -> p c", p=128), slow=True)
            v_bo = dma("scalar", "Lc", bias_o[:, :], bo_d.rearrange("(c p) -> p c", p=128), slow=True)
            dma("scalar", "Lc", b3sb[:, :], b3s_d.rearrange("(p o) -> p o", o=1))
            dma("scalar", "Lc", b1fb[:, :], b1f_d.rearrange("(p o) -> p o", o=1))
            dma("scalar", "Lc", b2fb[:, :], b2f_d.rearrange("(p o) -> p o", o=1))
            dma("scalar", "Lc", b1t[:, :], b1v_d.rearrange("(t p) -> p t", p=128), slow=True)
            v_b2t = dma("scalar", "Lc", b2t[:, :], b2v_d.rearrange("(t p) -> p t", p=128), slow=True)
            dma("scalar", "Lc", W1sb[:, :], W1_d[:, :])
            dma("scalar", "Lc", W2sb[:, :], W2_d[:, :])
            lw3 = dma("scalar", "Lc", W3sb[:, :], W3s_d[:, :])

            def emit_ln_stats_pair(xtA, xtB):
                chains = []
                for (xt, c, b) in ((xtA, 0, 0), (xtB, 8, 12)):
                    chains.append([
                        lambda v, xt=xt, b=b: v.bn_stats(bnb[:, b:b + 6], xt[:, 0:512]),
                        lambda v, xt=xt, b=b: v.bn_stats(bnb[:, b + 6:b + 12], xt[:, 512:1024]),
                        lambda v, c=c, b=b: v.reduce_sum(agg[:, c:c + 1], bnb[:, b + 1:b + 12:3], axis=AX),
                        lambda v, c=c, b=b: v.tensor_tensor(agg[:, c + 2:c + 6], bnb[:, b + 1:b + 12:3], bnb[:, b + 1:b + 12:3], ALU.mult),
                        lambda v, c=c: v.reduce_sum(agg[:, c + 1:c + 2], agg[:, c + 2:c + 6], axis=AX),
                        lambda v, c=c, b=b: v.reduce_sum(agg[:, c + 6:c + 7], bnb[:, b + 2:b + 12:3], axis=AX),
                        lambda v, c=c: v.tensor_scalar(agg[:, c:c + 1], agg[:, c:c + 1], 0.25, None, ALU.mult),
                        lambda v, c=c: v.tensor_tensor(agg[:, c + 7:c + 8], agg[:, c:c + 1], agg[:, c:c + 1], ALU.mult),
                        lambda v, c=c: v.tensor_scalar(agg[:, c + 6:c + 7], agg[:, c + 6:c + 7], 1.0 / 1024.0, None, ALU.mult),
                        lambda v, c=c: v.tensor_scalar(agg[:, c + 1:c + 2], agg[:, c + 1:c + 2], 0.25, None, ALU.mult),
                        lambda v, c=c: v.tensor_tensor(agg[:, c + 6:c + 7], agg[:, c + 6:c + 7], agg[:, c + 1:c + 2], ALU.add),
                        lambda v, c=c: v.tensor_tensor(agg[:, c + 6:c + 7], agg[:, c + 6:c + 7], agg[:, c + 7:c + 8], ALU.subtract),
                        lambda v, c=c: v.tensor_scalar(agg[:, c + 6:c + 7], agg[:, c + 6:c + 7], 1e-5, None, ALU.add),
                        lambda v, c=c: v.reciprocal(agg[:, c + 1:c + 2], agg[:, c + 6:c + 7]),
                    ])
                last = None
                for k in range(len(chains[0])):
                    for t in range(2):
                        last = instr("vector", chains[t][k],
                                     inc=("MV" if k == len(chains[0]) - 1 else None))
                return last

            lnst3 = lnst[:, :].rearrange("p (c q) -> p c q", q=128)
            stg3 = lob[:, :].rearrange("p (c q) -> p c q", q=128)
            BAt4 = BA[:, :].rearrange("p (c i q) -> p c i q", c=NCH, i=NTILE)
            prev_tt = 0
            for pi in range(NTILE // 2):
                iA, iB = 2 * pi, 2 * pi + 1
                xtA = B8[:, iA * D:(iA + 1) * D]
                xtB = B8[:, iB * D:(iB + 1) * D]
                wait("vector", "Gx", xld[-1])
                if pi > 0:
                    wait("vector", "Tln", prev_tt)
                vrec = emit_ln_stats_pair(xtA, xtB)
                wait("scalar", "MV", vrec)
                instr("scalar", lambda s: s.activation(agg[:, 2:3], agg[:, 1:2], AF.Sqrt))
                vsq = instr("scalar", lambda s: s.activation(agg[:, 10:11], agg[:, 9:10], AF.Sqrt), inc="MS")
                wait("vector", "MS", vsq)
                vnA = instr("vector", lambda v, xtA=xtA: v.tensor_scalar(
                    lnst[:, :], xtA, agg[:, 0:1], agg[:, 2:3], ALU.subtract, ALU.mult), inc="MV")
                vnB = instr("vector", lambda v, xtB=xtB: v.tensor_scalar(
                    lob[:, :], xtB, agg[:, 8:9], agg[:, 10:11], ALU.subtract, ALU.mult), inc="MV")
                wait("sync", "MV", vnA)
                dmat("sync", "Tln", BAt4[:, :, iA, :], lnst3)
                wait("sync", "MV", vnB)
                prev_tt = dmat("sync", "Tln", BAt4[:, :, iB, :], stg3)
            lnT_all = prev_tt
            ckpt("p1")

            def emit_proj(evac):
                prev_ev = 0
                for m in range(NCH):
                    if prev_ev:
                        wait("tensor", evac_sem, prev_ev)
                    for c in range(NCH):
                        for s in range(4):
                            instr("tensor", lambda t, m=m, c=c, s=s: t.matmul(
                                PS4[s][:, 0:512],
                                wbuf[:, c * D + m * 128: c * D + (m + 1) * 128],
                                BAv[:, c, s * 512:(s + 1) * 512],
                                start=(c == 0), stop=(c == NCH - 1)),
                                inc=("MT" if c == NCH - 1 else None))
                    prev_ev = evac(m, n["MT"])
                return prev_ev

            wait("tensor", "Tln", lnT_all)
            wait("tensor", "Gw", wu_ld)
            wait("vector", "Lc", n["Lc"])
            evac_sem = "MV"

            def evac_u(m, mtv):
                last = None
                for s in range(4):
                    wait("vector", "MT", mtv)
                    last = instr("vector", lambda v, m=m, s=s: v.tensor_scalar(
                        BB[:, m * NT + s * 512: m * NT + (s + 1) * 512],
                        PS4[s][:, 0:512], bias_u[:, m:m + 1], None, ALU.add), inc="MV")
                return last

            emit_proj(evac_u)
            u_mm_done = n["MT"]
            u_done = n["MV"]
            ckpt("p2u")

            wait("gpsimd", "MT", u_mm_done)
            wv_ld = dma("gpsimd", "Gw", wbuf[:, :].rearrange("p (c d) -> p c d", c=NCH),
                        Wv_d.rearrange("(c p) d -> p c d", p=128))
            wait("tensor", "Gw", wv_ld)

            vst = [0, 0]

            def evac_v(m, mtv):
                for s in range(4):
                    half = s % 2
                    wait("scalar", "MT", mtv)
                    wait("scalar", "Gv0" if half == 0 else "Gv1", vst[half])
                    vv = instr("scalar", lambda sc, m=m, s=s, half=half: sc.activation(
                        stage[:, half * 512:(half + 1) * 512], PS4[s][:, 0:512], AF.Identity,
                        bias=bias_v[:, m:m + 1], scale=1.0), inc="MS")
                    wait("gpsimd", "MS", vv)
                    vst[half] = dma("gpsimd", "Gv0" if half == 0 else "Gv1",
                                    iA1[m, :, s * 512:(s + 1) * 512],
                                    stage[:, half * 512:(half + 1) * 512])
                return None

            evac_sem = "MS"

            def evac_v_wrap(m, mtv):
                evac_v(m, mtv)
                return n["MS"]

            wait("scalar", "Lc", n["Lc"])
            emit_proj(evac_v_wrap)
            v_mm_done = n["MT"]
            v_store_done = None
            ckpt("p2")

            wait("gpsimd", "Gv0", vst[0])
            wait("gpsimd", "Gv1", vst[1])
            a2a1 = instr("gpsimd", lambda g: g.collective_compute(
                "AllToAll", ALU.bypass, replica_groups=[list(range(NCORES))],
                ins=[iA1.tensor.ap().opt()], outs=[oA1.tensor.ap().opt()]), inc="CC")
            wait("gpsimd", "CC", a2a1)

            peb = sbuf("peb", [64, 2 * 512], F32, scope=pe_sc)
            decb = sbuf("decb", [1, 4096], F32, scope=pe_sc)
            dma("scalar", "Lc", decb[0:1, :], cns["cDec"][0:1, :])
            g1s = BA[:, 0:8192].bitcast(F32)
            g2s = BA[:, 8192:16384].bitcast(F32)

            wait("tensor", "Lc", n["Lc"])
            prev_ms = 0
            pe_mt = {}
            for s8 in range(8):
                if s8 >= 1:
                    wait("scalar", "MT", pe_mt[s8 - 1])
                lv = dma("scalar", "Lp", peb[0:64, (s8 % 2) * 512:(s8 % 2) * 512 + 512],
                         peT_d[:, s8 * 512:(s8 + 1) * 512])
                wait("tensor", "Lp", lv)
                if prev_ms:
                    wait("tensor", "MS", prev_ms)
                vm = instr("tensor", lambda t, s8=s8: t.matmul(
                    psE[0:64, 0:512], W1sb[0:64, :],
                    peb[0:64, (s8 % 2) * 512:(s8 % 2) * 512 + 512],
                    start=True, stop=True), inc="MT")
                pe_mt[s8] = vm
                wait("scalar", "MT", vm)
                prev_ms = instr("scalar", lambda sc, s8=s8: sc.activation(
                    g1s[0:64, s8 * 512:(s8 + 1) * 512], psE[0:64, 0:512], AF.Identity,
                    bias=b1fb[0:64, 0:1], scale=1.0), inc="MS")

            def emit_gelu(buf):
                last = None
                for hh in range(4):
                    src = buf[0:64, hh * 1024:(hh + 1) * 1024]
                    t1 = hstg[0:64, 0:1024]
                    t2 = hstg[0:64, 1024:2048]
                    vq = instr("scalar", lambda s, src=src, t1=t1: s.activation(t1, src, AF.Square), inc="MS")
                    wait("vector", "MS", vq)
                    for a, b in ((0, 512), (512, 1024)):
                        instr("vector", lambda v, t1=t1, a=a, b=b: v.tensor_scalar(
                            t1[:, a:b], t1[:, a:b], 0.044715, 1.0, ALU.mult, ALU.add))
                    vw = None
                    for a, b in ((0, 512), (512, 1024)):
                        vw = instr("vector", lambda v, t1=t1, src=src, a=a, b=b: v.tensor_tensor(
                            t1[:, a:b], t1[:, a:b], src[:, a:b], ALU.mult), inc="MV")
                    wait("scalar", "MV", vw)
                    vt = instr("scalar", lambda s, t1=t1, t2=t2: s.activation(
                        t2, t1, AF.Tanh, scale=0.7978845608028654), inc="MS")
                    wait("vector", "MS", vt)
                    for a, b in ((0, 512), (512, 1024)):
                        instr("vector", lambda v, t2=t2, a=a, b=b: v.tensor_scalar(
                            t2[:, a:b], t2[:, a:b], 1.0, 0.5, ALU.add, ALU.mult))
                    for a, b in ((0, 512), (512, 1024)):
                        last = instr("vector", lambda v, src=src, t2=t2, a=a, b=b: v.tensor_tensor(
                            src[:, a:b], t2[:, a:b], src[:, a:b], ALU.mult), inc="MV")
                return last

            wait("scalar", "MS", n["MS"])
            vg1 = emit_gelu(g1s)
            wait("tensor", "MV", vg1)
            prev_ms = 0
            for s8 in range(8):
                if prev_ms:
                    wait("tensor", "MS", prev_ms)
                vm = instr("tensor", lambda t, s8=s8: t.matmul(
                    psE[0:64, 0:512], W2sb[0:64, :], g1s[0:64, s8 * 512:(s8 + 1) * 512],
                    start=True, stop=True), inc="MT")
                wait("scalar", "MT", vm)
                prev_ms = instr("scalar", lambda sc, s8=s8: sc.activation(
                    g2s[0:64, s8 * 512:(s8 + 1) * 512], psE[0:64, 0:512], AF.Identity,
                    bias=b2fb[0:64, 0:1], scale=1.0), inc="MS")
            wait("scalar", "MS", n["MS"])
            vg2 = emit_gelu(g2s)
            wait("tensor", "MV", vg2)
            wait("vector", "Gv0", vst[0])
            wait("vector", "Gv1", vst[1])
            prev_hv = 0
            hst_prev = {}
            for s8 in range(8):
                par8 = s8 % 2
                if prev_hv:
                    wait("tensor", "MV", prev_hv)
                instr("tensor", lambda t, s8=s8: t.matmul(
                    psE[:, 0:512], W3sb[0:64, :], g2s[0:64, s8 * 512:(s8 + 1) * 512],
                    start=True, stop=True))
                vdm = instr("tensor", lambda t, s8=s8: t.matmul(
                    psF[:, 0:512], sb_c["cOnes"][0:1, :], decb[0:1, s8 * 512:(s8 + 1) * 512],
                    start=True, stop=True), inc="MT")
                wait("vector", "MT", vdm)
                if s8 >= 2:
                    wait("vector", "Gh", hst_prev[s8 - 2])
                for a, b in ((0, 256), (256, 512)):
                    instr("vector", lambda v, a=a, b=b: v.tensor_scalar(
                        stage[:, a:b], psE[:, a:b], b3sb[:, 0:1], None, ALU.add))
                for a, b in ((0, 256), (256, 512)):
                    prev_hv = instr("vector", lambda v, par8=par8, a=a, b=b: v.tensor_tensor(
                        hstg[:, par8 * 512 + a:par8 * 512 + b], stage[:, a:b], psF[:, a:b], ALU.mult), inc="MV")
                wait("gpsimd", "MV", prev_hv)
                hst_prev[s8] = dma("gpsimd", "Gh", hscr[:, s8 * 512:(s8 + 1) * 512],
                                   hstg[:, par8 * 512:(par8 + 1) * 512])
            h_done = hst_prev[7]
            ckpt("p3h")
            pe_sc.close()

            zh32 = sbuf("zh32", [64, 128 * 64], BF16, scope=conv_sc)
            zv = sbuf("zv", [64, 4 * 256], BF16, scope=conv_sc)
            at_b = sbuf("at_b", [128, 2 * 195], BF16, scope=conv_sc)
            ath32 = sbuf("ath32", [64, 195], F32, scope=conv_sc)
            hb32 = sbuf("hb32", [128, 2 * 260], F32, scope=conv_sc)
            vh32 = sbuf("vh32", [128, 2 * 260], F32, scope=conv_sc)
            tmpb = sbuf("tmpb", [128, 2 * 520], F32, scope=conv_sc)
            ypb = sbuf("ypb", [128, 2 * 260], BF16, scope=conv_sc)
            esb = sbuf("esb", [65, 2 * 1024], BF16, scope=conv_sc)
            ysb = sbuf("ysb", [64, 4 * 256], BF16, scope=conv_sc)

            wait("sync", "Gh", h_done)
            wait("sync", "MT", pe_mt[7])
            zh_all = dma("sync", "Tzh", zh32[:, :].rearrange("a (c b) -> a c b", b=64),
                         hscr.rearrange("c (a b) -> a c b", b=64))

            w1n_t = w1n_d.rearrange("(t p) d -> t p d", p=128)
            wait("scalar", "MT", pe_mt[7])
            for ft in range(NFT):
                lv = dma("scalar", "Ls", wsc[:, :], w1n_t[ft])
                wait("vector", "Ls", lv)
                vr = instr("vector", lambda v, ft=ft: v.reduce_max(
                    sw1[:, ft:ft + 1], wsc[:, :], axis=AX, apply_absolute_value=True), inc="MV")
                wait("scalar", "MV", vr)
            w2n_t = w2n_d.rearrange("(t p) f -> t p f", p=128)
            for ct in range(NCH):
                for pc in range(4):
                    lv = dma("scalar", "Ls", wsc[:, :], w2n_t[ct][:, pc * 1024:(pc + 1) * 1024])
                    wait("vector", "Ls", lv)
                    vr = instr("vector", lambda v, ct=ct, pc=pc: v.reduce_max(
                        mstats[:, 4 * ct + pc:4 * ct + pc + 1], wsc[:, :],
                        axis=AX, apply_absolute_value=True), inc="MV")
                    wait("scalar", "MV", vr)
            for ct in range(NCH):
                instr("vector", lambda v, ct=ct: v.reduce_max(
                    sw2[:, ct:ct + 1], mstats[:, 4 * ct:4 * ct + 4], axis=AX))
            instr("vector", lambda v: v.tensor_scalar(sw1[:, :], sw1[:, :], 1e-8, 1.0 / 127.0, ALU.max, ALU.mult))
            instr("vector", lambda v: v.tensor_scalar(sw2[:, :], sw2[:, :], 1e-8, 1.0 / 127.0, ALU.max, ALU.mult))
            vrw1 = instr("vector", lambda v: v.reciprocal(rw1[:, :], sw1[:, :]), inc="MV")
            vrw2 = instr("vector", lambda v: v.reciprocal(rw2[:, :], sw2[:, :]), inc="MV")

            conv_c_mt = {}
            yst = {}
            yst2 = {}
            unit_cfree = {}
            unit_dfree = {}
            dh_mt = {}
            ad_mt = {}
            pr_mg = {}
            hbfree = {}
            yv_mv = {}
            ap_mt = {}
            e_mv = {}
            b_mt = {}
            y_mv = {}
            uc = 0
            pc_ = 0
            ps_ms0 = n["MS"]
            ps_mv0 = n["MV"]
            CP = [psA, psG]
            DPb = [psB, psH]
            DPa = [psC, psE]
            wait("tensor", "Tzh", zh_all)
            wait("tensor", "MS", ps_ms0)
            wait("tensor", "MV", ps_mv0)
            wait("sync", "CC", a2a1)
            wait("scalar", "CC", a2a1)
            oA1v = oA1.rearrange("(b hi) p (ah qq) -> p hi ah b qq", hi=2, qq=64)
            iA2v = iA2.rearrange("(b hi) p (ah qq) -> p hi ah b qq", hi=2, qq=64)
            NCONV = int(os.environ.get("ATH_CONV_N", "128"))
            zld0 = {}
            zld1 = {}

            def issue_zv_loads(d):
                if d >= NCONV:
                    return
                sl4 = d % 4
                if d >= 4:
                    wait("sync", "MT", conv_c_mt[d - 4])
                    wait("scalar", "MT", conv_c_mt[d - 4])
                zvd = zv[:, sl4 * 256:(sl4 + 1) * 256].rearrange("a (b q) -> a b q", b=4)
                zld0[d] = dma("sync", "Tz0", zvd[0:32], oA1v[d, 0])
                zld1[d] = dma("scalar", "Tz1", zvd[32:64], oA1v[d, 1])

            for d in range(4):
                issue_zv_loads(d)
            for dl in range(NCONV):
                slot = dl % 2
                sl4 = dl % 4
                cb, db = CP[uc % 2], DPb[uc % 2]
                if uc >= 2:
                    wait("tensor", *unit_cfree[uc - 2])
                elif uc == 0:
                    wait("tensor", "MS", ps_ms0)
                hcm = instr("tensor", lambda t, dl=dl, cb=cb: t.matmul(
                    cb[0:64, 0:130], zh32[:, dl * 64:(dl + 1) * 64], sb_c["cR1h"][:, :],
                    start=True, stop=True), inc="MT")
                wait("scalar", "MT", hcm)
                if dl >= 1:
                    wait("scalar", "MT", dh_mt[dl - 1])
                instr("scalar", lambda s, cb=cb: s.activation(
                    ath32[0:64, 65:195], cb[0:64, 0:130], AF.Identity))
                hae = instr("scalar", lambda s, cb=cb: s.activation(
                    ath32[0:64, 0:65], cb[0:64, 65:130], AF.Identity, scale=-1.0), inc="MS")
                unit_cfree[uc] = ("MS", hae)
                if uc >= 2:
                    wait("tensor", *unit_dfree[uc - 2])
                elif uc == 0:
                    wait("tensor", "MS", ps_ms0)
                wait("tensor", "MS", hae)
                instr("tensor", lambda t, db=db: t.matmul(
                    db[:, 0:130], sb_c["cG1f"][:, :], ath32[0:64, 65:195], start=True, stop=False))
                hdm = instr("tensor", lambda t, db=db: t.matmul(
                    db[:, 0:130], sb_c["cG2f"][:, :], ath32[0:64, 0:130], start=False, stop=True), inc="MT")
                dh_mt[dl] = hdm
                wait("scalar", "MT", hdm)
                if dl >= 2:
                    wait("scalar", "MG", hbfree[dl - 2])
                instr("scalar", lambda s, db=db, slot=slot: s.activation(
                    hb32[:, slot * 260:slot * 260 + 130], db[:, 0:130], AF.Identity))
                hEv = instr("scalar", lambda s, db=db, slot=slot: s.activation(
                    hb32[:, slot * 260 + 130:slot * 260 + 260], db[:, 0:130], AF.Identity), inc="MS")
                unit_dfree[uc] = ("MS", hEv)
                uc += 1
                for pp in range(2):
                    ps_ = pc_ % 2
                    cb, db, da = CP[uc % 2], DPb[uc % 2], DPa[uc % 2]
                    if uc >= 2:
                        wait("tensor", *unit_cfree[uc - 2])
                    wait("tensor", "Tz0", zld0[dl])
                    wait("tensor", "Tz1", zld1[dl])
                    cm = instr("tensor", lambda t, cb=cb, sl4=sl4, pp=pp: t.matmul(
                        cb[0:128, 0:130], zv[:, sl4 * 256 + pp * 128:sl4 * 256 + (pp + 1) * 128],
                        sb_c["cR1h"][:, :], start=True, stop=True), inc="MT")
                    if pp == 1:
                        conv_c_mt[dl] = cm
                        issue_zv_loads(dl + 4)
                    wait("vector", "MT", cm)
                    if pc_ >= 2:
                        wait("vector", "MT", ad_mt[pc_ - 2])
                    instr("vector", lambda v, cb=cb, ps_=ps_: v.tensor_copy(
                        at_b[:, ps_ * 195 + 65:ps_ * 195 + 195], cb[:, 0:130]))
                    ae = instr("vector", lambda v, cb=cb, ps_=ps_: v.tensor_scalar(
                        at_b[:, ps_ * 195:ps_ * 195 + 65], cb[:, 65:130], -1.0, None, ALU.mult), inc="MV")
                    unit_cfree[uc] = ("MV", ae)
                    if uc >= 2:
                        wait("tensor", *unit_dfree[uc - 2])
                    wait("tensor", "MV", ae)
                    dm = None
                    for sg, pb in ((1, db), (0, da)):
                        instr("tensor", lambda t, pb=pb, ps_=ps_, sg=sg: t.matmul(
                            pb[:, 0:130], sb_c["cG1d"][sg * 64:sg * 64 + 64, :],
                            at_b[sg * 64:sg * 64 + 64, ps_ * 195 + 65:ps_ * 195 + 195],
                            start=True, stop=False))
                        dm = instr("tensor", lambda t, pb=pb, ps_=ps_, sg=sg: t.matmul(
                            pb[:, 0:130], sb_c["cG2d"][sg * 64:sg * 64 + 64, :],
                            at_b[sg * 64:sg * 64 + 64, ps_ * 195:ps_ * 195 + 130],
                            start=False, stop=True), inc="MT")
                    ad_mt[pc_] = dm
                    wait("scalar", "MT", dm)
                    if pc_ >= 2:
                        wait("scalar", "MG", pr_mg[pc_ - 2])
                    instr("scalar", lambda s, db=db, ps_=ps_: s.activation(
                        vh32[:, ps_ * 260:ps_ * 260 + 130], db[:, 0:130], AF.Identity))
                    vEv = instr("scalar", lambda s, da=da, ps_=ps_: s.activation(
                        vh32[:, ps_ * 260 + 130:ps_ * 260 + 260], da[:, 0:130], AF.Identity), inc="MS")
                    unit_dfree[uc] = ("MS", vEv)
                    uc += 1
                    wait("gpsimd", "MS", max(vEv, hEv))
                    if pc_ >= 2:
                        wait("gpsimd", "MV", yv_mv[pc_ - 2])
                    vv4 = vh32[:, ps_ * 260:(ps_ + 1) * 260].rearrange("p (s t c) -> p s t c", s=2, t=2)
                    hb4 = hb32[:, slot * 260:(slot + 1) * 260].rearrange("p (s t c) -> p s t c", s=2, t=2)
                    tp4 = tmpb[:, ps_ * 520:(ps_ + 1) * 520].rearrange("p (q s c) -> p q s c", q=4, s=2)
                    instr("gpsimd", lambda g, vv4=vv4, hb4=hb4, tp4=tp4: g.tensor_tensor(
                        tp4[:, 0], vv4[:, :, 0], hb4[:, :, 0], ALU.mult))
                    instr("gpsimd", lambda g, vv4=vv4, hb4=hb4, tp4=tp4: g.tensor_tensor(
                        tp4[:, 1], vv4[:, :, 1], hb4[:, :, 1], ALU.mult))
                    instr("gpsimd", lambda g, vv4=vv4, hb4=hb4, tp4=tp4: g.tensor_tensor(
                        tp4[:, 2], vv4[:, :, 0], hb4[:, :, 1], ALU.mult))
                    pg = instr("gpsimd", lambda g, vv4=vv4, hb4=hb4, tp4=tp4: g.tensor_tensor(
                        tp4[:, 3], vv4[:, :, 1], hb4[:, :, 0], ALU.mult), inc="MG")
                    pr_mg[pc_] = pg
                    if pp == 1:
                        hbfree[dl] = pg
                    wait("vector", "MG", pg)
                    if pc_ >= 2:
                        wait("vector", "MT", ap_mt[pc_ - 2])
                    yp4 = ypb[:, ps_ * 260:(ps_ + 1) * 260].rearrange("p (s t c) -> p s t c", s=2, t=2)
                    instr("vector", lambda v, yp4=yp4, tp4=tp4: v.tensor_tensor(
                        yp4[:, :, 0], tp4[:, 0], tp4[:, 1], ALU.subtract))
                    ym = instr("vector", lambda v, yp4=yp4, tp4=tp4: v.tensor_tensor(
                        yp4[:, :, 1], tp4[:, 2], tp4[:, 3], ALU.add), inc="MV")
                    yv_mv[pc_] = ym
                    ab = psD
                    if pc_ >= 1:
                        wait("tensor", "MV", e_mv[pc_ - 1])
                    else:
                        wait("tensor", "MS", ps_ms0)
                    wait("tensor", "MV", ym)
                    am = None
                    for sg in range(2):
                        base = ps_ * 260 + (1 - sg) * 130
                        instr("tensor", lambda t, ab=ab, sg=sg, base=base: t.matmul(
                            ab[0:65, sg * 256:(sg + 1) * 256], ypb[:, base:base + 65],
                            sb_c["cR3"][:, :], start=True, stop=False))
                        am = instr("tensor", lambda t, ab=ab, sg=sg, base=base: t.matmul(
                            ab[0:65, sg * 256:(sg + 1) * 256], ypb[:, base + 65:base + 130],
                            sb_c["cR4h"][:, :], start=False, stop=True), inc="MT")
                    ap_mt[pc_] = am
                    wait("vector", "MT", am)
                    if dl >= 2:
                        wait("vector", "MT", b_mt[dl - 2])
                    ee = instr("vector", lambda v, ab=ab, slot=slot, pp=pp: v.tensor_copy(
                        esb[:, slot * 1024 + pp * 512:slot * 1024 + (pp + 1) * 512],
                        ab[0:65, 0:512]), inc="MV")
                    e_mv[pc_] = ee
                    pc_ += 1
                bb = psF
                if dl >= 1:
                    wait("tensor", "MV", y_mv[dl - 1])
                else:
                    wait("tensor", "MV", ps_mv0)
                wait("tensor", "MV", e_mv[pc_ - 1])
                es4 = esb[:, slot * 1024:(slot + 1) * 1024].rearrange("p (u j q) -> p u j q", u=4, j=4)
                instr("tensor", lambda t, bb=bb, es4=es4: t.matmul(
                    bb[0:64, 0:256], sb_c["cCO2"][:, :], es4[:, :, 0], start=True, stop=False))
                instr("tensor", lambda t, bb=bb, es4=es4: t.matmul(
                    bb[0:64, 0:256], sb_c["cSI2n"][:, :], es4[:, :, 2], start=False, stop=False))
                instr("tensor", lambda t, bb=bb, es4=es4: t.matmul(
                    bb[0:64, 0:256], sb_c["cCO2s"][:, :], es4[:, :, 1], start=False, stop=False))
                bm = instr("tensor", lambda t, bb=bb, es4=es4: t.matmul(
                    bb[0:64, 0:256], sb_c["cSI2ns"][:, :], es4[:, :, 3], start=False, stop=True), inc="MT")
                b_mt[dl] = bm
                wait("vector", "MT", bm)
                if dl >= 4:
                    wait("vector", "Ty", yst[dl - 4])
                    wait("vector", "Ty2", yst2[dl - 4])
                ye = instr("vector", lambda v, bb=bb, sl4=sl4: v.tensor_copy(
                    ysb[:, sl4 * 256:(sl4 + 1) * 256], bb[0:64, 0:256]), inc="MV")
                y_mv[dl] = ye
                wait("sync", "MV", ye)
                wait("gpsimd", "MV", ye)
                ysv = ysb[:, sl4 * 256:(sl4 + 1) * 256].rearrange("a (b q) -> a b q", b=4)
                yst[dl] = dma("sync", "Ty", iA2v[dl, 0], ysv[0:32])
                yst2[dl] = dma("gpsimd", "Ty2", iA2v[dl, 1], ysv[32:64])
            conv_out_T = yst[max(yst)] if yst else 0
            conv_out_T2 = yst2[max(yst2)] if yst2 else 0
            ckpt("p5")

            wait("gpsimd", "Ty", conv_out_T)
            wait("gpsimd", "Ty2", conv_out_T2)
            a2a2 = instr("gpsimd", lambda g: g.collective_compute(
                "AllToAll", ALU.bypass, replica_groups=[list(range(NCORES))],
                ins=[iA2.tensor.ap().opt()], outs=[oA2.tensor.ap().opt()]), inc="CC")
            wait("gpsimd", "CC", a2a2)
            vcl = dma("gpsimd", "Ga", BC[:, :].rearrange("p (j q) -> p j q", j=NCORES),
                      oA2[:, :, :].rearrange("j p q -> p j q"))

            wait("vector", "Ga", vcl)
            wait("vector", "MV", u_done)
            ypr = instr("vector", lambda v: v.tensor_tensor(BA[:, :], BB[:, :], BC[:, :], ALU.mult), inc="MV")
            wait("gpsimd", "MT", v_mm_done)
            wo_ld = dma("gpsimd", "Gw", wbuf[:, :].rearrange("p (c d) -> p c d", c=NCH),
                        Wo_d.rearrange("(c p) d -> p c d", p=128))
            wait("tensor", "Gw", wo_ld)
            wait("tensor", "MV", ypr)
            evac_sem = "MV"

            BBtile = BB[:, :].rearrange("p (i c q) -> p i c q", i=NTILE, c=NCH)

            def evac_o(m, mtv):
                last = None
                for s in range(4):
                    wait("vector", "MT", mtv)
                    last = instr("vector", lambda v, m=m, s=s: v.tensor_scalar(
                        BBtile[:, 4 * s:4 * s + 4, m, :],
                        PS4[s][:, 0:512], bias_o[:, m:m + 1], None, ALU.add), inc="MV")
                return last

            emit_proj(evac_o)

            res_add = []
            x2lo_t = x2lo.rearrange("(i p) d -> i p d", p=128)
            x2st = {}
            mv0_p7 = n["MV"]
            for i in range(NTILE):
                tb = BC[:, (i % 2) * 1024:(i % 2) * 1024 + 1024]
                lo = lnst if i % 2 == 0 else lob
                wait("sync", "MV", res_add[i - 2] if i >= 2 else mv0_p7)
                src2 = BB[:, i * D:(i + 1) * D]
                tt = dmat("sync", "Tat", tb.rearrange("t (c p) -> t c p", p=128), src2)
                wait("vector", "Tat", tt)
                if i >= 2:
                    wait("vector", "Ls", x2st[i - 2])
                for a, b in ((0, 512), (512, 1024)):
                    instr("vector", lambda v, i=i, a=a, b=b, tb=tb: v.tensor_tensor(
                        stage[:, a:b], B8[:, i * D + a:i * D + b], tb[:, a:b], ALU.add))
                for a, b in ((0, 512), (512, 1024)):
                    instr("vector", lambda v, i=i, a=a, b=b: v.tensor_copy(
                        B8[:, i * D + a:i * D + b], stage[:, a:b]))
                ra = None
                for a, b in ((0, 512), (512, 1024)):
                    ra = instr("vector", lambda v, i=i, a=a, b=b, lo=lo: v.tensor_tensor(
                        lo[:, a:b], stage[:, a:b], B8[:, i * D + a:i * D + b], ALU.subtract), inc="MV")
                res_add.append(ra)
                wait("scalar", "MV", res_add[-1])
                x2st[i] = dma("scalar", "Ls", x2lo_t[i], lo[:, :])

            for pi in range(NTILE // 2):
                iA, iB = 2 * pi, 2 * pi + 1
                xtA = B8[:, iA * D:(iA + 1) * D]
                xtB = B8[:, iB * D:(iB + 1) * D]
                wait("vector", "MV", res_add[iB])
                vrec = emit_ln_stats_pair(xtA, xtB)
                instr("vector", lambda v, iA=iA: v.tensor_copy(mstats[:, 2 * iA:2 * iA + 1], agg[:, 0:1]))
                instr("vector", lambda v, iB=iB: v.tensor_copy(mstats[:, 2 * iB:2 * iB + 1], agg[:, 8:9]))
                wait("scalar", "MV", vrec)
                instr("scalar", lambda s, iA=iA: s.activation(
                    mstats[:, 2 * iA + 1:2 * iA + 2], agg[:, 1:2], AF.Sqrt))
                vsq = instr("scalar", lambda s, iB=iB: s.activation(
                    mstats[:, 2 * iB + 1:2 * iB + 2], agg[:, 9:10], AF.Sqrt), inc="MS")
                wait("vector", "MS", vsq)
                instr("vector", lambda v, xtA=xtA, iA=iA: v.tensor_scalar(
                    stage[:, 0:1024], xtA, mstats[:, 2 * iA:2 * iA + 1], mstats[:, 2 * iA + 1:2 * iA + 2],
                    ALU.subtract, ALU.mult))
                instr("vector", lambda v, xtB=xtB, iB=iB: v.tensor_scalar(
                    hstg[:, 0:1024], xtB, mstats[:, 2 * iB:2 * iB + 1], mstats[:, 2 * iB + 1:2 * iB + 2],
                    ALU.subtract, ALU.mult))
                instr("vector", lambda v, iA=iA: v.reduce_max(
                    mstats[:, 32 + iA:33 + iA], stage[:, 0:1024], axis=AX))
                instr("vector", lambda v, iB=iB: v.reduce_max(
                    mstats[:, 32 + iB:33 + iB], hstg[:, 0:1024], axis=AX), inc="MV")
            wait("vector", "MV", n["MV"])
            instr("vector", lambda v: v.reduce_max(lmax[:, 0:1], mstats[:, 32:48], axis=AX), inc="MV")
            wait("tensor", "MV", n["MV"])
            vtm = instr("tensor", lambda t: t.matmul(
                psE[0:1, 0:128], lmax[:, 0:1], sb_c["cI128"][:, :], start=True, stop=True), inc="MT")
            wait("vector", "MT", vtm)
            vgm = instr("vector", lambda v: v.reduce_max(gsc[0:1, 0:1], psE[0:1, 0:128], axis=AX), inc="MV")
            wait("gpsimd", "MV", vgm)
            mx1 = dma("gpsimd", "Gm", mxi1[:, :], gsc[0:1, 0:1])
            wait("gpsimd", "Gm", mx1)
            ar1 = instr("gpsimd", lambda g: g.collective_compute(
                "AllReduce", ALU.max, replica_groups=[list(range(NCORES))],
                ins=[mxi1.tensor.ap().opt()], outs=[mxo1.tensor.ap().opt()]), inc="CC")
            wait("gpsimd", "CC", ar1)
            mx1b = dma("gpsimd", "Gm", gsc[0:1, 1:2], mxo1[:, :])
            wait("vector", "Gm", mx1b)
            vv = instr("vector", lambda v: v.tensor_scalar(gsc[0:1, 1:2], gsc[0:1, 1:2], 1e-8, None, ALU.max), inc="MV")
            wait("vector", "MV", vv)
            vv = instr("vector", lambda v: v.reciprocal(gsc[0:1, 2:3], gsc[0:1, 1:2]), inc="MV")
            wait("vector", "MV", vv)
            vv = instr("vector", lambda v: v.tensor_scalar(gsc[0:1, 2:3], gsc[0:1, 2:3], 255.0, None, ALU.mult), inc="MV")
            wait("vector", "MV", vv)
            vsx = instr("vector", lambda v: v.tensor_scalar(gsc[0:1, 3:4], gsc[0:1, 1:2], 1.0 / 255.0, None, ALU.mult), inc="MV")
            wait("tensor", "MV", vsx)
            wait("tensor", "MS", n["MS"])
            vbb = instr("tensor", lambda t: t.matmul(
                psE[:, 0:2], sb_c["cOnes"][0:1, :], gsc[0:1, 2:4], start=True, stop=True), inc="MT")
            wait("vector", "MT", vbb)
            vrx = instr("vector", lambda v: v.tensor_copy(rxb[:, 0:2], psE[:, 0:2]), inc="MV")
            wait("vector", "MV", max(vrw1, vrx))
            H1 = ((0, 16), (16, 32))
            for a, b in H1:
                instr("vector", lambda v, a=a, b=b: v.tensor_scalar(
                    sxw1[:, a:b], sw1[:, a:b], rxb[:, 1:2], None, ALU.mult))
            for a, b in H1:
                instr("vector", lambda v, a=a, b=b: v.tensor_tensor(
                    bq1[:, a:b], b1t[:, a:b], rw1[:, a:b], ALU.mult))
            for a, b in H1:
                instr("vector", lambda v, a=a, b=b: v.tensor_scalar(
                    bq1[:, a:b], bq1[:, a:b], rxb[:, 0:1], None, ALU.mult))
            for a, b in H1:
                instr("vector", lambda v, a=a, b=b: v.tensor_scalar(
                    bq1[:, a:b], bq1[:, a:b], MAGIC, MAGIC, ALU.add, ALU.subtract))
            vex = None
            for a, b in H1:
                vex = instr("vector", lambda v, a=a, b=b: v.tensor_tensor(
                    exv1[:, a:b], sxw1[:, a:b], bq1[:, a:b], ALU.mult), inc="MV")
            conv_sc.close()
            attn_sc.close()
            rwb1 = sbuf("rwb1", [128, FF], BF16, scope=ffnA_sc)
            rwb2 = sbuf("rwb2", [128, D], BF16, scope=ffnA_sc)
            wf1 = sbuf("wf1", [128, 4 * 1024], F32, scope=ffnA_sc)
            qw1 = sbuf("qw1", [128, 4 * 1024], BF16, scope=ffnA_sc)
            wp2 = sbuf("wp2", [128, 1024], F32, scope=ffnA_sc)
            qp2 = sbuf("qp2", [128, 1024], BF16, scope=ffnA_sc)
            hst2 = sbuf("hst2", [128, 2048], F32, scope=ffnA_sc)

            wait("gpsimd", "MV", vex)
            wait("scalar", "MV", vex)

            def emit_bcast(rwt, nrows, dst, vgate):
                wait("sync", "MT", n["MT"])
                wait("tensor", "MV", vgate)
                vtr = instr("tensor", lambda t, rwt=rwt, nrows=nrows: t.matmul(
                    psE[0:nrows, 0:128], rwt[:, :], sb_c["cI128"][:, :], start=True, stop=True), inc="MT")
                wait("vector", "MT", vtr)
                vcp = instr("vector", lambda v, rwt=rwt, nrows=nrows: v.tensor_copy(
                    stage[0:nrows, 0:128], psE[0:nrows, 0:128]), inc="MV")
                out = None
                prev = 0
                for h0 in range(0, nrows, 16):
                    rr = min(16, nrows - h0)
                    wait("sync", "MV", max(vcp, prev))
                    row = dma("sync", "Trow",
                              wf1[0:1, 0:rr * 128].rearrange("o (a b) -> o a b", b=128),
                              stage[h0:h0 + rr, 0:128])
                    wait("tensor", "Trow", row)
                    for sI in range((rr * 128) // 512):
                        if prev:
                            wait("tensor", "MV", prev)
                        vbm = instr("tensor", lambda t, sI=sI: t.matmul(
                            psE[:, 0:512], sb_c["cOnes"][0:1, :], wf1[0:1, sI * 512:(sI + 1) * 512],
                            start=True, stop=True), inc="MT")
                        wait("vector", "MT", vbm)
                        out = instr("vector", lambda v, sI=sI, dst=dst, h0=h0: v.tensor_copy(
                            dst[:, h0 * 128 + sI * 512: h0 * 128 + (sI + 1) * 512], psE[:, 0:512]), inc="MV")
                        prev = out
                return out

            wait("sync", "Tat", n["Tat"])
            wait("sync", "Tln", n["Tln"])
            bc1 = emit_bcast(rw1, NFT, rwb1, max(vrw1, vex))
            ckpt("p8a")
            bc2 = emit_bcast(rw2, NCH, rwb2, max(vrw2, bc1))
            ckpt("p8b")

            HS = ((0, 512), (512, 1024))
            qtt = {}
            wait("vector", "Trow", n["Trow"])
            wait("vector", "Ls", n["Ls"])
            wait("sync", "Trow", n["Trow"])
            for i in range(NTILE):
                xt = B8[:, i * D:(i + 1) * D]
                buf = lnst if i % 2 == 0 else lob
                wait("vector", "MV", vrx)
                if i >= 2:
                    wait("vector", "Tqx", qtt[i - 2])
                for a, b in HS:
                    instr("vector", lambda v, xt=xt, i=i, a=a, b=b: v.tensor_scalar(
                        stage[:, a:b], xt[:, a:b], mstats[:, 2 * i:2 * i + 1], mstats[:, 2 * i + 1:2 * i + 2],
                        ALU.subtract, ALU.mult))
                for a, b in HS:
                    instr("vector", lambda v, a=a, b=b: v.tensor_scalar(
                        stage[:, a:b], stage[:, a:b], rxb[:, 0:1], MAGIC, ALU.mult, ALU.add))
                for a, b in HS:
                    instr("vector", lambda v, a=a, b=b: v.tensor_scalar(
                        stage[:, a:b], stage[:, a:b], MAGIC, 0.0, ALU.subtract, ALU.max))
                vq = None
                for a, b in HS:
                    vq = instr("vector", lambda v, buf=buf, a=a, b=b: v.tensor_scalar(
                        buf[:, a:b], stage[:, a:b], 255.0, None, ALU.min), inc="MV")
                wait("sync", "MV", vq)
                src3 = buf[:, :].rearrange("p (c q) -> p c q", q=128)
                dst3 = BA[:, :].rearrange("p (c i q) -> p c i q", c=NCH, i=NTILE)[:, :, i, :]
                qtt[i] = dmat("sync", "Tqx", dst3, src3)
            qxT_done = qtt[NTILE - 1]
            ckpt("p8")

            w2T_t = w2T_d.rearrange("(f p) (c q) -> f p c q", p=128, q=128)
            pq_vv = 0
            pq_st = 0
            for fc in range(NFT):
                if pq_st:
                    wait("gpsimd", "Gq", pq_st)
                lv = dma("gpsimd", "Lq2", wp2[:, :].rearrange("p (c q) -> p c q", q=128), w2T_t[fc])
                wait("gpsimd", "Lq2", lv)
                wait("gpsimd", "MV", max(bc2, pq_vv))
                vgq = instr("gpsimd", lambda g: g.tensor_tensor(wp2[:, :], wp2[:, :], rwb2[:, :], ALU.mult), inc="MG")
                wait("vector", "MG", vgq)
                pq_vv = instr("vector", lambda v: v.tensor_scalar(
                    qp2[:, :], wp2[:, :], MAGIC, MAGIC, ALU.add, ALU.subtract), inc="MV")
                wait("gpsimd", "MV", pq_vv)
                pq_st = dma("gpsimd", "Gq",
                            qw2d.rearrange("m p (f q) -> p m f q", q=128)[:, :, fc],
                            qp2[:, :].rearrange("p (m q) -> p m q", q=128))
            w2q_done = pq_st
            ckpt("p8q")

            w1T_t = w1T_d.rearrange("(c p) (t q) -> t p c q", p=128, q=128)
            hst_store = {}
            ff1_vq = {}
            ff1_mg = {}
            ff1_ms = {}
            ff1_mmlast = {}
            PS8B = [psE, psF, psG, psH]
            HSL = [hstg, hst2]
            for ft in range(NFT):
                par = ft % 2
                sl4 = ft % 4
                if ft == 0:
                    wait("scalar", "MV", bc2)
                if ft >= 4:
                    wait("scalar", "MG", ff1_mg[ft - 4])
                lv = dma("scalar", "Lw0" if par == 0 else "Lw1",
                         wf1[:, sl4 * 1024:(sl4 + 1) * 1024].rearrange("p (c q) -> p c q", q=128),
                         w1T_t[ft])
                wait("gpsimd", "Lw0" if par == 0 else "Lw1", lv)
                wait("gpsimd", "MV", bc1)
                if ft >= 4:
                    wait("gpsimd", "MV", ff1_vq[ft - 4])
                for c8 in range(NCH):
                    instr("gpsimd", lambda g, sl4=sl4, c8=c8, ft=ft: g.tensor_tensor(
                        wf1[:, sl4 * 1024 + c8 * 128: sl4 * 1024 + (c8 + 1) * 128],
                        wf1[:, sl4 * 1024 + c8 * 128: sl4 * 1024 + (c8 + 1) * 128],
                        rwb1[:, ft * 128:(ft + 1) * 128], ALU.mult),
                        inc=("MG" if c8 == NCH - 1 else None))
                ff1_mg[ft] = n["MG"]
                wait("vector", "MG", n["MG"])
                if ft >= 4:
                    wait("vector", "MT", ff1_mmlast[ft - 4])
                vq1 = instr("vector", lambda v, sl4=sl4: v.tensor_scalar(
                    qw1[:, sl4 * 1024:(sl4 + 1) * 1024], wf1[:, sl4 * 1024:(sl4 + 1) * 1024],
                    MAGIC, MAGIC, ALU.add, ALU.subtract), inc="MV")
                ff1_vq[ft] = vq1
                wait("tensor", "MV", vq1)
                if ft == 0:
                    wait("tensor", "Tqx", qxT_done)
                if ft >= 2:
                    wait("tensor", "MS", ff1_ms[ft - 2])
                PSET = PS4 if par == 0 else PS8B
                for c8 in range(NCH):
                    for s in range(4):
                        instr("tensor", lambda t, sl4=sl4, c8=c8, s=s, PSET=PSET: t.matmul(
                            PSET[s][:, 0:512], qw1[:, sl4 * 1024 + c8 * 128: sl4 * 1024 + (c8 + 1) * 128],
                            BAv[:, c8, s * 512:(s + 1) * 512],
                            start=(c8 == 0), stop=(c8 == NCH - 1)),
                            inc=("MT" if c8 == NCH - 1 else None))
                ff1_mmlast[ft] = n["MT"]
                mtv = n["MT"]
                if ft >= 2:
                    wait("scalar", "Gh", hst_store[ft - 2])
                hsl = HSL[par]
                for s in range(4):
                    wait("scalar", "MT", mtv)
                    instr("scalar", lambda sc, ft=ft, s=s, PSET=PSET, hsl=hsl: sc.activation(
                        hsl[:, s * 512:(s + 1) * 512],
                        PSET[s][:, 0:512], AF.Relu, bias=exv1[:, ft:ft + 1], scale=sxw1[:, ft:ft + 1]),
                        inc=("MS" if s == 3 else None))
                ff1_ms[ft] = n["MS"]
                wait("vector", "MS", n["MS"])
                vmx = instr("vector", lambda v, ft=ft, hsl=hsl: v.reduce_max(
                    mstats[:, ft:ft + 1], hsl[:, 0:NT], axis=AX), inc="MV")
                wait("gpsimd", "MV", vmx)
                hst_store[ft] = dma("gpsimd", "Gh", hdr[ft], hsl[:, 0:NT])

            wait("tensor", "MS", n["MS"])
            wait("vector", "MV", n["MV"])
            instr("vector", lambda v: v.reduce_max(lmax[:, 1:2], mstats[:, 0:32], axis=AX), inc="MV")
            wait("tensor", "MV", n["MV"])
            vtm2 = instr("tensor", lambda t: t.matmul(
                psE[0:1, 0:128], lmax[:, 1:2], sb_c["cI128"][:, :], start=True, stop=True), inc="MT")
            wait("vector", "MT", vtm2)
            vgm2 = instr("vector", lambda v: v.reduce_max(gsc[0:1, 4:5], psE[0:1, 0:128], axis=AX), inc="MV")
            wait("gpsimd", "MV", vgm2)
            mx2 = dma("gpsimd", "Gm", mxi2[:, :], gsc[0:1, 4:5])
            wait("gpsimd", "Gm", mx2)
            ar2 = instr("gpsimd", lambda g: g.collective_compute(
                "AllReduce", ALU.max, replica_groups=[list(range(NCORES))],
                ins=[mxi2.tensor.ap().opt()], outs=[mxo2.tensor.ap().opt()]), inc="CC")
            wait("gpsimd", "CC", ar2)
            mx2b = dma("gpsimd", "Gm", gsc[0:1, 5:6], mxo2[:, :])
            wait("vector", "Gm", mx2b)
            vv = instr("vector", lambda v: v.tensor_scalar(gsc[0:1, 5:6], gsc[0:1, 5:6], 1e-8, None, ALU.max), inc="MV")
            wait("vector", "MV", vv)
            vv = instr("vector", lambda v: v.reciprocal(gsc[0:1, 6:7], gsc[0:1, 5:6]), inc="MV")
            wait("vector", "MV", vv)
            vv = instr("vector", lambda v: v.tensor_scalar(gsc[0:1, 6:7], gsc[0:1, 6:7], 255.0, None, ALU.mult), inc="MV")
            wait("vector", "MV", vv)
            vsh = instr("vector", lambda v: v.tensor_scalar(gsc[0:1, 7:8], gsc[0:1, 5:6], 1.0 / 255.0, None, ALU.mult), inc="MV")
            wait("tensor", "MV", vsh)
            vbb2 = instr("tensor", lambda t: t.matmul(
                psE[:, 0:2], sb_c["cOnes"][0:1, :], gsc[0:1, 6:8], start=True, stop=True), inc="MT")
            wait("vector", "MT", vbb2)
            vrh = instr("vector", lambda v: v.tensor_copy(rxb[:, 2:4], psE[:, 0:2]), inc="MV")
            wait("vector", "Lc", n["Lc"])
            wait("vector", "MV", vrh)
            H2 = ((0, 4), (4, 8))
            for a, b in H2:
                instr("vector", lambda v, a=a, b=b: v.tensor_scalar(
                    shw2[:, a:b], sw2[:, a:b], rxb[:, 3:4], None, ALU.mult))
            for a, b in H2:
                instr("vector", lambda v, a=a, b=b: v.tensor_tensor(
                    bq2[:, a:b], b2t[:, a:b], rw2[:, a:b], ALU.mult))
            for a, b in H2:
                instr("vector", lambda v, a=a, b=b: v.tensor_scalar(
                    bq2[:, a:b], bq2[:, a:b], rxb[:, 2:3], None, ALU.mult))
            for a, b in H2:
                instr("vector", lambda v, a=a, b=b: v.tensor_scalar(
                    bq2[:, a:b], bq2[:, a:b], MAGIC, MAGIC, ALU.add, ALU.subtract))
            vex2 = None
            for a, b in H2:
                vex2 = instr("vector", lambda v, a=a, b=b: v.tensor_tensor(
                    exv2[:, a:b], shw2[:, a:b], bq2[:, a:b], ALU.mult), inc="MV")
            ckpt("p9")

            ffnA_sc.close()
            qh = sbuf("qh", [128, NFT * 256], BF16, scope=ffnB_sc)
            qw2m = sbuf("qw2m", [128, 4 * 4096], BF16, scope=ffnB_sc)
            hbf = sbuf("hbf", [128, 2048], BF16, scope=ffnB_sc)

            wait("sync", "MV", vex2)
            wait("scalar", "MV", vex2)
            wait("gpsimd", "MV", vex2)
            wait("scalar", "Gh", n["Gh"])
            wait("sync", "Gh", n["Gh"])
            wait("gpsimd", "Gh", n["Gh"])

            ffn_ev = []
            qhl = 0
            qw_mt = {}
            qwld = {}
            for s2 in range(8):
                for piece in range(4):
                    if qhl:
                        wait("scalar", "MV", qhl)
                    lv = dma("scalar", "Lh", hbf[:, :].rearrange("p (c q) -> p c q", q=256),
                             hdr[piece * 8:(piece + 1) * 8, :, s2 * 256:(s2 + 1) * 256]
                             .rearrange("f p q -> p f q"))
                    wait("vector", "Lh", lv)
                    wait("vector", "MV", vrh)
                    instr("vector", lambda v: v.tensor_scalar(
                        stage[:, 0:1024], hbf[:, 0:1024], rxb[:, 2:3], MAGIC, ALU.mult, ALU.add))
                    instr("vector", lambda v: v.tensor_scalar(
                        hstg[:, 0:1024], hbf[:, 1024:2048], rxb[:, 2:3], MAGIC, ALU.mult, ALU.add))
                    instr("vector", lambda v, piece=piece: v.tensor_scalar(
                        qh[:, piece * 2048: piece * 2048 + 1024], stage[:, 0:1024],
                        MAGIC, None, ALU.subtract))
                    qhl = instr("vector", lambda v, piece=piece: v.tensor_scalar(
                        qh[:, piece * 2048 + 1024: piece * 2048 + 2048], hstg[:, 0:1024],
                        MAGIC, None, ALU.subtract), inc="MV")
                qh_ready = qhl
                for m in range(NCH):
                    ck = s2 * NCH + m
                    sl4 = ck % 4
                    qeng, qsm = (("sync", "Tq"), ("scalar", "Lq"), ("gpsimd", "Gq"))[ck % 3]
                    wait(qeng, "Gq", w2q_done)
                    key = ck
                    if ck >= 4:
                        wait(qeng, "MT", qw_mt[ck - 4])
                    qwld[key] = dma(qeng, qsm,
                                    qw2m[:, sl4 * 4096:(sl4 + 1) * 4096], qw2d[m])
                    wait("tensor", qsm, qwld[key])
                    wait("tensor", "MV", qh_ready)
                    pp = psE if ck % 2 == 0 else psF
                    if len(ffn_ev) >= 2:
                        wait("tensor", "MV", ffn_ev[-2])
                    for fc in range(NFT):
                        instr("tensor", lambda t, sl4=sl4, fc=fc, pp=pp: t.matmul(
                            pp[:, 0:256], qw2m[:, sl4 * 4096 + fc * 128: sl4 * 4096 + (fc + 1) * 128],
                            qh[:, fc * 256:(fc + 1) * 256],
                            start=(fc == 0), stop=(fc == NFT - 1)),
                            inc=("MT" if fc == NFT - 1 else None))
                    qw_mt[key] = n["MT"]
                    wait("vector", "MT", n["MT"])
                    wait("vector", "MV", vex2)
                    ffn_ev.append(instr("vector", lambda v, m=m, s2=s2, pp=pp: v.tensor_scalar(
                        BBtile[:, 2 * s2:2 * s2 + 2, m, :], pp[:, 0:256],
                        bq2[:, m:m + 1], shw2[:, m:m + 1], ALU.add, ALU.mult), inc="MV"))

            ckpt("p10")
            wait("sync", "Trow", n["Trow"])
            wait("scalar", "Ls", n["Ls"])
            y_tiled = y_out.rearrange("(i p) d -> i p d", p=128)
            ffn2_all = n["MV"]
            ost = {}
            out_add = 0
            for i in range(NTILE):
                wait("sync", "MV", ffn2_all if i == 0 else out_add)
                src2 = BB[:, i * D:(i + 1) * D]
                dst3 = lnst[:, :].rearrange("t (c p) -> t c p", p=128)
                tt = dmat("sync", "Tout", dst3, src2)
                if i >= 1:
                    wait("scalar", "MV", out_add)
                lol = dma("scalar", "Lh", lob[:, :], x2lo_t[i])
                wait("vector", "Tout", tt)
                wait("vector", "Lh", lol)
                if i >= 2:
                    wait("vector", "O0" if i % 2 == 0 else "O1", ost[i - 2])
                half = i % 2
                obuf = stage if half == 0 else hstg
                for a, b in ((0, 512), (512, 1024)):
                    instr("vector", lambda v, i=i, obuf=obuf, a=a, b=b: v.tensor_tensor(
                        obuf[:, a:b], B8[:, i * D + a:i * D + b], lnst[:, a:b], ALU.add))
                for a, b in ((0, 512), (512, 1024)):
                    out_add = instr("vector", lambda v, obuf=obuf, a=a, b=b: v.tensor_tensor(
                        obuf[:, a:b], obuf[:, a:b], lob[:, a:b], ALU.add), inc="MV")
                wait("scalar", "MV", out_add)
                ost[i] = dma("scalar", "O0" if i % 2 == 0 else "O1", y_tiled[i], obuf[:, 0:1024])
            if DBG:
                dma("scalar", "O0", dbg_oA1[:, :, :], oA1[:, :, :])
                dma("scalar", "O0", dbg_iA2[:, :, :], iA2[:, :, :])
                dma("scalar", "O0", dbg_B8[:, :], B8[:, :])
                dma("scalar", "O0", dbg_qx[:, :], BA[:, :])
                dma("scalar", "O0", dbg_gsc[:, :], gsc[:, :])
                dma("scalar", "O0", dbg_hdr[0], hdr[0])
                dma("scalar", "O0", dbg_hdr[1], hdr[31])
            wait("scalar", "O0", n["O0"])
            wait("scalar", "O1", n["O1"])
            ffnB_sc.close()


        except _SkipRest:
            pass
        for _sc in (ffnB_sc, ffnA_sc, conv_sc, pe_sc, attn_sc):
            try:
                _sc.close()
            except Exception:
                pass

        @block.sync
        def _(e):
            for fn in streams["sync"]:
                fn(e)

        def _audit():
            actual = {nm: 0 for nm in SEMNAMES}
            byid = {SEMS[nm].sem_num if hasattr(SEMS[nm], "sem_num") else None: nm for nm in SEMNAMES}
            names = {}
            for i in nc.all_instructions():
                si = getattr(i, "sync_info", None)
                if si is None:
                    continue
                for u in si.on_update:
                    if u.ant_name in actual:
                        actual[u.ant_name] += u.update_value
            bad = {k: (actual[k], n[k]) for k in SEMNAMES if actual[k] != n[k]}
            assert not bad, f"semaphore accounting mismatch: {bad}"

        @block.scalar
        def _(e):
            for fn in streams["scalar"]:
                fn(e)

        @block.vector
        def _(e):
            for fn in streams["vector"]:
                fn(e)

        @block.tensor
        def _(e):
            for fn in streams["tensor"]:
                fn(e)

        @block.gpsimd
        def _(e):
            for fn in streams["gpsimd"]:
                fn(e)

        _audit()

    return nc


_NC_CACHE = None


def kernel(L0_attn_x, L0_attn_Wu, L0_attn_bu, L0_attn_Wv, L0_attn_bv,
           L0_attn_W1, L0_attn_b1, L0_attn_W2, L0_attn_b2, L0_attn_W3,
           L0_attn_b3, L0_attn_Wo, L0_attn_bo, L0_ffn_w1, L0_ffn_b1,
           L0_ffn_w2, L0_ffn_b2):
    global _NC_CACHE
    x = np.ascontiguousarray(np.asarray(L0_attn_x, np.float32))
    B, Lq, Dm = x.shape
    if _NC_CACHE is None:
        _NC_CACHE = build_nc()
    nc = _NC_CACHE
    f32c = lambda a: np.ascontiguousarray(np.asarray(a, np.float32))
    pos = np.arange(4096, dtype=np.float32)[:, None]
    div = np.exp(np.arange(0, 64, 2, dtype=np.float32) * -(np.log(10000.0) / 64.0))
    pe = np.zeros((4096, 64), np.float32)
    pe[:, 0::2] = np.sin(pos * div)
    pe[:, 1::2] = np.cos(pos * div)
    com = dict(
        Wu=f32c(L0_attn_Wu), Wv=f32c(L0_attn_Wv), Wo=f32c(L0_attn_Wo),
        bu=f32c(L0_attn_bu), bv=f32c(L0_attn_bv), bo=f32c(L0_attn_bo),
        W1=f32c(L0_attn_W1), W2=f32c(L0_attn_W2),
        b1f=f32c(L0_attn_b1), b2f=f32c(L0_attn_b2),
        peT=np.ascontiguousarray(pe.T),
        w1T=f32c(np.asarray(L0_ffn_w1).T), w1n=f32c(L0_ffn_w1), b1v=f32c(L0_ffn_b1),
        w2T=f32c(np.asarray(L0_ffn_w2).T), w2n=f32c(L0_ffn_w2), b2v=f32c(L0_ffn_b2),
    )
    com.update({"d" + k: v for k, v in CONSTS.items()})
    W3 = f32c(L0_attn_W3)
    b3 = f32c(L0_attn_b3)
    in_maps = []
    for c in range(NCORES):
        b, h = c // 2, c % 2
        m = dict(com)
        m["x_in"] = np.ascontiguousarray(x[b, h * NT:(h + 1) * NT, :])
        m["W3s"] = np.ascontiguousarray(W3[:, c * 128:(c + 1) * 128])
        m["b3s"] = np.ascontiguousarray(b3[c * 128:(c + 1) * 128])
        in_maps.append(m)
    res = run_bass_kernel_spmd(nc, in_maps, core_ids=list(range(NCORES)))
    out = np.empty((B, Lq, Dm), np.float32)
    for c in range(NCORES):
        b, h = c // 2, c % 2
        out[b, h * NT:(h + 1) * NT, :] = res.results[c]["y_out"]
    return out


# version 10
# speedup vs baseline: 1.2950x; 1.0180x over previous
import os
os.environ.setdefault("TRN_ALLOW_ICI_ALL_TO_ALL", "1")

import numpy as np
import ml_dtypes
import concourse.bass as bass
import concourse.mybir as mybir
from concourse.bass_utils import run_bass_kernel_spmd
from contextlib import ExitStack

F32 = mybir.dt.float32
BF16 = mybir.dt.bfloat16
AF = mybir.ActivationFunctionType
ALU = mybir.AluOpType
AX = mybir.AxisListType.X

MAGIC = float(np.float32(1.5 * 2 ** 23))
NCORES = 8
NT = 2048
D = 1024
FF = 4096
NTILE = NT // 128
NCH = D // 128
NFT = FF // 128


def _bf(x):
    return np.ascontiguousarray(np.asarray(x, np.float32)).astype(ml_dtypes.bfloat16)


def host_constants():
    k = np.arange(128)
    W1c = np.exp(-2j * np.pi * np.outer(k, k) / 128.0)
    F1r, F1i = W1c.real.astype(np.float32), W1c.imag.astype(np.float32)
    W2c = np.exp(+2j * np.pi * np.outer(k, k) / 128.0)
    F2r, F2i = W2c.real.astype(np.float32), W2c.imag.astype(np.float32)
    cR1h = np.concatenate([F1r[0:64, 0:65], F1i[0:64, 0:65]], axis=1)
    cG1 = F1r[0:64, :]
    cG2 = F1i[0:64, :]
    cG1d = np.concatenate([cG1, cG1], axis=0)
    cG2d = np.concatenate([cG2, cG2], axis=0)
    cR3 = np.concatenate([F2r, F2i], axis=1)
    cR4h = np.concatenate([-F2i, F2r], axis=1)
    a = np.arange(64)
    k65 = np.arange(65)
    alpha = np.where((k65 == 0) | (k65 == 64), 1.0, 2.0)[:, None].astype(np.float32)
    CO2 = (alpha * np.cos(2 * np.pi * np.outer(k65, a) / 128.0) / 16384.0).astype(np.float32)
    SI2n = (-alpha * np.sin(2 * np.pi * np.outer(k65, a) / 128.0) / 16384.0).astype(np.float32)
    CO2s = np.zeros((65, 64), np.float32); CO2s[:, 1:] = CO2[:, :-1]
    SI2ns = np.zeros((65, 64), np.float32); SI2ns[:, 1:] = SI2n[:, :-1]
    pos = np.arange(4096, dtype=np.float32)[:, None]
    div = np.exp(np.arange(0, 64, 2, dtype=np.float32) * -(np.log(10000.0) / 64.0))
    pe = np.zeros((4096, 64), np.float32)
    pe[:, 0::2] = np.sin(pos * div)
    pe[:, 1::2] = np.cos(pos * div)
    dec = np.exp(-0.01 * np.arange(4096, dtype=np.float32)).reshape(1, 4096)
    return dict(
        cR1h=_bf(cR1h), cG1d=_bf(cG1d), cG2d=_bf(cG2d), cR3=_bf(cR3), cR4h=_bf(cR4h),
        cCO2=_bf(CO2), cSI2n=_bf(SI2n), cCO2s=_bf(CO2s), cSI2ns=_bf(SI2ns),
        cG1f=np.ascontiguousarray(cG1.astype(np.float32)),
        cG2f=np.ascontiguousarray(cG2.astype(np.float32)),
        cDec=np.ascontiguousarray(dec.astype(np.float32)),
        cI128=np.eye(128, dtype=np.float32),
        cOnes=np.ones((1, 128), np.float32),
    )


CONSTS = host_constants()


def build_nc():
    nc = bass.Bass("TRN2", target_bir_lowering=False, debug=False, detect_race_conditions=False)
    DI = lambda nm, s, dt=F32: nc.dram_tensor(nm, s, dt, kind="ExternalInput").ap()
    x_in = DI("x_in", [NT, D])
    Wu_d = DI("Wu", [D, D]); Wv_d = DI("Wv", [D, D]); Wo_d = DI("Wo", [D, D])
    bu_d = DI("bu", [D]); bv_d = DI("bv", [D]); bo_d = DI("bo", [D])
    W1_d = DI("W1", [64, 64]); W2_d = DI("W2", [64, 64])
    b1f_d = DI("b1f", [64]); b2f_d = DI("b2f", [64])
    W3s_d = DI("W3s", [64, 128]); b3s_d = DI("b3s", [128])
    peT_d = DI("peT", [64, 4096])
    w1T_d = DI("w1T", [D, FF]); w1n_d = DI("w1n", [FF, D]); b1v_d = DI("b1v", [FF])
    w2T_d = DI("w2T", [FF, D]); w2n_d = DI("w2n", [D, FF]); b2v_d = DI("b2v", [D])
    cns = {k: DI("d" + k, list(v.shape), BF16 if v.dtype == ml_dtypes.bfloat16 else F32)
           for k, v in CONSTS.items()}
    y_out = nc.dram_tensor("y_out", [NT, D], F32, kind="ExternalOutput").ap()
    DBG = os.environ.get("ATH_DEBUG", "") == "1"
    if DBG:
        dbg_oA1 = nc.dram_tensor("dbg_oA1", [NCORES, 128, NT], BF16, kind="ExternalOutput").ap()
        dbg_iA2 = nc.dram_tensor("dbg_iA2", [NCORES, 128, NT], BF16, kind="ExternalOutput").ap()
        dbg_B8 = nc.dram_tensor("dbg_B8", [128, NTILE * D], BF16, kind="ExternalOutput").ap()
        dbg_qx = nc.dram_tensor("dbg_qx", [128, NCH * NT], BF16, kind="ExternalOutput").ap()
        dbg_gsc = nc.dram_tensor("dbg_gsc", [1, 8], F32, kind="ExternalOutput").ap()
        dbg_hdr = nc.dram_tensor("dbg_hdr", [2, 128, NT], BF16, kind="ExternalOutput").ap()
    iA1 = nc.dram_tensor("iA1", [NCORES, 128, NT], BF16).ap()
    oA1 = nc.dram_tensor("oA1", [NCORES, 128, NT], BF16).ap()
    iA2 = nc.dram_tensor("iA2", [NCORES, 128, NT], BF16).ap()
    oA2 = nc.dram_tensor("oA2", [NCORES, 128, NT], BF16).ap()
    mxi1 = nc.dram_tensor("mxi1", [1, 1], F32).ap()
    mxo1 = nc.dram_tensor("mxo1", [1, 1], F32).ap()
    mxi2 = nc.dram_tensor("mxi2", [1, 1], F32).ap()
    mxo2 = nc.dram_tensor("mxo2", [1, 1], F32).ap()
    hdr = nc.dram_tensor("hdr", [NFT, 128, NT], BF16).ap()
    qw2d = nc.dram_tensor("qw2d", [NCH, 128, NFT * 128], BF16).ap()
    hscr = nc.dram_tensor("hscr", [128, 4096], BF16).ap()
    x2lo = nc.dram_tensor("x2lo", [NT, D], BF16).ap()

    with ExitStack() as ctx:
        def sbuf(nm, shape, dt=BF16, scope=None):
            return (scope or ctx).enter_context(nc.sbuf_tensor(nm, shape, dt))

        def psum(nm):
            return ctx.enter_context(nc.psum_tensor(nm, [128, 512], F32))

        def sem(nm):
            return ctx.enter_context(nc.semaphore(nm))

        B8 = sbuf("B8", [128, NTILE * D])
        BA = sbuf("BA", [128, NCH * NT])
        BB = sbuf("BB", [128, NCH * NT])
        BC = sbuf("BC", [128, NCH * NT])
        hstg = sbuf("hstg", [128, NT], F32)
        stage = sbuf("stage", [128, 1024], F32)
        lnst = sbuf("lnst", [128, 1024])
        sb_c = {k: sbuf("s" + k, list(v.shape),
                        BF16 if v.dtype == ml_dtypes.bfloat16 else F32)
                for k, v in CONSTS.items() if k != "cDec"}
        bias_u = sbuf("bias_u", [128, NCH], F32)
        bias_v = sbuf("bias_v", [128, NCH], F32)
        bias_o = sbuf("bias_o", [128, NCH], F32)
        b3sb = sbuf("b3sb", [128, 1], F32)
        b1fb = sbuf("b1fb", [64, 1], F32)
        b2fb = sbuf("b2fb", [64, 1], F32)
        b1t = sbuf("b1t", [128, NFT], F32)
        b2t = sbuf("b2t", [128, NCH], F32)
        rw1 = sbuf("rw1", [128, NFT], F32)
        rw2 = sbuf("rw2", [128, NCH], F32)
        sw1 = sbuf("sw1", [128, NFT], F32)
        sw2 = sbuf("sw2", [128, NCH], F32)
        sxw1 = sbuf("sxw1", [128, NFT], F32)
        bq1 = sbuf("bq1", [128, NFT], F32)
        exv1 = sbuf("exv1", [128, NFT], F32)
        shw2 = sbuf("shw2", [128, NCH], F32)
        bq2 = sbuf("bq2", [128, NCH], F32)
        exv2 = sbuf("exv2", [128, NCH], F32)
        mstats = sbuf("mstats", [128, 3 * NTILE], F32)
        bnb = sbuf("bnb", [128, 24], F32)
        agg = sbuf("agg", [128, 16], F32)
        lmax = sbuf("lmax", [128, 2], F32)
        gsc = sbuf("gsc", [1, 8], F32)
        rxb = sbuf("rxb", [128, 4], F32)
        lob = sbuf("lob", [128, 1024], BF16)
        W1sb = sbuf("W1sb", [64, 64], F32)
        W2sb = sbuf("W2sb", [64, 64], F32)
        W3sb = sbuf("W3sb", [64, 128], F32)
        psA = psum("psA"); psB = psum("psB"); psC = psum("psC"); psD = psum("psD")
        psE = psum("psE"); psF = psum("psF"); psG = psum("psG"); psH = psum("psH")
        PS4 = [psA, psB, psC, psD]
        SEMNAMES = ["CC", "MT", "MV", "MS", "MG",
                    "Gx", "Gw", "Gv0", "Gv1", "Ga", "Gm", "Gh", "Gq",
                    "Lc", "Lp", "Ls", "Lq2", "Lw0", "Lw1", "Lh", "Lq",
                    "Tln", "Tzh", "Tz0", "Tz1", "Ty", "Ty2", "Trow", "Tat", "Tqx", "Tq", "Tout",
                    "O0", "O1"]
        SEMS = {nm: sem(nm) for nm in SEMNAMES}
        attn_sc = ExitStack()
        pe_sc = ExitStack()
        conv_sc = ExitStack()
        ffnA_sc = ExitStack()
        ffnB_sc = ExitStack()
        block = ctx.enter_context(nc.Block())

        n = {nm: 0 for nm in SEMNAMES}
        streams = {e: [] for e in ("tensor", "vector", "scalar", "sync", "gpsimd")}

        class _SkipRest(Exception):
            pass

        _STOP = os.environ.get("ATH_STOP_PHASE", "")

        def ckpt(name):
            if _STOP == name:
                raise _SkipRest()

        def dma(eng, sm, out, in_, slow=False):
            n[sm] += 16
            hs = SEMS[sm]
            streams[eng].append(lambda e, out=out, in_=in_, hs=hs, slow=slow:
                                e.dma_start(out=out, in_=in_,
                                            allow_slow_non_contiguous=slow).then_inc(hs, 16))
            return n[sm]

        def dmat(eng, sm, out, in_):
            n[sm] += 16
            hs = SEMS[sm]
            streams[eng].append(lambda e, out=out, in_=in_, hs=hs:
                                e.dma_start_transpose(out, in_).then_inc(hs, 16))
            return n[sm]

        def wait(eng, sm, v):
            if v <= 0:
                return
            hs = SEMS[sm]
            streams[eng].append(lambda e, hs=hs, v=v: e.wait_ge(hs, v))

        def instr(eng, f, inc=None):
            if inc is None:
                streams[eng].append(f)
                return None
            n[inc] += 1
            hs = SEMS[inc]
            streams[eng].append(lambda e, f=f, hs=hs: f(e).then_inc(hs, 1))
            return n[inc]

        BAv = BA[:, :].rearrange("p (c q) -> p c q", c=NCH)

        try:
            x_tiled = x_in.rearrange("(i p) d -> i p d", p=128)
            xld = [dma("gpsimd", "Gx", B8[:, i * D:(i + 1) * D], x_tiled[i]) for i in range(NTILE)]
            wbuf = sbuf("wbuf", [128, NCH * D], scope=attn_sc)
            wsc = sbuf("wsc", [128, 1024], F32, scope=attn_sc)
            wu_ld = dma("gpsimd", "Gw", wbuf[:, :].rearrange("p (c d) -> p c d", c=NCH),
                        Wu_d.rearrange("(c p) d -> p c d", p=128))
            cld = {k: dma("scalar", "Lc", sb_c[k][:, :], cns[k][:, :]) for k in CONSTS
                   if k != "cDec"}
            dma("scalar", "Lc", bias_u[:, :], bu_d.rearrange("(c p) -> p c", p=128), slow=True)
            dma("scalar", "Lc", bias_v[:, :], bv_d.rearrange("(c p) -> p c", p=128), slow=True)
            v_bo = dma("scalar", "Lc", bias_o[:, :], bo_d.rearrange("(c p) -> p c", p=128), slow=True)
            dma("scalar", "Lc", b3sb[:, :], b3s_d.rearrange("(p o) -> p o", o=1))
            dma("scalar", "Lc", b1fb[:, :], b1f_d.rearrange("(p o) -> p o", o=1))
            dma("scalar", "Lc", b2fb[:, :], b2f_d.rearrange("(p o) -> p o", o=1))
            dma("scalar", "Lc", b1t[:, :], b1v_d.rearrange("(t p) -> p t", p=128), slow=True)
            v_b2t = dma("scalar", "Lc", b2t[:, :], b2v_d.rearrange("(t p) -> p t", p=128), slow=True)
            dma("scalar", "Lc", W1sb[:, :], W1_d[:, :])
            dma("scalar", "Lc", W2sb[:, :], W2_d[:, :])
            lw3 = dma("scalar", "Lc", W3sb[:, :], W3s_d[:, :])

            def emit_ln_stats_pair(xtA, xtB):
                chains = []
                for (xt, c, b) in ((xtA, 0, 0), (xtB, 8, 12)):
                    chains.append([
                        lambda v, xt=xt, b=b: v.bn_stats(bnb[:, b:b + 6], xt[:, 0:512]),
                        lambda v, xt=xt, b=b: v.bn_stats(bnb[:, b + 6:b + 12], xt[:, 512:1024]),
                        lambda v, c=c, b=b: v.reduce_sum(agg[:, c:c + 1], bnb[:, b + 1:b + 12:3], axis=AX),
                        lambda v, c=c, b=b: v.tensor_tensor(agg[:, c + 2:c + 6], bnb[:, b + 1:b + 12:3], bnb[:, b + 1:b + 12:3], ALU.mult),
                        lambda v, c=c: v.reduce_sum(agg[:, c + 1:c + 2], agg[:, c + 2:c + 6], axis=AX),
                        lambda v, c=c, b=b: v.reduce_sum(agg[:, c + 6:c + 7], bnb[:, b + 2:b + 12:3], axis=AX),
                        lambda v, c=c: v.tensor_scalar(agg[:, c:c + 1], agg[:, c:c + 1], 0.25, None, ALU.mult),
                        lambda v, c=c: v.tensor_tensor(agg[:, c + 7:c + 8], agg[:, c:c + 1], agg[:, c:c + 1], ALU.mult),
                        lambda v, c=c: v.tensor_scalar(agg[:, c + 6:c + 7], agg[:, c + 6:c + 7], 1.0 / 1024.0, None, ALU.mult),
                        lambda v, c=c: v.tensor_scalar(agg[:, c + 1:c + 2], agg[:, c + 1:c + 2], 0.25, None, ALU.mult),
                        lambda v, c=c: v.tensor_tensor(agg[:, c + 6:c + 7], agg[:, c + 6:c + 7], agg[:, c + 1:c + 2], ALU.add),
                        lambda v, c=c: v.tensor_tensor(agg[:, c + 6:c + 7], agg[:, c + 6:c + 7], agg[:, c + 7:c + 8], ALU.subtract),
                        lambda v, c=c: v.tensor_scalar(agg[:, c + 6:c + 7], agg[:, c + 6:c + 7], 1e-5, None, ALU.add),
                        lambda v, c=c: v.reciprocal(agg[:, c + 1:c + 2], agg[:, c + 6:c + 7]),
                    ])
                last = None
                for k in range(len(chains[0])):
                    for t in range(2):
                        last = instr("vector", chains[t][k],
                                     inc=("MV" if k == len(chains[0]) - 1 else None))
                return last

            lnst3 = lnst[:, :].rearrange("p (c q) -> p c q", q=128)
            stg3 = lob[:, :].rearrange("p (c q) -> p c q", q=128)
            BAt4 = BA[:, :].rearrange("p (c i q) -> p c i q", c=NCH, i=NTILE)
            prev_tt = 0
            for pi in range(NTILE // 2):
                iA, iB = 2 * pi, 2 * pi + 1
                xtA = B8[:, iA * D:(iA + 1) * D]
                xtB = B8[:, iB * D:(iB + 1) * D]
                wait("vector", "Gx", xld[-1])
                if pi > 0:
                    wait("vector", "Tln", prev_tt)
                vrec = emit_ln_stats_pair(xtA, xtB)
                wait("scalar", "MV", vrec)
                instr("scalar", lambda s: s.activation(agg[:, 2:3], agg[:, 1:2], AF.Sqrt))
                vsq = instr("scalar", lambda s: s.activation(agg[:, 10:11], agg[:, 9:10], AF.Sqrt), inc="MS")
                wait("vector", "MS", vsq)
                vnA = instr("vector", lambda v, xtA=xtA: v.tensor_scalar(
                    lnst[:, :], xtA, agg[:, 0:1], agg[:, 2:3], ALU.subtract, ALU.mult), inc="MV")
                vnB = instr("vector", lambda v, xtB=xtB: v.tensor_scalar(
                    lob[:, :], xtB, agg[:, 8:9], agg[:, 10:11], ALU.subtract, ALU.mult), inc="MV")
                wait("sync", "MV", vnA)
                dmat("sync", "Tln", BAt4[:, :, iA, :], lnst3)
                wait("sync", "MV", vnB)
                prev_tt = dmat("sync", "Tln", BAt4[:, :, iB, :], stg3)
            lnT_all = prev_tt
            ckpt("p1")

            def emit_proj(evac):
                prev_ev = 0
                for m in range(NCH):
                    if prev_ev:
                        wait("tensor", evac_sem, prev_ev)
                    for c in range(NCH):
                        for s in range(4):
                            instr("tensor", lambda t, m=m, c=c, s=s: t.matmul(
                                PS4[s][:, 0:512],
                                wbuf[:, c * D + m * 128: c * D + (m + 1) * 128],
                                BAv[:, c, s * 512:(s + 1) * 512],
                                start=(c == 0), stop=(c == NCH - 1)),
                                inc=("MT" if c == NCH - 1 else None))
                    prev_ev = evac(m, n["MT"])
                return prev_ev

            wait("tensor", "Tln", lnT_all)
            wait("tensor", "Gw", wu_ld)
            wait("vector", "Lc", n["Lc"])
            evac_sem = "MV"

            def evac_u(m, mtv):
                last = None
                for s in range(4):
                    wait("vector", "MT", mtv)
                    last = instr("vector", lambda v, m=m, s=s: v.tensor_scalar(
                        BB[:, m * NT + s * 512: m * NT + (s + 1) * 512],
                        PS4[s][:, 0:512], bias_u[:, m:m + 1], None, ALU.add), inc="MV")
                return last

            emit_proj(evac_u)
            u_mm_done = n["MT"]
            u_done = n["MV"]
            ckpt("p2u")

            wait("gpsimd", "MT", u_mm_done)
            wv_ld = dma("gpsimd", "Gw", wbuf[:, :].rearrange("p (c d) -> p c d", c=NCH),
                        Wv_d.rearrange("(c p) d -> p c d", p=128))
            wait("tensor", "Gw", wv_ld)

            vst = [0, 0]

            def evac_v(m, mtv):
                for s in range(4):
                    half = s % 2
                    wait("scalar", "MT", mtv)
                    wait("scalar", "Gv0" if half == 0 else "Gv1", vst[half])
                    vv = instr("scalar", lambda sc, m=m, s=s, half=half: sc.activation(
                        stage[:, half * 512:(half + 1) * 512], PS4[s][:, 0:512], AF.Identity,
                        bias=bias_v[:, m:m + 1], scale=1.0), inc="MS")
                    wait("gpsimd", "MS", vv)
                    vst[half] = dma("gpsimd", "Gv0" if half == 0 else "Gv1",
                                    iA1[m, :, s * 512:(s + 1) * 512],
                                    stage[:, half * 512:(half + 1) * 512])
                return None

            evac_sem = "MS"

            def evac_v_wrap(m, mtv):
                evac_v(m, mtv)
                return n["MS"]

            wait("scalar", "Lc", n["Lc"])
            emit_proj(evac_v_wrap)
            v_mm_done = n["MT"]
            v_store_done = None
            ckpt("p2")

            wait("gpsimd", "Gv0", vst[0])
            wait("gpsimd", "Gv1", vst[1])
            a2a1 = instr("gpsimd", lambda g: g.collective_compute(
                "AllToAll", ALU.bypass, replica_groups=[list(range(NCORES))],
                ins=[iA1.tensor.ap().opt()], outs=[oA1.tensor.ap().opt()]), inc="CC")
            wait("gpsimd", "CC", a2a1)

            peb = sbuf("peb", [64, 2 * 512], F32, scope=pe_sc)
            decb = sbuf("decb", [1, 4096], F32, scope=pe_sc)
            dma("scalar", "Lc", decb[0:1, :], cns["cDec"][0:1, :])
            g1s = BA[:, 0:8192].bitcast(F32)
            g2s = BA[:, 8192:16384].bitcast(F32)

            wait("tensor", "Lc", n["Lc"])
            prev_ms = 0
            pe_mt = {}
            for s8 in range(8):
                if s8 >= 1:
                    wait("scalar", "MT", pe_mt[s8 - 1])
                lv = dma("scalar", "Lp", peb[0:64, (s8 % 2) * 512:(s8 % 2) * 512 + 512],
                         peT_d[:, s8 * 512:(s8 + 1) * 512])
                wait("tensor", "Lp", lv)
                if prev_ms:
                    wait("tensor", "MS", prev_ms)
                vm = instr("tensor", lambda t, s8=s8: t.matmul(
                    psE[0:64, 0:512], W1sb[0:64, :],
                    peb[0:64, (s8 % 2) * 512:(s8 % 2) * 512 + 512],
                    start=True, stop=True), inc="MT")
                pe_mt[s8] = vm
                wait("scalar", "MT", vm)
                prev_ms = instr("scalar", lambda sc, s8=s8: sc.activation(
                    g1s[0:64, s8 * 512:(s8 + 1) * 512], psE[0:64, 0:512], AF.Identity,
                    bias=b1fb[0:64, 0:1], scale=1.0), inc="MS")

            def emit_gelu(buf):
                last = None
                for hh in range(4):
                    src = buf[0:64, hh * 1024:(hh + 1) * 1024]
                    t1 = hstg[0:64, 0:1024]
                    t2 = hstg[0:64, 1024:2048]
                    vq = instr("scalar", lambda s, src=src, t1=t1: s.activation(t1, src, AF.Square), inc="MS")
                    wait("vector", "MS", vq)
                    for a, b in ((0, 512), (512, 1024)):
                        instr("vector", lambda v, t1=t1, a=a, b=b: v.tensor_scalar(
                            t1[:, a:b], t1[:, a:b], 0.044715, 1.0, ALU.mult, ALU.add))
                    vw = None
                    for a, b in ((0, 512), (512, 1024)):
                        vw = instr("vector", lambda v, t1=t1, src=src, a=a, b=b: v.tensor_tensor(
                            t1[:, a:b], t1[:, a:b], src[:, a:b], ALU.mult), inc="MV")
                    wait("scalar", "MV", vw)
                    vt = instr("scalar", lambda s, t1=t1, t2=t2: s.activation(
                        t2, t1, AF.Tanh, scale=0.7978845608028654), inc="MS")
                    wait("vector", "MS", vt)
                    for a, b in ((0, 512), (512, 1024)):
                        instr("vector", lambda v, t2=t2, a=a, b=b: v.tensor_scalar(
                            t2[:, a:b], t2[:, a:b], 1.0, 0.5, ALU.add, ALU.mult))
                    for a, b in ((0, 512), (512, 1024)):
                        last = instr("vector", lambda v, src=src, t2=t2, a=a, b=b: v.tensor_tensor(
                            src[:, a:b], t2[:, a:b], src[:, a:b], ALU.mult), inc="MV")
                return last

            wait("scalar", "MS", n["MS"])
            vg1 = emit_gelu(g1s)
            wait("tensor", "MV", vg1)
            prev_ms = 0
            for s8 in range(8):
                if prev_ms:
                    wait("tensor", "MS", prev_ms)
                vm = instr("tensor", lambda t, s8=s8: t.matmul(
                    psE[0:64, 0:512], W2sb[0:64, :], g1s[0:64, s8 * 512:(s8 + 1) * 512],
                    start=True, stop=True), inc="MT")
                wait("scalar", "MT", vm)
                prev_ms = instr("scalar", lambda sc, s8=s8: sc.activation(
                    g2s[0:64, s8 * 512:(s8 + 1) * 512], psE[0:64, 0:512], AF.Identity,
                    bias=b2fb[0:64, 0:1], scale=1.0), inc="MS")
            wait("scalar", "MS", n["MS"])
            vg2 = emit_gelu(g2s)
            wait("tensor", "MV", vg2)
            wait("vector", "Gv0", vst[0])
            wait("vector", "Gv1", vst[1])
            prev_hv = 0
            hst_prev = {}
            for s8 in range(8):
                par8 = s8 % 2
                if prev_hv:
                    wait("tensor", "MV", prev_hv)
                instr("tensor", lambda t, s8=s8: t.matmul(
                    psE[:, 0:512], W3sb[0:64, :], g2s[0:64, s8 * 512:(s8 + 1) * 512],
                    start=True, stop=True))
                vdm = instr("tensor", lambda t, s8=s8: t.matmul(
                    psF[:, 0:512], sb_c["cOnes"][0:1, :], decb[0:1, s8 * 512:(s8 + 1) * 512],
                    start=True, stop=True), inc="MT")
                wait("vector", "MT", vdm)
                if s8 >= 2:
                    wait("vector", "Gh", hst_prev[s8 - 2])
                for a, b in ((0, 256), (256, 512)):
                    instr("vector", lambda v, a=a, b=b: v.tensor_scalar(
                        stage[:, a:b], psE[:, a:b], b3sb[:, 0:1], None, ALU.add))
                for a, b in ((0, 256), (256, 512)):
                    prev_hv = instr("vector", lambda v, par8=par8, a=a, b=b: v.tensor_tensor(
                        hstg[:, par8 * 512 + a:par8 * 512 + b], stage[:, a:b], psF[:, a:b], ALU.mult), inc="MV")
                wait("gpsimd", "MV", prev_hv)
                hst_prev[s8] = dma("gpsimd", "Gh", hscr[:, s8 * 512:(s8 + 1) * 512],
                                   hstg[:, par8 * 512:(par8 + 1) * 512])
            h_done = hst_prev[7]
            ckpt("p3h")
            pe_sc.close()

            zh32 = sbuf("zh32", [64, 128 * 64], BF16, scope=conv_sc)
            zv = sbuf("zv", [64, 6 * 256], BF16, scope=conv_sc)
            at_b = sbuf("at_b", [128, 2 * 195], BF16, scope=conv_sc)
            ath32 = sbuf("ath32", [64, 195], F32, scope=conv_sc)
            hb32 = sbuf("hb32", [128, 2 * 260], F32, scope=conv_sc)
            vh32 = sbuf("vh32", [128, 2 * 260], F32, scope=conv_sc)
            tmpb = sbuf("tmpb", [128, 2 * 520], F32, scope=conv_sc)
            ypb = sbuf("ypb", [128, 2 * 260], BF16, scope=conv_sc)
            esb = sbuf("esb", [65, 2 * 1024], BF16, scope=conv_sc)
            ysb = sbuf("ysb", [64, 6 * 256], BF16, scope=conv_sc)

            wait("sync", "Gh", h_done)
            wait("sync", "MT", pe_mt[7])
            zh_all = dma("sync", "Tzh", zh32[:, :].rearrange("a (c b) -> a c b", b=64),
                         hscr.rearrange("c (a b) -> a c b", b=64))

            w1n_t = w1n_d.rearrange("(t p) d -> t p d", p=128)
            wait("scalar", "MT", pe_mt[7])
            for ft in range(NFT):
                lv = dma("scalar", "Ls", wsc[:, :], w1n_t[ft])
                wait("vector", "Ls", lv)
                vr = instr("vector", lambda v, ft=ft: v.reduce_max(
                    sw1[:, ft:ft + 1], wsc[:, :], axis=AX, apply_absolute_value=True), inc="MV")
                wait("scalar", "MV", vr)
            w2n_t = w2n_d.rearrange("(t p) f -> t p f", p=128)
            for ct in range(NCH):
                for pc in range(4):
                    lv = dma("scalar", "Ls", wsc[:, :], w2n_t[ct][:, pc * 1024:(pc + 1) * 1024])
                    wait("vector", "Ls", lv)
                    vr = instr("vector", lambda v, ct=ct, pc=pc: v.reduce_max(
                        mstats[:, 4 * ct + pc:4 * ct + pc + 1], wsc[:, :],
                        axis=AX, apply_absolute_value=True), inc="MV")
                    wait("scalar", "MV", vr)
            for ct in range(NCH):
                instr("vector", lambda v, ct=ct: v.reduce_max(
                    sw2[:, ct:ct + 1], mstats[:, 4 * ct:4 * ct + 4], axis=AX))
            instr("vector", lambda v: v.tensor_scalar(sw1[:, :], sw1[:, :], 1e-8, 1.0 / 127.0, ALU.max, ALU.mult))
            instr("vector", lambda v: v.tensor_scalar(sw2[:, :], sw2[:, :], 1e-8, 1.0 / 127.0, ALU.max, ALU.mult))
            vrw1 = instr("vector", lambda v: v.reciprocal(rw1[:, :], sw1[:, :]), inc="MV")
            vrw2 = instr("vector", lambda v: v.reciprocal(rw2[:, :], sw2[:, :]), inc="MV")

            conv_c_mt = {}
            yst = {}
            yst2 = {}
            unit_cfree = {}
            unit_dfree = {}
            dh_mt = {}
            ad_mt = {}
            pr_mg = {}
            hbfree = {}
            yv_mv = {}
            ap_mt = {}
            e_mv = {}
            b_mt = {}
            y_mv = {}
            uc = 0
            pc_ = 0
            ps_ms0 = n["MS"]
            ps_mv0 = n["MV"]
            CP = [psA, psG]
            DPb = [psB, psH]
            DPa = [psC, psE]
            wait("tensor", "Tzh", zh_all)
            wait("tensor", "MS", ps_ms0)
            wait("tensor", "MV", ps_mv0)
            wait("sync", "CC", a2a1)
            wait("scalar", "CC", a2a1)
            oA1v = oA1.rearrange("(b hi) p (ah qq) -> p hi ah b qq", hi=2, qq=64)
            iA2v = iA2.rearrange("(b hi) p (ah qq) -> p hi ah b qq", hi=2, qq=64)
            NCONV = int(os.environ.get("ATH_CONV_N", "128"))
            zld0 = {}
            zld1 = {}

            def issue_zv_loads(d):
                if d >= NCONV:
                    return
                if d >= 6:
                    wait("sync", "MT", conv_c_mt[d - 6])
                    wait("scalar", "MT", conv_c_mt[d - 6])
                zvd = zv[:, (d % 6) * 256:(d % 6 + 1) * 256].rearrange("a (b q) -> a b q", b=4)
                zld0[d] = dma("sync", "Tz0", zvd[0:32], oA1v[d, 0])
                zld1[d] = dma("scalar", "Tz1", zvd[32:64], oA1v[d, 1])

            for d in range(6):
                issue_zv_loads(d)
            for dl in range(NCONV):
                slot = dl % 2
                sl4 = dl % 4
                cb, db = CP[uc % 2], DPb[uc % 2]
                if uc >= 2:
                    wait("tensor", *unit_cfree[uc - 2])
                elif uc == 0:
                    wait("tensor", "MS", ps_ms0)
                hcm = instr("tensor", lambda t, dl=dl, cb=cb: t.matmul(
                    cb[0:64, 0:130], zh32[:, dl * 64:(dl + 1) * 64], sb_c["cR1h"][:, :],
                    start=True, stop=True), inc="MT")
                wait("scalar", "MT", hcm)
                if dl >= 1:
                    wait("scalar", "MT", dh_mt[dl - 1])
                instr("scalar", lambda s, cb=cb: s.activation(
                    ath32[0:64, 65:195], cb[0:64, 0:130], AF.Identity))
                hae = instr("scalar", lambda s, cb=cb: s.activation(
                    ath32[0:64, 0:65], cb[0:64, 65:130], AF.Identity, scale=-1.0), inc="MS")
                unit_cfree[uc] = ("MS", hae)
                if uc >= 2:
                    wait("tensor", *unit_dfree[uc - 2])
                elif uc == 0:
                    wait("tensor", "MS", ps_ms0)
                wait("tensor", "MS", hae)
                instr("tensor", lambda t, db=db: t.matmul(
                    db[:, 0:130], sb_c["cG1f"][:, :], ath32[0:64, 65:195], start=True, stop=False))
                hdm = instr("tensor", lambda t, db=db: t.matmul(
                    db[:, 0:130], sb_c["cG2f"][:, :], ath32[0:64, 0:130], start=False, stop=True), inc="MT")
                dh_mt[dl] = hdm
                wait("scalar", "MT", hdm)
                if dl >= 2:
                    wait("scalar", "MG", hbfree[dl - 2])
                instr("scalar", lambda s, db=db, slot=slot: s.activation(
                    hb32[:, slot * 260:slot * 260 + 130], db[:, 0:130], AF.Identity))
                hEv = instr("scalar", lambda s, db=db, slot=slot: s.activation(
                    hb32[:, slot * 260 + 130:slot * 260 + 260], db[:, 0:130], AF.Identity), inc="MS")
                unit_dfree[uc] = ("MS", hEv)
                uc += 1
                for pp in range(2):
                    ps_ = pc_ % 2
                    cb, db, da = CP[uc % 2], DPb[uc % 2], DPa[uc % 2]
                    if uc >= 2:
                        wait("tensor", *unit_cfree[uc - 2])
                    wait("tensor", "Tz0", zld0[dl])
                    wait("tensor", "Tz1", zld1[dl])
                    sl8 = dl % 6
                    cm = instr("tensor", lambda t, cb=cb, sl8=sl8, pp=pp: t.matmul(
                        cb[0:128, 0:130], zv[:, sl8 * 256 + pp * 128:sl8 * 256 + (pp + 1) * 128],
                        sb_c["cR1h"][:, :], start=True, stop=True), inc="MT")
                    if pp == 1:
                        conv_c_mt[dl] = cm
                        issue_zv_loads(dl + 6)
                    wait("vector", "MT", cm)
                    if pc_ >= 2:
                        wait("vector", "MT", ad_mt[pc_ - 2])
                    instr("vector", lambda v, cb=cb, ps_=ps_: v.tensor_copy(
                        at_b[:, ps_ * 195 + 65:ps_ * 195 + 195], cb[:, 0:130]))
                    ae = instr("vector", lambda v, cb=cb, ps_=ps_: v.tensor_scalar(
                        at_b[:, ps_ * 195:ps_ * 195 + 65], cb[:, 65:130], -1.0, None, ALU.mult), inc="MV")
                    unit_cfree[uc] = ("MV", ae)
                    if uc >= 2:
                        wait("tensor", *unit_dfree[uc - 2])
                    wait("tensor", "MV", ae)
                    dm = None
                    for sg, pb in ((1, db), (0, da)):
                        instr("tensor", lambda t, pb=pb, ps_=ps_, sg=sg: t.matmul(
                            pb[:, 0:130], sb_c["cG1d"][sg * 64:sg * 64 + 64, :],
                            at_b[sg * 64:sg * 64 + 64, ps_ * 195 + 65:ps_ * 195 + 195],
                            start=True, stop=False))
                        dm = instr("tensor", lambda t, pb=pb, ps_=ps_, sg=sg: t.matmul(
                            pb[:, 0:130], sb_c["cG2d"][sg * 64:sg * 64 + 64, :],
                            at_b[sg * 64:sg * 64 + 64, ps_ * 195:ps_ * 195 + 130],
                            start=False, stop=True), inc="MT")
                    ad_mt[pc_] = dm
                    wait("scalar", "MT", dm)
                    if pc_ >= 2:
                        wait("scalar", "MG", pr_mg[pc_ - 2])
                    instr("scalar", lambda s, db=db, ps_=ps_: s.activation(
                        vh32[:, ps_ * 260:ps_ * 260 + 130], db[:, 0:130], AF.Identity))
                    vEv = instr("scalar", lambda s, da=da, ps_=ps_: s.activation(
                        vh32[:, ps_ * 260 + 130:ps_ * 260 + 260], da[:, 0:130], AF.Identity), inc="MS")
                    unit_dfree[uc] = ("MS", vEv)
                    uc += 1
                    wait("gpsimd", "MS", max(vEv, hEv))
                    if pc_ >= 2:
                        wait("gpsimd", "MV", yv_mv[pc_ - 2])
                    vv4 = vh32[:, ps_ * 260:(ps_ + 1) * 260].rearrange("p (s t c) -> p s t c", s=2, t=2)
                    hb4 = hb32[:, slot * 260:(slot + 1) * 260].rearrange("p (s t c) -> p s t c", s=2, t=2)
                    tp4 = tmpb[:, ps_ * 520:(ps_ + 1) * 520].rearrange("p (q s c) -> p q s c", q=4, s=2)
                    instr("gpsimd", lambda g, vv4=vv4, hb4=hb4, tp4=tp4: g.tensor_tensor(
                        tp4[:, 0], vv4[:, :, 0], hb4[:, :, 0], ALU.mult))
                    instr("gpsimd", lambda g, vv4=vv4, hb4=hb4, tp4=tp4: g.tensor_tensor(
                        tp4[:, 1], vv4[:, :, 1], hb4[:, :, 1], ALU.mult))
                    instr("gpsimd", lambda g, vv4=vv4, hb4=hb4, tp4=tp4: g.tensor_tensor(
                        tp4[:, 2], vv4[:, :, 0], hb4[:, :, 1], ALU.mult))
                    pg = instr("gpsimd", lambda g, vv4=vv4, hb4=hb4, tp4=tp4: g.tensor_tensor(
                        tp4[:, 3], vv4[:, :, 1], hb4[:, :, 0], ALU.mult), inc="MG")
                    pr_mg[pc_] = pg
                    if pp == 1:
                        hbfree[dl] = pg
                    wait("vector", "MG", pg)
                    if pc_ >= 2:
                        wait("vector", "MT", ap_mt[pc_ - 2])
                    yp4 = ypb[:, ps_ * 260:(ps_ + 1) * 260].rearrange("p (s t c) -> p s t c", s=2, t=2)
                    instr("vector", lambda v, yp4=yp4, tp4=tp4: v.tensor_tensor(
                        yp4[:, :, 0], tp4[:, 0], tp4[:, 1], ALU.subtract))
                    ym = instr("vector", lambda v, yp4=yp4, tp4=tp4: v.tensor_tensor(
                        yp4[:, :, 1], tp4[:, 2], tp4[:, 3], ALU.add), inc="MV")
                    yv_mv[pc_] = ym
                    ab = psD
                    if pc_ >= 1:
                        wait("tensor", "MV", e_mv[pc_ - 1])
                    else:
                        wait("tensor", "MS", ps_ms0)
                    wait("tensor", "MV", ym)
                    am = None
                    for sg in range(2):
                        base = ps_ * 260 + (1 - sg) * 130
                        instr("tensor", lambda t, ab=ab, sg=sg, base=base: t.matmul(
                            ab[0:65, sg * 256:(sg + 1) * 256], ypb[:, base:base + 65],
                            sb_c["cR3"][:, :], start=True, stop=False))
                        am = instr("tensor", lambda t, ab=ab, sg=sg, base=base: t.matmul(
                            ab[0:65, sg * 256:(sg + 1) * 256], ypb[:, base + 65:base + 130],
                            sb_c["cR4h"][:, :], start=False, stop=True), inc="MT")
                    ap_mt[pc_] = am
                    wait("vector", "MT", am)
                    if dl >= 2:
                        wait("vector", "MT", b_mt[dl - 2])
                    ee = instr("vector", lambda v, ab=ab, slot=slot, pp=pp: v.tensor_copy(
                        esb[:, slot * 1024 + pp * 512:slot * 1024 + (pp + 1) * 512],
                        ab[0:65, 0:512]), inc="MV")
                    e_mv[pc_] = ee
                    pc_ += 1
                bb = psF
                if dl >= 1:
                    wait("tensor", "MV", y_mv[dl - 1])
                else:
                    wait("tensor", "MV", ps_mv0)
                wait("tensor", "MV", e_mv[pc_ - 1])
                es4 = esb[:, slot * 1024:(slot + 1) * 1024].rearrange("p (u j q) -> p u j q", u=4, j=4)
                instr("tensor", lambda t, bb=bb, es4=es4: t.matmul(
                    bb[0:64, 0:256], sb_c["cCO2"][:, :], es4[:, :, 0], start=True, stop=False))
                instr("tensor", lambda t, bb=bb, es4=es4: t.matmul(
                    bb[0:64, 0:256], sb_c["cSI2n"][:, :], es4[:, :, 2], start=False, stop=False))
                instr("tensor", lambda t, bb=bb, es4=es4: t.matmul(
                    bb[0:64, 0:256], sb_c["cCO2s"][:, :], es4[:, :, 1], start=False, stop=False))
                bm = instr("tensor", lambda t, bb=bb, es4=es4: t.matmul(
                    bb[0:64, 0:256], sb_c["cSI2ns"][:, :], es4[:, :, 3], start=False, stop=True), inc="MT")
                b_mt[dl] = bm
                wait("vector", "MT", bm)
                sl8 = dl % 6
                if dl >= 6:
                    wait("vector", "Ty", yst[dl - 6])
                    wait("vector", "Ty2", yst2[dl - 6])
                ye = instr("vector", lambda v, bb=bb, sl8=sl8: v.tensor_copy(
                    ysb[:, sl8 * 256:(sl8 + 1) * 256], bb[0:64, 0:256]), inc="MV")
                y_mv[dl] = ye
                wait("sync", "MV", ye)
                wait("gpsimd", "MV", ye)
                ysv = ysb[:, sl8 * 256:(sl8 + 1) * 256].rearrange("a (b q) -> a b q", b=4)
                yst[dl] = dma("sync", "Ty", iA2v[dl, 0], ysv[0:32])
                yst2[dl] = dma("gpsimd", "Ty2", iA2v[dl, 1], ysv[32:64])
            conv_out_T = yst[max(yst)] if yst else 0
            conv_out_T2 = yst2[max(yst2)] if yst2 else 0
            ckpt("p5")

            wait("gpsimd", "Ty", conv_out_T)
            wait("gpsimd", "Ty2", conv_out_T2)
            a2a2 = instr("gpsimd", lambda g: g.collective_compute(
                "AllToAll", ALU.bypass, replica_groups=[list(range(NCORES))],
                ins=[iA2.tensor.ap().opt()], outs=[oA2.tensor.ap().opt()]), inc="CC")
            wait("gpsimd", "CC", a2a2)
            vcl = dma("gpsimd", "Ga", BC[:, :].rearrange("p (j q) -> p j q", j=NCORES),
                      oA2[:, :, :].rearrange("j p q -> p j q"))

            wait("vector", "Ga", vcl)
            wait("vector", "MV", u_done)
            ypr = instr("vector", lambda v: v.tensor_tensor(BA[:, :], BB[:, :], BC[:, :], ALU.mult), inc="MV")
            wait("gpsimd", "MT", v_mm_done)
            wo_ld = dma("gpsimd", "Gw", wbuf[:, :].rearrange("p (c d) -> p c d", c=NCH),
                        Wo_d.rearrange("(c p) d -> p c d", p=128))
            wait("tensor", "Gw", wo_ld)
            wait("tensor", "MV", ypr)
            evac_sem = "MV"

            BBtile = BB[:, :].rearrange("p (i c q) -> p i c q", i=NTILE, c=NCH)

            def evac_o(m, mtv):
                last = None
                for s in range(4):
                    wait("vector", "MT", mtv)
                    last = instr("vector", lambda v, m=m, s=s: v.tensor_scalar(
                        BBtile[:, 4 * s:4 * s + 4, m, :],
                        PS4[s][:, 0:512], bias_o[:, m:m + 1], None, ALU.add), inc="MV")
                return last

            emit_proj(evac_o)

            res_add = []
            x2lo_t = x2lo.rearrange("(i p) d -> i p d", p=128)
            x2st = {}
            mv0_p7 = n["MV"]
            for i in range(NTILE):
                tb = BC[:, (i % 2) * 1024:(i % 2) * 1024 + 1024]
                lo = lnst if i % 2 == 0 else lob
                wait("sync", "MV", res_add[i - 2] if i >= 2 else mv0_p7)
                src2 = BB[:, i * D:(i + 1) * D]
                tt = dmat("sync", "Tat", tb.rearrange("t (c p) -> t c p", p=128), src2)
                wait("vector", "Tat", tt)
                if i >= 2:
                    wait("vector", "Ls", x2st[i - 2])
                for a, b in ((0, 512), (512, 1024)):
                    instr("vector", lambda v, i=i, a=a, b=b, tb=tb: v.tensor_tensor(
                        stage[:, a:b], B8[:, i * D + a:i * D + b], tb[:, a:b], ALU.add))
                for a, b in ((0, 512), (512, 1024)):
                    instr("vector", lambda v, i=i, a=a, b=b: v.tensor_copy(
                        B8[:, i * D + a:i * D + b], stage[:, a:b]))
                ra = None
                for a, b in ((0, 512), (512, 1024)):
                    ra = instr("vector", lambda v, i=i, a=a, b=b, lo=lo: v.tensor_tensor(
                        lo[:, a:b], stage[:, a:b], B8[:, i * D + a:i * D + b], ALU.subtract), inc="MV")
                res_add.append(ra)
                wait("scalar", "MV", res_add[-1])
                x2st[i] = dma("scalar", "Ls", x2lo_t[i], lo[:, :])

            for pi in range(NTILE // 2):
                iA, iB = 2 * pi, 2 * pi + 1
                xtA = B8[:, iA * D:(iA + 1) * D]
                xtB = B8[:, iB * D:(iB + 1) * D]
                wait("vector", "MV", res_add[iB])
                vrec = emit_ln_stats_pair(xtA, xtB)
                instr("vector", lambda v, iA=iA: v.tensor_copy(mstats[:, 2 * iA:2 * iA + 1], agg[:, 0:1]))
                instr("vector", lambda v, iB=iB: v.tensor_copy(mstats[:, 2 * iB:2 * iB + 1], agg[:, 8:9]))
                wait("scalar", "MV", vrec)
                instr("scalar", lambda s, iA=iA: s.activation(
                    mstats[:, 2 * iA + 1:2 * iA + 2], agg[:, 1:2], AF.Sqrt))
                vsq = instr("scalar", lambda s, iB=iB: s.activation(
                    mstats[:, 2 * iB + 1:2 * iB + 2], agg[:, 9:10], AF.Sqrt), inc="MS")
                wait("vector", "MS", vsq)
                instr("vector", lambda v, xtA=xtA, iA=iA: v.tensor_scalar(
                    stage[:, 0:1024], xtA, mstats[:, 2 * iA:2 * iA + 1], mstats[:, 2 * iA + 1:2 * iA + 2],
                    ALU.subtract, ALU.mult))
                instr("vector", lambda v, xtB=xtB, iB=iB: v.tensor_scalar(
                    hstg[:, 0:1024], xtB, mstats[:, 2 * iB:2 * iB + 1], mstats[:, 2 * iB + 1:2 * iB + 2],
                    ALU.subtract, ALU.mult))
                instr("vector", lambda v, iA=iA: v.reduce_max(
                    mstats[:, 32 + iA:33 + iA], stage[:, 0:1024], axis=AX))
                instr("vector", lambda v, iB=iB: v.reduce_max(
                    mstats[:, 32 + iB:33 + iB], hstg[:, 0:1024], axis=AX), inc="MV")
            wait("vector", "MV", n["MV"])
            instr("vector", lambda v: v.reduce_max(lmax[:, 0:1], mstats[:, 32:48], axis=AX), inc="MV")
            wait("tensor", "MV", n["MV"])
            vtm = instr("tensor", lambda t: t.matmul(
                psE[0:1, 0:128], lmax[:, 0:1], sb_c["cI128"][:, :], start=True, stop=True), inc="MT")
            wait("vector", "MT", vtm)
            vgm = instr("vector", lambda v: v.reduce_max(gsc[0:1, 0:1], psE[0:1, 0:128], axis=AX), inc="MV")
            wait("gpsimd", "MV", vgm)
            mx1 = dma("gpsimd", "Gm", mxi1[:, :], gsc[0:1, 0:1])
            wait("gpsimd", "Gm", mx1)
            ar1 = instr("gpsimd", lambda g: g.collective_compute(
                "AllReduce", ALU.max, replica_groups=[list(range(NCORES))],
                ins=[mxi1.tensor.ap().opt()], outs=[mxo1.tensor.ap().opt()]), inc="CC")
            wait("gpsimd", "CC", ar1)
            mx1b = dma("gpsimd", "Gm", gsc[0:1, 1:2], mxo1[:, :])
            wait("vector", "Gm", mx1b)
            vv = instr("vector", lambda v: v.tensor_scalar(gsc[0:1, 1:2], gsc[0:1, 1:2], 1e-8, None, ALU.max), inc="MV")
            wait("vector", "MV", vv)
            vv = instr("vector", lambda v: v.reciprocal(gsc[0:1, 2:3], gsc[0:1, 1:2]), inc="MV")
            wait("vector", "MV", vv)
            vv = instr("vector", lambda v: v.tensor_scalar(gsc[0:1, 2:3], gsc[0:1, 2:3], 255.0, None, ALU.mult), inc="MV")
            wait("vector", "MV", vv)
            vsx = instr("vector", lambda v: v.tensor_scalar(gsc[0:1, 3:4], gsc[0:1, 1:2], 1.0 / 255.0, None, ALU.mult), inc="MV")
            wait("tensor", "MV", vsx)
            wait("tensor", "MS", n["MS"])
            vbb = instr("tensor", lambda t: t.matmul(
                psE[:, 0:2], sb_c["cOnes"][0:1, :], gsc[0:1, 2:4], start=True, stop=True), inc="MT")
            wait("vector", "MT", vbb)
            vrx = instr("vector", lambda v: v.tensor_copy(rxb[:, 0:2], psE[:, 0:2]), inc="MV")
            wait("vector", "MV", max(vrw1, vrx))
            H1 = ((0, 16), (16, 32))
            for a, b in H1:
                instr("vector", lambda v, a=a, b=b: v.tensor_scalar(
                    sxw1[:, a:b], sw1[:, a:b], rxb[:, 1:2], None, ALU.mult))
            for a, b in H1:
                instr("vector", lambda v, a=a, b=b: v.tensor_tensor(
                    bq1[:, a:b], b1t[:, a:b], rw1[:, a:b], ALU.mult))
            for a, b in H1:
                instr("vector", lambda v, a=a, b=b: v.tensor_scalar(
                    bq1[:, a:b], bq1[:, a:b], rxb[:, 0:1], None, ALU.mult))
            for a, b in H1:
                instr("vector", lambda v, a=a, b=b: v.tensor_scalar(
                    bq1[:, a:b], bq1[:, a:b], MAGIC, MAGIC, ALU.add, ALU.subtract))
            vex = None
            for a, b in H1:
                vex = instr("vector", lambda v, a=a, b=b: v.tensor_tensor(
                    exv1[:, a:b], sxw1[:, a:b], bq1[:, a:b], ALU.mult), inc="MV")
            conv_sc.close()
            attn_sc.close()
            rwb1 = sbuf("rwb1", [128, FF], BF16, scope=ffnA_sc)
            rwb2 = sbuf("rwb2", [128, D], BF16, scope=ffnA_sc)
            wf1 = sbuf("wf1", [128, 4 * 1024], F32, scope=ffnA_sc)
            qw1 = sbuf("qw1", [128, 4 * 1024], BF16, scope=ffnA_sc)
            wp2 = sbuf("wp2", [128, 1024], F32, scope=ffnA_sc)
            qp2 = sbuf("qp2", [128, 1024], BF16, scope=ffnA_sc)
            hst2 = sbuf("hst2", [128, 2048], F32, scope=ffnA_sc)

            wait("gpsimd", "MV", vex)
            wait("scalar", "MV", vex)

            def emit_bcast(rwt, nrows, dst, vgate):
                wait("sync", "MT", n["MT"])
                wait("tensor", "MV", vgate)
                vtr = instr("tensor", lambda t, rwt=rwt, nrows=nrows: t.matmul(
                    psE[0:nrows, 0:128], rwt[:, :], sb_c["cI128"][:, :], start=True, stop=True), inc="MT")
                wait("vector", "MT", vtr)
                vcp = instr("vector", lambda v, rwt=rwt, nrows=nrows: v.tensor_copy(
                    stage[0:nrows, 0:128], psE[0:nrows, 0:128]), inc="MV")
                out = None
                prev = 0
                for h0 in range(0, nrows, 16):
                    rr = min(16, nrows - h0)
                    wait("sync", "MV", max(vcp, prev))
                    row = dma("sync", "Trow",
                              wf1[0:1, 0:rr * 128].rearrange("o (a b) -> o a b", b=128),
                              stage[h0:h0 + rr, 0:128])
                    wait("tensor", "Trow", row)
                    for sI in range((rr * 128) // 512):
                        if prev:
                            wait("tensor", "MV", prev)
                        vbm = instr("tensor", lambda t, sI=sI: t.matmul(
                            psE[:, 0:512], sb_c["cOnes"][0:1, :], wf1[0:1, sI * 512:(sI + 1) * 512],
                            start=True, stop=True), inc="MT")
                        wait("vector", "MT", vbm)
                        out = instr("vector", lambda v, sI=sI, dst=dst, h0=h0: v.tensor_copy(
                            dst[:, h0 * 128 + sI * 512: h0 * 128 + (sI + 1) * 512], psE[:, 0:512]), inc="MV")
                        prev = out
                return out

            wait("sync", "Tat", n["Tat"])
            wait("sync", "Tln", n["Tln"])
            bc1 = emit_bcast(rw1, NFT, rwb1, max(vrw1, vex))
            ckpt("p8a")
            bc2 = emit_bcast(rw2, NCH, rwb2, max(vrw2, bc1))
            ckpt("p8b")

            HS = ((0, 512), (512, 1024))
            qtt = {}
            wait("vector", "Trow", n["Trow"])
            wait("vector", "Ls", n["Ls"])
            wait("sync", "Trow", n["Trow"])
            for i in range(NTILE):
                xt = B8[:, i * D:(i + 1) * D]
                buf = lnst if i % 2 == 0 else lob
                wait("vector", "MV", vrx)
                if i >= 2:
                    wait("vector", "Tqx", qtt[i - 2])
                for a, b in HS:
                    instr("vector", lambda v, xt=xt, i=i, a=a, b=b: v.tensor_scalar(
                        stage[:, a:b], xt[:, a:b], mstats[:, 2 * i:2 * i + 1], mstats[:, 2 * i + 1:2 * i + 2],
                        ALU.subtract, ALU.mult))
                for a, b in HS:
                    instr("vector", lambda v, a=a, b=b: v.tensor_scalar(
                        stage[:, a:b], stage[:, a:b], rxb[:, 0:1], MAGIC, ALU.mult, ALU.add))
                for a, b in HS:
                    instr("vector", lambda v, a=a, b=b: v.tensor_scalar(
                        stage[:, a:b], stage[:, a:b], MAGIC, 0.0, ALU.subtract, ALU.max))
                vq = None
                for a, b in HS:
                    vq = instr("vector", lambda v, buf=buf, a=a, b=b: v.tensor_scalar(
                        buf[:, a:b], stage[:, a:b], 255.0, None, ALU.min), inc="MV")
                wait("sync", "MV", vq)
                src3 = buf[:, :].rearrange("p (c q) -> p c q", q=128)
                dst3 = BA[:, :].rearrange("p (c i q) -> p c i q", c=NCH, i=NTILE)[:, :, i, :]
                qtt[i] = dmat("sync", "Tqx", dst3, src3)
            qxT_done = qtt[NTILE - 1]
            ckpt("p8")

            w2T_t = w2T_d.rearrange("(f p) (c q) -> f p c q", p=128, q=128)
            pq_vv = 0
            pq_st = 0
            for fc in range(NFT):
                if pq_st:
                    wait("gpsimd", "Gq", pq_st)
                lv = dma("gpsimd", "Lq2", wp2[:, :].rearrange("p (c q) -> p c q", q=128), w2T_t[fc])
                wait("gpsimd", "Lq2", lv)
                wait("gpsimd", "MV", max(bc2, pq_vv))
                vgq = instr("gpsimd", lambda g: g.tensor_tensor(wp2[:, :], wp2[:, :], rwb2[:, :], ALU.mult), inc="MG")
                wait("vector", "MG", vgq)
                pq_vv = instr("vector", lambda v: v.tensor_scalar(
                    qp2[:, :], wp2[:, :], MAGIC, MAGIC, ALU.add, ALU.subtract), inc="MV")
                wait("gpsimd", "MV", pq_vv)
                pq_st = dma("gpsimd", "Gq",
                            qw2d.rearrange("m p (f q) -> p m f q", q=128)[:, :, fc],
                            qp2[:, :].rearrange("p (m q) -> p m q", q=128))
            w2q_done = pq_st
            ckpt("p8q")

            w1T_t = w1T_d.rearrange("(c p) (t q) -> t p c q", p=128, q=128)
            hst_store = {}
            ff1_vq = {}
            ff1_mg = {}
            ff1_ms = {}
            ff1_mmlast = {}
            PS8B = [psE, psF, psG, psH]
            HSL = [hstg, hst2]
            for ft in range(NFT):
                par = ft % 2
                sl4 = ft % 4
                if ft == 0:
                    wait("scalar", "MV", bc2)
                if ft >= 4:
                    wait("scalar", "MG", ff1_mg[ft - 4])
                lv = dma("scalar", "Lw0" if par == 0 else "Lw1",
                         wf1[:, sl4 * 1024:(sl4 + 1) * 1024].rearrange("p (c q) -> p c q", q=128),
                         w1T_t[ft])
                wait("gpsimd", "Lw0" if par == 0 else "Lw1", lv)
                wait("gpsimd", "MV", bc1)
                if ft >= 4:
                    wait("gpsimd", "MV", ff1_vq[ft - 4])
                for c8 in range(NCH):
                    instr("gpsimd", lambda g, sl4=sl4, c8=c8, ft=ft: g.tensor_tensor(
                        wf1[:, sl4 * 1024 + c8 * 128: sl4 * 1024 + (c8 + 1) * 128],
                        wf1[:, sl4 * 1024 + c8 * 128: sl4 * 1024 + (c8 + 1) * 128],
                        rwb1[:, ft * 128:(ft + 1) * 128], ALU.mult),
                        inc=("MG" if c8 == NCH - 1 else None))
                ff1_mg[ft] = n["MG"]
                wait("vector", "MG", n["MG"])
                if ft >= 4:
                    wait("vector", "MT", ff1_mmlast[ft - 4])
                vq1 = instr("vector", lambda v, sl4=sl4: v.tensor_scalar(
                    qw1[:, sl4 * 1024:(sl4 + 1) * 1024], wf1[:, sl4 * 1024:(sl4 + 1) * 1024],
                    MAGIC, MAGIC, ALU.add, ALU.subtract), inc="MV")
                ff1_vq[ft] = vq1
                wait("tensor", "MV", vq1)
                if ft == 0:
                    wait("tensor", "Tqx", qxT_done)
                if ft >= 2:
                    wait("tensor", "MS", ff1_ms[ft - 2])
                PSET = PS4 if par == 0 else PS8B
                for c8 in range(NCH):
                    for s in range(4):
                        instr("tensor", lambda t, sl4=sl4, c8=c8, s=s, PSET=PSET: t.matmul(
                            PSET[s][:, 0:512], qw1[:, sl4 * 1024 + c8 * 128: sl4 * 1024 + (c8 + 1) * 128],
                            BAv[:, c8, s * 512:(s + 1) * 512],
                            start=(c8 == 0), stop=(c8 == NCH - 1)),
                            inc=("MT" if c8 == NCH - 1 else None))
                ff1_mmlast[ft] = n["MT"]
                mtv = n["MT"]
                if ft >= 2:
                    wait("scalar", "Gh", hst_store[ft - 2])
                hsl = HSL[par]
                for s in range(4):
                    wait("scalar", "MT", mtv)
                    instr("scalar", lambda sc, ft=ft, s=s, PSET=PSET, hsl=hsl: sc.activation(
                        hsl[:, s * 512:(s + 1) * 512],
                        PSET[s][:, 0:512], AF.Relu, bias=exv1[:, ft:ft + 1], scale=sxw1[:, ft:ft + 1]),
                        inc=("MS" if s == 3 else None))
                ff1_ms[ft] = n["MS"]
                wait("vector", "MS", n["MS"])
                vmx = instr("vector", lambda v, ft=ft, hsl=hsl: v.reduce_max(
                    mstats[:, ft:ft + 1], hsl[:, 0:NT], axis=AX), inc="MV")
                wait("gpsimd", "MV", vmx)
                hst_store[ft] = dma("gpsimd", "Gh", hdr[ft], hsl[:, 0:NT])

            wait("tensor", "MS", n["MS"])
            wait("vector", "MV", n["MV"])
            instr("vector", lambda v: v.reduce_max(lmax[:, 1:2], mstats[:, 0:32], axis=AX), inc="MV")
            wait("tensor", "MV", n["MV"])
            vtm2 = instr("tensor", lambda t: t.matmul(
                psE[0:1, 0:128], lmax[:, 1:2], sb_c["cI128"][:, :], start=True, stop=True), inc="MT")
            wait("vector", "MT", vtm2)
            vgm2 = instr("vector", lambda v: v.reduce_max(gsc[0:1, 4:5], psE[0:1, 0:128], axis=AX), inc="MV")
            wait("gpsimd", "MV", vgm2)
            mx2 = dma("gpsimd", "Gm", mxi2[:, :], gsc[0:1, 4:5])
            wait("gpsimd", "Gm", mx2)
            ar2 = instr("gpsimd", lambda g: g.collective_compute(
                "AllReduce", ALU.max, replica_groups=[list(range(NCORES))],
                ins=[mxi2.tensor.ap().opt()], outs=[mxo2.tensor.ap().opt()]), inc="CC")
            wait("gpsimd", "CC", ar2)
            mx2b = dma("gpsimd", "Gm", gsc[0:1, 5:6], mxo2[:, :])
            wait("vector", "Gm", mx2b)
            vv = instr("vector", lambda v: v.tensor_scalar(gsc[0:1, 5:6], gsc[0:1, 5:6], 1e-8, None, ALU.max), inc="MV")
            wait("vector", "MV", vv)
            vv = instr("vector", lambda v: v.reciprocal(gsc[0:1, 6:7], gsc[0:1, 5:6]), inc="MV")
            wait("vector", "MV", vv)
            vv = instr("vector", lambda v: v.tensor_scalar(gsc[0:1, 6:7], gsc[0:1, 6:7], 255.0, None, ALU.mult), inc="MV")
            wait("vector", "MV", vv)
            vsh = instr("vector", lambda v: v.tensor_scalar(gsc[0:1, 7:8], gsc[0:1, 5:6], 1.0 / 255.0, None, ALU.mult), inc="MV")
            wait("tensor", "MV", vsh)
            vbb2 = instr("tensor", lambda t: t.matmul(
                psE[:, 0:2], sb_c["cOnes"][0:1, :], gsc[0:1, 6:8], start=True, stop=True), inc="MT")
            wait("vector", "MT", vbb2)
            vrh = instr("vector", lambda v: v.tensor_copy(rxb[:, 2:4], psE[:, 0:2]), inc="MV")
            wait("vector", "Lc", n["Lc"])
            wait("vector", "MV", vrh)
            H2 = ((0, 4), (4, 8))
            for a, b in H2:
                instr("vector", lambda v, a=a, b=b: v.tensor_scalar(
                    shw2[:, a:b], sw2[:, a:b], rxb[:, 3:4], None, ALU.mult))
            for a, b in H2:
                instr("vector", lambda v, a=a, b=b: v.tensor_tensor(
                    bq2[:, a:b], b2t[:, a:b], rw2[:, a:b], ALU.mult))
            for a, b in H2:
                instr("vector", lambda v, a=a, b=b: v.tensor_scalar(
                    bq2[:, a:b], bq2[:, a:b], rxb[:, 2:3], None, ALU.mult))
            for a, b in H2:
                instr("vector", lambda v, a=a, b=b: v.tensor_scalar(
                    bq2[:, a:b], bq2[:, a:b], MAGIC, MAGIC, ALU.add, ALU.subtract))
            vex2 = None
            for a, b in H2:
                vex2 = instr("vector", lambda v, a=a, b=b: v.tensor_tensor(
                    exv2[:, a:b], shw2[:, a:b], bq2[:, a:b], ALU.mult), inc="MV")
            ckpt("p9")

            ffnA_sc.close()
            qw2m = sbuf("qw2m", [128, 3 * 4096], BF16, scope=ffnB_sc)
            hbf = sbuf("hbf", [128, 2 * 2048], BF16, scope=ffnB_sc)

            wait("sync", "MV", vex2)
            wait("scalar", "MV", vex2)
            wait("gpsimd", "MV", vex2)
            wait("scalar", "Gh", n["Gh"])
            wait("sync", "Gh", n["Gh"])
            wait("gpsimd", "Gh", n["Gh"])

            qh = BA
            wait("vector", "MT", ff1_mmlast[NFT - 1])
            ffn_ev = []
            qhl = 0
            qhl_hist = []
            qw_mt = {}
            qwld = {}
            s2_mt = {}
            for s2 in range(8):
                qs = s2 % 2
                if s2 >= 2:
                    wait("vector", "MT", s2_mt[s2 - 2])
                for piece in range(4):
                    hs2 = (s2 * 4 + piece) % 2
                    if len(qhl_hist) >= 2:
                        wait("scalar", "MV", qhl_hist[-2])
                    lv = dma("scalar", "Lh",
                             hbf[:, hs2 * 2048:(hs2 + 1) * 2048].rearrange("p (c q) -> p c q", q=256),
                             hdr[piece * 8:(piece + 1) * 8, :, s2 * 256:(s2 + 1) * 256]
                             .rearrange("f p q -> p f q"))
                    wait("vector", "Lh", lv)
                    wait("vector", "MV", vrh)
                    instr("vector", lambda v, hs2=hs2: v.tensor_scalar(
                        stage[:, 0:1024], hbf[:, hs2 * 2048:hs2 * 2048 + 1024],
                        rxb[:, 2:3], MAGIC, ALU.mult, ALU.add))
                    instr("vector", lambda v, hs2=hs2: v.tensor_scalar(
                        hstg[:, 0:1024], hbf[:, hs2 * 2048 + 1024:hs2 * 2048 + 2048],
                        rxb[:, 2:3], MAGIC, ALU.mult, ALU.add))
                    instr("vector", lambda v, qs=qs, piece=piece: v.tensor_scalar(
                        qh[:, qs * 8192 + piece * 2048: qs * 8192 + piece * 2048 + 1024],
                        stage[:, 0:1024], MAGIC, None, ALU.subtract))
                    qhl = instr("vector", lambda v, qs=qs, piece=piece: v.tensor_scalar(
                        qh[:, qs * 8192 + piece * 2048 + 1024: qs * 8192 + piece * 2048 + 2048],
                        hstg[:, 0:1024], MAGIC, None, ALU.subtract), inc="MV")
                    qhl_hist.append(qhl)
                qh_ready = qhl
                for m in range(NCH):
                    ck = s2 * NCH + m
                    sl3 = ck % 3
                    qeng, qsm = (("sync", "Tq"), ("scalar", "Lq"), ("gpsimd", "Gq"))[sl3]
                    wait(qeng, "Gq", w2q_done)
                    key = ck
                    if ck >= 3:
                        wait(qeng, "MT", qw_mt[ck - 3])
                    qwld[key] = dma(qeng, qsm,
                                    qw2m[:, sl3 * 4096:(sl3 + 1) * 4096], qw2d[m])
                    wait("tensor", qsm, qwld[key])
                    wait("tensor", "MV", qh_ready)
                    pp = psE if ck % 2 == 0 else psF
                    if len(ffn_ev) >= 2:
                        wait("tensor", "MV", ffn_ev[-2])
                    for fc in range(NFT):
                        instr("tensor", lambda t, sl3=sl3, fc=fc, pp=pp, qs=qs: t.matmul(
                            pp[:, 0:256], qw2m[:, sl3 * 4096 + fc * 128: sl3 * 4096 + (fc + 1) * 128],
                            qh[:, qs * 8192 + fc * 256:qs * 8192 + (fc + 1) * 256],
                            start=(fc == 0), stop=(fc == NFT - 1)),
                            inc=("MT" if fc == NFT - 1 else None))
                    qw_mt[key] = n["MT"]
                    wait("vector", "MT", n["MT"])
                    wait("vector", "MV", vex2)
                    ffn_ev.append(instr("vector", lambda v, m=m, s2=s2, pp=pp: v.tensor_scalar(
                        BBtile[:, 2 * s2:2 * s2 + 2, m, :], pp[:, 0:256],
                        bq2[:, m:m + 1], shw2[:, m:m + 1], ALU.add, ALU.mult), inc="MV"))
                s2_mt[s2] = n["MT"]

            ckpt("p10")
            wait("sync", "Trow", n["Trow"])
            wait("scalar", "Ls", n["Ls"])
            y_tiled = y_out.rearrange("(i p) d -> i p d", p=128)
            ffn2_all = n["MV"]
            ost = {}
            out_add = 0
            for i in range(NTILE):
                wait("sync", "MV", ffn2_all if i == 0 else out_add)
                src2 = BB[:, i * D:(i + 1) * D]
                dst3 = lnst[:, :].rearrange("t (c p) -> t c p", p=128)
                tt = dmat("sync", "Tout", dst3, src2)
                if i >= 1:
                    wait("scalar", "MV", out_add)
                lol = dma("scalar", "Lh", lob[:, :], x2lo_t[i])
                wait("vector", "Tout", tt)
                wait("vector", "Lh", lol)
                if i >= 2:
                    wait("vector", "O0" if i % 2 == 0 else "O1", ost[i - 2])
                half = i % 2
                obuf = stage if half == 0 else hstg
                for a, b in ((0, 512), (512, 1024)):
                    instr("vector", lambda v, i=i, obuf=obuf, a=a, b=b: v.tensor_tensor(
                        obuf[:, a:b], B8[:, i * D + a:i * D + b], lnst[:, a:b], ALU.add))
                for a, b in ((0, 512), (512, 1024)):
                    out_add = instr("vector", lambda v, obuf=obuf, a=a, b=b: v.tensor_tensor(
                        obuf[:, a:b], obuf[:, a:b], lob[:, a:b], ALU.add), inc="MV")
                wait("scalar", "MV", out_add)
                ost[i] = dma("scalar", "O0" if i % 2 == 0 else "O1", y_tiled[i], obuf[:, 0:1024])
            if DBG:
                dma("scalar", "O0", dbg_oA1[:, :, :], oA1[:, :, :])
                dma("scalar", "O0", dbg_iA2[:, :, :], iA2[:, :, :])
                dma("scalar", "O0", dbg_B8[:, :], B8[:, :])
                dma("scalar", "O0", dbg_qx[:, :], BA[:, :])
                dma("scalar", "O0", dbg_gsc[:, :], gsc[:, :])
                dma("scalar", "O0", dbg_hdr[0], hdr[0])
                dma("scalar", "O0", dbg_hdr[1], hdr[31])
            wait("scalar", "O0", n["O0"])
            wait("scalar", "O1", n["O1"])
            ffnB_sc.close()


        except _SkipRest:
            pass
        for _sc in (ffnB_sc, ffnA_sc, conv_sc, pe_sc, attn_sc):
            try:
                _sc.close()
            except Exception:
                pass

        @block.sync
        def _(e):
            for fn in streams["sync"]:
                fn(e)

        def _audit():
            actual = {nm: 0 for nm in SEMNAMES}
            byid = {SEMS[nm].sem_num if hasattr(SEMS[nm], "sem_num") else None: nm for nm in SEMNAMES}
            names = {}
            for i in nc.all_instructions():
                si = getattr(i, "sync_info", None)
                if si is None:
                    continue
                for u in si.on_update:
                    if u.ant_name in actual:
                        actual[u.ant_name] += u.update_value
            bad = {k: (actual[k], n[k]) for k in SEMNAMES if actual[k] != n[k]}
            assert not bad, f"semaphore accounting mismatch: {bad}"

        @block.scalar
        def _(e):
            for fn in streams["scalar"]:
                fn(e)

        @block.vector
        def _(e):
            for fn in streams["vector"]:
                fn(e)

        @block.tensor
        def _(e):
            for fn in streams["tensor"]:
                fn(e)

        @block.gpsimd
        def _(e):
            for fn in streams["gpsimd"]:
                fn(e)

        _audit()

    return nc


_NC_CACHE = None


def kernel(L0_attn_x, L0_attn_Wu, L0_attn_bu, L0_attn_Wv, L0_attn_bv,
           L0_attn_W1, L0_attn_b1, L0_attn_W2, L0_attn_b2, L0_attn_W3,
           L0_attn_b3, L0_attn_Wo, L0_attn_bo, L0_ffn_w1, L0_ffn_b1,
           L0_ffn_w2, L0_ffn_b2):
    global _NC_CACHE
    x = np.ascontiguousarray(np.asarray(L0_attn_x, np.float32))
    B, Lq, Dm = x.shape
    if _NC_CACHE is None:
        _NC_CACHE = build_nc()
    nc = _NC_CACHE
    f32c = lambda a: np.ascontiguousarray(np.asarray(a, np.float32))
    pos = np.arange(4096, dtype=np.float32)[:, None]
    div = np.exp(np.arange(0, 64, 2, dtype=np.float32) * -(np.log(10000.0) / 64.0))
    pe = np.zeros((4096, 64), np.float32)
    pe[:, 0::2] = np.sin(pos * div)
    pe[:, 1::2] = np.cos(pos * div)
    com = dict(
        Wu=f32c(L0_attn_Wu), Wv=f32c(L0_attn_Wv), Wo=f32c(L0_attn_Wo),
        bu=f32c(L0_attn_bu), bv=f32c(L0_attn_bv), bo=f32c(L0_attn_bo),
        W1=f32c(L0_attn_W1), W2=f32c(L0_attn_W2),
        b1f=f32c(L0_attn_b1), b2f=f32c(L0_attn_b2),
        peT=np.ascontiguousarray(pe.T),
        w1T=f32c(np.asarray(L0_ffn_w1).T), w1n=f32c(L0_ffn_w1), b1v=f32c(L0_ffn_b1),
        w2T=f32c(np.asarray(L0_ffn_w2).T), w2n=f32c(L0_ffn_w2), b2v=f32c(L0_ffn_b2),
    )
    com.update({"d" + k: v for k, v in CONSTS.items()})
    W3 = f32c(L0_attn_W3)
    b3 = f32c(L0_attn_b3)
    in_maps = []
    for c in range(NCORES):
        b, h = c // 2, c % 2
        m = dict(com)
        m["x_in"] = np.ascontiguousarray(x[b, h * NT:(h + 1) * NT, :])
        m["W3s"] = np.ascontiguousarray(W3[:, c * 128:(c + 1) * 128])
        m["b3s"] = np.ascontiguousarray(b3[c * 128:(c + 1) * 128])
        in_maps.append(m)
    res = run_bass_kernel_spmd(nc, in_maps, core_ids=list(range(NCORES)))
    out = np.empty((B, Lq, Dm), np.float32)
    for c in range(NCORES):
        b, h = c // 2, c % 2
        out[b, h * NT:(h + 1) * NT, :] = res.results[c]["y_out"]
    return out


# version 11
# speedup vs baseline: 1.3005x; 1.0042x over previous
import os
os.environ.setdefault("TRN_ALLOW_ICI_ALL_TO_ALL", "1")

import numpy as np
import ml_dtypes
import concourse.bass as bass
import concourse.mybir as mybir
from concourse.bass_utils import run_bass_kernel_spmd
from contextlib import ExitStack

F32 = mybir.dt.float32
BF16 = mybir.dt.bfloat16
AF = mybir.ActivationFunctionType
ALU = mybir.AluOpType
AX = mybir.AxisListType.X

MAGIC = float(np.float32(1.5 * 2 ** 23))
NCORES = 8
NT = 2048
D = 1024
FF = 4096
NTILE = NT // 128
NCH = D // 128
NFT = FF // 128


def _bf(x):
    return np.ascontiguousarray(np.asarray(x, np.float32)).astype(ml_dtypes.bfloat16)


def host_constants():
    k = np.arange(128)
    W1c = np.exp(-2j * np.pi * np.outer(k, k) / 128.0)
    F1r, F1i = W1c.real.astype(np.float32), W1c.imag.astype(np.float32)
    W2c = np.exp(+2j * np.pi * np.outer(k, k) / 128.0)
    F2r, F2i = W2c.real.astype(np.float32), W2c.imag.astype(np.float32)
    cR1h = np.concatenate([F1r[0:64, 0:65], F1i[0:64, 0:65]], axis=1)
    cG1 = F1r[0:64, :]
    cG2 = F1i[0:64, :]
    cG1d = np.concatenate([cG1, cG1], axis=0)
    cG2d = np.concatenate([cG2, cG2], axis=0)
    cR3 = np.concatenate([F2r, F2i], axis=1)
    cR4h = np.concatenate([-F2i, F2r], axis=1)
    a = np.arange(64)
    k65 = np.arange(65)
    alpha = np.where((k65 == 0) | (k65 == 64), 1.0, 2.0)[:, None].astype(np.float32)
    CO2 = (alpha * np.cos(2 * np.pi * np.outer(k65, a) / 128.0) / 16384.0).astype(np.float32)
    SI2n = (-alpha * np.sin(2 * np.pi * np.outer(k65, a) / 128.0) / 16384.0).astype(np.float32)
    CO2s = np.zeros((65, 64), np.float32); CO2s[:, 1:] = CO2[:, :-1]
    SI2ns = np.zeros((65, 64), np.float32); SI2ns[:, 1:] = SI2n[:, :-1]
    pos = np.arange(4096, dtype=np.float32)[:, None]
    div = np.exp(np.arange(0, 64, 2, dtype=np.float32) * -(np.log(10000.0) / 64.0))
    pe = np.zeros((4096, 64), np.float32)
    pe[:, 0::2] = np.sin(pos * div)
    pe[:, 1::2] = np.cos(pos * div)
    dec = np.exp(-0.01 * np.arange(4096, dtype=np.float32)).reshape(1, 4096)
    return dict(
        cR1h=_bf(cR1h), cG1d=_bf(cG1d), cG2d=_bf(cG2d), cR3=_bf(cR3), cR4h=_bf(cR4h),
        cCO2=_bf(CO2), cSI2n=_bf(SI2n), cCO2s=_bf(CO2s), cSI2ns=_bf(SI2ns),
        cG1f=np.ascontiguousarray(cG1.astype(np.float32)),
        cG2f=np.ascontiguousarray(cG2.astype(np.float32)),
        cDec=np.ascontiguousarray(dec.astype(np.float32)),
        cI128=np.eye(128, dtype=np.float32),
        cOnes=np.ones((1, 128), np.float32),
    )


CONSTS = host_constants()


def build_nc():
    nc = bass.Bass("TRN2", target_bir_lowering=False, debug=False, detect_race_conditions=False)
    DI = lambda nm, s, dt=F32: nc.dram_tensor(nm, s, dt, kind="ExternalInput").ap()
    x_in = DI("x_in", [NT, D])
    Wu_d = DI("Wu", [D, D]); Wv_d = DI("Wv", [D, D]); Wo_d = DI("Wo", [D, D])
    bu_d = DI("bu", [D]); bv_d = DI("bv", [D]); bo_d = DI("bo", [D])
    W1_d = DI("W1", [64, 64]); W2_d = DI("W2", [64, 64])
    b1f_d = DI("b1f", [64]); b2f_d = DI("b2f", [64])
    W3s_d = DI("W3s", [64, 128]); b3s_d = DI("b3s", [128])
    peT_d = DI("peT", [64, 4096])
    w1T_d = DI("w1T", [D, FF]); w1n_d = DI("w1n", [FF, D]); b1v_d = DI("b1v", [FF])
    w2T_d = DI("w2T", [FF, D]); w2n_d = DI("w2n", [D, FF]); b2v_d = DI("b2v", [D])
    cns = {k: DI("d" + k, list(v.shape), BF16 if v.dtype == ml_dtypes.bfloat16 else F32)
           for k, v in CONSTS.items()}
    y_out = nc.dram_tensor("y_out", [NT, D], F32, kind="ExternalOutput").ap()
    DBG = os.environ.get("ATH_DEBUG", "") == "1"
    if DBG:
        dbg_oA1 = nc.dram_tensor("dbg_oA1", [NCORES, 128, NT], BF16, kind="ExternalOutput").ap()
        dbg_iA2 = nc.dram_tensor("dbg_iA2", [NCORES, 128, NT], BF16, kind="ExternalOutput").ap()
        dbg_B8 = nc.dram_tensor("dbg_B8", [128, NTILE * D], BF16, kind="ExternalOutput").ap()
        dbg_qx = nc.dram_tensor("dbg_qx", [128, NCH * NT], BF16, kind="ExternalOutput").ap()
        dbg_gsc = nc.dram_tensor("dbg_gsc", [1, 8], F32, kind="ExternalOutput").ap()
        dbg_hdr = nc.dram_tensor("dbg_hdr", [2, 128, NT], BF16, kind="ExternalOutput").ap()
    iA1 = nc.dram_tensor("iA1", [NCORES, 128, NT], BF16).ap()
    oA1 = nc.dram_tensor("oA1", [NCORES, 128, NT], BF16).ap()
    iA2 = nc.dram_tensor("iA2", [NCORES, 128, NT], BF16).ap()
    oA2 = nc.dram_tensor("oA2", [NCORES, 128, NT], BF16).ap()
    mxi1 = nc.dram_tensor("mxi1", [1, 1], F32).ap()
    mxo1 = nc.dram_tensor("mxo1", [1, 1], F32).ap()
    mxi2 = nc.dram_tensor("mxi2", [1, 1], F32).ap()
    mxo2 = nc.dram_tensor("mxo2", [1, 1], F32).ap()
    hdr = nc.dram_tensor("hdr", [NFT, 128, NT], BF16).ap()
    qw2d = nc.dram_tensor("qw2d", [NCH, 128, NFT * 128], BF16).ap()
    hscr = nc.dram_tensor("hscr", [128, 4096], BF16).ap()
    x2lo = nc.dram_tensor("x2lo", [NT, D], BF16).ap()

    with ExitStack() as ctx:
        def sbuf(nm, shape, dt=BF16, scope=None):
            return (scope or ctx).enter_context(nc.sbuf_tensor(nm, shape, dt))

        def psum(nm):
            return ctx.enter_context(nc.psum_tensor(nm, [128, 512], F32))

        def sem(nm):
            return ctx.enter_context(nc.semaphore(nm))

        B8 = sbuf("B8", [128, NTILE * D])
        BA = sbuf("BA", [128, NCH * NT])
        BB = sbuf("BB", [128, NCH * NT])
        BC = sbuf("BC", [128, NCH * NT])
        hstg = sbuf("hstg", [128, NT], F32)
        stage = sbuf("stage", [128, 1024], F32)
        lnst = sbuf("lnst", [128, 1024])
        sb_c = {k: sbuf("s" + k, list(v.shape),
                        BF16 if v.dtype == ml_dtypes.bfloat16 else F32)
                for k, v in CONSTS.items() if k != "cDec"}
        bias_u = sbuf("bias_u", [128, NCH], F32)
        bias_v = sbuf("bias_v", [128, NCH], F32)
        bias_o = sbuf("bias_o", [128, NCH], F32)
        b3sb = sbuf("b3sb", [128, 1], F32)
        b1fb = sbuf("b1fb", [64, 1], F32)
        b2fb = sbuf("b2fb", [64, 1], F32)
        b1t = sbuf("b1t", [128, NFT], F32)
        b2t = sbuf("b2t", [128, NCH], F32)
        rw1 = sbuf("rw1", [128, NFT], F32)
        rw2 = sbuf("rw2", [128, NCH], F32)
        sw1 = sbuf("sw1", [128, NFT], F32)
        sw2 = sbuf("sw2", [128, NCH], F32)
        sxw1 = sbuf("sxw1", [128, NFT], F32)
        bq1 = sbuf("bq1", [128, NFT], F32)
        exv1 = sbuf("exv1", [128, NFT], F32)
        shw2 = sbuf("shw2", [128, NCH], F32)
        bq2 = sbuf("bq2", [128, NCH], F32)
        exv2 = sbuf("exv2", [128, NCH], F32)
        mstats = sbuf("mstats", [128, 3 * NTILE], F32)
        bnb = sbuf("bnb", [128, 24], F32)
        agg = sbuf("agg", [128, 16], F32)
        lmax = sbuf("lmax", [128, 2], F32)
        gsc = sbuf("gsc", [1, 8], F32)
        rxb = sbuf("rxb", [128, 4], F32)
        lob = sbuf("lob", [128, 1024], BF16)
        W1sb = sbuf("W1sb", [64, 64], F32)
        W2sb = sbuf("W2sb", [64, 64], F32)
        W3sb = sbuf("W3sb", [64, 128], F32)
        psA = psum("psA"); psB = psum("psB"); psC = psum("psC"); psD = psum("psD")
        psE = psum("psE"); psF = psum("psF"); psG = psum("psG"); psH = psum("psH")
        PS4 = [psA, psB, psC, psD]
        SEMNAMES = ["CC", "MT", "MV", "MS", "MG",
                    "Gx", "Gw", "Gv0", "Gv1", "Ga", "Gm", "Gh", "Gq",
                    "Lc", "Lp", "Ls", "Lq2", "Lw0", "Lw1", "Lh", "Lq",
                    "Tln", "Tzh", "Tz0", "Tz1", "Ty", "Ty2", "Trow", "Tat", "Tqx", "Tq", "Tout",
                    "O0", "O1"]
        SEMS = {nm: sem(nm) for nm in SEMNAMES}
        attn_sc = ExitStack()
        pe_sc = ExitStack()
        conv_sc = ExitStack()
        ffnA_sc = ExitStack()
        ffnB_sc = ExitStack()
        block = ctx.enter_context(nc.Block())

        n = {nm: 0 for nm in SEMNAMES}
        streams = {e: [] for e in ("tensor", "vector", "scalar", "sync", "gpsimd")}

        class _SkipRest(Exception):
            pass

        _STOP = os.environ.get("ATH_STOP_PHASE", "")

        def ckpt(name):
            if _STOP == name:
                raise _SkipRest()

        def dma(eng, sm, out, in_, slow=False):
            n[sm] += 16
            hs = SEMS[sm]
            streams[eng].append(lambda e, out=out, in_=in_, hs=hs, slow=slow:
                                e.dma_start(out=out, in_=in_,
                                            allow_slow_non_contiguous=slow).then_inc(hs, 16))
            return n[sm]

        def dmat(eng, sm, out, in_):
            n[sm] += 16
            hs = SEMS[sm]
            streams[eng].append(lambda e, out=out, in_=in_, hs=hs:
                                e.dma_start_transpose(out, in_).then_inc(hs, 16))
            return n[sm]

        def wait(eng, sm, v):
            if v <= 0:
                return
            hs = SEMS[sm]
            streams[eng].append(lambda e, hs=hs, v=v: e.wait_ge(hs, v))

        def instr(eng, f, inc=None):
            if inc is None:
                streams[eng].append(f)
                return None
            n[inc] += 1
            hs = SEMS[inc]
            streams[eng].append(lambda e, f=f, hs=hs: f(e).then_inc(hs, 1))
            return n[inc]

        BAv = BA[:, :].rearrange("p (c q) -> p c q", c=NCH)

        try:
            x_tiled = x_in.rearrange("(i p) d -> i p d", p=128)
            xld = [dma("gpsimd", "Gx", B8[:, i * D:(i + 1) * D], x_tiled[i]) for i in range(NTILE)]
            wbuf = sbuf("wbuf", [128, NCH * D], scope=attn_sc)
            wsc = sbuf("wsc", [128, 1024], F32, scope=attn_sc)
            wu_ld = dma("gpsimd", "Gw", wbuf[:, :].rearrange("p (c d) -> p c d", c=NCH),
                        Wu_d.rearrange("(c p) d -> p c d", p=128))
            cld = {k: dma("scalar", "Lc", sb_c[k][:, :], cns[k][:, :]) for k in CONSTS
                   if k != "cDec"}
            dma("scalar", "Lc", bias_u[:, :], bu_d.rearrange("(c p) -> p c", p=128), slow=True)
            dma("scalar", "Lc", bias_v[:, :], bv_d.rearrange("(c p) -> p c", p=128), slow=True)
            v_bo = dma("scalar", "Lc", bias_o[:, :], bo_d.rearrange("(c p) -> p c", p=128), slow=True)
            dma("scalar", "Lc", b3sb[:, :], b3s_d.rearrange("(p o) -> p o", o=1))
            dma("scalar", "Lc", b1fb[:, :], b1f_d.rearrange("(p o) -> p o", o=1))
            dma("scalar", "Lc", b2fb[:, :], b2f_d.rearrange("(p o) -> p o", o=1))
            dma("scalar", "Lc", b1t[:, :], b1v_d.rearrange("(t p) -> p t", p=128), slow=True)
            v_b2t = dma("scalar", "Lc", b2t[:, :], b2v_d.rearrange("(t p) -> p t", p=128), slow=True)
            dma("scalar", "Lc", W1sb[:, :], W1_d[:, :])
            dma("scalar", "Lc", W2sb[:, :], W2_d[:, :])
            lw3 = dma("scalar", "Lc", W3sb[:, :], W3s_d[:, :])

            def emit_ln_stats_pair(xtA, xtB):
                chains = []
                for (xt, c, b) in ((xtA, 0, 0), (xtB, 8, 12)):
                    chains.append([
                        lambda v, xt=xt, b=b: v.bn_stats(bnb[:, b:b + 6], xt[:, 0:512]),
                        lambda v, xt=xt, b=b: v.bn_stats(bnb[:, b + 6:b + 12], xt[:, 512:1024]),
                        lambda v, c=c, b=b: v.reduce_sum(agg[:, c:c + 1], bnb[:, b + 1:b + 12:3], axis=AX),
                        lambda v, c=c, b=b: v.tensor_tensor(agg[:, c + 2:c + 6], bnb[:, b + 1:b + 12:3], bnb[:, b + 1:b + 12:3], ALU.mult),
                        lambda v, c=c: v.reduce_sum(agg[:, c + 1:c + 2], agg[:, c + 2:c + 6], axis=AX),
                        lambda v, c=c, b=b: v.reduce_sum(agg[:, c + 6:c + 7], bnb[:, b + 2:b + 12:3], axis=AX),
                        lambda v, c=c: v.tensor_scalar(agg[:, c:c + 1], agg[:, c:c + 1], 0.25, None, ALU.mult),
                        lambda v, c=c: v.tensor_tensor(agg[:, c + 7:c + 8], agg[:, c:c + 1], agg[:, c:c + 1], ALU.mult),
                        lambda v, c=c: v.tensor_scalar(agg[:, c + 6:c + 7], agg[:, c + 6:c + 7], 1.0 / 1024.0, None, ALU.mult),
                        lambda v, c=c: v.tensor_scalar(agg[:, c + 1:c + 2], agg[:, c + 1:c + 2], 0.25, None, ALU.mult),
                        lambda v, c=c: v.tensor_tensor(agg[:, c + 6:c + 7], agg[:, c + 6:c + 7], agg[:, c + 1:c + 2], ALU.add),
                        lambda v, c=c: v.tensor_tensor(agg[:, c + 6:c + 7], agg[:, c + 6:c + 7], agg[:, c + 7:c + 8], ALU.subtract),
                        lambda v, c=c: v.tensor_scalar(agg[:, c + 6:c + 7], agg[:, c + 6:c + 7], 1e-5, None, ALU.add),
                        lambda v, c=c: v.reciprocal(agg[:, c + 1:c + 2], agg[:, c + 6:c + 7]),
                    ])
                last = None
                for k in range(len(chains[0])):
                    for t in range(2):
                        last = instr("vector", chains[t][k],
                                     inc=("MV" if k == len(chains[0]) - 1 else None))
                return last

            lnst3 = lnst[:, :].rearrange("p (c q) -> p c q", q=128)
            stg3 = lob[:, :].rearrange("p (c q) -> p c q", q=128)
            BAt4 = BA[:, :].rearrange("p (c i q) -> p c i q", c=NCH, i=NTILE)
            prev_tt = 0
            for pi in range(NTILE // 2):
                iA, iB = 2 * pi, 2 * pi + 1
                xtA = B8[:, iA * D:(iA + 1) * D]
                xtB = B8[:, iB * D:(iB + 1) * D]
                wait("vector", "Gx", xld[-1])
                if pi > 0:
                    wait("vector", "Tln", prev_tt)
                vrec = emit_ln_stats_pair(xtA, xtB)
                wait("scalar", "MV", vrec)
                instr("scalar", lambda s: s.activation(agg[:, 2:3], agg[:, 1:2], AF.Sqrt))
                vsq = instr("scalar", lambda s: s.activation(agg[:, 10:11], agg[:, 9:10], AF.Sqrt), inc="MS")
                wait("vector", "MS", vsq)
                vnA = instr("vector", lambda v, xtA=xtA: v.tensor_scalar(
                    lnst[:, :], xtA, agg[:, 0:1], agg[:, 2:3], ALU.subtract, ALU.mult), inc="MV")
                vnB = instr("vector", lambda v, xtB=xtB: v.tensor_scalar(
                    lob[:, :], xtB, agg[:, 8:9], agg[:, 10:11], ALU.subtract, ALU.mult), inc="MV")
                wait("sync", "MV", vnA)
                dmat("sync", "Tln", BAt4[:, :, iA, :], lnst3)
                wait("sync", "MV", vnB)
                prev_tt = dmat("sync", "Tln", BAt4[:, :, iB, :], stg3)
            lnT_all = prev_tt
            ckpt("p1")

            def emit_proj(evac):
                prev_ev = 0
                for m in range(NCH):
                    if prev_ev:
                        wait("tensor", evac_sem, prev_ev)
                    for c in range(NCH):
                        for s in range(4):
                            instr("tensor", lambda t, m=m, c=c, s=s: t.matmul(
                                PS4[s][:, 0:512],
                                wbuf[:, c * D + m * 128: c * D + (m + 1) * 128],
                                BAv[:, c, s * 512:(s + 1) * 512],
                                start=(c == 0), stop=(c == NCH - 1)),
                                inc=("MT" if c == NCH - 1 else None))
                    prev_ev = evac(m, n["MT"])
                return prev_ev

            wait("tensor", "Tln", lnT_all)
            wait("tensor", "Gw", wu_ld)
            wait("vector", "Lc", n["Lc"])
            evac_sem = "MV"

            def evac_u(m, mtv):
                last = None
                for s in range(4):
                    wait("vector", "MT", mtv)
                    last = instr("vector", lambda v, m=m, s=s: v.tensor_scalar(
                        BB[:, m * NT + s * 512: m * NT + (s + 1) * 512],
                        PS4[s][:, 0:512], bias_u[:, m:m + 1], None, ALU.add), inc="MV")
                return last

            emit_proj(evac_u)
            u_mm_done = n["MT"]
            u_done = n["MV"]
            ckpt("p2u")

            wait("gpsimd", "MT", u_mm_done)
            wv_ld = dma("gpsimd", "Gw", wbuf[:, :].rearrange("p (c d) -> p c d", c=NCH),
                        Wv_d.rearrange("(c p) d -> p c d", p=128))
            wait("tensor", "Gw", wv_ld)

            vst = [0, 0]

            def evac_v(m, mtv):
                for s in range(4):
                    half = s % 2
                    wait("scalar", "MT", mtv)
                    wait("scalar", "Gv0" if half == 0 else "Gv1", vst[half])
                    vv = instr("scalar", lambda sc, m=m, s=s, half=half: sc.activation(
                        stage[:, half * 512:(half + 1) * 512], PS4[s][:, 0:512], AF.Identity,
                        bias=bias_v[:, m:m + 1], scale=1.0), inc="MS")
                    wait("gpsimd", "MS", vv)
                    vst[half] = dma("gpsimd", "Gv0" if half == 0 else "Gv1",
                                    iA1[m, :, s * 512:(s + 1) * 512],
                                    stage[:, half * 512:(half + 1) * 512])
                return None

            evac_sem = "MS"

            def evac_v_wrap(m, mtv):
                evac_v(m, mtv)
                return n["MS"]

            wait("scalar", "Lc", n["Lc"])
            emit_proj(evac_v_wrap)
            v_mm_done = n["MT"]
            v_store_done = None
            ckpt("p2")

            wait("gpsimd", "Gv0", vst[0])
            wait("gpsimd", "Gv1", vst[1])
            a2a1 = instr("gpsimd", lambda g: g.collective_compute(
                "AllToAll", ALU.bypass, replica_groups=[list(range(NCORES))],
                ins=[iA1.tensor.ap().opt()], outs=[oA1.tensor.ap().opt()]), inc="CC")
            wait("gpsimd", "CC", a2a1)

            peb = sbuf("peb", [64, 2 * 512], F32, scope=pe_sc)
            decb = sbuf("decb", [1, 4096], F32, scope=pe_sc)
            dma("scalar", "Lc", decb[0:1, :], cns["cDec"][0:1, :])
            g1s = BA[:, 0:8192].bitcast(F32)
            g2s = BA[:, 8192:16384].bitcast(F32)

            wait("tensor", "Lc", n["Lc"])
            prev_ms = 0
            pe_mt = {}
            for s8 in range(8):
                if s8 >= 1:
                    wait("scalar", "MT", pe_mt[s8 - 1])
                lv = dma("scalar", "Lp", peb[0:64, (s8 % 2) * 512:(s8 % 2) * 512 + 512],
                         peT_d[:, s8 * 512:(s8 + 1) * 512])
                wait("tensor", "Lp", lv)
                if prev_ms:
                    wait("tensor", "MS", prev_ms)
                vm = instr("tensor", lambda t, s8=s8: t.matmul(
                    psE[0:64, 0:512], W1sb[0:64, :],
                    peb[0:64, (s8 % 2) * 512:(s8 % 2) * 512 + 512],
                    start=True, stop=True), inc="MT")
                pe_mt[s8] = vm
                wait("scalar", "MT", vm)
                prev_ms = instr("scalar", lambda sc, s8=s8: sc.activation(
                    g1s[0:64, s8 * 512:(s8 + 1) * 512], psE[0:64, 0:512], AF.Identity,
                    bias=b1fb[0:64, 0:1], scale=1.0), inc="MS")

            def emit_gelu(buf):
                last = None
                for hh in range(4):
                    src = buf[0:64, hh * 1024:(hh + 1) * 1024]
                    t1 = hstg[0:64, 0:1024]
                    t2 = hstg[0:64, 1024:2048]
                    vq = instr("scalar", lambda s, src=src, t1=t1: s.activation(t1, src, AF.Square), inc="MS")
                    wait("vector", "MS", vq)
                    for a, b in ((0, 512), (512, 1024)):
                        instr("vector", lambda v, t1=t1, a=a, b=b: v.tensor_scalar(
                            t1[:, a:b], t1[:, a:b], 0.044715, 1.0, ALU.mult, ALU.add))
                    vw = None
                    for a, b in ((0, 512), (512, 1024)):
                        vw = instr("vector", lambda v, t1=t1, src=src, a=a, b=b: v.tensor_tensor(
                            t1[:, a:b], t1[:, a:b], src[:, a:b], ALU.mult), inc="MV")
                    wait("scalar", "MV", vw)
                    vt = instr("scalar", lambda s, t1=t1, t2=t2: s.activation(
                        t2, t1, AF.Tanh, scale=0.7978845608028654), inc="MS")
                    wait("vector", "MS", vt)
                    for a, b in ((0, 512), (512, 1024)):
                        instr("vector", lambda v, t2=t2, a=a, b=b: v.tensor_scalar(
                            t2[:, a:b], t2[:, a:b], 1.0, 0.5, ALU.add, ALU.mult))
                    for a, b in ((0, 512), (512, 1024)):
                        last = instr("vector", lambda v, src=src, t2=t2, a=a, b=b: v.tensor_tensor(
                            src[:, a:b], t2[:, a:b], src[:, a:b], ALU.mult), inc="MV")
                return last

            wait("scalar", "MS", n["MS"])
            vg1 = emit_gelu(g1s)
            wait("tensor", "MV", vg1)
            prev_ms = 0
            for s8 in range(8):
                if prev_ms:
                    wait("tensor", "MS", prev_ms)
                vm = instr("tensor", lambda t, s8=s8: t.matmul(
                    psE[0:64, 0:512], W2sb[0:64, :], g1s[0:64, s8 * 512:(s8 + 1) * 512],
                    start=True, stop=True), inc="MT")
                wait("scalar", "MT", vm)
                prev_ms = instr("scalar", lambda sc, s8=s8: sc.activation(
                    g2s[0:64, s8 * 512:(s8 + 1) * 512], psE[0:64, 0:512], AF.Identity,
                    bias=b2fb[0:64, 0:1], scale=1.0), inc="MS")
            wait("scalar", "MS", n["MS"])
            vg2 = emit_gelu(g2s)
            wait("tensor", "MV", vg2)
            wait("vector", "Gv0", vst[0])
            wait("vector", "Gv1", vst[1])
            prev_hv = 0
            hst_prev = {}
            for s8 in range(8):
                par8 = s8 % 2
                if prev_hv:
                    wait("tensor", "MV", prev_hv)
                instr("tensor", lambda t, s8=s8: t.matmul(
                    psE[:, 0:512], W3sb[0:64, :], g2s[0:64, s8 * 512:(s8 + 1) * 512],
                    start=True, stop=True))
                vdm = instr("tensor", lambda t, s8=s8: t.matmul(
                    psF[:, 0:512], sb_c["cOnes"][0:1, :], decb[0:1, s8 * 512:(s8 + 1) * 512],
                    start=True, stop=True), inc="MT")
                wait("vector", "MT", vdm)
                if s8 >= 2:
                    wait("vector", "Gh", hst_prev[s8 - 2])
                for a, b in ((0, 256), (256, 512)):
                    instr("vector", lambda v, a=a, b=b: v.tensor_scalar(
                        stage[:, a:b], psE[:, a:b], b3sb[:, 0:1], None, ALU.add))
                for a, b in ((0, 256), (256, 512)):
                    prev_hv = instr("vector", lambda v, par8=par8, a=a, b=b: v.tensor_tensor(
                        hstg[:, par8 * 512 + a:par8 * 512 + b], stage[:, a:b], psF[:, a:b], ALU.mult), inc="MV")
                wait("gpsimd", "MV", prev_hv)
                hst_prev[s8] = dma("gpsimd", "Gh", hscr[:, s8 * 512:(s8 + 1) * 512],
                                   hstg[:, par8 * 512:(par8 + 1) * 512])
            h_done = hst_prev[7]
            ckpt("p3h")
            pe_sc.close()

            zh32 = sbuf("zh32", [64, 128 * 64], BF16, scope=conv_sc)
            zv = sbuf("zv", [64, 6 * 256], BF16, scope=conv_sc)
            at_b = sbuf("at_b", [128, 2 * 195], BF16, scope=conv_sc)
            ath32 = sbuf("ath32", [64, 195], F32, scope=conv_sc)
            hb32 = sbuf("hb32", [128, 2 * 260], F32, scope=conv_sc)
            vh32 = sbuf("vh32", [128, 2 * 260], F32, scope=conv_sc)
            tmpb = sbuf("tmpb", [128, 2 * 520], F32, scope=conv_sc)
            ypb = sbuf("ypb", [128, 2 * 260], BF16, scope=conv_sc)
            esb = sbuf("esb", [65, 2 * 1024], BF16, scope=conv_sc)
            ysb = sbuf("ysb", [64, 6 * 256], BF16, scope=conv_sc)

            wait("sync", "Gh", h_done)
            wait("sync", "MT", pe_mt[7])
            zh_all = dma("sync", "Tzh", zh32[:, :].rearrange("a (c b) -> a c b", b=64),
                         hscr.rearrange("c (a b) -> a c b", b=64))

            w1n_t = w1n_d.rearrange("(t p) d -> t p d", p=128)
            wait("scalar", "MT", pe_mt[7])
            for ft in range(NFT):
                lv = dma("scalar", "Ls", wsc[:, :], w1n_t[ft])
                wait("vector", "Ls", lv)
                vr = instr("vector", lambda v, ft=ft: v.reduce_max(
                    sw1[:, ft:ft + 1], wsc[:, :], axis=AX, apply_absolute_value=True), inc="MV")
                wait("scalar", "MV", vr)
            w2n_t = w2n_d.rearrange("(t p) f -> t p f", p=128)
            for ct in range(NCH):
                for pc in range(4):
                    lv = dma("scalar", "Ls", wsc[:, :], w2n_t[ct][:, pc * 1024:(pc + 1) * 1024])
                    wait("vector", "Ls", lv)
                    vr = instr("vector", lambda v, ct=ct, pc=pc: v.reduce_max(
                        mstats[:, 4 * ct + pc:4 * ct + pc + 1], wsc[:, :],
                        axis=AX, apply_absolute_value=True), inc="MV")
                    wait("scalar", "MV", vr)
            for ct in range(NCH):
                instr("vector", lambda v, ct=ct: v.reduce_max(
                    sw2[:, ct:ct + 1], mstats[:, 4 * ct:4 * ct + 4], axis=AX))
            instr("vector", lambda v: v.tensor_scalar(sw1[:, :], sw1[:, :], 1e-8, 1.0 / 127.0, ALU.max, ALU.mult))
            instr("vector", lambda v: v.tensor_scalar(sw2[:, :], sw2[:, :], 1e-8, 1.0 / 127.0, ALU.max, ALU.mult))
            vrw1 = instr("vector", lambda v: v.reciprocal(rw1[:, :], sw1[:, :]), inc="MV")
            vrw2 = instr("vector", lambda v: v.reciprocal(rw2[:, :], sw2[:, :]), inc="MV")

            conv_c_mt = {}
            yst = {}
            yst2 = {}
            unit_cfree = {}
            unit_dfree = {}
            dh_mt = {}
            ad_mt = {}
            pr_mg = {}
            hbfree = {}
            yv_mv = {}
            ap_mt = {}
            e_mv = {}
            b_mt = {}
            y_mv = {}
            uc = 0
            pc_ = 0
            ps_ms0 = n["MS"]
            ps_mv0 = n["MV"]
            CP = [psA, psG]
            DPb = [psB, psH]
            DPa = [psC, psE]
            wait("tensor", "Tzh", zh_all)
            wait("tensor", "MS", ps_ms0)
            wait("tensor", "MV", ps_mv0)
            wait("sync", "CC", a2a1)
            wait("scalar", "CC", a2a1)
            oA1v = oA1.rearrange("(b hi) p (ah qq) -> p hi ah b qq", hi=2, qq=64)
            iA2v = iA2.rearrange("(b hi) p (ah qq) -> p hi ah b qq", hi=2, qq=64)
            NCONV = int(os.environ.get("ATH_CONV_N", "128"))
            zld0 = {}
            zld1 = {}

            def issue_zv_loads(d):
                if d >= NCONV:
                    return
                if d >= 6:
                    wait("sync", "MT", conv_c_mt[d - 6])
                    wait("scalar", "MT", conv_c_mt[d - 6])
                zvd = zv[:, (d % 6) * 256:(d % 6 + 1) * 256].rearrange("a (b q) -> a b q", b=4)
                zld0[d] = dma("sync", "Tz0", zvd[0:32], oA1v[d, 0])
                zld1[d] = dma("scalar", "Tz1", zvd[32:64], oA1v[d, 1])

            for d in range(6):
                issue_zv_loads(d)
            for dl in range(NCONV):
                slot = dl % 2
                sl4 = dl % 4
                cb, db = CP[uc % 2], DPb[uc % 2]
                if uc >= 2:
                    wait("tensor", *unit_cfree[uc - 2])
                elif uc == 0:
                    wait("tensor", "MS", ps_ms0)
                hcm = instr("tensor", lambda t, dl=dl, cb=cb: t.matmul(
                    cb[0:64, 0:130], zh32[:, dl * 64:(dl + 1) * 64], sb_c["cR1h"][:, :],
                    start=True, stop=True), inc="MT")
                wait("scalar", "MT", hcm)
                if dl >= 1:
                    wait("scalar", "MT", dh_mt[dl - 1])
                instr("scalar", lambda s, cb=cb: s.activation(
                    ath32[0:64, 65:195], cb[0:64, 0:130], AF.Identity))
                hae = instr("scalar", lambda s, cb=cb: s.activation(
                    ath32[0:64, 0:65], cb[0:64, 65:130], AF.Identity, scale=-1.0), inc="MS")
                unit_cfree[uc] = ("MS", hae)
                if uc >= 2:
                    wait("tensor", *unit_dfree[uc - 2])
                elif uc == 0:
                    wait("tensor", "MS", ps_ms0)
                wait("tensor", "MS", hae)
                instr("tensor", lambda t, db=db: t.matmul(
                    db[:, 0:130], sb_c["cG1f"][:, :], ath32[0:64, 65:195], start=True, stop=False))
                hdm = instr("tensor", lambda t, db=db: t.matmul(
                    db[:, 0:130], sb_c["cG2f"][:, :], ath32[0:64, 0:130], start=False, stop=True), inc="MT")
                dh_mt[dl] = hdm
                wait("scalar", "MT", hdm)
                if dl >= 2:
                    wait("scalar", "MG", hbfree[dl - 2])
                instr("scalar", lambda s, db=db, slot=slot: s.activation(
                    hb32[:, slot * 260:slot * 260 + 130], db[:, 0:130], AF.Identity))
                hEv = instr("scalar", lambda s, db=db, slot=slot: s.activation(
                    hb32[:, slot * 260 + 130:slot * 260 + 260], db[:, 0:130], AF.Identity), inc="MS")
                unit_dfree[uc] = ("MS", hEv)
                uc += 1
                for pp in range(2):
                    ps_ = pc_ % 2
                    cb, db, da = CP[uc % 2], DPb[uc % 2], DPa[uc % 2]
                    if uc >= 2:
                        wait("tensor", *unit_cfree[uc - 2])
                    wait("tensor", "Tz0", zld0[dl])
                    wait("tensor", "Tz1", zld1[dl])
                    sl8 = dl % 6
                    cm = instr("tensor", lambda t, cb=cb, sl8=sl8, pp=pp: t.matmul(
                        cb[0:128, 0:130], zv[:, sl8 * 256 + pp * 128:sl8 * 256 + (pp + 1) * 128],
                        sb_c["cR1h"][:, :], start=True, stop=True), inc="MT")
                    if pp == 1:
                        conv_c_mt[dl] = cm
                        issue_zv_loads(dl + 6)
                    wait("vector", "MT", cm)
                    if pc_ >= 2:
                        wait("vector", "MT", ad_mt[pc_ - 2])
                    instr("vector", lambda v, cb=cb, ps_=ps_: v.tensor_copy(
                        at_b[:, ps_ * 195 + 65:ps_ * 195 + 195], cb[:, 0:130]))
                    ae = instr("vector", lambda v, cb=cb, ps_=ps_: v.tensor_scalar(
                        at_b[:, ps_ * 195:ps_ * 195 + 65], cb[:, 65:130], -1.0, None, ALU.mult), inc="MV")
                    unit_cfree[uc] = ("MV", ae)
                    if uc >= 2:
                        wait("tensor", *unit_dfree[uc - 2])
                    wait("tensor", "MV", ae)
                    dm = None
                    for sg, pb in ((1, db), (0, da)):
                        instr("tensor", lambda t, pb=pb, ps_=ps_, sg=sg: t.matmul(
                            pb[:, 0:130], sb_c["cG1d"][sg * 64:sg * 64 + 64, :],
                            at_b[sg * 64:sg * 64 + 64, ps_ * 195 + 65:ps_ * 195 + 195],
                            start=True, stop=False))
                        dm = instr("tensor", lambda t, pb=pb, ps_=ps_, sg=sg: t.matmul(
                            pb[:, 0:130], sb_c["cG2d"][sg * 64:sg * 64 + 64, :],
                            at_b[sg * 64:sg * 64 + 64, ps_ * 195:ps_ * 195 + 130],
                            start=False, stop=True), inc="MT")
                    ad_mt[pc_] = dm
                    wait("scalar", "MT", dm)
                    if pc_ >= 2:
                        wait("scalar", "MG", pr_mg[pc_ - 2])
                    instr("scalar", lambda s, db=db, ps_=ps_: s.activation(
                        vh32[:, ps_ * 260:ps_ * 260 + 130], db[:, 0:130], AF.Identity))
                    vEv = instr("scalar", lambda s, da=da, ps_=ps_: s.activation(
                        vh32[:, ps_ * 260 + 130:ps_ * 260 + 260], da[:, 0:130], AF.Identity), inc="MS")
                    unit_dfree[uc] = ("MS", vEv)
                    uc += 1
                    wait("gpsimd", "MS", max(vEv, hEv))
                    if pc_ >= 2:
                        wait("gpsimd", "MV", yv_mv[pc_ - 2])
                    vv4 = vh32[:, ps_ * 260:(ps_ + 1) * 260].rearrange("p (s t c) -> p s t c", s=2, t=2)
                    hb4 = hb32[:, slot * 260:(slot + 1) * 260].rearrange("p (s t c) -> p s t c", s=2, t=2)
                    tp4 = tmpb[:, ps_ * 520:(ps_ + 1) * 520].rearrange("p (q s c) -> p q s c", q=4, s=2)
                    instr("gpsimd", lambda g, vv4=vv4, hb4=hb4, tp4=tp4: g.tensor_tensor(
                        tp4[:, 0], vv4[:, :, 0], hb4[:, :, 0], ALU.mult))
                    instr("gpsimd", lambda g, vv4=vv4, hb4=hb4, tp4=tp4: g.tensor_tensor(
                        tp4[:, 1], vv4[:, :, 1], hb4[:, :, 1], ALU.mult))
                    instr("gpsimd", lambda g, vv4=vv4, hb4=hb4, tp4=tp4: g.tensor_tensor(
                        tp4[:, 2], vv4[:, :, 0], hb4[:, :, 1], ALU.mult))
                    pg = instr("gpsimd", lambda g, vv4=vv4, hb4=hb4, tp4=tp4: g.tensor_tensor(
                        tp4[:, 3], vv4[:, :, 1], hb4[:, :, 0], ALU.mult), inc="MG")
                    pr_mg[pc_] = pg
                    if pp == 1:
                        hbfree[dl] = pg
                    wait("vector", "MG", pg)
                    if pc_ >= 2:
                        wait("vector", "MT", ap_mt[pc_ - 2])
                    yp4 = ypb[:, ps_ * 260:(ps_ + 1) * 260].rearrange("p (s t c) -> p s t c", s=2, t=2)
                    instr("vector", lambda v, yp4=yp4, tp4=tp4: v.tensor_tensor(
                        yp4[:, :, 0], tp4[:, 0], tp4[:, 1], ALU.subtract))
                    ym = instr("vector", lambda v, yp4=yp4, tp4=tp4: v.tensor_tensor(
                        yp4[:, :, 1], tp4[:, 2], tp4[:, 3], ALU.add), inc="MV")
                    yv_mv[pc_] = ym
                    ab = psD
                    if pc_ >= 1:
                        wait("tensor", "MV", e_mv[pc_ - 1])
                    else:
                        wait("tensor", "MS", ps_ms0)
                    wait("tensor", "MV", ym)
                    am = None
                    for sg in range(2):
                        base = ps_ * 260 + (1 - sg) * 130
                        instr("tensor", lambda t, ab=ab, sg=sg, base=base: t.matmul(
                            ab[0:65, sg * 256:(sg + 1) * 256], ypb[:, base:base + 65],
                            sb_c["cR3"][:, :], start=True, stop=False))
                        am = instr("tensor", lambda t, ab=ab, sg=sg, base=base: t.matmul(
                            ab[0:65, sg * 256:(sg + 1) * 256], ypb[:, base + 65:base + 130],
                            sb_c["cR4h"][:, :], start=False, stop=True), inc="MT")
                    ap_mt[pc_] = am
                    wait("vector", "MT", am)
                    if dl >= 2:
                        wait("vector", "MT", b_mt[dl - 2])
                    ee = instr("vector", lambda v, ab=ab, slot=slot, pp=pp: v.tensor_copy(
                        esb[:, slot * 1024 + pp * 512:slot * 1024 + (pp + 1) * 512],
                        ab[0:65, 0:512]), inc="MV")
                    e_mv[pc_] = ee
                    pc_ += 1
                bb = psF
                if dl >= 1:
                    wait("tensor", "MV", y_mv[dl - 1])
                else:
                    wait("tensor", "MV", ps_mv0)
                wait("tensor", "MV", e_mv[pc_ - 1])
                es4 = esb[:, slot * 1024:(slot + 1) * 1024].rearrange("p (u j q) -> p u j q", u=4, j=4)
                instr("tensor", lambda t, bb=bb, es4=es4: t.matmul(
                    bb[0:64, 0:256], sb_c["cCO2"][:, :], es4[:, :, 0], start=True, stop=False))
                instr("tensor", lambda t, bb=bb, es4=es4: t.matmul(
                    bb[0:64, 0:256], sb_c["cSI2n"][:, :], es4[:, :, 2], start=False, stop=False))
                instr("tensor", lambda t, bb=bb, es4=es4: t.matmul(
                    bb[0:64, 0:256], sb_c["cCO2s"][:, :], es4[:, :, 1], start=False, stop=False))
                bm = instr("tensor", lambda t, bb=bb, es4=es4: t.matmul(
                    bb[0:64, 0:256], sb_c["cSI2ns"][:, :], es4[:, :, 3], start=False, stop=True), inc="MT")
                b_mt[dl] = bm
                wait("vector", "MT", bm)
                sl8 = dl % 6
                if dl >= 6:
                    wait("vector", "Ty", yst[dl - 6])
                    wait("vector", "Ty2", yst2[dl - 6])
                ye = instr("vector", lambda v, bb=bb, sl8=sl8: v.tensor_copy(
                    ysb[:, sl8 * 256:(sl8 + 1) * 256], bb[0:64, 0:256]), inc="MV")
                y_mv[dl] = ye
                wait("sync", "MV", ye)
                wait("gpsimd", "MV", ye)
                ysv = ysb[:, sl8 * 256:(sl8 + 1) * 256].rearrange("a (b q) -> a b q", b=4)
                yst[dl] = dma("sync", "Ty", iA2v[dl, 0], ysv[0:32])
                yst2[dl] = dma("gpsimd", "Ty2", iA2v[dl, 1], ysv[32:64])
            conv_out_T = yst[max(yst)] if yst else 0
            conv_out_T2 = yst2[max(yst2)] if yst2 else 0
            ckpt("p5")

            wait("gpsimd", "Ty", conv_out_T)
            wait("gpsimd", "Ty2", conv_out_T2)
            a2a2 = instr("gpsimd", lambda g: g.collective_compute(
                "AllToAll", ALU.bypass, replica_groups=[list(range(NCORES))],
                ins=[iA2.tensor.ap().opt()], outs=[oA2.tensor.ap().opt()]), inc="CC")
            wait("gpsimd", "CC", a2a2)
            vcl = dma("gpsimd", "Ga", BC[:, :].rearrange("p (j q) -> p j q", j=NCORES),
                      oA2[:, :, :].rearrange("j p q -> p j q"))

            wait("vector", "Ga", vcl)
            wait("vector", "MV", u_done)
            ypr = instr("vector", lambda v: v.tensor_tensor(BA[:, :], BB[:, :], BC[:, :], ALU.mult), inc="MV")
            wait("gpsimd", "MT", v_mm_done)
            wo_ld = dma("gpsimd", "Gw", wbuf[:, :].rearrange("p (c d) -> p c d", c=NCH),
                        Wo_d.rearrange("(c p) d -> p c d", p=128))
            wait("tensor", "Gw", wo_ld)
            wait("tensor", "MV", ypr)
            evac_sem = "MV"

            BBtile = BB[:, :].rearrange("p (i c q) -> p i c q", i=NTILE, c=NCH)

            def evac_o(m, mtv):
                last = None
                for s in range(4):
                    wait("vector", "MT", mtv)
                    last = instr("vector", lambda v, m=m, s=s: v.tensor_scalar(
                        BBtile[:, 4 * s:4 * s + 4, m, :],
                        PS4[s][:, 0:512], bias_o[:, m:m + 1], None, ALU.add), inc="MV")
                return last

            emit_proj(evac_o)

            res_add = []
            x2lo_t = x2lo.rearrange("(i p) d -> i p d", p=128)
            x2st = {}
            mv0_p7 = n["MV"]
            for i in range(NTILE):
                tb = BC[:, (i % 2) * 1024:(i % 2) * 1024 + 1024]
                lo = lnst if i % 2 == 0 else lob
                wait("sync", "MV", res_add[i - 2] if i >= 2 else mv0_p7)
                src2 = BB[:, i * D:(i + 1) * D]
                tt = dmat("sync", "Tat", tb.rearrange("t (c p) -> t c p", p=128), src2)
                wait("vector", "Tat", tt)
                if i >= 2:
                    wait("vector", "Ls", x2st[i - 2])
                for a, b in ((0, 512), (512, 1024)):
                    instr("vector", lambda v, i=i, a=a, b=b, tb=tb: v.tensor_tensor(
                        stage[:, a:b], B8[:, i * D + a:i * D + b], tb[:, a:b], ALU.add))
                for a, b in ((0, 512), (512, 1024)):
                    instr("vector", lambda v, i=i, a=a, b=b: v.tensor_copy(
                        B8[:, i * D + a:i * D + b], stage[:, a:b]))
                ra = None
                for a, b in ((0, 512), (512, 1024)):
                    ra = instr("vector", lambda v, i=i, a=a, b=b, lo=lo: v.tensor_tensor(
                        lo[:, a:b], stage[:, a:b], B8[:, i * D + a:i * D + b], ALU.subtract), inc="MV")
                res_add.append(ra)
                wait("scalar", "MV", res_add[-1])
                x2st[i] = dma("scalar", "Ls", x2lo_t[i], lo[:, :])

            qtt_pair = {}
            wait("vector", "Ls", n["Ls"])
            BAq = BA[:, :].rearrange("p (c i q) -> p c i q", c=NCH, i=NTILE)
            for pi in range(NTILE // 2):
                iA, iB = 2 * pi, 2 * pi + 1
                xtA = B8[:, iA * D:(iA + 1) * D]
                xtB = B8[:, iB * D:(iB + 1) * D]
                wait("vector", "MV", res_add[iB])
                if pi >= 1:
                    wait("vector", "Tqx", qtt_pair[pi - 1])
                vrec = emit_ln_stats_pair(xtA, xtB)
                instr("vector", lambda v, iA=iA: v.tensor_copy(mstats[:, 2 * iA:2 * iA + 1], agg[:, 0:1]))
                instr("vector", lambda v, iB=iB: v.tensor_copy(mstats[:, 2 * iB:2 * iB + 1], agg[:, 8:9]))
                wait("scalar", "MV", vrec)
                instr("scalar", lambda s, iA=iA: s.activation(
                    mstats[:, 2 * iA + 1:2 * iA + 2], agg[:, 1:2], AF.Sqrt))
                vsq = instr("scalar", lambda s, iB=iB: s.activation(
                    mstats[:, 2 * iB + 1:2 * iB + 2], agg[:, 9:10], AF.Sqrt), inc="MS")
                wait("vector", "MS", vsq)
                vnA = instr("vector", lambda v, xtA=xtA, iA=iA: v.tensor_scalar(
                    lnst[:, :], xtA, mstats[:, 2 * iA:2 * iA + 1], mstats[:, 2 * iA + 1:2 * iA + 2],
                    ALU.subtract, ALU.mult), inc="MV")
                vnB = instr("vector", lambda v, xtB=xtB, iB=iB: v.tensor_scalar(
                    lob[:, :], xtB, mstats[:, 2 * iB:2 * iB + 1], mstats[:, 2 * iB + 1:2 * iB + 2],
                    ALU.subtract, ALU.mult), inc="MV")
                instr("vector", lambda v, iA=iA: v.reduce_max(
                    mstats[:, 32 + iA:33 + iA], lnst[:, :], axis=AX))
                instr("vector", lambda v, iB=iB: v.reduce_max(
                    mstats[:, 32 + iB:33 + iB], lob[:, :], axis=AX), inc="MV")
                wait("sync", "MV", vnA)
                dmat("sync", "Tqx", BAq[:, :, iA, :], lnst[:, :].rearrange("p (c q) -> p c q", q=128))
                wait("sync", "MV", vnB)
                qtt_pair[pi] = dmat("sync", "Tqx", BAq[:, :, iB, :],
                                    lob[:, :].rearrange("p (c q) -> p c q", q=128))
            ln2T_done = qtt_pair[NTILE // 2 - 1]
            wait("vector", "MV", n["MV"])
            instr("vector", lambda v: v.reduce_max(lmax[:, 0:1], mstats[:, 32:48], axis=AX), inc="MV")
            wait("tensor", "MV", n["MV"])
            vtm = instr("tensor", lambda t: t.matmul(
                psE[0:1, 0:128], lmax[:, 0:1], sb_c["cI128"][:, :], start=True, stop=True), inc="MT")
            wait("vector", "MT", vtm)
            vgm = instr("vector", lambda v: v.reduce_max(gsc[0:1, 0:1], psE[0:1, 0:128], axis=AX), inc="MV")
            wait("gpsimd", "MV", vgm)
            mx1 = dma("gpsimd", "Gm", mxi1[:, :], gsc[0:1, 0:1])
            wait("gpsimd", "Gm", mx1)
            ar1 = instr("gpsimd", lambda g: g.collective_compute(
                "AllReduce", ALU.max, replica_groups=[list(range(NCORES))],
                ins=[mxi1.tensor.ap().opt()], outs=[mxo1.tensor.ap().opt()]), inc="CC")
            wait("gpsimd", "CC", ar1)
            mx1b = dma("gpsimd", "Gm", gsc[0:1, 1:2], mxo1[:, :])
            wait("vector", "Gm", mx1b)
            vv = instr("vector", lambda v: v.tensor_scalar(gsc[0:1, 1:2], gsc[0:1, 1:2], 1e-8, None, ALU.max), inc="MV")
            wait("vector", "MV", vv)
            vv = instr("vector", lambda v: v.reciprocal(gsc[0:1, 2:3], gsc[0:1, 1:2]), inc="MV")
            wait("vector", "MV", vv)
            vv = instr("vector", lambda v: v.tensor_scalar(gsc[0:1, 2:3], gsc[0:1, 2:3], 255.0, None, ALU.mult), inc="MV")
            wait("vector", "MV", vv)
            vsx = instr("vector", lambda v: v.tensor_scalar(gsc[0:1, 3:4], gsc[0:1, 1:2], 1.0 / 255.0, None, ALU.mult), inc="MV")
            wait("tensor", "MV", vsx)
            wait("tensor", "MS", n["MS"])
            vbb = instr("tensor", lambda t: t.matmul(
                psE[:, 0:2], sb_c["cOnes"][0:1, :], gsc[0:1, 2:4], start=True, stop=True), inc="MT")
            wait("vector", "MT", vbb)
            vrx = instr("vector", lambda v: v.tensor_copy(rxb[:, 0:2], psE[:, 0:2]), inc="MV")
            wait("vector", "MV", max(vrw1, vrx))
            H1 = ((0, 16), (16, 32))
            for a, b in H1:
                instr("vector", lambda v, a=a, b=b: v.tensor_scalar(
                    sxw1[:, a:b], sw1[:, a:b], rxb[:, 1:2], None, ALU.mult))
            for a, b in H1:
                instr("vector", lambda v, a=a, b=b: v.tensor_tensor(
                    bq1[:, a:b], b1t[:, a:b], rw1[:, a:b], ALU.mult))
            for a, b in H1:
                instr("vector", lambda v, a=a, b=b: v.tensor_scalar(
                    bq1[:, a:b], bq1[:, a:b], rxb[:, 0:1], None, ALU.mult))
            for a, b in H1:
                instr("vector", lambda v, a=a, b=b: v.tensor_scalar(
                    bq1[:, a:b], bq1[:, a:b], MAGIC, MAGIC, ALU.add, ALU.subtract))
            vex = None
            for a, b in H1:
                vex = instr("vector", lambda v, a=a, b=b: v.tensor_tensor(
                    exv1[:, a:b], sxw1[:, a:b], bq1[:, a:b], ALU.mult), inc="MV")
            conv_sc.close()
            attn_sc.close()
            rwb1 = sbuf("rwb1", [128, FF], BF16, scope=ffnA_sc)
            rwb2 = sbuf("rwb2", [128, D], BF16, scope=ffnA_sc)
            wf1 = sbuf("wf1", [128, 4 * 1024], F32, scope=ffnA_sc)
            qw1 = sbuf("qw1", [128, 4 * 1024], BF16, scope=ffnA_sc)
            wp2 = sbuf("wp2", [128, 1024], F32, scope=ffnA_sc)
            qp2 = sbuf("qp2", [128, 1024], BF16, scope=ffnA_sc)
            hst2 = sbuf("hst2", [128, 2048], F32, scope=ffnA_sc)

            wait("gpsimd", "MV", vex)
            wait("scalar", "MV", vex)

            def emit_bcast(rwt, nrows, dst, vgate):
                wait("sync", "MT", n["MT"])
                wait("tensor", "MV", vgate)
                vtr = instr("tensor", lambda t, rwt=rwt, nrows=nrows: t.matmul(
                    psE[0:nrows, 0:128], rwt[:, :], sb_c["cI128"][:, :], start=True, stop=True), inc="MT")
                wait("vector", "MT", vtr)
                vcp = instr("vector", lambda v, rwt=rwt, nrows=nrows: v.tensor_copy(
                    stage[0:nrows, 0:128], psE[0:nrows, 0:128]), inc="MV")
                out = None
                prev = 0
                for h0 in range(0, nrows, 16):
                    rr = min(16, nrows - h0)
                    wait("sync", "MV", max(vcp, prev))
                    row = dma("sync", "Trow",
                              wf1[0:1, 0:rr * 128].rearrange("o (a b) -> o a b", b=128),
                              stage[h0:h0 + rr, 0:128])
                    wait("tensor", "Trow", row)
                    for sI in range((rr * 128) // 512):
                        if prev:
                            wait("tensor", "MV", prev)
                        vbm = instr("tensor", lambda t, sI=sI: t.matmul(
                            psE[:, 0:512], sb_c["cOnes"][0:1, :], wf1[0:1, sI * 512:(sI + 1) * 512],
                            start=True, stop=True), inc="MT")
                        wait("vector", "MT", vbm)
                        out = instr("vector", lambda v, sI=sI, dst=dst, h0=h0: v.tensor_copy(
                            dst[:, h0 * 128 + sI * 512: h0 * 128 + (sI + 1) * 512], psE[:, 0:512]), inc="MV")
                        prev = out
                return out

            wait("sync", "Tat", n["Tat"])
            wait("sync", "Tln", n["Tln"])
            wait("sync", "Tqx", n["Tqx"])
            bc1 = emit_bcast(rw1, NFT, rwb1, max(vrw1, vex))
            ckpt("p8a")
            bc2 = emit_bcast(rw2, NCH, rwb2, max(vrw2, bc1))
            ckpt("p8b")

            wait("vector", "Tqx", ln2T_done)
            wait("vector", "Trow", n["Trow"])
            qdone = None
            for k in range(0, NTILE, 2):
                KK = (k, k + 1)
                wait("vector", "MV", vrx)
                for kk in KK:
                    sbuf_ = stage if kk % 2 == 0 else hstg
                    instr("vector", lambda v, kk=kk, sbuf_=sbuf_: v.tensor_scalar(
                        sbuf_[:, 0:1024], BA[:, kk * 1024:(kk + 1) * 1024],
                        rxb[:, 0:1], MAGIC, ALU.mult, ALU.add))
                for kk in KK:
                    sbuf_ = stage if kk % 2 == 0 else hstg
                    instr("vector", lambda v, sbuf_=sbuf_: v.tensor_scalar(
                        sbuf_[:, 0:1024], sbuf_[:, 0:1024], MAGIC, 0.0, ALU.subtract, ALU.max))
                for kk in KK:
                    sbuf_ = stage if kk % 2 == 0 else hstg
                    qdone = instr("vector", lambda v, kk=kk, sbuf_=sbuf_: v.tensor_scalar(
                        BA[:, kk * 1024:(kk + 1) * 1024], sbuf_[:, 0:1024],
                        255.0, None, ALU.min), inc="MV")
            qxT_done = ln2T_done
            qx_mv = qdone
            ckpt("p8")

            w2T_t = w2T_d.rearrange("(f p) (c q) -> f p c q", p=128, q=128)
            pq_vv = 0
            pq_st = 0
            for fc in range(NFT):
                if pq_st:
                    wait("gpsimd", "Gq", pq_st)
                lv = dma("gpsimd", "Lq2", wp2[:, :].rearrange("p (c q) -> p c q", q=128), w2T_t[fc])
                wait("gpsimd", "Lq2", lv)
                wait("gpsimd", "MV", max(bc2, pq_vv))
                vgq = instr("gpsimd", lambda g: g.tensor_tensor(wp2[:, :], wp2[:, :], rwb2[:, :], ALU.mult), inc="MG")
                wait("vector", "MG", vgq)
                pq_vv = instr("vector", lambda v: v.tensor_scalar(
                    qp2[:, :], wp2[:, :], MAGIC, MAGIC, ALU.add, ALU.subtract), inc="MV")
                wait("gpsimd", "MV", pq_vv)
                pq_st = dma("gpsimd", "Gq",
                            qw2d.rearrange("m p (f q) -> p m f q", q=128)[:, :, fc],
                            qp2[:, :].rearrange("p (m q) -> p m q", q=128))
            w2q_done = pq_st
            ckpt("p8q")

            w1T_t = w1T_d.rearrange("(c p) (t q) -> t p c q", p=128, q=128)
            hst_store = {}
            ff1_vq = {}
            ff1_mg = {}
            ff1_ms = {}
            ff1_mmlast = {}
            PS8B = [psE, psF, psG, psH]
            HSL = [hstg, hst2]
            for ft in range(NFT):
                par = ft % 2
                sl4 = ft % 4
                if ft == 0:
                    wait("scalar", "MV", bc2)
                if ft >= 4:
                    wait("scalar", "MG", ff1_mg[ft - 4])
                lv = dma("scalar", "Lw0" if par == 0 else "Lw1",
                         wf1[:, sl4 * 1024:(sl4 + 1) * 1024].rearrange("p (c q) -> p c q", q=128),
                         w1T_t[ft])
                wait("gpsimd", "Lw0" if par == 0 else "Lw1", lv)
                wait("gpsimd", "MV", bc1)
                if ft >= 4:
                    wait("gpsimd", "MV", ff1_vq[ft - 4])
                for c8 in range(NCH):
                    instr("gpsimd", lambda g, sl4=sl4, c8=c8, ft=ft: g.tensor_tensor(
                        wf1[:, sl4 * 1024 + c8 * 128: sl4 * 1024 + (c8 + 1) * 128],
                        wf1[:, sl4 * 1024 + c8 * 128: sl4 * 1024 + (c8 + 1) * 128],
                        rwb1[:, ft * 128:(ft + 1) * 128], ALU.mult),
                        inc=("MG" if c8 == NCH - 1 else None))
                ff1_mg[ft] = n["MG"]
                wait("vector", "MG", n["MG"])
                if ft >= 4:
                    wait("vector", "MT", ff1_mmlast[ft - 4])
                vq1 = instr("vector", lambda v, sl4=sl4: v.tensor_scalar(
                    qw1[:, sl4 * 1024:(sl4 + 1) * 1024], wf1[:, sl4 * 1024:(sl4 + 1) * 1024],
                    MAGIC, MAGIC, ALU.add, ALU.subtract), inc="MV")
                ff1_vq[ft] = vq1
                wait("tensor", "MV", vq1)
                if ft == 0:
                    wait("tensor", "Tqx", qxT_done)
                    wait("tensor", "MV", qx_mv)
                if ft >= 2:
                    wait("tensor", "MS", ff1_ms[ft - 2])
                PSET = PS4 if par == 0 else PS8B
                for c8 in range(NCH):
                    for s in range(4):
                        instr("tensor", lambda t, sl4=sl4, c8=c8, s=s, PSET=PSET: t.matmul(
                            PSET[s][:, 0:512], qw1[:, sl4 * 1024 + c8 * 128: sl4 * 1024 + (c8 + 1) * 128],
                            BAv[:, c8, s * 512:(s + 1) * 512],
                            start=(c8 == 0), stop=(c8 == NCH - 1)),
                            inc=("MT" if c8 == NCH - 1 else None))
                ff1_mmlast[ft] = n["MT"]
                mtv = n["MT"]
                if ft >= 2:
                    wait("scalar", "Gh", hst_store[ft - 2])
                hsl = HSL[par]
                for s in range(4):
                    wait("scalar", "MT", mtv)
                    instr("scalar", lambda sc, ft=ft, s=s, PSET=PSET, hsl=hsl: sc.activation(
                        hsl[:, s * 512:(s + 1) * 512],
                        PSET[s][:, 0:512], AF.Relu, bias=exv1[:, ft:ft + 1], scale=sxw1[:, ft:ft + 1]),
                        inc=("MS" if s == 3 else None))
                ff1_ms[ft] = n["MS"]
                wait("vector", "MS", n["MS"])
                vmx = instr("vector", lambda v, ft=ft, hsl=hsl: v.reduce_max(
                    mstats[:, ft:ft + 1], hsl[:, 0:NT], axis=AX), inc="MV")
                wait("gpsimd", "MV", vmx)
                hst_store[ft] = dma("gpsimd", "Gh", hdr[ft], hsl[:, 0:NT])

            wait("tensor", "MS", n["MS"])
            wait("vector", "MV", n["MV"])
            instr("vector", lambda v: v.reduce_max(lmax[:, 1:2], mstats[:, 0:32], axis=AX), inc="MV")
            wait("tensor", "MV", n["MV"])
            vtm2 = instr("tensor", lambda t: t.matmul(
                psE[0:1, 0:128], lmax[:, 1:2], sb_c["cI128"][:, :], start=True, stop=True), inc="MT")
            wait("vector", "MT", vtm2)
            vgm2 = instr("vector", lambda v: v.reduce_max(gsc[0:1, 4:5], psE[0:1, 0:128], axis=AX), inc="MV")
            wait("gpsimd", "MV", vgm2)
            mx2 = dma("gpsimd", "Gm", mxi2[:, :], gsc[0:1, 4:5])
            wait("gpsimd", "Gm", mx2)
            ar2 = instr("gpsimd", lambda g: g.collective_compute(
                "AllReduce", ALU.max, replica_groups=[list(range(NCORES))],
                ins=[mxi2.tensor.ap().opt()], outs=[mxo2.tensor.ap().opt()]), inc="CC")
            wait("gpsimd", "CC", ar2)
            mx2b = dma("gpsimd", "Gm", gsc[0:1, 5:6], mxo2[:, :])
            wait("vector", "Gm", mx2b)
            vv = instr("vector", lambda v: v.tensor_scalar(gsc[0:1, 5:6], gsc[0:1, 5:6], 1e-8, None, ALU.max), inc="MV")
            wait("vector", "MV", vv)
            vv = instr("vector", lambda v: v.reciprocal(gsc[0:1, 6:7], gsc[0:1, 5:6]), inc="MV")
            wait("vector", "MV", vv)
            vv = instr("vector", lambda v: v.tensor_scalar(gsc[0:1, 6:7], gsc[0:1, 6:7], 255.0, None, ALU.mult), inc="MV")
            wait("vector", "MV", vv)
            vsh = instr("vector", lambda v: v.tensor_scalar(gsc[0:1, 7:8], gsc[0:1, 5:6], 1.0 / 255.0, None, ALU.mult), inc="MV")
            wait("tensor", "MV", vsh)
            vbb2 = instr("tensor", lambda t: t.matmul(
                psE[:, 0:2], sb_c["cOnes"][0:1, :], gsc[0:1, 6:8], start=True, stop=True), inc="MT")
            wait("vector", "MT", vbb2)
            vrh = instr("vector", lambda v: v.tensor_copy(rxb[:, 2:4], psE[:, 0:2]), inc="MV")
            wait("vector", "Lc", n["Lc"])
            wait("vector", "MV", vrh)
            H2 = ((0, 4), (4, 8))
            for a, b in H2:
                instr("vector", lambda v, a=a, b=b: v.tensor_scalar(
                    shw2[:, a:b], sw2[:, a:b], rxb[:, 3:4], None, ALU.mult))
            for a, b in H2:
                instr("vector", lambda v, a=a, b=b: v.tensor_tensor(
                    bq2[:, a:b], b2t[:, a:b], rw2[:, a:b], ALU.mult))
            for a, b in H2:
                instr("vector", lambda v, a=a, b=b: v.tensor_scalar(
                    bq2[:, a:b], bq2[:, a:b], rxb[:, 2:3], None, ALU.mult))
            for a, b in H2:
                instr("vector", lambda v, a=a, b=b: v.tensor_scalar(
                    bq2[:, a:b], bq2[:, a:b], MAGIC, MAGIC, ALU.add, ALU.subtract))
            vex2 = None
            for a, b in H2:
                vex2 = instr("vector", lambda v, a=a, b=b: v.tensor_tensor(
                    exv2[:, a:b], shw2[:, a:b], bq2[:, a:b], ALU.mult), inc="MV")
            ckpt("p9")

            ffnA_sc.close()
            qw2m = sbuf("qw2m", [128, 3 * 4096], BF16, scope=ffnB_sc)
            hbf = sbuf("hbf", [128, 2 * 2048], BF16, scope=ffnB_sc)

            wait("sync", "MV", vex2)
            wait("scalar", "MV", vex2)
            wait("gpsimd", "MV", vex2)
            wait("scalar", "Gh", n["Gh"])
            wait("sync", "Gh", n["Gh"])
            wait("gpsimd", "Gh", n["Gh"])

            qh = BA
            wait("vector", "MT", ff1_mmlast[NFT - 1])
            ffn_ev = []
            qhl = 0
            qhl_hist = []
            qw_mt = {}
            qwld = {}
            s2_mt = {}
            for s2 in range(8):
                qs = s2 % 2
                if s2 >= 2:
                    wait("vector", "MT", s2_mt[s2 - 2])
                for piece in range(4):
                    hs2 = (s2 * 4 + piece) % 2
                    if len(qhl_hist) >= 2:
                        wait("scalar", "MV", qhl_hist[-2])
                    lv = dma("scalar", "Lh",
                             hbf[:, hs2 * 2048:(hs2 + 1) * 2048].rearrange("p (c q) -> p c q", q=256),
                             hdr[piece * 8:(piece + 1) * 8, :, s2 * 256:(s2 + 1) * 256]
                             .rearrange("f p q -> p f q"))
                    wait("vector", "Lh", lv)
                    wait("vector", "MV", vrh)
                    instr("vector", lambda v, hs2=hs2: v.tensor_scalar(
                        stage[:, 0:1024], hbf[:, hs2 * 2048:hs2 * 2048 + 1024],
                        rxb[:, 2:3], MAGIC, ALU.mult, ALU.add))
                    instr("vector", lambda v, hs2=hs2: v.tensor_scalar(
                        hstg[:, 0:1024], hbf[:, hs2 * 2048 + 1024:hs2 * 2048 + 2048],
                        rxb[:, 2:3], MAGIC, ALU.mult, ALU.add))
                    instr("vector", lambda v, qs=qs, piece=piece: v.tensor_scalar(
                        qh[:, qs * 8192 + piece * 2048: qs * 8192 + piece * 2048 + 1024],
                        stage[:, 0:1024], MAGIC, None, ALU.subtract))
                    qhl = instr("vector", lambda v, qs=qs, piece=piece: v.tensor_scalar(
                        qh[:, qs * 8192 + piece * 2048 + 1024: qs * 8192 + piece * 2048 + 2048],
                        hstg[:, 0:1024], MAGIC, None, ALU.subtract), inc="MV")
                    qhl_hist.append(qhl)
                qh_ready = qhl
                for m in range(NCH):
                    ck = s2 * NCH + m
                    sl3 = ck % 3
                    qeng, qsm = (("sync", "Tq"), ("scalar", "Lq"), ("gpsimd", "Gq"))[sl3]
                    wait(qeng, "Gq", w2q_done)
                    key = ck
                    if ck >= 3:
                        wait(qeng, "MT", qw_mt[ck - 3])
                    qwld[key] = dma(qeng, qsm,
                                    qw2m[:, sl3 * 4096:(sl3 + 1) * 4096], qw2d[m])
                    wait("tensor", qsm, qwld[key])
                    wait("tensor", "MV", qh_ready)
                    pp = psE if ck % 2 == 0 else psF
                    if len(ffn_ev) >= 2:
                        wait("tensor", "MV", ffn_ev[-2])
                    for fc in range(NFT):
                        instr("tensor", lambda t, sl3=sl3, fc=fc, pp=pp, qs=qs: t.matmul(
                            pp[:, 0:256], qw2m[:, sl3 * 4096 + fc * 128: sl3 * 4096 + (fc + 1) * 128],
                            qh[:, qs * 8192 + fc * 256:qs * 8192 + (fc + 1) * 256],
                            start=(fc == 0), stop=(fc == NFT - 1)),
                            inc=("MT" if fc == NFT - 1 else None))
                    qw_mt[key] = n["MT"]
                    wait("vector", "MT", n["MT"])
                    wait("vector", "MV", vex2)
                    ffn_ev.append(instr("vector", lambda v, m=m, s2=s2, pp=pp: v.tensor_scalar(
                        BBtile[:, 2 * s2:2 * s2 + 2, m, :], pp[:, 0:256],
                        bq2[:, m:m + 1], shw2[:, m:m + 1], ALU.add, ALU.mult), inc="MV"))
                s2_mt[s2] = n["MT"]

            ckpt("p10")
            wait("sync", "Trow", n["Trow"])
            wait("scalar", "Ls", n["Ls"])
            y_tiled = y_out.rearrange("(i p) d -> i p d", p=128)
            ffn2_all = n["MV"]
            ost = {}
            out_add = 0
            for i in range(NTILE):
                wait("sync", "MV", ffn2_all if i == 0 else out_add)
                src2 = BB[:, i * D:(i + 1) * D]
                dst3 = lnst[:, :].rearrange("t (c p) -> t c p", p=128)
                tt = dmat("sync", "Tout", dst3, src2)
                if i >= 1:
                    wait("scalar", "MV", out_add)
                lol = dma("scalar", "Lh", lob[:, :], x2lo_t[i])
                wait("vector", "Tout", tt)
                wait("vector", "Lh", lol)
                if i >= 2:
                    wait("vector", "O0" if i % 2 == 0 else "O1", ost[i - 2])
                half = i % 2
                obuf = stage if half == 0 else hstg
                for a, b in ((0, 512), (512, 1024)):
                    instr("vector", lambda v, i=i, obuf=obuf, a=a, b=b: v.tensor_tensor(
                        obuf[:, a:b], B8[:, i * D + a:i * D + b], lnst[:, a:b], ALU.add))
                for a, b in ((0, 512), (512, 1024)):
                    out_add = instr("vector", lambda v, obuf=obuf, a=a, b=b: v.tensor_tensor(
                        obuf[:, a:b], obuf[:, a:b], lob[:, a:b], ALU.add), inc="MV")
                wait("scalar", "MV", out_add)
                ost[i] = dma("scalar", "O0" if i % 2 == 0 else "O1", y_tiled[i], obuf[:, 0:1024])
            if DBG:
                dma("scalar", "O0", dbg_oA1[:, :, :], oA1[:, :, :])
                dma("scalar", "O0", dbg_iA2[:, :, :], iA2[:, :, :])
                dma("scalar", "O0", dbg_B8[:, :], B8[:, :])
                dma("scalar", "O0", dbg_qx[:, :], BA[:, :])
                dma("scalar", "O0", dbg_gsc[:, :], gsc[:, :])
                dma("scalar", "O0", dbg_hdr[0], hdr[0])
                dma("scalar", "O0", dbg_hdr[1], hdr[31])
            wait("scalar", "O0", n["O0"])
            wait("scalar", "O1", n["O1"])
            ffnB_sc.close()


        except _SkipRest:
            pass
        for _sc in (ffnB_sc, ffnA_sc, conv_sc, pe_sc, attn_sc):
            try:
                _sc.close()
            except Exception:
                pass

        @block.sync
        def _(e):
            for fn in streams["sync"]:
                fn(e)

        def _audit():
            actual = {nm: 0 for nm in SEMNAMES}
            byid = {SEMS[nm].sem_num if hasattr(SEMS[nm], "sem_num") else None: nm for nm in SEMNAMES}
            names = {}
            for i in nc.all_instructions():
                si = getattr(i, "sync_info", None)
                if si is None:
                    continue
                for u in si.on_update:
                    if u.ant_name in actual:
                        actual[u.ant_name] += u.update_value
            bad = {k: (actual[k], n[k]) for k in SEMNAMES if actual[k] != n[k]}
            assert not bad, f"semaphore accounting mismatch: {bad}"

        @block.scalar
        def _(e):
            for fn in streams["scalar"]:
                fn(e)

        @block.vector
        def _(e):
            for fn in streams["vector"]:
                fn(e)

        @block.tensor
        def _(e):
            for fn in streams["tensor"]:
                fn(e)

        @block.gpsimd
        def _(e):
            for fn in streams["gpsimd"]:
                fn(e)

        _audit()

    return nc


_NC_CACHE = None


def kernel(L0_attn_x, L0_attn_Wu, L0_attn_bu, L0_attn_Wv, L0_attn_bv,
           L0_attn_W1, L0_attn_b1, L0_attn_W2, L0_attn_b2, L0_attn_W3,
           L0_attn_b3, L0_attn_Wo, L0_attn_bo, L0_ffn_w1, L0_ffn_b1,
           L0_ffn_w2, L0_ffn_b2):
    global _NC_CACHE
    x = np.ascontiguousarray(np.asarray(L0_attn_x, np.float32))
    B, Lq, Dm = x.shape
    if _NC_CACHE is None:
        _NC_CACHE = build_nc()
    nc = _NC_CACHE
    f32c = lambda a: np.ascontiguousarray(np.asarray(a, np.float32))
    pos = np.arange(4096, dtype=np.float32)[:, None]
    div = np.exp(np.arange(0, 64, 2, dtype=np.float32) * -(np.log(10000.0) / 64.0))
    pe = np.zeros((4096, 64), np.float32)
    pe[:, 0::2] = np.sin(pos * div)
    pe[:, 1::2] = np.cos(pos * div)
    com = dict(
        Wu=f32c(L0_attn_Wu), Wv=f32c(L0_attn_Wv), Wo=f32c(L0_attn_Wo),
        bu=f32c(L0_attn_bu), bv=f32c(L0_attn_bv), bo=f32c(L0_attn_bo),
        W1=f32c(L0_attn_W1), W2=f32c(L0_attn_W2),
        b1f=f32c(L0_attn_b1), b2f=f32c(L0_attn_b2),
        peT=np.ascontiguousarray(pe.T),
        w1T=f32c(np.asarray(L0_ffn_w1).T), w1n=f32c(L0_ffn_w1), b1v=f32c(L0_ffn_b1),
        w2T=f32c(np.asarray(L0_ffn_w2).T), w2n=f32c(L0_ffn_w2), b2v=f32c(L0_ffn_b2),
    )
    com.update({"d" + k: v for k, v in CONSTS.items()})
    W3 = f32c(L0_attn_W3)
    b3 = f32c(L0_attn_b3)
    in_maps = []
    for c in range(NCORES):
        b, h = c // 2, c % 2
        m = dict(com)
        m["x_in"] = np.ascontiguousarray(x[b, h * NT:(h + 1) * NT, :])
        m["W3s"] = np.ascontiguousarray(W3[:, c * 128:(c + 1) * 128])
        m["b3s"] = np.ascontiguousarray(b3[c * 128:(c + 1) * 128])
        in_maps.append(m)
    res = run_bass_kernel_spmd(nc, in_maps, core_ids=list(range(NCORES)))
    out = np.empty((B, Lq, Dm), np.float32)
    for c in range(NCORES):
        b, h = c // 2, c % 2
        out[b, h * NT:(h + 1) * NT, :] = res.results[c]["y_out"]
    return out


# version 12
# speedup vs baseline: 1.4641x; 1.1259x over previous
import os
os.environ.setdefault("TRN_ALLOW_ICI_ALL_TO_ALL", "1")

import numpy as np
import ml_dtypes
import concourse.bass as bass
import concourse.mybir as mybir
from concourse.bass_utils import run_bass_kernel_spmd
from contextlib import ExitStack

F32 = mybir.dt.float32
BF16 = mybir.dt.bfloat16
AF = mybir.ActivationFunctionType
ALU = mybir.AluOpType
AX = mybir.AxisListType.X

MAGIC = float(np.float32(1.5 * 2 ** 23))
NCORES = 8
NT = 2048
D = 1024
FF = 4096
NTILE = NT // 128
NCH = D // 128
NFT = FF // 128


def _bf(x):
    return np.ascontiguousarray(np.asarray(x, np.float32)).astype(ml_dtypes.bfloat16)


def host_constants():
    k = np.arange(128)
    W1c = np.exp(-2j * np.pi * np.outer(k, k) / 128.0)
    F1r, F1i = W1c.real.astype(np.float32), W1c.imag.astype(np.float32)
    W2c = np.exp(+2j * np.pi * np.outer(k, k) / 128.0)
    F2r, F2i = W2c.real.astype(np.float32), W2c.imag.astype(np.float32)
    cR1h = np.concatenate([F1r[0:64, 0:65], F1i[0:64, 0:65]], axis=1)
    cG1 = F1r[0:64, :]
    cG2 = F1i[0:64, :]
    cG1d = np.concatenate([cG1, cG1], axis=0)
    cG2d = np.concatenate([cG2, cG2], axis=0)
    cR3 = np.concatenate([F2r, F2i], axis=1)
    cR4h = np.concatenate([-F2i, F2r], axis=1)
    a = np.arange(64)
    k65 = np.arange(65)
    alpha = np.where((k65 == 0) | (k65 == 64), 1.0, 2.0)[:, None].astype(np.float32)
    CO2 = (alpha * np.cos(2 * np.pi * np.outer(k65, a) / 128.0) / 16384.0).astype(np.float32)
    SI2n = (-alpha * np.sin(2 * np.pi * np.outer(k65, a) / 128.0) / 16384.0).astype(np.float32)
    CO2s = np.zeros((65, 64), np.float32); CO2s[:, 1:] = CO2[:, :-1]
    SI2ns = np.zeros((65, 64), np.float32); SI2ns[:, 1:] = SI2n[:, :-1]
    pos = np.arange(4096, dtype=np.float32)[:, None]
    div = np.exp(np.arange(0, 64, 2, dtype=np.float32) * -(np.log(10000.0) / 64.0))
    pe = np.zeros((4096, 64), np.float32)
    pe[:, 0::2] = np.sin(pos * div)
    pe[:, 1::2] = np.cos(pos * div)
    dec = np.exp(-0.01 * np.arange(4096, dtype=np.float32)).reshape(1, 4096)
    return dict(
        cR1h=_bf(cR1h), cG1d=_bf(cG1d), cG2d=_bf(cG2d), cR3=_bf(cR3), cR4h=_bf(cR4h),
        cCO2=_bf(CO2), cSI2n=_bf(SI2n), cCO2s=_bf(CO2s), cSI2ns=_bf(SI2ns),
        cG1f=np.ascontiguousarray(cG1.astype(np.float32)),
        cG2f=np.ascontiguousarray(cG2.astype(np.float32)),
        cDec=np.ascontiguousarray(dec.astype(np.float32)),
        cI128=np.eye(128, dtype=np.float32),
        cOnes=np.ones((1, 128), np.float32),
    )


CONSTS = host_constants()


def build_nc():
    nc = bass.Bass("TRN2", target_bir_lowering=False, debug=False, detect_race_conditions=False)
    DI = lambda nm, s, dt=F32: nc.dram_tensor(nm, s, dt, kind="ExternalInput").ap()
    x_in = DI("x_in", [NT, D])
    Wu_d = DI("Wu", [D, D]); Wv_d = DI("Wv", [D, D]); Wo_d = DI("Wo", [D, D])
    bu_d = DI("bu", [D]); bv_d = DI("bv", [D]); bo_d = DI("bo", [D])
    W1_d = DI("W1", [64, 64]); W2_d = DI("W2", [64, 64])
    b1f_d = DI("b1f", [64]); b2f_d = DI("b2f", [64])
    W3s_d = DI("W3s", [64, 128]); b3s_d = DI("b3s", [128])
    peT_d = DI("peT", [64, 4096])
    w1T_d = DI("w1T", [D, FF]); w1n_d = DI("w1n", [FF, D]); b1v_d = DI("b1v", [FF])
    w2T_d = DI("w2T", [FF, D]); w2n_d = DI("w2n", [D, FF]); b2v_d = DI("b2v", [D])
    cns = {k: DI("d" + k, list(v.shape), BF16 if v.dtype == ml_dtypes.bfloat16 else F32)
           for k, v in CONSTS.items()}
    y_out = nc.dram_tensor("y_out", [NT, D], F32, kind="ExternalOutput").ap()
    DBG = os.environ.get("ATH_DEBUG", "") == "1"
    if DBG:
        dbg_oA1 = nc.dram_tensor("dbg_oA1", [NCORES, 128, NT], BF16, kind="ExternalOutput").ap()
        dbg_iA2 = nc.dram_tensor("dbg_iA2", [NCORES, 128, NT], BF16, kind="ExternalOutput").ap()
        dbg_B8 = nc.dram_tensor("dbg_B8", [128, NTILE * D], BF16, kind="ExternalOutput").ap()
        dbg_qx = nc.dram_tensor("dbg_qx", [128, NCH * NT], BF16, kind="ExternalOutput").ap()
        dbg_gsc = nc.dram_tensor("dbg_gsc", [1, 8], F32, kind="ExternalOutput").ap()
        dbg_hdr = nc.dram_tensor("dbg_hdr", [2, 128, NT], BF16, kind="ExternalOutput").ap()
    iA1 = nc.dram_tensor("iA1", [NCORES, 128, NT], BF16).ap()
    oA1 = nc.dram_tensor("oA1", [NCORES, 128, NT], BF16).ap()
    iA2 = nc.dram_tensor("iA2", [NCORES, 128, NT], BF16).ap()
    oA2 = nc.dram_tensor("oA2", [NCORES, 128, NT], BF16).ap()
    mxi1 = nc.dram_tensor("mxi1", [1, 1], F32).ap()
    mxo1 = nc.dram_tensor("mxo1", [1, 1], F32).ap()
    mxi2 = nc.dram_tensor("mxi2", [1, 1], F32).ap()
    mxo2 = nc.dram_tensor("mxo2", [1, 1], F32).ap()
    hdr = nc.dram_tensor("hdr", [NFT, 128, NT], BF16).ap()
    qw2d = nc.dram_tensor("qw2d", [NCH, 128, NFT * 128], BF16).ap()
    hscr = nc.dram_tensor("hscr", [128, 4096], BF16).ap()
    x2lo = nc.dram_tensor("x2lo", [NT, D], BF16).ap()

    with ExitStack() as ctx:
        def sbuf(nm, shape, dt=BF16, scope=None):
            return (scope or ctx).enter_context(nc.sbuf_tensor(nm, shape, dt))

        def psum(nm):
            return ctx.enter_context(nc.psum_tensor(nm, [128, 512], F32))

        def sem(nm):
            return ctx.enter_context(nc.semaphore(nm))

        B8 = sbuf("B8", [128, NTILE * D])
        BA = sbuf("BA", [128, NCH * NT])
        BB = sbuf("BB", [128, NCH * NT])
        BC = sbuf("BC", [128, NCH * NT])
        hstg = sbuf("hstg", [128, NT], F32)
        stage = sbuf("stage", [128, 1024], F32)
        lnst = sbuf("lnst", [128, 1024])
        sb_c = {k: sbuf("s" + k, list(v.shape),
                        BF16 if v.dtype == ml_dtypes.bfloat16 else F32)
                for k, v in CONSTS.items() if k != "cDec"}
        bias_u = sbuf("bias_u", [128, NCH], F32)
        bias_v = sbuf("bias_v", [128, NCH], F32)
        bias_o = sbuf("bias_o", [128, NCH], F32)
        b3sb = sbuf("b3sb", [128, 1], F32)
        b1fb = sbuf("b1fb", [64, 1], F32)
        b2fb = sbuf("b2fb", [64, 1], F32)
        b1t = sbuf("b1t", [128, NFT], F32)
        b2t = sbuf("b2t", [128, NCH], F32)
        rw1 = sbuf("rw1", [128, NFT], F32)
        rw2 = sbuf("rw2", [128, NCH], F32)
        sw1 = sbuf("sw1", [128, NFT], F32)
        sw2 = sbuf("sw2", [128, NCH], F32)
        sxw1 = sbuf("sxw1", [128, NFT], F32)
        bq1 = sbuf("bq1", [128, NFT], F32)
        exv1 = sbuf("exv1", [128, NFT], F32)
        shw2 = sbuf("shw2", [128, NCH], F32)
        bq2 = sbuf("bq2", [128, NCH], F32)
        exv2 = sbuf("exv2", [128, NCH], F32)
        mstats = sbuf("mstats", [128, 3 * NTILE], F32)
        bnb = sbuf("bnb", [128, 24], F32)
        agg = sbuf("agg", [128, 16], F32)
        lmax = sbuf("lmax", [128, 2], F32)
        gsc = sbuf("gsc", [1, 8], F32)
        rxb = sbuf("rxb", [128, 4], F32)
        lob = sbuf("lob", [128, 1024], BF16)
        W1sb = sbuf("W1sb", [64, 64], F32)
        W2sb = sbuf("W2sb", [64, 64], F32)
        W3sb = sbuf("W3sb", [64, 128], F32)
        psA = psum("psA"); psB = psum("psB"); psC = psum("psC"); psD = psum("psD")
        psE = psum("psE"); psF = psum("psF"); psG = psum("psG"); psH = psum("psH")
        PS4 = [psA, psB, psC, psD]
        SEMNAMES = ["CC", "MT", "MV", "MS", "MG",
                    "Gx", "Gw", "Gv0", "Gv1", "Ga", "Gm", "Gh", "Gq",
                    "Lc", "Lp", "Ls", "Lq2", "Lw0", "Lw1", "Lh", "Lq",
                    "Tln", "Tzh", "Tz0", "Tz1", "Ty", "Ty2", "Trow", "Tat", "Tqx", "Tq", "Tout",
                    "O0", "O1"]
        SEMS = {nm: sem(nm) for nm in SEMNAMES}
        attn_sc = ExitStack()
        pe_sc = ExitStack()
        conv_sc = ExitStack()
        ffnA_sc = ExitStack()
        ffnB_sc = ExitStack()
        block = ctx.enter_context(nc.Block())

        n = {nm: 0 for nm in SEMNAMES}
        streams = {e: [] for e in ("tensor", "vector", "scalar", "sync", "gpsimd")}

        class _SkipRest(Exception):
            pass

        _STOP = os.environ.get("ATH_STOP_PHASE", "")

        def ckpt(name):
            if _STOP == name:
                raise _SkipRest()

        def dma(eng, sm, out, in_, slow=False):
            n[sm] += 16
            hs = SEMS[sm]
            streams[eng].append(lambda e, out=out, in_=in_, hs=hs, slow=slow:
                                e.dma_start(out=out, in_=in_,
                                            allow_slow_non_contiguous=slow).then_inc(hs, 16))
            return n[sm]

        def dmat(eng, sm, out, in_):
            n[sm] += 16
            hs = SEMS[sm]
            streams[eng].append(lambda e, out=out, in_=in_, hs=hs:
                                e.dma_start_transpose(out, in_).then_inc(hs, 16))
            return n[sm]

        def wait(eng, sm, v):
            if v <= 0:
                return
            hs = SEMS[sm]
            streams[eng].append(lambda e, hs=hs, v=v: e.wait_ge(hs, v))

        def instr(eng, f, inc=None):
            if inc is None:
                streams[eng].append(f)
                return None
            n[inc] += 1
            hs = SEMS[inc]
            streams[eng].append(lambda e, f=f, hs=hs: f(e).then_inc(hs, 1))
            return n[inc]

        BAv = BA[:, :].rearrange("p (c q) -> p c q", c=NCH)

        try:
            x_tiled = x_in.rearrange("(i p) d -> i p d", p=128)
            xld = [dma("gpsimd", "Gx", B8[:, i * D:(i + 1) * D], x_tiled[i]) for i in range(NTILE)]
            wbuf = sbuf("wbuf", [128, NCH * D], scope=attn_sc)
            wsc = sbuf("wsc", [128, 1024], F32, scope=attn_sc)
            wu_ld = dma("gpsimd", "Gw", wbuf[:, :].rearrange("p (c d) -> p c d", c=NCH),
                        Wu_d.rearrange("(c p) d -> p c d", p=128))
            cld = {k: dma("scalar", "Lc", sb_c[k][:, :], cns[k][:, :]) for k in CONSTS
                   if k != "cDec"}
            dma("scalar", "Lc", bias_u[:, :], bu_d.rearrange("(c p) -> p c", p=128), slow=True)
            dma("scalar", "Lc", bias_v[:, :], bv_d.rearrange("(c p) -> p c", p=128), slow=True)
            v_bo = dma("scalar", "Lc", bias_o[:, :], bo_d.rearrange("(c p) -> p c", p=128), slow=True)
            dma("scalar", "Lc", b3sb[:, :], b3s_d.rearrange("(p o) -> p o", o=1))
            dma("scalar", "Lc", b1fb[:, :], b1f_d.rearrange("(p o) -> p o", o=1))
            dma("scalar", "Lc", b2fb[:, :], b2f_d.rearrange("(p o) -> p o", o=1))
            dma("scalar", "Lc", b1t[:, :], b1v_d.rearrange("(t p) -> p t", p=128), slow=True)
            v_b2t = dma("scalar", "Lc", b2t[:, :], b2v_d.rearrange("(t p) -> p t", p=128), slow=True)
            dma("scalar", "Lc", W1sb[:, :], W1_d[:, :])
            dma("scalar", "Lc", W2sb[:, :], W2_d[:, :])
            lw3 = dma("scalar", "Lc", W3sb[:, :], W3s_d[:, :])

            def emit_ln_stats_pair(xtA, xtB):
                chains = []
                for (xt, c, b) in ((xtA, 0, 0), (xtB, 8, 12)):
                    chains.append([
                        lambda v, xt=xt, b=b: v.bn_stats(bnb[:, b:b + 6], xt[:, 0:512]),
                        lambda v, xt=xt, b=b: v.bn_stats(bnb[:, b + 6:b + 12], xt[:, 512:1024]),
                        lambda v, c=c, b=b: v.reduce_sum(agg[:, c:c + 1], bnb[:, b + 1:b + 12:3], axis=AX),
                        lambda v, c=c, b=b: v.tensor_tensor(agg[:, c + 2:c + 6], bnb[:, b + 1:b + 12:3], bnb[:, b + 1:b + 12:3], ALU.mult),
                        lambda v, c=c: v.reduce_sum(agg[:, c + 1:c + 2], agg[:, c + 2:c + 6], axis=AX),
                        lambda v, c=c, b=b: v.reduce_sum(agg[:, c + 6:c + 7], bnb[:, b + 2:b + 12:3], axis=AX),
                        lambda v, c=c: v.tensor_scalar(agg[:, c:c + 1], agg[:, c:c + 1], 0.25, None, ALU.mult),
                        lambda v, c=c: v.tensor_tensor(agg[:, c + 7:c + 8], agg[:, c:c + 1], agg[:, c:c + 1], ALU.mult),
                        lambda v, c=c: v.tensor_scalar(agg[:, c + 6:c + 7], agg[:, c + 6:c + 7], 1.0 / 1024.0, None, ALU.mult),
                        lambda v, c=c: v.tensor_scalar(agg[:, c + 1:c + 2], agg[:, c + 1:c + 2], 0.25, None, ALU.mult),
                        lambda v, c=c: v.tensor_tensor(agg[:, c + 6:c + 7], agg[:, c + 6:c + 7], agg[:, c + 1:c + 2], ALU.add),
                        lambda v, c=c: v.tensor_tensor(agg[:, c + 6:c + 7], agg[:, c + 6:c + 7], agg[:, c + 7:c + 8], ALU.subtract),
                        lambda v, c=c: v.tensor_scalar(agg[:, c + 6:c + 7], agg[:, c + 6:c + 7], 1e-5, None, ALU.add),
                        lambda v, c=c: v.reciprocal(agg[:, c + 1:c + 2], agg[:, c + 6:c + 7]),
                    ])
                last = None
                for k in range(len(chains[0])):
                    for t in range(2):
                        last = instr("vector", chains[t][k],
                                     inc=("MV" if k == len(chains[0]) - 1 else None))
                return last

            lnst3 = lnst[:, :].rearrange("p (c q) -> p c q", q=128)
            stg3 = lob[:, :].rearrange("p (c q) -> p c q", q=128)
            BAt4 = BA[:, :].rearrange("p (c i q) -> p c i q", c=NCH, i=NTILE)
            prev_tt = 0
            for pi in range(NTILE // 2):
                iA, iB = 2 * pi, 2 * pi + 1
                xtA = B8[:, iA * D:(iA + 1) * D]
                xtB = B8[:, iB * D:(iB + 1) * D]
                wait("vector", "Gx", xld[-1])
                if pi > 0:
                    wait("vector", "Tln", prev_tt)
                vrec = emit_ln_stats_pair(xtA, xtB)
                wait("scalar", "MV", vrec)
                instr("scalar", lambda s: s.activation(agg[:, 2:3], agg[:, 1:2], AF.Sqrt))
                vsq = instr("scalar", lambda s: s.activation(agg[:, 10:11], agg[:, 9:10], AF.Sqrt), inc="MS")
                wait("vector", "MS", vsq)
                vnA = instr("vector", lambda v, xtA=xtA: v.tensor_scalar(
                    lnst[:, :], xtA, agg[:, 0:1], agg[:, 2:3], ALU.subtract, ALU.mult), inc="MV")
                vnB = instr("vector", lambda v, xtB=xtB: v.tensor_scalar(
                    lob[:, :], xtB, agg[:, 8:9], agg[:, 10:11], ALU.subtract, ALU.mult), inc="MV")
                wait("sync", "MV", vnA)
                dmat("sync", "Tln", BAt4[:, :, iA, :], lnst3)
                wait("sync", "MV", vnB)
                prev_tt = dmat("sync", "Tln", BAt4[:, :, iB, :], stg3)
            lnT_all = prev_tt
            ckpt("p1")

            def emit_proj(evac):
                prev_ev = 0
                for m in range(NCH):
                    if prev_ev:
                        wait("tensor", evac_sem, prev_ev)
                    for c in range(NCH):
                        for s in range(4):
                            instr("tensor", lambda t, m=m, c=c, s=s: t.matmul(
                                PS4[s][:, 0:512],
                                wbuf[:, c * D + m * 128: c * D + (m + 1) * 128],
                                BAv[:, c, s * 512:(s + 1) * 512],
                                start=(c == 0), stop=(c == NCH - 1)),
                                inc=("MT" if c == NCH - 1 else None))
                    prev_ev = evac(m, n["MT"])
                return prev_ev

            wait("tensor", "Tln", lnT_all)
            wait("tensor", "Gw", wu_ld)
            wait("vector", "Lc", n["Lc"])
            evac_sem = "MV"

            def evac_u(m, mtv):
                last = None
                for s in range(4):
                    wait("vector", "MT", mtv)
                    last = instr("vector", lambda v, m=m, s=s: v.tensor_scalar(
                        BB[:, m * NT + s * 512: m * NT + (s + 1) * 512],
                        PS4[s][:, 0:512], bias_u[:, m:m + 1], None, ALU.add), inc="MV")
                return last

            emit_proj(evac_u)
            u_mm_done = n["MT"]
            u_done = n["MV"]
            ckpt("p2u")

            wait("gpsimd", "MT", u_mm_done)
            wv_ld = dma("gpsimd", "Gw", wbuf[:, :].rearrange("p (c d) -> p c d", c=NCH),
                        Wv_d.rearrange("(c p) d -> p c d", p=128))
            wait("tensor", "Gw", wv_ld)

            vst = [0, 0]

            def evac_v(m, mtv):
                for s in range(4):
                    half = s % 2
                    wait("scalar", "MT", mtv)
                    wait("scalar", "Gv0" if half == 0 else "Gv1", vst[half])
                    vv = instr("scalar", lambda sc, m=m, s=s, half=half: sc.activation(
                        stage[:, half * 512:(half + 1) * 512], PS4[s][:, 0:512], AF.Identity,
                        bias=bias_v[:, m:m + 1], scale=1.0), inc="MS")
                    wait("gpsimd", "MS", vv)
                    vst[half] = dma("gpsimd", "Gv0" if half == 0 else "Gv1",
                                    iA1[m, :, s * 512:(s + 1) * 512],
                                    stage[:, half * 512:(half + 1) * 512])
                return None

            evac_sem = "MS"

            def evac_v_wrap(m, mtv):
                evac_v(m, mtv)
                return n["MS"]

            wait("scalar", "Lc", n["Lc"])
            emit_proj(evac_v_wrap)
            v_mm_done = n["MT"]
            v_store_done = None
            ckpt("p2")

            wait("gpsimd", "Gv0", vst[0])
            wait("gpsimd", "Gv1", vst[1])
            a2a1 = instr("gpsimd", lambda g: g.collective_compute(
                "AllToAll", ALU.bypass, replica_groups=[list(range(NCORES))],
                ins=[iA1.tensor.ap().opt()], outs=[oA1.tensor.ap().opt()]), inc="CC")
            wait("gpsimd", "CC", a2a1)

            peb = sbuf("peb", [64, 2 * 512], F32, scope=pe_sc)
            decb = sbuf("decb", [1, 4096], F32, scope=pe_sc)
            dma("scalar", "Lc", decb[0:1, :], cns["cDec"][0:1, :])
            g1s = BA[:, 0:8192].bitcast(F32)
            g2s = BA[:, 8192:16384].bitcast(F32)

            wait("tensor", "Lc", n["Lc"])
            prev_ms = 0
            pe_mt = {}
            for s8 in range(8):
                if s8 >= 1:
                    wait("scalar", "MT", pe_mt[s8 - 1])
                lv = dma("scalar", "Lp", peb[0:64, (s8 % 2) * 512:(s8 % 2) * 512 + 512],
                         peT_d[:, s8 * 512:(s8 + 1) * 512])
                wait("tensor", "Lp", lv)
                if prev_ms:
                    wait("tensor", "MS", prev_ms)
                vm = instr("tensor", lambda t, s8=s8: t.matmul(
                    psE[0:64, 0:512], W1sb[0:64, :],
                    peb[0:64, (s8 % 2) * 512:(s8 % 2) * 512 + 512],
                    start=True, stop=True), inc="MT")
                pe_mt[s8] = vm
                wait("scalar", "MT", vm)
                prev_ms = instr("scalar", lambda sc, s8=s8: sc.activation(
                    g1s[0:64, s8 * 512:(s8 + 1) * 512], psE[0:64, 0:512], AF.Identity,
                    bias=b1fb[0:64, 0:1], scale=1.0), inc="MS")

            def emit_gelu(buf):
                last = None
                for hh in range(4):
                    src = buf[0:64, hh * 1024:(hh + 1) * 1024]
                    t1 = hstg[0:64, 0:1024]
                    t2 = hstg[0:64, 1024:2048]
                    vq = instr("scalar", lambda s, src=src, t1=t1: s.activation(t1, src, AF.Square), inc="MS")
                    wait("vector", "MS", vq)
                    for a, b in ((0, 512), (512, 1024)):
                        instr("vector", lambda v, t1=t1, a=a, b=b: v.tensor_scalar(
                            t1[:, a:b], t1[:, a:b], 0.044715, 1.0, ALU.mult, ALU.add))
                    vw = None
                    for a, b in ((0, 512), (512, 1024)):
                        vw = instr("vector", lambda v, t1=t1, src=src, a=a, b=b: v.tensor_tensor(
                            t1[:, a:b], t1[:, a:b], src[:, a:b], ALU.mult), inc="MV")
                    wait("scalar", "MV", vw)
                    vt = instr("scalar", lambda s, t1=t1, t2=t2: s.activation(
                        t2, t1, AF.Tanh, scale=0.7978845608028654), inc="MS")
                    wait("vector", "MS", vt)
                    for a, b in ((0, 512), (512, 1024)):
                        instr("vector", lambda v, t2=t2, a=a, b=b: v.tensor_scalar(
                            t2[:, a:b], t2[:, a:b], 1.0, 0.5, ALU.add, ALU.mult))
                    for a, b in ((0, 512), (512, 1024)):
                        last = instr("vector", lambda v, src=src, t2=t2, a=a, b=b: v.tensor_tensor(
                            src[:, a:b], t2[:, a:b], src[:, a:b], ALU.mult), inc="MV")
                return last

            wait("scalar", "MS", n["MS"])
            vg1 = emit_gelu(g1s)
            wait("tensor", "MV", vg1)
            prev_ms = 0
            for s8 in range(8):
                if prev_ms:
                    wait("tensor", "MS", prev_ms)
                vm = instr("tensor", lambda t, s8=s8: t.matmul(
                    psE[0:64, 0:512], W2sb[0:64, :], g1s[0:64, s8 * 512:(s8 + 1) * 512],
                    start=True, stop=True), inc="MT")
                wait("scalar", "MT", vm)
                prev_ms = instr("scalar", lambda sc, s8=s8: sc.activation(
                    g2s[0:64, s8 * 512:(s8 + 1) * 512], psE[0:64, 0:512], AF.Identity,
                    bias=b2fb[0:64, 0:1], scale=1.0), inc="MS")
            wait("scalar", "MS", n["MS"])
            vg2 = emit_gelu(g2s)
            wait("tensor", "MV", vg2)
            wait("vector", "Gv0", vst[0])
            wait("vector", "Gv1", vst[1])
            prev_hv = 0
            hst_prev = {}
            for s8 in range(8):
                par8 = s8 % 2
                if prev_hv:
                    wait("tensor", "MV", prev_hv)
                instr("tensor", lambda t, s8=s8: t.matmul(
                    psE[:, 0:512], W3sb[0:64, :], g2s[0:64, s8 * 512:(s8 + 1) * 512],
                    start=True, stop=True))
                vdm = instr("tensor", lambda t, s8=s8: t.matmul(
                    psF[:, 0:512], sb_c["cOnes"][0:1, :], decb[0:1, s8 * 512:(s8 + 1) * 512],
                    start=True, stop=True), inc="MT")
                wait("vector", "MT", vdm)
                if s8 >= 2:
                    wait("vector", "Gh", hst_prev[s8 - 2])
                for a, b in ((0, 256), (256, 512)):
                    instr("vector", lambda v, a=a, b=b: v.tensor_scalar(
                        stage[:, a:b], psE[:, a:b], b3sb[:, 0:1], None, ALU.add))
                for a, b in ((0, 256), (256, 512)):
                    prev_hv = instr("vector", lambda v, par8=par8, a=a, b=b: v.tensor_tensor(
                        hstg[:, par8 * 512 + a:par8 * 512 + b], stage[:, a:b], psF[:, a:b], ALU.mult), inc="MV")
                wait("gpsimd", "MV", prev_hv)
                hst_prev[s8] = dma("gpsimd", "Gh", hscr[:, s8 * 512:(s8 + 1) * 512],
                                   hstg[:, par8 * 512:(par8 + 1) * 512])
            h_done = hst_prev[7]
            ckpt("p3h")
            pe_sc.close()

            zh32 = sbuf("zh32", [64, 128 * 64], BF16, scope=conv_sc)
            zv = sbuf("zv", [64, 6 * 256], BF16, scope=conv_sc)
            at_b = sbuf("at_b", [128, 2 * 195], BF16, scope=conv_sc)
            ath32 = sbuf("ath32", [64, 195], F32, scope=conv_sc)
            hb32 = sbuf("hb32", [128, 2 * 260], F32, scope=conv_sc)
            vh32 = sbuf("vh32", [128, 2 * 260], F32, scope=conv_sc)
            tmpb = sbuf("tmpb", [128, 2 * 520], F32, scope=conv_sc)
            ypb = sbuf("ypb", [128, 2 * 260], BF16, scope=conv_sc)
            esb = sbuf("esb", [65, 2 * 1024], BF16, scope=conv_sc)
            ysb = sbuf("ysb", [64, 6 * 256], BF16, scope=conv_sc)

            wait("sync", "Gh", h_done)
            wait("scalar", "Gh", h_done)
            wait("gpsimd", "Gh", h_done)
            wait("sync", "MT", pe_mt[7])
            wait("scalar", "MT", pe_mt[7])
            wait("gpsimd", "MT", pe_mt[7])
            zh3d = zh32[:, :].rearrange("a (c b) -> a c b", b=64)
            hs3d = hscr.rearrange("c (a b) -> a c b", b=64)
            for qe, c0, c1 in (("sync", 0, 32), ("scalar", 32, 64),
                               ("sync", 64, 96), ("scalar", 96, 128)):
                zh_all = dma(qe, "Tzh", zh3d[:, c0:c1], hs3d[:, c0:c1])

            w1n_t = w1n_d.rearrange("(t p) d -> t p d", p=128)
            wait("scalar", "MT", pe_mt[7])
            for ft in range(NFT):
                lv = dma("scalar", "Ls", wsc[:, :], w1n_t[ft])
                wait("vector", "Ls", lv)
                vr = instr("vector", lambda v, ft=ft: v.reduce_max(
                    sw1[:, ft:ft + 1], wsc[:, :], axis=AX, apply_absolute_value=True), inc="MV")
                wait("scalar", "MV", vr)
            w2n_t = w2n_d.rearrange("(t p) f -> t p f", p=128)
            for ct in range(NCH):
                for pc in range(4):
                    lv = dma("scalar", "Ls", wsc[:, :], w2n_t[ct][:, pc * 1024:(pc + 1) * 1024])
                    wait("vector", "Ls", lv)
                    vr = instr("vector", lambda v, ct=ct, pc=pc: v.reduce_max(
                        mstats[:, 4 * ct + pc:4 * ct + pc + 1], wsc[:, :],
                        axis=AX, apply_absolute_value=True), inc="MV")
                    wait("scalar", "MV", vr)
            for ct in range(NCH):
                instr("vector", lambda v, ct=ct: v.reduce_max(
                    sw2[:, ct:ct + 1], mstats[:, 4 * ct:4 * ct + 4], axis=AX))
            instr("vector", lambda v: v.tensor_scalar(sw1[:, :], sw1[:, :], 1e-8, 1.0 / 127.0, ALU.max, ALU.mult))
            instr("vector", lambda v: v.tensor_scalar(sw2[:, :], sw2[:, :], 1e-8, 1.0 / 127.0, ALU.max, ALU.mult))
            vrw1 = instr("vector", lambda v: v.reciprocal(rw1[:, :], sw1[:, :]), inc="MV")
            vrw2 = instr("vector", lambda v: v.reciprocal(rw2[:, :], sw2[:, :]), inc="MV")

            conv_c_mt = {}
            yst = {}
            yst2 = {}
            unit_cfree = {}
            unit_dfree = {}
            dh_mt = {}
            ad_mt = {}
            pr_mg = {}
            hbfree = {}
            yv_mv = {}
            ap_mt = {}
            e_mv = {}
            b_mt = {}
            y_mv = {}
            uc = 0
            pc_ = 0
            ps_ms0 = n["MS"]
            ps_mv0 = n["MV"]
            CP = [psA, psG]
            DPb = [psB, psH]
            DPa = [psC, psE]
            wait("tensor", "Tzh", zh_all)
            wait("tensor", "MS", ps_ms0)
            wait("tensor", "MV", ps_mv0)
            wait("sync", "CC", a2a1)
            wait("scalar", "CC", a2a1)
            oA1v = oA1.rearrange("(b hi) p (ah qq) -> p hi ah b qq", hi=2, qq=64)
            iA2v = iA2.rearrange("(b hi) p (ah qq) -> p hi ah b qq", hi=2, qq=64)
            NCONV = int(os.environ.get("ATH_CONV_N", "128"))
            zld0 = {}
            zld1 = {}

            def issue_zv_loads(d):
                if d >= NCONV:
                    return
                if d >= 6:
                    wait("sync", "MT", conv_c_mt[d - 6])
                    wait("scalar", "MT", conv_c_mt[d - 6])
                zvd = zv[:, (d % 6) * 256:(d % 6 + 1) * 256].rearrange("a (b q) -> a b q", b=4)
                zld0[d] = dma("sync", "Tz0", zvd[0:32], oA1v[d, 0])
                zld1[d] = dma("scalar", "Tz1", zvd[32:64], oA1v[d, 1])

            for d in range(6):
                issue_zv_loads(d)
            for dl in range(NCONV):
                slot = dl % 2
                sl8 = dl % 6
                u0 = 3 * dl
                p0, p1 = 2 * dl, 2 * dl + 1
                s0, s1 = p0 % 2, p1 % 2
                cbh, dbh = CP[u0 % 2], DPb[u0 % 2]
                cb0, db0, da0 = CP[(u0 + 1) % 2], DPb[(u0 + 1) % 2], DPa[(u0 + 1) % 2]
                cb1, db1, da1 = CP[(u0 + 2) % 2], DPb[(u0 + 2) % 2], DPa[(u0 + 2) % 2]
                if u0 >= 2:
                    wait("tensor", *unit_cfree[u0 - 2])
                elif u0 == 0:
                    wait("tensor", "MS", ps_ms0)
                hcm = instr("tensor", lambda t, dl=dl, cbh=cbh: t.matmul(
                    cbh[0:64, 0:130], zh32[:, dl * 64:(dl + 1) * 64], sb_c["cR1h"][:, :],
                    start=True, stop=True), inc="MT")
                wait("scalar", "MT", hcm)
                if dl >= 1:
                    wait("scalar", "MT", dh_mt[dl - 1])
                instr("scalar", lambda s, cbh=cbh: s.activation(
                    ath32[0:64, 65:195], cbh[0:64, 0:130], AF.Identity))
                hae = instr("scalar", lambda s, cbh=cbh: s.activation(
                    ath32[0:64, 0:65], cbh[0:64, 65:130], AF.Identity, scale=-1.0), inc="MS")
                unit_cfree[u0] = ("MS", hae)
                if u0 + 1 >= 2:
                    wait("tensor", *unit_cfree[u0 - 1])
                wait("tensor", "Tz0", zld0[dl])
                wait("tensor", "Tz1", zld1[dl])
                cm0 = instr("tensor", lambda t, cb0=cb0, sl8=sl8: t.matmul(
                    cb0[0:128, 0:130], zv[:, sl8 * 256:sl8 * 256 + 128],
                    sb_c["cR1h"][:, :], start=True, stop=True), inc="MT")
                wait("vector", "MT", cm0)
                if p0 >= 2:
                    wait("vector", "MT", ad_mt[p0 - 2])
                instr("vector", lambda v, cb0=cb0, s0=s0: v.tensor_copy(
                    at_b[:, s0 * 195 + 65:s0 * 195 + 195], cb0[:, 0:130]))
                ae0 = instr("vector", lambda v, cb0=cb0, s0=s0: v.tensor_scalar(
                    at_b[:, s0 * 195:s0 * 195 + 65], cb0[:, 65:130], -1.0, None, ALU.mult), inc="MV")
                unit_cfree[u0 + 1] = ("MV", ae0)
                wait("tensor", *unit_cfree[u0])
                cm1 = instr("tensor", lambda t, cb1=cb1, sl8=sl8: t.matmul(
                    cb1[0:128, 0:130], zv[:, sl8 * 256 + 128:sl8 * 256 + 256],
                    sb_c["cR1h"][:, :], start=True, stop=True), inc="MT")
                conv_c_mt[dl] = cm1
                issue_zv_loads(dl + 6)
                wait("vector", "MT", cm1)
                if p1 >= 2:
                    wait("vector", "MT", ad_mt[p1 - 2])
                instr("vector", lambda v, cb1=cb1, s1=s1: v.tensor_copy(
                    at_b[:, s1 * 195 + 65:s1 * 195 + 195], cb1[:, 0:130]))
                ae1 = instr("vector", lambda v, cb1=cb1, s1=s1: v.tensor_scalar(
                    at_b[:, s1 * 195:s1 * 195 + 65], cb1[:, 65:130], -1.0, None, ALU.mult), inc="MV")
                unit_cfree[u0 + 2] = ("MV", ae1)
                if u0 >= 2:
                    wait("tensor", *unit_dfree[u0 - 2])
                elif u0 == 0:
                    wait("tensor", "MS", ps_ms0)
                wait("tensor", "MS", hae)
                instr("tensor", lambda t, dbh=dbh: t.matmul(
                    dbh[:, 0:130], sb_c["cG1f"][:, :], ath32[0:64, 65:195], start=True, stop=False))
                hdm = instr("tensor", lambda t, dbh=dbh: t.matmul(
                    dbh[:, 0:130], sb_c["cG2f"][:, :], ath32[0:64, 0:130], start=False, stop=True), inc="MT")
                dh_mt[dl] = hdm
                wait("scalar", "MT", hdm)
                if dl >= 2:
                    wait("scalar", "MG", hbfree[dl - 2])
                instr("scalar", lambda s, dbh=dbh, slot=slot: s.activation(
                    hb32[:, slot * 260:slot * 260 + 130], dbh[:, 0:130], AF.Identity))
                hEv = instr("scalar", lambda s, dbh=dbh, slot=slot: s.activation(
                    hb32[:, slot * 260 + 130:slot * 260 + 260], dbh[:, 0:130], AF.Identity), inc="MS")
                unit_dfree[u0] = ("MS", hEv)

                def emit_pair_D(pb, pa, sp, ae, pidx):
                    if u0 + pidx >= 1:
                        wait("tensor", *unit_dfree[u0 + pidx - 1])
                    wait("tensor", "MV", ae)
                    dm = None
                    for sg, bank in ((1, pb), (0, pa)):
                        instr("tensor", lambda t, bank=bank, sp=sp, sg=sg: t.matmul(
                            bank[:, 0:130], sb_c["cG1d"][sg * 64:sg * 64 + 64, :],
                            at_b[sg * 64:sg * 64 + 64, sp * 195 + 65:sp * 195 + 195],
                            start=True, stop=False))
                        dm = instr("tensor", lambda t, bank=bank, sp=sp, sg=sg: t.matmul(
                            bank[:, 0:130], sb_c["cG2d"][sg * 64:sg * 64 + 64, :],
                            at_b[sg * 64:sg * 64 + 64, sp * 195:sp * 195 + 130],
                            start=False, stop=True), inc="MT")
                    return dm

                dm0 = emit_pair_D(db0, da0, s0, ae0, 0)
                ad_mt[p0] = dm0
                dm1 = emit_pair_D(db1, da1, s1, ae1, 1)
                ad_mt[p1] = dm1
                unit_dfree[u0 + 1] = None
                wait("scalar", "MT", dm0)
                if p0 >= 2:
                    wait("scalar", "MG", pr_mg[p0 - 2])
                instr("scalar", lambda s, db0=db0, s0=s0: s.activation(
                    vh32[:, s0 * 260:s0 * 260 + 130], db0[:, 0:130], AF.Identity))
                vE0 = instr("scalar", lambda s, da0=da0, s0=s0: s.activation(
                    vh32[:, s0 * 260 + 130:s0 * 260 + 260], da0[:, 0:130], AF.Identity), inc="MS")
                unit_dfree[u0 + 1] = ("MS", vE0)
                wait("scalar", "MT", dm1)
                if p1 >= 2:
                    wait("scalar", "MG", pr_mg[p1 - 2])
                instr("scalar", lambda s, db1=db1, s1=s1: s.activation(
                    vh32[:, s1 * 260:s1 * 260 + 130], db1[:, 0:130], AF.Identity))
                vE1 = instr("scalar", lambda s, da1=da1, s1=s1: s.activation(
                    vh32[:, s1 * 260 + 130:s1 * 260 + 260], da1[:, 0:130], AF.Identity), inc="MS")
                unit_dfree[u0 + 2] = ("MS", vE1)

                def emit_prod(sp, vE, pidx):
                    wait("gpsimd", "MS", max(vE, hEv))
                    pc = 2 * dl + pidx
                    if pc >= 2:
                        wait("gpsimd", "MV", yv_mv[pc - 2])
                    vv4 = vh32[:, sp * 260:(sp + 1) * 260].rearrange("p (s t c) -> p s t c", s=2, t=2)
                    hb4 = hb32[:, slot * 260:(slot + 1) * 260].rearrange("p (s t c) -> p s t c", s=2, t=2)
                    tp4 = tmpb[:, sp * 520:(sp + 1) * 520].rearrange("p (q s c) -> p q s c", q=4, s=2)
                    instr("gpsimd", lambda g, vv4=vv4, hb4=hb4, tp4=tp4: g.tensor_tensor(
                        tp4[:, 0], vv4[:, :, 0], hb4[:, :, 0], ALU.mult))
                    instr("gpsimd", lambda g, vv4=vv4, hb4=hb4, tp4=tp4: g.tensor_tensor(
                        tp4[:, 1], vv4[:, :, 1], hb4[:, :, 1], ALU.mult))
                    instr("gpsimd", lambda g, vv4=vv4, hb4=hb4, tp4=tp4: g.tensor_tensor(
                        tp4[:, 2], vv4[:, :, 0], hb4[:, :, 1], ALU.mult))
                    pg = instr("gpsimd", lambda g, vv4=vv4, hb4=hb4, tp4=tp4: g.tensor_tensor(
                        tp4[:, 3], vv4[:, :, 1], hb4[:, :, 0], ALU.mult), inc="MG")
                    pr_mg[pc] = pg
                    wait("vector", "MG", pg)
                    if pc >= 2:
                        wait("vector", "MT", ap_mt[pc - 2])
                    yp4 = ypb[:, sp * 260:(sp + 1) * 260].rearrange("p (s t c) -> p s t c", s=2, t=2)
                    instr("vector", lambda v, yp4=yp4, tp4=tp4: v.tensor_tensor(
                        yp4[:, :, 0], tp4[:, 0], tp4[:, 1], ALU.subtract))
                    ym = instr("vector", lambda v, yp4=yp4, tp4=tp4: v.tensor_tensor(
                        yp4[:, :, 1], tp4[:, 2], tp4[:, 3], ALU.add), inc="MV")
                    yv_mv[pc] = ym
                    return pg, ym

                pg0, ym0 = emit_prod(s0, vE0, 0)
                pg1, ym1 = emit_prod(s1, vE1, 1)
                hbfree[dl] = pg1

                def emit_Ap(sp, ym, pc):
                    ab = psD
                    if pc >= 1:
                        wait("tensor", "MV", e_mv[pc - 1])
                    else:
                        wait("tensor", "MS", ps_ms0)
                    wait("tensor", "MV", ym)
                    am = None
                    for sg in range(2):
                        base = sp * 260 + (1 - sg) * 130
                        instr("tensor", lambda t, ab=ab, sg=sg, base=base: t.matmul(
                            ab[0:65, sg * 256:(sg + 1) * 256], ypb[:, base:base + 65],
                            sb_c["cR3"][:, :], start=True, stop=False))
                        am = instr("tensor", lambda t, ab=ab, sg=sg, base=base: t.matmul(
                            ab[0:65, sg * 256:(sg + 1) * 256], ypb[:, base + 65:base + 130],
                            sb_c["cR4h"][:, :], start=False, stop=True), inc="MT")
                    ap_mt[pc] = am
                    wait("vector", "MT", am)
                    if dl >= 2:
                        wait("vector", "MT", b_mt[dl - 2])
                    pp = pc % 2
                    ee = instr("vector", lambda v, ab=ab, slot=slot, pp=pp: v.tensor_copy(
                        esb[:, slot * 1024 + pp * 512:slot * 1024 + (pp + 1) * 512],
                        ab[0:65, 0:512]), inc="MV")
                    e_mv[pc] = ee

                emit_Ap(s0, ym0, p0)
                emit_Ap(s1, ym1, p1)
                pc_ = p1 + 1
                bb = psF
                if dl >= 1:
                    wait("tensor", "MV", y_mv[dl - 1])
                else:
                    wait("tensor", "MV", ps_mv0)
                wait("tensor", "MV", e_mv[p1])
                es4 = esb[:, slot * 1024:(slot + 1) * 1024].rearrange("p (u j q) -> p u j q", u=4, j=4)
                instr("tensor", lambda t, bb=bb, es4=es4: t.matmul(
                    bb[0:64, 0:256], sb_c["cCO2"][:, :], es4[:, :, 0], start=True, stop=False))
                instr("tensor", lambda t, bb=bb, es4=es4: t.matmul(
                    bb[0:64, 0:256], sb_c["cSI2n"][:, :], es4[:, :, 2], start=False, stop=False))
                instr("tensor", lambda t, bb=bb, es4=es4: t.matmul(
                    bb[0:64, 0:256], sb_c["cCO2s"][:, :], es4[:, :, 1], start=False, stop=False))
                bm = instr("tensor", lambda t, bb=bb, es4=es4: t.matmul(
                    bb[0:64, 0:256], sb_c["cSI2ns"][:, :], es4[:, :, 3], start=False, stop=True), inc="MT")
                b_mt[dl] = bm
                wait("vector", "MT", bm)
                if dl >= 6:
                    wait("vector", "Ty", yst[dl - 6])
                    wait("vector", "Ty2", yst2[dl - 6])
                ye = instr("vector", lambda v, bb=bb, sl8=sl8: v.tensor_copy(
                    ysb[:, sl8 * 256:(sl8 + 1) * 256], bb[0:64, 0:256]), inc="MV")
                y_mv[dl] = ye
                wait("sync", "MV", ye)
                wait("gpsimd", "MV", ye)
                ysv = ysb[:, sl8 * 256:(sl8 + 1) * 256].rearrange("a (b q) -> a b q", b=4)
                yst[dl] = dma("sync", "Ty", iA2v[dl, 0], ysv[0:32])
                yst2[dl] = dma("gpsimd", "Ty2", iA2v[dl, 1], ysv[32:64])
            conv_out_T = yst[max(yst)] if yst else 0
            conv_out_T2 = yst2[max(yst2)] if yst2 else 0
            ckpt("p5")

            wait("gpsimd", "Ty", conv_out_T)
            wait("gpsimd", "Ty2", conv_out_T2)
            a2a2 = instr("gpsimd", lambda g: g.collective_compute(
                "AllToAll", ALU.bypass, replica_groups=[list(range(NCORES))],
                ins=[iA2.tensor.ap().opt()], outs=[oA2.tensor.ap().opt()]), inc="CC")
            wait("gpsimd", "CC", a2a2)
            vcl = dma("gpsimd", "Ga", BC[:, :].rearrange("p (j q) -> p j q", j=NCORES),
                      oA2[:, :, :].rearrange("j p q -> p j q"))

            wait("vector", "Ga", vcl)
            wait("vector", "MV", u_done)
            ypr = instr("vector", lambda v: v.tensor_tensor(BA[:, :], BB[:, :], BC[:, :], ALU.mult), inc="MV")
            wait("gpsimd", "MT", v_mm_done)
            wo_ld = dma("gpsimd", "Gw", wbuf[:, :].rearrange("p (c d) -> p c d", c=NCH),
                        Wo_d.rearrange("(c p) d -> p c d", p=128))
            wait("tensor", "Gw", wo_ld)
            wait("tensor", "MV", ypr)
            evac_sem = "MV"

            BBtile = BB[:, :].rearrange("p (i c q) -> p i c q", i=NTILE, c=NCH)

            def evac_o(m, mtv):
                last = None
                for s in range(4):
                    wait("vector", "MT", mtv)
                    last = instr("vector", lambda v, m=m, s=s: v.tensor_scalar(
                        BBtile[:, 4 * s:4 * s + 4, m, :],
                        PS4[s][:, 0:512], bias_o[:, m:m + 1], None, ALU.add), inc="MV")
                return last

            emit_proj(evac_o)

            res_add = []
            x2lo_t = x2lo.rearrange("(i p) d -> i p d", p=128)
            x2st = {}
            mv0_p7 = n["MV"]
            for i in range(NTILE):
                tb = BC[:, (i % 2) * 1024:(i % 2) * 1024 + 1024]
                lo = lnst if i % 2 == 0 else lob
                wait("sync", "MV", res_add[i - 2] if i >= 2 else mv0_p7)
                src2 = BB[:, i * D:(i + 1) * D]
                tt = dmat("sync", "Tat", tb.rearrange("t (c p) -> t c p", p=128), src2)
                wait("vector", "Tat", tt)
                if i >= 2:
                    wait("vector", "Ls", x2st[i - 2])
                for a, b in ((0, 512), (512, 1024)):
                    instr("vector", lambda v, i=i, a=a, b=b, tb=tb: v.tensor_tensor(
                        stage[:, a:b], B8[:, i * D + a:i * D + b], tb[:, a:b], ALU.add))
                for a, b in ((0, 512), (512, 1024)):
                    instr("vector", lambda v, i=i, a=a, b=b: v.tensor_copy(
                        B8[:, i * D + a:i * D + b], stage[:, a:b]))
                ra = None
                for a, b in ((0, 512), (512, 1024)):
                    ra = instr("vector", lambda v, i=i, a=a, b=b, lo=lo: v.tensor_tensor(
                        lo[:, a:b], stage[:, a:b], B8[:, i * D + a:i * D + b], ALU.subtract), inc="MV")
                res_add.append(ra)
                wait("scalar", "MV", res_add[-1])
                x2st[i] = dma("scalar", "Ls", x2lo_t[i], lo[:, :])

            qtt_pair = {}
            wait("vector", "Ls", n["Ls"])
            BAq = BA[:, :].rearrange("p (c i q) -> p c i q", c=NCH, i=NTILE)
            for pi in range(NTILE // 2):
                iA, iB = 2 * pi, 2 * pi + 1
                xtA = B8[:, iA * D:(iA + 1) * D]
                xtB = B8[:, iB * D:(iB + 1) * D]
                wait("vector", "MV", res_add[iB])
                if pi >= 1:
                    wait("vector", "Tqx", qtt_pair[pi - 1])
                vrec = emit_ln_stats_pair(xtA, xtB)
                instr("vector", lambda v, iA=iA: v.tensor_copy(mstats[:, 2 * iA:2 * iA + 1], agg[:, 0:1]))
                instr("vector", lambda v, iB=iB: v.tensor_copy(mstats[:, 2 * iB:2 * iB + 1], agg[:, 8:9]))
                wait("scalar", "MV", vrec)
                instr("scalar", lambda s, iA=iA: s.activation(
                    mstats[:, 2 * iA + 1:2 * iA + 2], agg[:, 1:2], AF.Sqrt))
                vsq = instr("scalar", lambda s, iB=iB: s.activation(
                    mstats[:, 2 * iB + 1:2 * iB + 2], agg[:, 9:10], AF.Sqrt), inc="MS")
                wait("vector", "MS", vsq)
                vnA = instr("vector", lambda v, xtA=xtA, iA=iA: v.tensor_scalar(
                    lnst[:, :], xtA, mstats[:, 2 * iA:2 * iA + 1], mstats[:, 2 * iA + 1:2 * iA + 2],
                    ALU.subtract, ALU.mult), inc="MV")
                vnB = instr("vector", lambda v, xtB=xtB, iB=iB: v.tensor_scalar(
                    lob[:, :], xtB, mstats[:, 2 * iB:2 * iB + 1], mstats[:, 2 * iB + 1:2 * iB + 2],
                    ALU.subtract, ALU.mult), inc="MV")
                instr("vector", lambda v, iA=iA: v.reduce_max(
                    mstats[:, 32 + iA:33 + iA], lnst[:, :], axis=AX))
                instr("vector", lambda v, iB=iB: v.reduce_max(
                    mstats[:, 32 + iB:33 + iB], lob[:, :], axis=AX), inc="MV")
                wait("sync", "MV", vnA)
                dmat("sync", "Tqx", BAq[:, :, iA, :], lnst[:, :].rearrange("p (c q) -> p c q", q=128))
                wait("sync", "MV", vnB)
                qtt_pair[pi] = dmat("sync", "Tqx", BAq[:, :, iB, :],
                                    lob[:, :].rearrange("p (c q) -> p c q", q=128))
            ln2T_done = qtt_pair[NTILE // 2 - 1]
            wait("vector", "MV", n["MV"])
            instr("vector", lambda v: v.reduce_max(lmax[:, 0:1], mstats[:, 32:48], axis=AX), inc="MV")
            wait("tensor", "MV", n["MV"])
            vtm = instr("tensor", lambda t: t.matmul(
                psE[0:1, 0:128], lmax[:, 0:1], sb_c["cI128"][:, :], start=True, stop=True), inc="MT")
            wait("vector", "MT", vtm)
            vgm = instr("vector", lambda v: v.reduce_max(gsc[0:1, 0:1], psE[0:1, 0:128], axis=AX), inc="MV")
            wait("gpsimd", "MV", vgm)
            mx1 = dma("gpsimd", "Gm", mxi1[:, :], gsc[0:1, 0:1])
            wait("gpsimd", "Gm", mx1)
            ar1 = instr("gpsimd", lambda g: g.collective_compute(
                "AllReduce", ALU.max, replica_groups=[list(range(NCORES))],
                ins=[mxi1.tensor.ap().opt()], outs=[mxo1.tensor.ap().opt()]), inc="CC")
            wait("gpsimd", "CC", ar1)
            mx1b = dma("gpsimd", "Gm", gsc[0:1, 1:2], mxo1[:, :])
            wait("vector", "Gm", mx1b)
            vv = instr("vector", lambda v: v.tensor_scalar(gsc[0:1, 1:2], gsc[0:1, 1:2], 1e-8, None, ALU.max), inc="MV")
            wait("vector", "MV", vv)
            vv = instr("vector", lambda v: v.reciprocal(gsc[0:1, 2:3], gsc[0:1, 1:2]), inc="MV")
            wait("vector", "MV", vv)
            vv = instr("vector", lambda v: v.tensor_scalar(gsc[0:1, 2:3], gsc[0:1, 2:3], 255.0, None, ALU.mult), inc="MV")
            wait("vector", "MV", vv)
            vsx = instr("vector", lambda v: v.tensor_scalar(gsc[0:1, 3:4], gsc[0:1, 1:2], 1.0 / 255.0, None, ALU.mult), inc="MV")
            wait("tensor", "MV", vsx)
            wait("tensor", "MS", n["MS"])
            vbb = instr("tensor", lambda t: t.matmul(
                psE[:, 0:2], sb_c["cOnes"][0:1, :], gsc[0:1, 2:4], start=True, stop=True), inc="MT")
            wait("vector", "MT", vbb)
            vrx = instr("vector", lambda v: v.tensor_copy(rxb[:, 0:2], psE[:, 0:2]), inc="MV")
            wait("vector", "MV", max(vrw1, vrx))
            H1 = ((0, 16), (16, 32))
            for a, b in H1:
                instr("vector", lambda v, a=a, b=b: v.tensor_scalar(
                    sxw1[:, a:b], sw1[:, a:b], rxb[:, 1:2], None, ALU.mult))
            for a, b in H1:
                instr("vector", lambda v, a=a, b=b: v.tensor_tensor(
                    bq1[:, a:b], b1t[:, a:b], rw1[:, a:b], ALU.mult))
            for a, b in H1:
                instr("vector", lambda v, a=a, b=b: v.tensor_scalar(
                    bq1[:, a:b], bq1[:, a:b], rxb[:, 0:1], None, ALU.mult))
            for a, b in H1:
                instr("vector", lambda v, a=a, b=b: v.tensor_scalar(
                    bq1[:, a:b], bq1[:, a:b], MAGIC, MAGIC, ALU.add, ALU.subtract))
            vex = None
            for a, b in H1:
                vex = instr("vector", lambda v, a=a, b=b: v.tensor_tensor(
                    exv1[:, a:b], sxw1[:, a:b], bq1[:, a:b], ALU.mult), inc="MV")
            conv_sc.close()
            attn_sc.close()
            rwb1 = sbuf("rwb1", [128, FF], BF16, scope=ffnA_sc)
            rwb2 = sbuf("rwb2", [128, D], BF16, scope=ffnA_sc)
            wf1 = sbuf("wf1", [128, 4 * 1024], F32, scope=ffnA_sc)
            qw1 = sbuf("qw1", [128, 4 * 1024], BF16, scope=ffnA_sc)
            wp2 = sbuf("wp2", [128, 1024], F32, scope=ffnA_sc)
            qp2 = sbuf("qp2", [128, 1024], BF16, scope=ffnA_sc)
            hst2 = sbuf("hst2", [128, 2048], F32, scope=ffnA_sc)

            wait("gpsimd", "MV", vex)
            wait("scalar", "MV", vex)

            def emit_bcast(rwt, nrows, dst, vgate):
                wait("sync", "MT", n["MT"])
                wait("tensor", "MV", vgate)
                vtr = instr("tensor", lambda t, rwt=rwt, nrows=nrows: t.matmul(
                    psE[0:nrows, 0:128], rwt[:, :], sb_c["cI128"][:, :], start=True, stop=True), inc="MT")
                wait("vector", "MT", vtr)
                vcp = instr("vector", lambda v, rwt=rwt, nrows=nrows: v.tensor_copy(
                    stage[0:nrows, 0:128], psE[0:nrows, 0:128]), inc="MV")
                out = None
                prev = 0
                for h0 in range(0, nrows, 16):
                    rr = min(16, nrows - h0)
                    wait("sync", "MV", max(vcp, prev))
                    row = dma("sync", "Trow",
                              wf1[0:1, 0:rr * 128].rearrange("o (a b) -> o a b", b=128),
                              stage[h0:h0 + rr, 0:128])
                    wait("tensor", "Trow", row)
                    for sI in range((rr * 128) // 512):
                        if prev:
                            wait("tensor", "MV", prev)
                        vbm = instr("tensor", lambda t, sI=sI: t.matmul(
                            psE[:, 0:512], sb_c["cOnes"][0:1, :], wf1[0:1, sI * 512:(sI + 1) * 512],
                            start=True, stop=True), inc="MT")
                        wait("vector", "MT", vbm)
                        out = instr("vector", lambda v, sI=sI, dst=dst, h0=h0: v.tensor_copy(
                            dst[:, h0 * 128 + sI * 512: h0 * 128 + (sI + 1) * 512], psE[:, 0:512]), inc="MV")
                        prev = out
                return out

            wait("sync", "Tat", n["Tat"])
            wait("sync", "Tln", n["Tln"])
            wait("sync", "Tqx", n["Tqx"])
            bc1 = emit_bcast(rw1, NFT, rwb1, max(vrw1, vex))
            ckpt("p8a")
            bc2 = emit_bcast(rw2, NCH, rwb2, max(vrw2, bc1))
            ckpt("p8b")

            wait("vector", "Tqx", ln2T_done)
            wait("vector", "Trow", n["Trow"])
            qdone = None
            for k in range(0, NTILE, 2):
                KK = (k, k + 1)
                wait("vector", "MV", vrx)
                for kk in KK:
                    sbuf_ = stage if kk % 2 == 0 else hstg
                    instr("vector", lambda v, kk=kk, sbuf_=sbuf_: v.tensor_scalar(
                        sbuf_[:, 0:1024], BA[:, kk * 1024:(kk + 1) * 1024],
                        rxb[:, 0:1], MAGIC, ALU.mult, ALU.add))
                for kk in KK:
                    sbuf_ = stage if kk % 2 == 0 else hstg
                    instr("vector", lambda v, sbuf_=sbuf_: v.tensor_scalar(
                        sbuf_[:, 0:1024], sbuf_[:, 0:1024], MAGIC, 0.0, ALU.subtract, ALU.max))
                for kk in KK:
                    sbuf_ = stage if kk % 2 == 0 else hstg
                    qdone = instr("vector", lambda v, kk=kk, sbuf_=sbuf_: v.tensor_scalar(
                        BA[:, kk * 1024:(kk + 1) * 1024], sbuf_[:, 0:1024],
                        255.0, None, ALU.min), inc="MV")
            qxT_done = ln2T_done
            qx_mv = qdone
            ckpt("p8")

            w2T_t = w2T_d.rearrange("(f p) (c q) -> f p c q", p=128, q=128)
            pq_vv = 0
            pq_st = 0
            for fc in range(NFT):
                if pq_st:
                    wait("gpsimd", "Gq", pq_st)
                lv = dma("gpsimd", "Lq2", wp2[:, :].rearrange("p (c q) -> p c q", q=128), w2T_t[fc])
                wait("gpsimd", "Lq2", lv)
                wait("gpsimd", "MV", max(bc2, pq_vv))
                vgq = instr("gpsimd", lambda g: g.tensor_tensor(wp2[:, :], wp2[:, :], rwb2[:, :], ALU.mult), inc="MG")
                wait("vector", "MG", vgq)
                pq_vv = instr("vector", lambda v: v.tensor_scalar(
                    qp2[:, :], wp2[:, :], MAGIC, MAGIC, ALU.add, ALU.subtract), inc="MV")
                wait("gpsimd", "MV", pq_vv)
                pq_st = dma("gpsimd", "Gq",
                            qw2d.rearrange("m p (f q) -> p m f q", q=128)[:, :, fc],
                            qp2[:, :].rearrange("p (m q) -> p m q", q=128))
            w2q_done = pq_st
            ckpt("p8q")

            w1T_t = w1T_d.rearrange("(c p) (t q) -> t p c q", p=128, q=128)
            hst_store = {}
            ff1_vq = {}
            ff1_mg = {}
            ff1_ms = {}
            ff1_mmlast = {}
            PS8B = [psE, psF, psG, psH]
            HSL = [hstg, hst2]
            for ft in range(NFT):
                par = ft % 2
                sl4 = ft % 4
                if ft == 0:
                    wait("scalar", "MV", bc2)
                if ft >= 4:
                    wait("scalar", "MG", ff1_mg[ft - 4])
                lv = dma("scalar", "Lw0" if par == 0 else "Lw1",
                         wf1[:, sl4 * 1024:(sl4 + 1) * 1024].rearrange("p (c q) -> p c q", q=128),
                         w1T_t[ft])
                wait("gpsimd", "Lw0" if par == 0 else "Lw1", lv)
                wait("gpsimd", "MV", bc1)
                if ft >= 4:
                    wait("gpsimd", "MV", ff1_vq[ft - 4])
                for c8 in range(NCH):
                    instr("gpsimd", lambda g, sl4=sl4, c8=c8, ft=ft: g.tensor_tensor(
                        wf1[:, sl4 * 1024 + c8 * 128: sl4 * 1024 + (c8 + 1) * 128],
                        wf1[:, sl4 * 1024 + c8 * 128: sl4 * 1024 + (c8 + 1) * 128],
                        rwb1[:, ft * 128:(ft + 1) * 128], ALU.mult),
                        inc=("MG" if c8 == NCH - 1 else None))
                ff1_mg[ft] = n["MG"]
                wait("vector", "MG", n["MG"])
                if ft >= 4:
                    wait("vector", "MT", ff1_mmlast[ft - 4])
                vq1 = instr("vector", lambda v, sl4=sl4: v.tensor_scalar(
                    qw1[:, sl4 * 1024:(sl4 + 1) * 1024], wf1[:, sl4 * 1024:(sl4 + 1) * 1024],
                    MAGIC, MAGIC, ALU.add, ALU.subtract), inc="MV")
                ff1_vq[ft] = vq1
                wait("tensor", "MV", vq1)
                if ft == 0:
                    wait("tensor", "Tqx", qxT_done)
                    wait("tensor", "MV", qx_mv)
                if ft >= 2:
                    wait("tensor", "MS", ff1_ms[ft - 2])
                PSET = PS4 if par == 0 else PS8B
                for c8 in range(NCH):
                    for s in range(4):
                        instr("tensor", lambda t, sl4=sl4, c8=c8, s=s, PSET=PSET: t.matmul(
                            PSET[s][:, 0:512], qw1[:, sl4 * 1024 + c8 * 128: sl4 * 1024 + (c8 + 1) * 128],
                            BAv[:, c8, s * 512:(s + 1) * 512],
                            start=(c8 == 0), stop=(c8 == NCH - 1)),
                            inc=("MT" if c8 == NCH - 1 else None))
                ff1_mmlast[ft] = n["MT"]
                mtv = n["MT"]
                if ft >= 2:
                    wait("scalar", "Gh", hst_store[ft - 2])
                hsl = HSL[par]
                for s in range(4):
                    wait("scalar", "MT", mtv)
                    instr("scalar", lambda sc, ft=ft, s=s, PSET=PSET, hsl=hsl: sc.activation(
                        hsl[:, s * 512:(s + 1) * 512],
                        PSET[s][:, 0:512], AF.Relu, bias=exv1[:, ft:ft + 1], scale=sxw1[:, ft:ft + 1]),
                        inc=("MS" if s == 3 else None))
                ff1_ms[ft] = n["MS"]
                wait("vector", "MS", n["MS"])
                vmx = instr("vector", lambda v, ft=ft, hsl=hsl: v.reduce_max(
                    mstats[:, ft:ft + 1], hsl[:, 0:NT], axis=AX), inc="MV")
                wait("gpsimd", "MV", vmx)
                hst_store[ft] = dma("gpsimd", "Gh", hdr[ft], hsl[:, 0:NT])

            wait("tensor", "MS", n["MS"])
            wait("vector", "MV", n["MV"])
            instr("vector", lambda v: v.reduce_max(lmax[:, 1:2], mstats[:, 0:32], axis=AX), inc="MV")
            wait("tensor", "MV", n["MV"])
            vtm2 = instr("tensor", lambda t: t.matmul(
                psE[0:1, 0:128], lmax[:, 1:2], sb_c["cI128"][:, :], start=True, stop=True), inc="MT")
            wait("vector", "MT", vtm2)
            vgm2 = instr("vector", lambda v: v.reduce_max(gsc[0:1, 4:5], psE[0:1, 0:128], axis=AX), inc="MV")
            wait("gpsimd", "MV", vgm2)
            mx2 = dma("gpsimd", "Gm", mxi2[:, :], gsc[0:1, 4:5])
            wait("gpsimd", "Gm", mx2)
            ar2 = instr("gpsimd", lambda g: g.collective_compute(
                "AllReduce", ALU.max, replica_groups=[list(range(NCORES))],
                ins=[mxi2.tensor.ap().opt()], outs=[mxo2.tensor.ap().opt()]), inc="CC")
            wait("gpsimd", "CC", ar2)
            mx2b = dma("gpsimd", "Gm", gsc[0:1, 5:6], mxo2[:, :])
            wait("vector", "Gm", mx2b)
            vv = instr("vector", lambda v: v.tensor_scalar(gsc[0:1, 5:6], gsc[0:1, 5:6], 1e-8, None, ALU.max), inc="MV")
            wait("vector", "MV", vv)
            vv = instr("vector", lambda v: v.reciprocal(gsc[0:1, 6:7], gsc[0:1, 5:6]), inc="MV")
            wait("vector", "MV", vv)
            vv = instr("vector", lambda v: v.tensor_scalar(gsc[0:1, 6:7], gsc[0:1, 6:7], 255.0, None, ALU.mult), inc="MV")
            wait("vector", "MV", vv)
            vsh = instr("vector", lambda v: v.tensor_scalar(gsc[0:1, 7:8], gsc[0:1, 5:6], 1.0 / 255.0, None, ALU.mult), inc="MV")
            wait("tensor", "MV", vsh)
            vbb2 = instr("tensor", lambda t: t.matmul(
                psE[:, 0:2], sb_c["cOnes"][0:1, :], gsc[0:1, 6:8], start=True, stop=True), inc="MT")
            wait("vector", "MT", vbb2)
            vrh = instr("vector", lambda v: v.tensor_copy(rxb[:, 2:4], psE[:, 0:2]), inc="MV")
            wait("vector", "Lc", n["Lc"])
            wait("vector", "MV", vrh)
            H2 = ((0, 4), (4, 8))
            for a, b in H2:
                instr("vector", lambda v, a=a, b=b: v.tensor_scalar(
                    shw2[:, a:b], sw2[:, a:b], rxb[:, 3:4], None, ALU.mult))
            for a, b in H2:
                instr("vector", lambda v, a=a, b=b: v.tensor_tensor(
                    bq2[:, a:b], b2t[:, a:b], rw2[:, a:b], ALU.mult))
            for a, b in H2:
                instr("vector", lambda v, a=a, b=b: v.tensor_scalar(
                    bq2[:, a:b], bq2[:, a:b], rxb[:, 2:3], None, ALU.mult))
            for a, b in H2:
                instr("vector", lambda v, a=a, b=b: v.tensor_scalar(
                    bq2[:, a:b], bq2[:, a:b], MAGIC, MAGIC, ALU.add, ALU.subtract))
            vex2 = None
            for a, b in H2:
                vex2 = instr("vector", lambda v, a=a, b=b: v.tensor_tensor(
                    exv2[:, a:b], shw2[:, a:b], bq2[:, a:b], ALU.mult), inc="MV")
            ckpt("p9")

            ffnA_sc.close()
            qw2m = sbuf("qw2m", [128, 3 * 4096], BF16, scope=ffnB_sc)
            hbf = sbuf("hbf", [128, 2 * 2048], BF16, scope=ffnB_sc)

            wait("sync", "MV", vex2)
            wait("scalar", "MV", vex2)
            wait("gpsimd", "MV", vex2)
            wait("scalar", "Gh", n["Gh"])
            wait("sync", "Gh", n["Gh"])
            wait("gpsimd", "Gh", n["Gh"])

            qh = BA
            wait("vector", "MT", ff1_mmlast[NFT - 1])
            ffn_ev = []
            qhl = 0
            qhl_hist = []
            qw_mt = {}
            qwld = {}
            s2_mt = {}
            for s2 in range(8):
                qs = s2 % 2
                if s2 >= 2:
                    wait("vector", "MT", s2_mt[s2 - 2])
                for piece in range(4):
                    hs2 = (s2 * 4 + piece) % 2
                    if len(qhl_hist) >= 2:
                        wait("scalar", "MV", qhl_hist[-2])
                    lv = dma("scalar", "Lh",
                             hbf[:, hs2 * 2048:(hs2 + 1) * 2048].rearrange("p (c q) -> p c q", q=256),
                             hdr[piece * 8:(piece + 1) * 8, :, s2 * 256:(s2 + 1) * 256]
                             .rearrange("f p q -> p f q"))
                    wait("vector", "Lh", lv)
                    wait("vector", "MV", vrh)
                    instr("vector", lambda v, hs2=hs2: v.tensor_scalar(
                        stage[:, 0:1024], hbf[:, hs2 * 2048:hs2 * 2048 + 1024],
                        rxb[:, 2:3], MAGIC, ALU.mult, ALU.add))
                    instr("vector", lambda v, hs2=hs2: v.tensor_scalar(
                        hstg[:, 0:1024], hbf[:, hs2 * 2048 + 1024:hs2 * 2048 + 2048],
                        rxb[:, 2:3], MAGIC, ALU.mult, ALU.add))
                    instr("vector", lambda v, qs=qs, piece=piece: v.tensor_scalar(
                        qh[:, qs * 8192 + piece * 2048: qs * 8192 + piece * 2048 + 1024],
                        stage[:, 0:1024], MAGIC, None, ALU.subtract))
                    qhl = instr("vector", lambda v, qs=qs, piece=piece: v.tensor_scalar(
                        qh[:, qs * 8192 + piece * 2048 + 1024: qs * 8192 + piece * 2048 + 2048],
                        hstg[:, 0:1024], MAGIC, None, ALU.subtract), inc="MV")
                    qhl_hist.append(qhl)
                qh_ready = qhl
                for m in range(NCH):
                    ck = s2 * NCH + m
                    sl3 = ck % 3
                    qeng, qsm = (("sync", "Tq"), ("scalar", "Lq"), ("gpsimd", "Gq"))[sl3]
                    wait(qeng, "Gq", w2q_done)
                    key = ck
                    if ck >= 3:
                        wait(qeng, "MT", qw_mt[ck - 3])
                    qwld[key] = dma(qeng, qsm,
                                    qw2m[:, sl3 * 4096:(sl3 + 1) * 4096], qw2d[m])
                    wait("tensor", qsm, qwld[key])
                    wait("tensor", "MV", qh_ready)
                    pp = psE if ck % 2 == 0 else psF
                    if len(ffn_ev) >= 2:
                        wait("tensor", "MV", ffn_ev[-2])
                    for fc in range(NFT):
                        instr("tensor", lambda t, sl3=sl3, fc=fc, pp=pp, qs=qs: t.matmul(
                            pp[:, 0:256], qw2m[:, sl3 * 4096 + fc * 128: sl3 * 4096 + (fc + 1) * 128],
                            qh[:, qs * 8192 + fc * 256:qs * 8192 + (fc + 1) * 256],
                            start=(fc == 0), stop=(fc == NFT - 1)),
                            inc=("MT" if fc == NFT - 1 else None))
                    qw_mt[key] = n["MT"]
                    wait("vector", "MT", n["MT"])
                    wait("vector", "MV", vex2)
                    ffn_ev.append(instr("vector", lambda v, m=m, s2=s2, pp=pp: v.tensor_scalar(
                        BBtile[:, 2 * s2:2 * s2 + 2, m, :], pp[:, 0:256],
                        bq2[:, m:m + 1], shw2[:, m:m + 1], ALU.add, ALU.mult), inc="MV"))
                s2_mt[s2] = n["MT"]

            ckpt("p10")
            wait("sync", "Trow", n["Trow"])
            wait("scalar", "Ls", n["Ls"])
            y_tiled = y_out.rearrange("(i p) d -> i p d", p=128)
            ffn2_all = n["MV"]
            ost = {}
            out_add = 0
            for i in range(NTILE):
                wait("sync", "MV", ffn2_all if i == 0 else out_add)
                src2 = BB[:, i * D:(i + 1) * D]
                dst3 = lnst[:, :].rearrange("t (c p) -> t c p", p=128)
                tt = dmat("sync", "Tout", dst3, src2)
                if i >= 1:
                    wait("scalar", "MV", out_add)
                lol = dma("scalar", "Lh", lob[:, :], x2lo_t[i])
                wait("vector", "Tout", tt)
                wait("vector", "Lh", lol)
                if i >= 2:
                    wait("vector", "O0" if i % 2 == 0 else "O1", ost[i - 2])
                half = i % 2
                obuf = stage if half == 0 else hstg
                for a, b in ((0, 512), (512, 1024)):
                    instr("vector", lambda v, i=i, obuf=obuf, a=a, b=b: v.tensor_tensor(
                        obuf[:, a:b], B8[:, i * D + a:i * D + b], lnst[:, a:b], ALU.add))
                for a, b in ((0, 512), (512, 1024)):
                    out_add = instr("vector", lambda v, obuf=obuf, a=a, b=b: v.tensor_tensor(
                        obuf[:, a:b], obuf[:, a:b], lob[:, a:b], ALU.add), inc="MV")
                wait("scalar", "MV", out_add)
                ost[i] = dma("scalar", "O0" if i % 2 == 0 else "O1", y_tiled[i], obuf[:, 0:1024])
            if DBG:
                dma("scalar", "O0", dbg_oA1[:, :, :], oA1[:, :, :])
                dma("scalar", "O0", dbg_iA2[:, :, :], iA2[:, :, :])
                dma("scalar", "O0", dbg_B8[:, :], B8[:, :])
                dma("scalar", "O0", dbg_qx[:, :], BA[:, :])
                dma("scalar", "O0", dbg_gsc[:, :], gsc[:, :])
                dma("scalar", "O0", dbg_hdr[0], hdr[0])
                dma("scalar", "O0", dbg_hdr[1], hdr[31])
            wait("scalar", "O0", n["O0"])
            wait("scalar", "O1", n["O1"])
            ffnB_sc.close()


        except _SkipRest:
            pass
        for _sc in (ffnB_sc, ffnA_sc, conv_sc, pe_sc, attn_sc):
            try:
                _sc.close()
            except Exception:
                pass

        @block.sync
        def _(e):
            for fn in streams["sync"]:
                fn(e)

        def _audit():
            actual = {nm: 0 for nm in SEMNAMES}
            byid = {SEMS[nm].sem_num if hasattr(SEMS[nm], "sem_num") else None: nm for nm in SEMNAMES}
            names = {}
            for i in nc.all_instructions():
                si = getattr(i, "sync_info", None)
                if si is None:
                    continue
                for u in si.on_update:
                    if u.ant_name in actual:
                        actual[u.ant_name] += u.update_value
            bad = {k: (actual[k], n[k]) for k in SEMNAMES if actual[k] != n[k]}
            assert not bad, f"semaphore accounting mismatch: {bad}"

        @block.scalar
        def _(e):
            for fn in streams["scalar"]:
                fn(e)

        @block.vector
        def _(e):
            for fn in streams["vector"]:
                fn(e)

        @block.tensor
        def _(e):
            for fn in streams["tensor"]:
                fn(e)

        @block.gpsimd
        def _(e):
            for fn in streams["gpsimd"]:
                fn(e)

        _audit()

    return nc


_NC_CACHE = None


def kernel(L0_attn_x, L0_attn_Wu, L0_attn_bu, L0_attn_Wv, L0_attn_bv,
           L0_attn_W1, L0_attn_b1, L0_attn_W2, L0_attn_b2, L0_attn_W3,
           L0_attn_b3, L0_attn_Wo, L0_attn_bo, L0_ffn_w1, L0_ffn_b1,
           L0_ffn_w2, L0_ffn_b2):
    global _NC_CACHE
    x = np.ascontiguousarray(np.asarray(L0_attn_x, np.float32))
    B, Lq, Dm = x.shape
    if _NC_CACHE is None:
        _NC_CACHE = build_nc()
    nc = _NC_CACHE
    f32c = lambda a: np.ascontiguousarray(np.asarray(a, np.float32))
    pos = np.arange(4096, dtype=np.float32)[:, None]
    div = np.exp(np.arange(0, 64, 2, dtype=np.float32) * -(np.log(10000.0) / 64.0))
    pe = np.zeros((4096, 64), np.float32)
    pe[:, 0::2] = np.sin(pos * div)
    pe[:, 1::2] = np.cos(pos * div)
    com = dict(
        Wu=f32c(L0_attn_Wu), Wv=f32c(L0_attn_Wv), Wo=f32c(L0_attn_Wo),
        bu=f32c(L0_attn_bu), bv=f32c(L0_attn_bv), bo=f32c(L0_attn_bo),
        W1=f32c(L0_attn_W1), W2=f32c(L0_attn_W2),
        b1f=f32c(L0_attn_b1), b2f=f32c(L0_attn_b2),
        peT=np.ascontiguousarray(pe.T),
        w1T=f32c(np.asarray(L0_ffn_w1).T), w1n=f32c(L0_ffn_w1), b1v=f32c(L0_ffn_b1),
        w2T=f32c(np.asarray(L0_ffn_w2).T), w2n=f32c(L0_ffn_w2), b2v=f32c(L0_ffn_b2),
    )
    com.update({"d" + k: v for k, v in CONSTS.items()})
    W3 = f32c(L0_attn_W3)
    b3 = f32c(L0_attn_b3)
    in_maps = []
    for c in range(NCORES):
        b, h = c // 2, c % 2
        m = dict(com)
        m["x_in"] = np.ascontiguousarray(x[b, h * NT:(h + 1) * NT, :])
        m["W3s"] = np.ascontiguousarray(W3[:, c * 128:(c + 1) * 128])
        m["b3s"] = np.ascontiguousarray(b3[c * 128:(c + 1) * 128])
        in_maps.append(m)
    res = run_bass_kernel_spmd(nc, in_maps, core_ids=list(range(NCORES)))
    out = np.empty((B, Lq, Dm), np.float32)
    for c in range(NCORES):
        b, h = c // 2, c % 2
        out[b, h * NT:(h + 1) * NT, :] = res.results[c]["y_out"]
    return out
